# version 11
# speedup vs baseline: 2.0026x; 1.0796x over previous
import numpy as np
from contextlib import ExitStack

import concourse.bass as bass
import concourse.mybir as mybir
from concourse.bass_utils import run_bass_kernel_spmd

F32 = mybir.dt.float32
BF16 = mybir.dt.bfloat16
I32 = mybir.dt.int32
AF = mybir.ActivationFunctionType
ALU = mybir.AluOpType
AX = mybir.AxisListType

HK, HV, DK, DV = 4, 8, 128, 128
KEY = HK * DK
VAL = HV * DV
NEXP = 8
EPS6 = 1e-6
EPS5 = 1e-5

COMPUTE_ENGINES = ("tensor", "vector", "scalar", "gpsimd", "sync")


class Res:
    __slots__ = ("name", "w", "readers", "excl")

    def __init__(self, name=""):
        self.name = name
        self.w = None
        self.readers = []
        self.excl = name.startswith("ps_")


class Tok:
    __slots__ = ("sem", "val", "pub")

    def __init__(self, sem, val, pub):
        self.sem = sem
        self.val = val
        self.pub = pub


class _Rec:
    __slots__ = ("emit", "waits", "inc", "q")

    def __init__(self, emit, waits, inc, q=None):
        self.emit = emit
        self.waits = waits
        self.inc = inc
        self.q = q


class KB:
    def __init__(self, nc, es, dma_incs=None):
        self.nc = nc
        self.es = es
        self.sems = {k: es.enter_context(nc.semaphore(k)) for k in COMPUTE_ENGINES}
        self.streams = {e: [] for e in COMPUTE_ENGINES}
        self.counts = {k: 0 for k in COMPUTE_ENGINES}
        self.clocks = {e: {} for e in COMPUTE_ENGINES}
        self.nops = 0
        self.dma_incs = dma_incs
        self.dma_seq = {"dsync": 0, "dgp": 0, "coll": 0}
        self.pl_order = []
        self.slot_sems = {}

    def _slot_sem(self, slot):
        if slot not in self.slot_sems:
            name = "ds%d" % len(self.slot_sems)
            self.sems[name] = self.es.enter_context(self.nc.semaphore(name))
            self.counts[name] = 0
            self.slot_sems[slot] = name
        return self.slot_sems[slot]

    def _issue(self, issue_eng, emit, reads, writes, sem_key, inc_amt, q=None):
        clk = self.clocks[issue_eng]
        need = {}
        toks = []
        for r in reads:
            if r.w is not None:
                toks.append(r.w)
            if r.excl:
                for rt in r.readers:
                    if rt.sem != sem_key:
                        toks.append(rt)
        for r in writes:
            if r.w is not None:
                toks.append(r.w)
            toks.extend(r.readers)
        for t in toks:
            if issue_eng == "tensor" and t.sem == "tensor":
                continue
            if clk.get(t.sem, 0) < t.val and need.get(t.sem, 0) < t.val:
                need[t.sem] = t.val
        waits = []
        for s, v in need.items():
            if clk.get(s, 0) >= v:
                continue
            waits.append((s, v))
        for t in toks:
            if t.sem in need and need[t.sem] == t.val:
                for s2, v2 in t.pub.items():
                    if clk.get(s2, 0) < v2:
                        clk[s2] = v2
        for s, v in need.items():
            if clk.get(s, 0) < v:
                clk[s] = v
        self.counts[sem_key] += inc_amt
        val = self.counts[sem_key]
        pub = dict(clk)
        pub[sem_key] = max(pub.get(sem_key, 0), val)
        tok = Tok(sem_key, val, pub)
        self.streams[issue_eng].append(_Rec(emit, waits, (sem_key, inc_amt), q))
        self.nops += 1
        for r in reads:
            r.readers.append(tok)
        for r in writes:
            r.w = tok
            r.readers = []
        return tok

    def op(self, eng, emit, reads=(), writes=()):
        return self._issue(eng, emit, reads, writes, eng, 1)

    def dma(self, queue, slot, emit, reads=(), writes=()):
        issue_eng = {"dsync": "sync", "dgp": "gpsimd"}[queue]
        i = self.dma_seq[queue]
        self.dma_seq[queue] += 1
        mult = self.dma_incs[queue][i] if self.dma_incs is not None else 1
        if issue_eng == "gpsimd":
            self.pl_order.append(queue)
        sem = self._slot_sem((queue, slot))
        return self._issue(issue_eng, emit, reads, writes, sem, 16 * mult, q=queue)

    def coll(self, emit, reads=(), writes=()):
        i = self.dma_seq["coll"]
        self.dma_seq["coll"] += 1
        mult = self.dma_incs["coll"][i] if self.dma_incs is not None else 1
        self.pl_order.append("coll")
        sem = self._slot_sem(("coll", 0))
        return self._issue("gpsimd", emit, reads, writes, sem, 16 * mult, q="coll")

    def barrier(self):
        toks = [Tok(k, self.counts[k], {k: self.counts[k]})
                for k in list(self.counts) if self.counts[k] > 0]
        for e in COMPUTE_ENGINES:
            clk = self.clocks[e]
            waits = []
            for t in toks:
                if clk.get(t.sem, 0) < t.val:
                    clk[t.sem] = t.val
                    waits.append((t.sem, t.val))
            if waits:
                self.streams[e].append(_Rec(None, waits, None))

    def replay(self, block, sentinels=False):
        sems = self.sems

        def section(eng_name):
            def run(eng):
                for rec in self.streams[eng_name]:
                    for (s, v) in rec.waits:
                        eng.wait_ge(sems[s], v)
                    if rec.emit is not None:
                        inst = rec.emit(eng)
                        inst.then_inc(sems[rec.inc[0]], rec.inc[1])
                        if sentinels and rec.q is not None:
                            eng.nop(nofuse=True)
            return run

        block.tensor(section("tensor"))
        block.vector(section("vector"))
        block.scalar(section("scalar"))
        block.gpsimd(section("gpsimd"))
        block.sync(section("sync"))


def host_constants():
    import ml_dtypes
    bf = ml_dtypes.bfloat16
    c = {}
    c["c_ident_bf"] = np.eye(128, dtype=bf)
    c["c_ident_f32"] = np.eye(128, dtype=np.float32)
    ii, jj = np.indices((128, 128))
    c["c_triu_f32"] = (jj >= ii).astype(bf)
    c["c_striu_f32"] = (jj > ii).astype(bf)
    c["c_stril_f32"] = (jj < ii).astype(bf)
    c["c_triu_bf"] = (jj >= ii).astype(bf)
    c["c_ones_row_bf"] = np.ones((1, 128), dtype=bf)
    c["c_ones_sq_bf"] = np.ones((128, 128), dtype=bf)
    rs = np.zeros((8, 8, 128), dtype=bf)
    for r in range(8):
        rs[r, r, :] = 1
    c["c_rowsel8_bf"] = np.ascontiguousarray(rs.transpose(1, 0, 2).reshape(8, 8 * 128))
    c["c_rowsel8_f32"] = c["c_rowsel8_bf"].astype(np.float32)
    c["c_ineg_bf"] = (-np.eye(128)).astype(bf)
    cs = np.zeros((4, 128, 8), dtype=bf)
    for kh in range(4):
        cs[kh, :, 2 * kh] = 1
        cs[kh, :, 2 * kh + 1] = 1
    c["c_colsel2_bf"] = np.ascontiguousarray(cs.transpose(1, 0, 2).reshape(128, 32))
    c1 = np.zeros((8, 128, 8), dtype=bf)
    for h in range(8):
        c1[h, :, h] = 1
    c["c_colsel1_bf"] = np.ascontiguousarray(c1.transpose(1, 0, 2).reshape(128, 64))
    return c


class Cfg:
    def __init__(self, S=2048, B=2, ncores=8, B_total=16, only=None, debug=False, trunc=None):
        self.debug = debug
        self.trunc = trunc
        self.S = S
        self.B = B
        self.ncores = ncores
        self.B_total = B_total
        self.only = only
        self.T = B * S
        self.NT = self.T // 128
        self.NTB = S // 128
        self.KSEL = S // 2


class P:
    pass


def declare_io(nc, cfg):
    p = P()
    p.cfg = cfg
    d = {}
    d["x"] = nc.dram_tensor("x", [cfg.T, 1024], F32, kind="ExternalInput").ap()
    wshapes = {
        "aw_ln1": [1024], "aw_ln2": [1024], "aW_qkvz": [1024, 3072], "aW_ba": [1024, 16],
        "aconv_w": [2048, 4], "adt_bias": [8], "aA_log": [8], "agate_norm_w": [128],
        "aW_out": [1024, 1024], "aW_router": [1024, 8],
        "aW_gate_e": [8, 1024, 512], "aW_up_e": [8, 1024, 512], "aW_down_e": [8, 512, 1024],
        "fw_gate_w": [1024, 8], "fw_gate_b": [8],
        "fW1": [8, 1024, 256], "fb1": [8, 256], "fW2": [8, 256, 1024], "fb2": [8, 1024],
    }
    for k, s in wshapes.items():
        d[k] = nc.dram_tensor(k, s, F32, kind="ExternalInput").ap()
    import ml_dtypes
    for k, v in host_constants().items():
        dt = BF16 if v.dtype == ml_dtypes.bfloat16 else F32
        d[k] = nc.dram_tensor(k, list(v.shape), dt, kind="ExternalInput").ap()
    if cfg.only in ("l2", "l3"):
        d["pdens"] = nc.dram_tensor("pdens", [cfg.ncores, cfg.S, 8], F32,
                                    kind="ExternalInput").ap()
    if cfg.only in ("l1", "l2"):
        d["pden_out"] = nc.dram_tensor("pden_out", [cfg.S, 8], F32, kind="ExternalOutput").ap()
    if cfg.only == "l2":
        d["r2x"] = nc.dram_tensor("r2x", [cfg.T, 1024], F32, kind="ExternalOutput").ap()
    else:
        d["y"] = nc.dram_tensor("y", [cfg.T, 1024], F32, kind="ExternalOutput").ap()
    if cfg.debug:
        for nm in ("dbg_mv", "dbg_sel", "dbg_route"):
            d[nm] = nc.dram_tensor(nm, [128, cfg.NT * 8], F32, kind="ExternalOutput").ap()
        d["dbg_xnfm"] = nc.dram_tensor("dbg_xnfm", [128, 4 * cfg.T], BF16, kind="ExternalOutput").ap()
    d["r1"] = nc.dram_tensor("r1", [cfg.T, 1024], F32, kind="Internal").ap()
    d["lnffn"] = nc.dram_tensor("lnffn", [cfg.T, 1024], BF16, kind="Internal").ap()
    d["lnbuf"] = nc.dram_tensor("lnbuf", [cfg.T, 1024], BF16, kind="Internal").ap()
    d["hbuf"] = nc.dram_tensor("hbuf", [cfg.T, 1024], F32, kind="Internal").ap()
    d["r2"] = nc.dram_tensor("r2", [cfg.T, 1024], F32, kind="Internal").ap()
    d["x2fm"] = nc.dram_tensor("x2fm", [8, 128, cfg.T], BF16, kind="Internal").ap()
    d["moepart"] = nc.dram_tensor("moepart", [8, 128, cfg.T], BF16, kind="Internal").ap()
    for i in (1, 2):
        d[f"cin{i}"] = nc.dram_tensor(f"cin{i}", [cfg.S, 8], F32, kind="Internal").ap()
        d[f"cout{i}"] = nc.dram_tensor(f"cout{i}", [cfg.S, 8], F32, kind="Internal").ap()
    p.d = d
    return p


def alloc_sbuf(p, nc, es):
    cfg = p.cfg
    sb = lambda n, shape, dt=F32: es.enter_context(nc.sbuf_tensor(n, shape, dt))
    t = {}
    TH = cfg.T // 2
    t["ident_bf"] = sb("ident_bf", [128, 128], BF16)
    t["ident_f32"] = sb("ident_f32", [128, 128], F32)
    t["triu"] = sb("triu", [128, 128], BF16)
    t["striu"] = sb("striu", [128, 128], BF16)
    t["stril"] = sb("stril", [128, 128], BF16)
    t["triu_bf"] = sb("triu_bf", [128, 128], BF16)
    t["ones_row_bf"] = sb("ones_row_bf", [1, 128], BF16)
    t["ones_sq_bf"] = sb("ones_sq_bf", [128, 128], BF16)
    t["rowsel8"] = sb("rowsel8", [8, 1024], BF16)
    t["rowsel8f"] = sb("rowsel8f", [8, 1024], F32)
    t["colsel2"] = sb("colsel2", [128, 32], BF16)
    t["colsel1"] = sb("colsel1", [128, 64], BF16)
    t["inegbf"] = sb("inegbf", [128, 128], BF16)
    t["wg"] = sb("wg", [128, 64], BF16)
    t["wgb_bc"] = sb("wgb_bc", [128, 8], F32)
    t["wgb_row"] = sb("wgb_row", [1, 8], BF16)
    t["b1c"] = sb("b1c", [128, 16], F32)
    t["b1st"] = sb("b1st", [8, 256], F32)
    t["b2r"] = sb("b2r", [8, 1024], BF16)
    t["wba"] = sb("wba", [128, 128], BF16)
    t["wrt"] = sb("wrt", [128, 64], BF16)
    t["convw"] = sb("convw", [128, 64], F32)
    t["wln1c"] = sb("wln1c", [128, 8], F32)
    t["wln2c"] = sb("wln2c", [128, 8], F32)
    t["gnwc"] = sb("gnwc", [128, 1], F32)
    t["hcol8"] = sb("hcol8", [8, 8], F32)
    t["wbuf"] = sb("wbuf", [128, 32768], BF16)
    t["xnfm"] = sb("xnfm", [128, max(8 * TH, 16384)], BF16)
    t["accbuf"] = sb("accbuf", [128, 8240], BF16)
    t["zbuf"] = sb("zbuf", [128, 2048], BF16)
    t["poolF"] = sb("poolF", [128, 2048], F32)
    t["poolB"] = sb("poolB", [128, 6144], BF16)
    t["states"] = sb("states", [128, 1024], BF16)
    t["xt"] = sb("xt", [128, 2 * 1024], F32)
    t["xnb"] = sb("xnb", [128, 2 * 1024], BF16)
    t["xnb2"] = sb("xnb2", [128, 1024], BF16)
    t["sq_scr"] = sb("sq_scr", [128, 1024], F32)
    t["cols"] = sb("cols", [128, 2 * 16], F32)
    t["outt"] = sb("outt", [128, 1024], F32)
    t["xres"] = sb("xres", [128, 1024], F32)
    t["glb"] = sb("glb", [8, 128], BF16)
    t["gl"] = sb("gl", [128, 2 * 8], F32)
    t["m8"] = sb("m8", [128, 8], F32)
    t["zap"] = sb("zap", [128, 8], F32)
    t["mv"] = sb("mv", [128, cfg.NT * 8], F32)
    t["rden"] = sb("rden", [128, cfg.NTB * 8], F32)
    t["route"] = sb("route", [128, cfg.NT * 8], F32)
    t["sel"] = sb("sel", [128, cfg.NT * 8], F32)
    t["rowfm"] = sb("rowfm", [8, cfg.T], BF16)
    t["bloE"] = sb("bloE", [128, 16], F32)
    t["bloM"] = sb("bloM", [128, 16], F32)
    t["bcand"] = sb("bcand", [128, 16], F32)
    t["bpi"] = sb("bpi", [128, 16], I32)
    t["bth"] = sb("bth", [128, 16], F32)
    t["btake"] = sb("btake", [128, 16], F32)
    t["cnt"] = sb("cnt", [128, 16], F32)
    t["gcnt"] = sb("gcnt", [128, 16], F32)
    t["kg"] = sb("kg", [128, 16], F32)
    t["cntbc"] = sb("cntbc", [128, cfg.NT * 8], F32)
    t["eqw"] = sb("eqw", [128, cfg.NT * 8], F32)
    t["cumw"] = sb("cumw", [128, cfg.NT * 8], F32)
    t["totbc"] = sb("totbc", [128, cfg.NT * 8], F32)
    t["cmpw"] = sb("cmpw", [128, cfg.NT * 8], BF16)
    t["eqbf"] = sb("eqbf", [128, cfg.NT * 8], BF16)
    t["h1m"] = sb("h1m", [128, 2 * 256], BF16)
    t["selbc"] = sb("selbc", [128, 256], BF16)
    t["ofm"] = sb("ofm", [128, 2 * 256], BF16)

    t["rowsF"] = sb("rowsF", [8, 6 * 256], F32)
    t["rowsB"] = sb("rowsB", [8, 8 * 256], BF16)
    t["colsT"] = sb("colsT", [128, 128], F32)
    p.t = t
    p.cpool_map = {}
    p.cpool_next = [0]
    t["cpool"] = sb("cpool", [128, 16], F32)
    p.R = {}
    return t


def alloc_psum(p, nc, es):
    ps = {}
    ps["T"] = es.enter_context(nc.psum_tensor("ps_T", [128, 1024], BF16))
    ps["g"] = es.enter_context(nc.psum_tensor("ps_g", [8, 512], F32))
    ps["misc"] = es.enter_context(nc.psum_tensor("ps_misc", [128, 512], F32))
    ps["h1"] = es.enter_context(nc.psum_tensor("ps_h1", [128, 512], F32))
    for i in range(4):
        ps[f"o{i}"] = es.enter_context(nc.psum_tensor(f"ps_o{i}", [128, 512], F32))
    p.ps = ps
    return ps


def R(p, name, n=None):
    if n is None:
        if name not in p.R:
            p.R[name] = Res(name)
        return p.R[name]
    if name not in p.R:
        p.R[name] = [Res(f"{name}[{i}]") for i in range(n)]
    return p.R[name]


def lp(p, fn):
    def wrap(e):
        with p.nc.allow_low_precision(reason="bf16 scale factors"):
            return fn(e)
    return wrap


def ccol(p, kb, val):
    if val in p.cpool_map:
        return p.cpool_map[val]
    i = p.cpool_next[0]
    p.cpool_next[0] += 1
    assert i < 16
    apc = p.t["cpool"][:, i:i + 1]
    rc = R(p, "consts")
    kb.op("gpsimd", lambda e, a=apc, v=val: e.memset(a, v), writes=[rc])
    p.nc.const_aps.register(F32, float(val), apc)
    p.cpool_map[val] = apc
    return apc



def load_constants(p, kb):
    t, d = p.t, p.d
    rc = R(p, "consts")
    pairs = [("ident_bf", "c_ident_bf"), ("ident_f32", "c_ident_f32"),
             ("triu", "c_triu_f32"), ("striu", "c_striu_f32"),
             ("stril", "c_stril_f32"), ("triu_bf", "c_triu_bf"),
             ("ones_row_bf", "c_ones_row_bf"), ("ones_sq_bf", "c_ones_sq_bf"),
             ("rowsel8", "c_rowsel8_bf"), ("rowsel8f", "c_rowsel8_f32"),
             ("inegbf", "c_ineg_bf"),
             ("colsel2", "c_colsel2_bf"), ("colsel1", "c_colsel1_bf")]
    for sbn, dn in pairs:
        kb.dma("dsync", "c", lambda e, a=t[sbn], b=d[dn]: e.dma_start(out=a[:], in_=b[:]),
               writes=[rc])
    kb.dma("dgp", "cw", lambda e: e.dma_start(out=t["wg"][:].rearrange("p (dt e) -> p dt e", e=8),
                                        in_=d["fw_gate_w"].rearrange("(dt p) e -> p dt e", p=128)),
           writes=[rc])
    kb.dma("dgp", "cw", lambda e: e.dma_start(out=t["wgb_row"][:], in_=d["fw_gate_b"][None, :]),
           writes=[rc])
    kb.dma("dsync", "c", lambda e: e.dma_start(out=t["b1st"][:], in_=d["fb1"][:, :]), writes=[rc])
    for it in range(2):
        kb.op("tensor", lambda e, it=it: e.transpose(p.ps["misc"][0:128, 0:8],
                                                     t["b1st"][0:8, it * 128:(it + 1) * 128],
                                                     t["ident_f32"][0:8, 0:8]),
              reads=[rc], writes=[R(p, "ps_misc")])
        kb.op("vector", lambda e, it=it: e.tensor_copy(t["b1c"][:, it * 8:(it + 1) * 8],
                                                       p.ps["misc"][0:128, 0:8]),
              reads=[R(p, "ps_misc")], writes=[rc])
    kb.dma("dgp", "cw", lambda e: e.dma_start(out=t["b2r"][:], in_=d["fb2"][:, :]), writes=[rc])
    pm = R(p, "ps_misc")
    kb.op("tensor", lambda e: e.matmul(p.ps["misc"][0:128, 0:8], t["ones_row_bf"][:],
                                       t["wgb_row"][:], start=True, stop=True),
          reads=[rc], writes=[pm])
    kb.op("vector", lambda e: e.tensor_copy(t["wgb_bc"][:], p.ps["misc"][0:128, 0:8]),
          reads=[pm], writes=[rc])


def load_ffn_weights(p, kb):
    t, d = p.t, p.d
    rw = R(p, "ffn_w")
    for ei in range(NEXP):
        kb.dma("dgp", "fw", lambda e, ei=ei: e.dma_start(
            out=t["wbuf"][:, ei * 2048: (ei + 1) * 2048].rearrange("p (dt c) -> p dt c", c=256),
            in_=d["fW1"][ei].rearrange("(dt p) c -> p dt c", p=128)),
            writes=[rw])
        kb.dma("dgp", "fw", lambda e, ei=ei: e.dma_start(
            out=t["wbuf"][:, 16384 + ei * 2048: 16384 + (ei + 1) * 2048].rearrange(
                "p (it c) -> p it c", c=1024),
            in_=d["fW2"][ei].rearrange("(it p) c -> p it c", p=128)),
            writes=[rw])


def w1_tile(p, e, dt, it):
    c = (e * 8 + dt) * 256 + it * 128
    return p.t["wbuf"][:, c:c + 128]


def w2_tile(p, e, it, dt):
    c = 16384 + (e * 2 + it) * 1024 + dt * 128
    return p.t["wbuf"][:, c:c + 128]


def emit_ln_half(p, kb, src, half, mode="ln", do_gate=None, store_dram=None,
                 zw_col=None, store_fm_dram=False):
    t, cfg, ps = p.t, p.cfg, p.ps
    TH = cfg.T // 2
    NTH = cfg.NT // 2
    rx = R(p, "xt_ring", 2)
    rxnb = R(p, "xnb_ring", 2)
    rcols = R(p, "cols_ring", 2)
    rscr = R(p, "sq_scr")
    rfm = R(p, "xnfm_t", NTH * 8)
    rmv = R(p, "mv")
    rgl = R(p, "gl_ring", 2)
    rglb = R(p, "glb_ring", 2)
    rm8 = R(p, "m8")
    rzap = R(p, "zap")
    rc = R(p, "consts")
    rpT = R(p, "ps_T")
    rpTg = rpT
    rpg = R(p, "ps_g")
    rln = R(p, "ln_dram")
    for tl in range(NTH):
        ti = half * NTH + tl
        ring = tl % 2
        xt = t["xt"][:, ring * 1024:(ring + 1) * 1024]
        kb.dma("dsync", ("x", ring), lambda e, a=xt, ti=ti: e.dma_start(
            out=a, in_=src[ti * 128:(ti + 1) * 128, :]),
            writes=[rx[ring]])
        cl = t["cols"][:, ring * 16:(ring + 1) * 16]
        kb.op("scalar", lambda e, a=cl: e.memzero(a[:, 1:2]), writes=[rcols[ring]])
        kb.op("scalar", lambda e, a=cl, b=xt: e.activation(t["sq_scr"][:], b, AF.Square,
                                                           accum_out=a[:, 1:2]),
              reads=[rx[ring]], writes=[rcols[ring], rscr])
        if mode in ("ln", "lnzrms"):
            kb.op("vector", lambda e, a=cl, b=xt: e.reduce_sum(a[:, 0:1], b, axis=AX.X),
                  reads=[rx[ring]], writes=[rcols[ring]])
            kb.op("vector", lambda e, a=cl: e.tensor_scalar(a[:, 2:3], a[:, 0:1], 1.0 / 1024,
                                                            None, ALU.mult),
                  reads=[rcols[ring]], writes=[rcols[ring]])
            kb.op("vector", lambda e, a=cl: e.tensor_mul(a[:, 4:5], a[:, 2:3], a[:, 2:3]),
                  reads=[rcols[ring]], writes=[rcols[ring]])
            kb.op("vector", lambda e, a=cl: e.tensor_scalar(a[:, 5:6], a[:, 1:2], 1.0 / 1024,
                                                            a[:, 4:5], ALU.mult, ALU.subtract),
                  reads=[rcols[ring]], writes=[rcols[ring]])
            kb.op("scalar", lambda e, a=cl, bc=ccol(p, kb, EPS5): e.activation(
                a[:, 6:7], a[:, 5:6], AF.Sqrt, bias=bc),
                reads=[rcols[ring], rc], writes=[rcols[ring]])
            kb.op("vector", lambda e, a=cl: e.reciprocal(a[:, 3:4], a[:, 6:7]),
                  reads=[rcols[ring]], writes=[rcols[ring]])
        else:
            kb.op("vector", lambda e, a=cl: e.memset(a[:, 2:3], 0.0),
                  writes=[rcols[ring]])
            kb.op("vector", lambda e, a=cl: e.tensor_scalar(a[:, 5:6], a[:, 1:2], 1.0 / 1024,
                                                            None, ALU.mult),
                  reads=[rcols[ring]], writes=[rcols[ring]])
            kb.op("scalar", lambda e, a=cl, bc=ccol(p, kb, EPS6): e.activation(
                a[:, 6:7], a[:, 5:6], AF.Sqrt, bias=bc),
                reads=[rcols[ring], rc], writes=[rcols[ring]])
            kb.op("vector", lambda e, a=cl: e.reciprocal(a[:, 3:4], a[:, 6:7]),
                  reads=[rcols[ring]], writes=[rcols[ring]])
        xnb = t["xnb"][:, ring * 1024:(ring + 1) * 1024]
        kb.op("vector", lambda e, a=xnb, b=xt, c=cl: e.tensor_scalar(
            a, b, c[:, 2:3], c[:, 3:4], ALU.subtract, ALU.mult),
            reads=[rx[ring], rcols[ring]], writes=[rxnb[ring]])
        if store_dram is not None:
            kb.dma("dsync", ("lnst", ring), lambda e, a=xnb, ti=ti: e.dma_start(
                out=store_dram[ti * 128:(ti + 1) * 128, :], in_=a),
                reads=[rxnb[ring]], writes=[rln])
        if mode == "lnzrms":
            kb.op("vector", lambda e, a=cl: e.tensor_mul(a[:, 7:8], a[:, 3:4], a[:, 3:4]),
                  reads=[rcols[ring]], writes=[rcols[ring]])
            kb.op("vector", lambda e, a=cl: e.tensor_mul(a[:, 8:9], a[:, 5:6], a[:, 7:8]),
                  reads=[rcols[ring]], writes=[rcols[ring]])
            kb.op("scalar", lambda e, a=cl, bc=ccol(p, kb, EPS6): e.activation(
                a[:, 9:10], a[:, 8:9], AF.Sqrt, bias=bc),
                reads=[rcols[ring], rc], writes=[rcols[ring]])
            kb.op("vector", lambda e, a=cl: e.reciprocal(a[:, 10:11], a[:, 9:10]),
                  reads=[rcols[ring]], writes=[rcols[ring]])
            xnb2 = t["xnb2"][:, 0:1024]
            rx2 = R(p, "xnb2")
            kb.op("vector", lambda e, a=xnb2, b=xnb, c=cl: e.tensor_scalar(
                a, b, c[:, 10:11], None, ALU.mult),
                reads=[rxnb[ring], rcols[ring]], writes=[rx2])
            tsrc = xnb2
            tread = rx2
        else:
            tsrc = xnb
            tread = rxnb[ring]
        for dt in range(8):
            pT = ps["T"][0:128, (dt % 4) * 128:(dt % 4) * 128 + 128]
            rp = rpT
            kb.op("tensor", lambda e, o=pT, a=tsrc, dt=dt: e.transpose(
                o, a[:, dt * 128:(dt + 1) * 128], t["ident_bf"][:]),
                reads=[tread, rc], writes=[rp])
            dst = t["xnfm"][:, dt * TH + tl * 128: dt * TH + tl * 128 + 128]
            eng = "vector" if dt % 2 == 0 else "scalar"
            if zw_col is None:
                if eng == "vector":
                    kb.op(eng, lambda e, a=dst, b=pT: e.tensor_copy(a, b),
                          reads=[rp], writes=[rfm[tl * 8 + dt]])
                else:
                    kb.op(eng, lambda e, a=dst, b=pT: e.copy(a, b),
                          reads=[rp], writes=[rfm[tl * 8 + dt]])
            else:
                if eng == "vector":
                    kb.op(eng, lambda e, a=dst, b=pT, dt=dt: e.tensor_scalar(
                        a, b, zw_col[:, dt:dt + 1], None, ALU.mult),
                        reads=[rp, rc], writes=[rfm[tl * 8 + dt]])
                else:
                    kb.op(eng, lambda e, a=dst, b=pT, dt=dt: e.mul(a, b, zw_col[:, dt:dt + 1]),
                          reads=[rp, rc], writes=[rfm[tl * 8 + dt]])
        if store_fm_dram:
            kb.dma("dsync", ("fmst", ring), lambda e, tl=tl, ti=ti: e.dma_start(
                out=p.d["x2fm"][:, :, ti * 128:(ti + 1) * 128].transpose([1, 0, 2]),
                in_=t["xnfm"][:, 0:8 * TH].rearrange("p (dt c) -> p dt c", c=TH)[
                    :, :, tl * 128:(tl + 1) * 128]),
                reads=[rfm[tl * 8 + dtq] for dtq in range(8)], writes=[R(p, "x2_dram")])
        if do_gate is not None:
            gw = t["wg"] if do_gate[0] == "ffn" else t["wrt"]
            pg = ps["g"][0:8, (tl % 2) * 128:(tl % 2) * 128 + 128]
            for dt in range(8):
                kb.op("tensor", lambda e, o=pg, dt=dt, tl=tl, gw=gw: e.matmul(
                    o, gw[:, dt * 8:(dt + 1) * 8],
                    t["xnfm"][:, dt * TH + tl * 128: dt * TH + tl * 128 + 128],
                    start=(dt == 0), stop=(dt == 7)),
                    reads=[rfm[tl * 8 + dt], rc], writes=[rpg])
            glb = t["glb"][0:8, 0:128]
            kb.op("vector", lambda e, a=glb, b=pg: e.tensor_copy(a, b),
                  reads=[rpg], writes=[rglb[tl % 2]])
            pT2 = ps["T"][0:128, 512:520]
            kb.op("tensor", lambda e, o=pT2, a=glb: e.transpose(o, a, t["ident_bf"][0:8, 0:8]),
                  reads=[rglb[tl % 2], rc], writes=[rpTg])
            gl = t["gl"][:, (tl % 2) * 8:(tl % 2) * 8 + 8]
            if do_gate[0] == "ffn":
                kb.op("vector", lambda e, a=gl, b=pT2: e.tensor_add(a, b, t["wgb_bc"][:]),
                      reads=[rpTg, rc], writes=[rgl[tl % 2]])
            else:
                kb.op("vector", lambda e, a=gl, b=pT2: e.tensor_copy(a, b),
                      reads=[rpTg], writes=[rgl[tl % 2]])
            cl2 = t["cols"][:, ring * 16 + 11: ring * 16 + 16]
            kb.op("vector", lambda e, a=cl2, b=gl: e.reduce_max(a[:, 0:1], b, axis=AX.X,
                                                                negate=True),
                  reads=[rgl[tl % 2]], writes=[rcols[ring]])
            kb.op("scalar", lambda e, a=gl, c=cl2: e.activation(a, a, AF.Exp, bias=c[:, 0:1]),
                  reads=[rcols[ring], rgl[tl % 2]], writes=[rgl[tl % 2]])
            kb.op("vector", lambda e, a=cl2, b=gl: e.reduce_sum(a[:, 1:2], b, axis=AX.X),
                  reads=[rgl[tl % 2]], writes=[rcols[ring]])
            kb.op("vector", lambda e, a=cl2: e.reciprocal(a[:, 2:3], a[:, 1:2]),
                  reads=[rcols[ring]], writes=[rcols[ring]])
            kb.op("vector", lambda e, a=gl, c=cl2: e.tensor_scalar(a, a, c[:, 2:3], None,
                                                                   ALU.mult),
                  reads=[rcols[ring], rgl[tl % 2]], writes=[rgl[tl % 2]])
            kb.op("vector", lambda e, a=gl: e.max(out=t["m8"][:], in_=a),
                  reads=[rgl[tl % 2]], writes=[rm8])
            kb.op("vector", lambda e: e.memset(t["m8"][:, 2:8], -1.0),
                  reads=[], writes=[rm8])
            kb.op("vector", lambda e, a=gl: e.match_replace(out=t["zap"][:],
                                                            in_to_replace=t["m8"][:, 0:8],
                                                            in_values=a, imm_value=0.0),
                  reads=[rgl[tl % 2], rm8], writes=[rzap])
            mvd = t["mv"][:, ti * 8:(ti + 1) * 8]
            kb.op("vector", lambda e, a=mvd, b=gl: e.tensor_sub(a, b, t["zap"][:]),
                  reads=[rgl[tl % 2], rzap], writes=[rmv])


def emit_retranspose_half(p, kb, src_bf, half):
    t, cfg, ps = p.t, p.cfg, p.ps
    TH = cfg.T // 2
    NTH = cfg.NT // 2
    rxnb = R(p, "xnb_ring", 2)
    rfm = R(p, "xnfm_t", NTH * 8)
    rc = R(p, "consts")
    rpT = R(p, "ps_T")
    for tl in range(NTH):
        ti = half * NTH + tl
        ring = tl % 2
        xnb = t["xnb"][:, ring * 1024:(ring + 1) * 1024]
        kb.dma("dsync", ("x", ring), lambda e, a=xnb, ti=ti: e.dma_start(
            out=a, in_=src_bf[ti * 128:(ti + 1) * 128, :]),
            writes=[rxnb[ring]])
        for dt in range(8):
            pT = ps["T"][0:128, (dt % 4) * 128:(dt % 4) * 128 + 128]
            rp = rpT
            kb.op("tensor", lambda e, o=pT, a=xnb, dt=dt: e.transpose(
                o, a[:, dt * 128:(dt + 1) * 128], t["ident_bf"][:]),
                reads=[rxnb[ring], rc], writes=[rp])
            dst = t["xnfm"][:, dt * TH + tl * 128: dt * TH + tl * 128 + 128]
            eng = "vector" if dt % 2 == 0 else "scalar"
            if eng == "vector":
                kb.op(eng, lambda e, a=dst, b=pT: e.tensor_copy(a, b),
                      reads=[rp], writes=[rfm[tl * 8 + dt]])
            else:
                kb.op(eng, lambda e, a=dst, b=pT: e.copy(a, b),
                      reads=[rp], writes=[rfm[tl * 8 + dt]])


def emit_pden_out(p, kb):
    t, cfg = p.t, p.cfg
    NTB = cfg.NTB
    rmv = R(p, "mv")
    rrden = R(p, "rden")
    pden = t["rden"]
    kb.op("vector", lambda e: e.tensor_add(pden[:, 0:NTB * 8], t["mv"][:, 0:NTB * 8],
                                           t["mv"][:, NTB * 8:2 * NTB * 8]),
          reads=[rmv], writes=[rrden])
    kb.dma("dsync", "pd", lambda e: e.dma_start(
        out=p.d["pden_out"].rearrange("(j p) e -> p j e", p=128),
        in_=pden[:, 0:NTB * 8].rearrange("p (j e) -> p j e", e=8)),
        reads=[rrden], writes=[R(p, "dst_dram")])


def emit_denom_from_inputs(p, kb, call_idx):
    t, cfg = p.t, p.cfg
    NTB = cfg.NTB
    R_ = cfg.ncores
    rrden = R(p, "rden")
    rpdl = R(p, "pdl")
    dstp = t["sq_scr"][:, 0:R_ * NTB * 8]
    rpdl = R(p, "sq_scr")
    kb.dma("dsync", "pd", lambda e: e.dma_start(
        out=dstp.rearrange("p (r j e) -> p r j e", r=R_, e=8),
        in_=p.d["pdens"].rearrange("r (j p) e -> p r j e", p=128)),
        writes=[rpdl])
    pden = t["rden"]
    kb.op("vector", lambda e: e.reduce_sum(
        pden[:, 0:NTB * 8].rearrange("p (j e) -> p j e", e=8),
        dstp.rearrange("p (r j e) -> p j e r", r=R_, e=8),
        axis=AX.X),
        reads=[rpdl], writes=[rrden])
    kb.op("vector", lambda e: e.tensor_scalar(pden[:, 0:NTB * 8], pden[:, 0:NTB * 8], EPS6,
                                              None, ALU.add),
          reads=[rrden], writes=[rrden])
    kb.op("vector", lambda e: e.reciprocal(pden[:, 0:NTB * 8], pden[:, 0:NTB * 8]),
          reads=[rrden], writes=[rrden])


def emit_ffn_route(p, kb, cin, cout):
    t, cfg, ps = p.t, p.cfg, p.ps
    NT, NTB = cfg.NT, cfg.NTB
    NB = cfg.B * 8
    assert cfg.B == 2
    rmv = R(p, "mv")
    rroute = R(p, "route")
    rrden = R(p, "rden")
    rsel = R(p, "sel")
    rselfm = R(p, "rowfm")
    rc = R(p, "consts")
    rcin = R(p, "dram_cin")
    rcout = R(p, "dram_cout")
    rpm = R(p, "ps_misc")
    rbis = R(p, "bis")
    rcnt = R(p, "cnt")
    rcmp = R(p, "cmpw")
    rcntbc = R(p, "cntbc")
    req = R(p, "eqw")
    reqb = R(p, "eqbf")
    rcum = R(p, "cumw")
    rtot = R(p, "totbc")
    pden = t["rden"]
    if cin is not None:
        kb.op("vector", lambda e: e.tensor_add(pden[:, 0:NTB * 8], t["mv"][:, 0:NTB * 8],
                                               t["mv"][:, NTB * 8:2 * NTB * 8]),
              reads=[rmv], writes=[rrden])
        kb.dma("dsync", "cden", lambda e: e.dma_start(
            out=cin.rearrange("(j p) e -> p j e", p=128),
            in_=pden[:, 0:NTB * 8].rearrange("p (j e) -> p j e", e=8)),
            reads=[rrden], writes=[rcin])
        kb.coll(lambda e: e.collective_compute(
            "AllReduce", ALU.add, replica_groups=[list(range(cfg.ncores))],
            ins=[cin[:]], outs=[cout[:]]),
            reads=[rcin], writes=[rcout])
        kb.dma("dsync", "cden", lambda e: e.dma_start(
            out=pden[:, 0:NTB * 8].rearrange("p (j e) -> p j e", e=8),
            in_=cout.rearrange("(j p) e -> p j e", p=128)),
            reads=[rcout], writes=[rrden])
        kb.op("vector", lambda e: e.tensor_scalar(pden[:, 0:NTB * 8], pden[:, 0:NTB * 8], EPS6,
                                                  None, ALU.add),
              reads=[rrden], writes=[rrden])
        kb.op("vector", lambda e: e.reciprocal(pden[:, 0:NTB * 8], pden[:, 0:NTB * 8]),
              reads=[rrden], writes=[rrden])
    cap = float(cfg.B_total)
    kb.op("vector", lambda e: e.scalar_tensor_tensor(
        t["route"][:, 0:NT * 8].rearrange("p (b j e) -> p b j e", b=cfg.B, e=8),
        t["mv"][:, 0:NT * 8].rearrange("p (b j e) -> p b j e", b=cfg.B, e=8),
        cap,
        pden[:, 0:NTB * 8].rearrange("p (j e) -> p j e", e=8).unsqueeze(1).to_broadcast(
            [128, cfg.B, NTB, 8]),
        ALU.mult, ALU.mult),
        reads=[rmv, rrden], writes=[rroute])

    def cmp_against(dst_ap, th_tile, op):
        return lambda e: e.tensor_tensor(
            dst_ap.rearrange("p (b j e) -> p b j e", b=cfg.B, e=8),
            t["route"][:, 0:NT * 8].rearrange("p (b j e) -> p b j e", b=cfg.B, e=8),
            th_tile[:, 0:NB].rearrange("p (b e) -> p b e", e=8).unsqueeze(2).to_broadcast(
                [128, cfg.B, NTB, 8]),
            op)

    def count_into(cnt_tile):
        for c0 in range(0, NT * 8, 512):
            w = min(512, NT * 8 - c0)
            kb.op("tensor", lambda e, c0=c0, w=w: e.matmul(
                ps["misc"][0:128, 0:w], t["ones_sq_bf"][:], t["cmpw"][:, c0:c0 + w],
                start=True, stop=True),
                reads=[rcmp, rc], writes=[rpm])
            kb.op("vector", lambda e, c0=c0, w=w: e.tensor_copy(t["cntbc"][:, c0:c0 + w],
                                                                ps["misc"][0:128, 0:w]),
                  reads=[rpm], writes=[rcntbc])
        kb.op("vector", lambda e: e.reduce_sum(
            cnt_tile[:, 0:NB].rearrange("p (b e) -> p b e", e=8),
            t["cntbc"][:, 0:NT * 8].rearrange("p (b j e) -> p b e j", b=cfg.B, e=8),
            axis=AX.X),
            reads=[rcntbc], writes=[rcnt])

    def theta_from_exp(src_tile):
        kb.op("vector", lambda e: e.tensor_scalar(t["bth"][:, 0:NB], src_tile[:, 0:NB],
                                                  8388608.0, None, ALU.mult),
              reads=[rbis], writes=[rbis])
        kb.op("vector", lambda e: e.tensor_copy(t["bpi"][:, 0:NB], t["bth"][:, 0:NB]),
              reads=[rbis], writes=[rbis])

    def take_update(lo_tile, delta):
        kb.op("vector", lambda e: e.tensor_scalar(t["btake"][:, 0:NB], t["cnt"][:, 0:NB],
                                                  float(cfg.KSEL), None, ALU.is_ge),
              reads=[rcnt], writes=[rbis])
        kb.op("vector", lambda e, dl=float(delta): e.tensor_scalar(
            t["btake"][:, 0:NB], t["btake"][:, 0:NB], dl, None, ALU.mult),
            reads=[rbis], writes=[rbis])
        kb.op("vector", lambda e: e.tensor_add(lo_tile[:, 0:NB], lo_tile[:, 0:NB],
                                               t["btake"][:, 0:NB]),
              reads=[rbis], writes=[rbis])

    kb.op("vector", lambda e: e.memset(t["bloE"][:, 0:NB], 0.0), writes=[rbis])
    delta = 128
    while delta >= 1:
        kb.op("vector", lambda e, dl=float(delta): e.tensor_scalar(
            t["bcand"][:, 0:NB], t["bloE"][:, 0:NB], dl, None, ALU.add),
            reads=[rbis], writes=[rbis])
        theta_from_exp(t["bcand"])
        kb.op("vector", cmp_against(t["cmpw"][:, 0:NT * 8], t["bpi"].bitcast(F32), ALU.is_ge),
              reads=[rroute, rbis], writes=[rcmp])
        count_into(t["cnt"])
        take_update(t["bloE"], delta)
        delta //= 2
    theta_from_exp(t["bloE"])
    bpf = t["bpi"].bitcast(F32)
    kb.op("vector", lambda e: e.memset(t["bloM"][:, 0:NB], float(2 ** 23)), writes=[rbis])
    delta = 2 ** 22
    while delta >= 1:
        kb.op("vector", lambda e, dl=float(delta): e.tensor_scalar(
            t["bcand"][:, 0:NB], t["bloM"][:, 0:NB], dl, None, ALU.add),
            reads=[rbis], writes=[rbis])
        kb.op("vector", lambda e: e.tensor_scalar(t["bcand"][:, 0:NB], t["bcand"][:, 0:NB],
                                                  float(2.0 ** -23), None, ALU.mult),
              reads=[rbis], writes=[rbis])
        kb.op("vector", lambda e: e.tensor_mul(t["bcand"][:, 0:NB], t["bcand"][:, 0:NB],
                                               bpf[:, 0:NB]),
              reads=[rbis], writes=[rbis])
        kb.op("vector", cmp_against(t["cmpw"][:, 0:NT * 8], t["bcand"], ALU.is_ge),
              reads=[rroute, rbis], writes=[rcmp])
        count_into(t["cnt"])
        take_update(t["bloM"], delta)
        delta //= 2
    kb.op("vector", lambda e: e.tensor_scalar(t["bth"][:, 0:NB], t["bloM"][:, 0:NB],
                                              float(2.0 ** -23), None, ALU.mult),
          reads=[rbis], writes=[rbis])
    kb.op("vector", lambda e: e.tensor_mul(t["bth"][:, 0:NB], t["bth"][:, 0:NB], bpf[:, 0:NB]),
          reads=[rbis], writes=[rbis])
    kb.op("vector", cmp_against(t["cmpw"][:, 0:NT * 8], t["bth"], ALU.is_gt),
          reads=[rroute, rbis], writes=[rcmp])
    count_into(t["gcnt"])
    kb.op("vector", cmp_against(t["sel"][:, 0:NT * 8], t["bth"], ALU.is_gt),
          reads=[rroute, rbis], writes=[rsel])
    kb.op("vector", cmp_against(t["eqw"][:, 0:NT * 8], t["bth"], ALU.is_equal),
          reads=[rroute, rbis], writes=[req])
    kb.op("vector", lambda e: e.tensor_copy(t["eqbf"][:, 0:NT * 8], t["eqw"][:, 0:NT * 8]),
          reads=[req], writes=[reqb])
    for c0 in range(0, NT * 8, 512):
        w = min(512, NT * 8 - c0)
        kb.op("tensor", lambda e, c0=c0, w=w: e.matmul(ps["misc"][0:128, 0:w], t["triu_bf"][:],
                                                       t["eqbf"][:, c0:c0 + w], start=True, stop=True),
              reads=[reqb, rc], writes=[rpm])
        kb.op("vector", lambda e, c0=c0, w=w: e.tensor_copy(t["cumw"][:, c0:c0 + w],
                                                            ps["misc"][0:128, 0:w]),
              reads=[rpm], writes=[rcum])
        kb.op("tensor", lambda e, c0=c0, w=w: e.matmul(ps["misc"][0:128, 0:w], t["ones_sq_bf"][:],
                                                       t["eqbf"][:, c0:c0 + w], start=True, stop=True),
              reads=[reqb, rc], writes=[rpm])
        kb.op("vector", lambda e, c0=c0, w=w: e.tensor_copy(t["totbc"][:, c0:c0 + w],
                                                            ps["misc"][0:128, 0:w]),
              reads=[rpm], writes=[rtot])
    sh = 1
    while sh < NTB:
        def emit_shift(e, sh=sh):
            v = t["totbc"][:, 0:NT * 8].rearrange("p (b j e) -> p b j e", b=cfg.B, e=8)
            return e.tensor_add(v[:, :, sh:NTB, :], v[:, :, sh:NTB, :], v[:, :, 0:NTB - sh, :])
        kb.op("vector", emit_shift, reads=[rtot], writes=[rtot])
        sh *= 2
    for c0 in range(0, NT * 8, 512):
        w = min(512, NT * 8 - c0)
        kb.op("tensor", lambda e, c0=c0, w=w: e.matmul(ps["misc"][0:128, 0:w], t["ones_sq_bf"][:],
                                                       t["eqbf"][:, c0:c0 + w], start=True, stop=True),
              reads=[reqb, rc], writes=[rpm])
        kb.op("vector", lambda e, c0=c0, w=w: e.tensor_copy(t["cntbc"][:, c0:c0 + w],
                                                            ps["misc"][0:128, 0:w]),
              reads=[rpm], writes=[rcntbc])
    kb.op("vector", lambda e: e.tensor_sub(t["totbc"][:, 0:NT * 8], t["totbc"][:, 0:NT * 8],
                                           t["cntbc"][:, 0:NT * 8]),
          reads=[rtot, rcntbc], writes=[rtot])
    kb.op("vector", lambda e: e.tensor_sub(t["cumw"][:, 0:NT * 8], t["cumw"][:, 0:NT * 8],
                                           t["eqw"][:, 0:NT * 8]),
          reads=[rcum, req], writes=[rcum])
    kb.op("vector", lambda e: e.tensor_add(t["cumw"][:, 0:NT * 8], t["cumw"][:, 0:NT * 8],
                                           t["totbc"][:, 0:NT * 8]),
          reads=[rcum, rtot], writes=[rcum])
    kb.op("vector", lambda e: e.tensor_scalar(t["kg"][:, 0:NB], t["gcnt"][:, 0:NB], -1.0,
                                              float(cfg.KSEL), ALU.mult, ALU.add),
          reads=[rcnt], writes=[rcnt])
    kb.op("vector", lambda e: e.tensor_tensor(
        t["cumw"][:, 0:NT * 8].rearrange("p (b j e) -> p b j e", b=cfg.B, e=8),
        t["cumw"][:, 0:NT * 8].rearrange("p (b j e) -> p b j e", b=cfg.B, e=8),
        t["kg"][:, 0:NB].rearrange("p (b e) -> p b e", e=8).unsqueeze(2).to_broadcast(
            [128, cfg.B, NTB, 8]),
        ALU.is_lt),
        reads=[rcum, rcnt], writes=[rcum])
    kb.op("vector", lambda e: e.tensor_mul(t["eqw"][:, 0:NT * 8], t["eqw"][:, 0:NT * 8],
                                           t["cumw"][:, 0:NT * 8]),
          reads=[req, rcum], writes=[req])
    kb.op("vector", lambda e: e.tensor_add(t["sel"][:, 0:NT * 8], t["sel"][:, 0:NT * 8],
                                           t["eqw"][:, 0:NT * 8]),
          reads=[rsel, req], writes=[rsel])
    rpTg = R(p, "ps_T")
    for ti in range(cfg.NT):
        kb.op("vector", lambda e, ti=ti: e.tensor_copy(t["eqbf"][:, ti * 8:(ti + 1) * 8],
                                                       t["sel"][:, ti * 8:(ti + 1) * 8]),
              reads=[rsel], writes=[reqb])
        kb.op("tensor", lambda e, ti=ti: e.transpose(p.ps["T"][0:8, 512:640],
                                                     t["eqbf"][:, ti * 8:(ti + 1) * 8],
                                                     t["ident_bf"][:]),
              reads=[reqb, rc], writes=[rpTg])
        kb.op("scalar", lambda e, ti=ti: e.copy(t["rowfm"][0:8, ti * 128:(ti + 1) * 128],
                                                p.ps["T"][0:8, 512:640]),
              reads=[rpTg], writes=[rselfm])


def emit_ffn_experts(p, kb, src_resid, dst, scale, half):
    t, cfg, ps = p.t, p.cfg, p.ps
    TH = cfg.T // 2
    NTH = cfg.NT // 2
    rfm = R(p, "xnfm_t", NTH * 8)
    rselfm = R(p, "rowfm")
    rc = R(p, "consts")
    rw = R(p, "ffn_w")
    rh1s = R(p, "ps_h1")
    rh1 = [rh1s, rh1s]
    rh1m = R(p, "h1m_r", 2)
    rselbc = R(p, "selbc")
    rpo = R(p, "ps_o", 4)
    rofm = R(p, "ofm_r", 2)
    rpTs = R(p, "ps_T")
    rpT = [rpTs, rpTs, rpTs, rpTs]
    routt = R(p, "outt_r")
    rxres = R(p, "xres_r")
    rpm = R(p, "ps_misc")
    for blkl in range(TH // 256):
        loc0 = blkl * 256
        tok0 = half * TH + loc0
        for e_i in range(NEXP):
            kb.op("tensor", lambda e, ei=e_i, tok0=tok0: e.matmul(
                ps["misc"][0:128, 0:256], t["rowsel8"][0:8, ei * 128:(ei + 1) * 128],
                t["rowfm"][0:8, tok0:tok0 + 256], start=True, stop=True),
                reads=[rselfm, rc], writes=[rpm])
            kb.op("vector", lambda e: e.tensor_copy(t["selbc"][:], ps["misc"][0:128, 0:256]),
                  reads=[rpm], writes=[rselbc])
            for it in range(2):
                ph = ps["h1"][0:128, it * 256:(it + 1) * 256]
                for dt in range(8):
                    kb.op("tensor", lambda e, ei=e_i, it=it, dt=dt, o=ph, loc0=loc0: e.matmul(
                        o, w1_tile(p, ei, dt, it),
                        t["xnfm"][:, dt * TH + loc0: dt * TH + loc0 + 256],
                        start=(dt == 0), stop=(dt == 7)),
                        reads=[rw, rfm[(loc0 // 128) * 8 + dt], rfm[(loc0 // 128 + 1) * 8 + dt]],
                        writes=[rh1[it]])
                kb.op("scalar", lambda e, ei=e_i, it=it, o=ph: e.activation(
                    t["h1m"][:, it * 256:(it + 1) * 256], o, AF.Gelu,
                    bias=t["b1c"][:, (it * 8 + ei):(it * 8 + ei) + 1]),
                    reads=[rh1[it], rc], writes=[rh1m[it]])
                kb.op("vector", lambda e, it=it: e.tensor_mul(
                    t["h1m"][:, it * 256:(it + 1) * 256], t["h1m"][:, it * 256:(it + 1) * 256],
                    t["selbc"][:]),
                    reads=[rselbc, rh1m[it]], writes=[rh1m[it]])
                for dt in range(8):
                    po = ps[f"o{dt // 2}"][0:128, (dt % 2) * 256:(dt % 2) * 256 + 256]
                    kb.op("tensor", lambda e, ei=e_i, it=it, dt=dt, o=po: e.matmul(
                        o, w2_tile(p, ei, it, dt), t["h1m"][:, it * 256:(it + 1) * 256],
                        start=(ei == 0 and it == 0 and dt % 2 == 0), stop=False),
                        reads=[rw, rh1m[it]], writes=[rpo[dt // 2]])
        for dt in range(8):
            po = ps[f"o{dt // 2}"][0:128, (dt % 2) * 256:(dt % 2) * 256 + 256]
            kb.op("tensor", lambda e, dt=dt, o=po, tok0=tok0: e.matmul(
                o, t["b2r"][0:8, dt * 128:(dt + 1) * 128], t["rowfm"][0:8, tok0:tok0 + 256],
                start=False, stop=(dt % 2 == 1)),
                reads=[rc, rselfm], writes=[rpo[dt // 2]])
        for sub in range(2):
            tt = tok0 + sub * 128
            xr = t["xres"][:, 0:1024]
            kb.dma("dsync", "xr", lambda e, a=xr, tt=tt: e.dma_start(
                out=a, in_=src_resid[tt:tt + 128, :]),
                writes=[rxres])
            for dt in range(8):
                po = ps[f"o{dt // 2}"][0:128, (dt % 2) * 256 + sub * 128:
                                       (dt % 2) * 256 + sub * 128 + 128]
                ofm = t["ofm"][:, (dt % 2) * 256:(dt % 2) * 256 + 128]
                kb.op("vector", lambda e, o=po, a=ofm, sc=scale: e.tensor_scalar(
                    a, o, sc, None, ALU.mult),
                    reads=[rpo[dt // 2]], writes=[rofm[dt % 2]])
                pT = ps["T"][0:128, (dt % 4) * 128:(dt % 4) * 128 + 128]
                kb.op("tensor", lambda e, a=ofm, o=pT: e.transpose(o, a, t["ident_bf"][:]),
                      reads=[rofm[dt % 2], rc], writes=[rpT[dt % 4]])
                outt = t["outt"][:, dt * 128:(dt + 1) * 128]
                xrs = t["xres"][:, dt * 128:(dt + 1) * 128]
                kb.op("vector", lambda e, a=outt, b=pT, c=xrs: e.tensor_add(a, b, c),
                      reads=[rpT[dt % 4], rxres], writes=[routt])
            kb.dma("dsync", "out", lambda e, tt=tt: e.dma_start(
                out=dst[tt:tt + 128, :], in_=t["outt"][:, 0:1024]),
                reads=[routt], writes=[R(p, "dst_dram")])


def emit_ffn_sublayer(p, kb, src, dst, cin, cout, scale=0.5, denom_from_inputs=False):
    if denom_from_inputs:
        emit_denom_from_inputs(p, kb, 0)
    for hf in (0, 1):
        emit_ln_half(p, kb, src, hf, mode="ln", do_gate=("ffn",),
                     store_dram=p.d["lnffn"])
    emit_ffn_route(p, kb, None if denom_from_inputs else cin, cout)
    emit_ffn_experts(p, kb, src, dst, scale, half=1)
    emit_retranspose_half(p, kb, p.d["lnffn"], half=0)
    emit_ffn_experts(p, kb, src, dst, scale, half=0)


def emit_output_done(p, kb):
    waits = []
    for slot, sem in kb.slot_sems.items():
        if slot[0] == "dsync" and ((isinstance(slot[1], tuple) and slot[1][0] == "out")
                                   or slot[1] in ("dbg", "pd", "out")):
            waits.append((sem, kb.counts[sem]))
    kb.streams["gpsimd"].append(_Rec(None, waits, None))


def _emit_all(p, kb, cfg):
    load_constants(p, kb)
    if cfg.only == "ffn1":
        load_ffn_weights(p, kb)
    if cfg.only == "ffn1":
        emit_ffn_sublayer(p, kb, p.d["x"], p.d["y"], p.d["cin1"], p.d["cout1"], scale=0.5)
    elif cfg.only == "attncore":
        load_attn_small_weights(p, kb)
        load_attn_big_weights(p, kb)
        emit_attn_core(p, kb, p.d["x"])
        rh = R(p, "h_dram")
        for ti in range(cfg.NT):
            kb.dma("dsync", "cp", lambda e, ti=ti: e.dma_start(
                out=p.d["y"][ti * 128:(ti + 1) * 128, :], in_=p.d["hbuf"][ti * 128:(ti + 1) * 128, :]),
                reads=[rh], writes=[R(p, "dst_dram")])
    elif cfg.only == "attn":
        load_attn_small_weights(p, kb)
        load_attn_big_weights(p, kb)
        emit_attn_core(p, kb, p.d["x"])
        kb.barrier()
        emit_attn_moe(p, kb, p.d["x"], p.d["y"])
    elif cfg.only == "full":
        load_attn_small_weights(p, kb)
        load_ffn_weights(p, kb)
        emit_ffn_sublayer(p, kb, p.d["x"], p.d["r1"], p.d["cin1"], p.d["cout1"], scale=0.5)
        kb.barrier()
        load_attn_big_weights(p, kb)
        emit_attn_core(p, kb, p.d["r1"])
        kb.barrier()
        emit_attn_moe(p, kb, p.d["r1"], p.d["r2"])
        kb.barrier()
        load_ffn_weights(p, kb)
        emit_ffn_sublayer(p, kb, p.d["r2"], p.d["y"], p.d["cin2"], p.d["cout2"], scale=0.5)
    elif cfg.only == "l1":
        for hf in (0, 1):
            emit_ln_half(p, kb, p.d["x"], hf, mode="ln", do_gate=("ffn",))
        emit_pden_out(p, kb)
    elif cfg.only == "l2":
        load_attn_small_weights(p, kb)
        load_ffn_weights(p, kb)
        emit_ffn_sublayer(p, kb, p.d["x"], p.d["r1"], None, None, scale=0.5,
                          denom_from_inputs=True)
        kb.barrier()
        if cfg.trunc != "ffn":
            load_attn_big_weights(p, kb)
            if cfg.trunc != "wts":
                emit_attn_core(p, kb, p.d["r1"])
        if cfg.trunc is None or cfg.trunc == "moe" or (
                isinstance(cfg.trunc, str) and cfg.trunc.startswith("moe_")):
            kb.barrier()
            mt = cfg.trunc[4:] if (isinstance(cfg.trunc, str)
                                   and cfg.trunc.startswith("moe_")) else None
            emit_attn_moe(p, kb, p.d["r1"], p.d["r2x"], moe_trunc=mt)
        if cfg.trunc is None:
            kb.barrier()
            for hf in (0, 1):
                emit_ln_half(p, kb, p.d["r2x"], hf, mode="ln", do_gate=("ffn",))
            emit_pden_out(p, kb)
    elif cfg.only == "l3":
        load_ffn_weights(p, kb)
        emit_ffn_sublayer(p, kb, p.d["x"], p.d["y"], None, None, scale=0.5,
                          denom_from_inputs=True)
    else:
        raise NotImplementedError(cfg.only)
    if cfg.debug:
        t = p.t
        kb.dma("dsync", "dbg", lambda e: e.dma_start(out=p.d["dbg_mv"][:, :], in_=t["mv"][:, 0:cfg.NT * 8]),
               reads=[R(p, "mv")], writes=[R(p, "dst_dram")])
        kb.dma("dsync", "dbg", lambda e: e.dma_start(out=p.d["dbg_sel"][:, :], in_=t["sel"][:, 0:cfg.NT * 8]),
               reads=[R(p, "sel")], writes=[R(p, "dst_dram")])
        kb.dma("dsync", "dbg", lambda e: e.dma_start(out=p.d["dbg_route"][:, :], in_=t["route"][:, 0:cfg.NT * 8]),
               reads=[R(p, "route")], writes=[R(p, "dst_dram")])
        kb.dma("dsync", "dbg", lambda e: e.dma_start(out=p.d["dbg_xnfm"][:, :],
                                                     in_=t["xnfm"][:, 0:4 * cfg.T]),
               reads=R(p, "xnfm_t", (cfg.NT // 2) * 8), writes=[R(p, "dst_dram")])
    emit_output_done(p, kb)


def _measure_splits(nc, kb):
    streams = {"SP": [], "PL": []}
    for ins in nc.all_instructions():
        nm = ins.__class__.__name__
        eng = getattr(getattr(ins, "engine", None), "value", "")
        key = "SP" if eng == "SP" else ("PL" if eng == "Pool" else None)
        if key is None:
            continue
        if nm == "InstDMACopy" or "Collective" in nm:
            streams[key].append("D")
        elif nm == "InstNoOp":
            streams[key].append("N")
    def group(seq):
        out = []
        cnt = 0
        for x in seq:
            if x == "D":
                cnt += 1
            else:
                out.append(max(cnt, 1))
                cnt = 0
        return out
    sp = group(streams["SP"])
    pl = group(streams["PL"])
    incs = {"dsync": sp, "dgp": [], "coll": []}
    assert len(pl) == len(kb.pl_order), (len(pl), len(kb.pl_order))
    for q, n in zip(kb.pl_order, pl):
        incs[q].append(n)
    return incs


def build_program(cfg):
    nc1 = bass.Bass("TRN2", target_bir_lowering=False)
    p1 = declare_io(nc1, cfg)
    p1.nc = nc1
    with ExitStack() as es:
        alloc_sbuf(p1, nc1, es)
        alloc_psum(p1, nc1, es)
        kb1 = KB(nc1, es)
        p1.kb = kb1
        _emit_all(p1, kb1, cfg)
        block = es.enter_context(nc1.Block())
        kb1.replay(block, sentinels=True)
    incs = _measure_splits(nc1, kb1)
    nc = bass.Bass("TRN2", target_bir_lowering=False)
    p = declare_io(nc, cfg)
    p.nc = nc
    es = ExitStack()
    with es:
        alloc_sbuf(p, nc, es)
        alloc_psum(p, nc, es)
        kb = KB(nc, es, dma_incs=incs)
        p.kb = kb
        _emit_all(p, kb, cfg)
        block = es.enter_context(nc.Block())
        kb.replay(block)
    return nc, p


def shard_inputs(inputs, cfg):
    c = host_constants()
    x = np.ascontiguousarray(np.asarray(inputs["L0_attn_hidden_states"], dtype=np.float32))
    B_total = x.shape[0]
    per = B_total // cfg.ncores
    name_map = {
        "aw_ln1": "L0_attn_w_ln1", "aw_ln2": "L0_attn_w_ln2", "aW_qkvz": "L0_attn_W_qkvz",
        "aW_ba": "L0_attn_W_ba", "aconv_w": "L0_attn_conv_w", "adt_bias": "L0_attn_dt_bias",
        "aA_log": "L0_attn_A_log", "agate_norm_w": "L0_attn_gate_norm_w",
        "aW_out": "L0_attn_W_out", "aW_router": "L0_attn_W_router",
        "aW_gate_e": "L0_attn_W_gate_e", "aW_up_e": "L0_attn_W_up_e",
        "aW_down_e": "L0_attn_W_down_e",
        "fw_gate_w": "L0_ffn_w_gate_w", "fw_gate_b": "L0_ffn_w_gate_b",
        "fW1": "L0_ffn_W1", "fb1": "L0_ffn_b1", "fW2": "L0_ffn_W2", "fb2": "L0_ffn_b2",
    }
    base = {k: np.ascontiguousarray(np.asarray(inputs[v], dtype=np.float32))
            for k, v in name_map.items()}
    base.update(c)
    in_maps = []
    for ci in range(cfg.ncores):
        m = dict(base)
        m["x"] = np.ascontiguousarray(
            x[ci * per:(ci + 1) * per].reshape(per * x.shape[1], x.shape[2]))
        in_maps.append(m)
    return in_maps


def kernel(**inputs):
    S, NC = 2048, 8
    cfg1 = Cfg(S=S, B=2, ncores=NC, B_total=16, only="l1")
    cfg2 = Cfg(S=S, B=2, ncores=NC, B_total=16, only="l2")
    cfg3 = Cfg(S=S, B=2, ncores=NC, B_total=16, only="l3")
    base_maps = shard_inputs(inputs, cfg1)
    nc1, _ = build_program(cfg1)
    res1 = run_bass_kernel_spmd(nc1, base_maps, core_ids=list(range(NC)))
    pd1 = np.ascontiguousarray(np.stack([np.asarray(r["pden_out"]) for r in res1.results]))
    nc2, _ = build_program(cfg2)
    maps2 = [dict(m, pdens=pd1) for m in base_maps]
    res2 = run_bass_kernel_spmd(nc2, maps2, core_ids=list(range(NC)))
    pd2 = np.ascontiguousarray(np.stack([np.asarray(r["pden_out"]) for r in res2.results]))
    nc3, _ = build_program(cfg3)
    maps3 = [dict(m, x=np.ascontiguousarray(np.asarray(res2.results[ci]["r2x"])), pdens=pd2)
             for ci, m in enumerate(base_maps)]
    res3 = run_bass_kernel_spmd(nc3, maps3, core_ids=list(range(NC)))
    outs = [np.asarray(r["y"]).reshape(cfg3.B, cfg3.S, 1024) for r in res3.results]
    return np.ascontiguousarray(np.concatenate(outs, axis=0))



def load_attn_small_weights(p, kb):
    t, d = p.t, p.d
    rc = R(p, "consts")
    kb.dma("dgp", "cw", lambda e: e.dma_start(
        out=t["wba"][:].rearrange("p (dt c) -> p dt c", c=16),
        in_=d["aW_ba"].rearrange("(dt p) c -> p dt c", p=128)), writes=[rc])
    kb.dma("dgp", "cw", lambda e: e.dma_start(
        out=t["wrt"][:].rearrange("p (dt e) -> p dt e", e=8),
        in_=d["aW_router"].rearrange("(dt p) e -> p dt e", p=128)), writes=[rc])
    kb.dma("dsync", "c", lambda e: e.dma_start(
        out=t["convw"][:].rearrange("p (ct j) -> p ct j", j=4),
        in_=d["aconv_w"].rearrange("(ct p) j -> p ct j", p=128)), writes=[rc])
    for nm, dn in (("wln1c", "aw_ln1"), ("wln2c", "aw_ln2")):
        kb.dma("dsync", "c", lambda e, dn=dn: e.dma_start(
            out=t["b1st"][0:8, 0:128], in_=d[dn].rearrange("(dt p) -> dt p", p=128)),
            writes=[rc])
        kb.op("tensor", lambda e: e.transpose(p.ps["misc"][0:128, 0:8], t["b1st"][0:8, 0:128],
                                              t["ident_f32"][0:8, 0:8]),
              reads=[rc], writes=[R(p, "ps_misc")])
        kb.op("vector", lambda e, nm=nm: e.tensor_scalar(t[nm][:], p.ps["misc"][0:128, 0:8],
                                                         1.0, None, ALU.add),
              reads=[R(p, "ps_misc")], writes=[rc])
    kb.dma("dsync", "c", lambda e: e.dma_start(out=t["gnwc"][:, 0:1], in_=d["agate_norm_w"][:, None]),
           writes=[rc])
    kb.dma("dsync", "c", lambda e: e.dma_start(out=t["hcol8"][0:8, 0:1], in_=d["adt_bias"][:, None]),
           writes=[rc])
    kb.dma("dsync", "c", lambda e: e.dma_start(out=t["hcol8"][0:8, 2:3], in_=d["aA_log"][:, None]),
           writes=[rc])
    kb.op("scalar", lambda e: e.activation(t["hcol8"][0:8, 3:4], t["hcol8"][0:8, 2:3], AF.Exp),
          reads=[rc], writes=[rc])
    kb.op("vector", lambda e: e.tensor_scalar(t["hcol8"][0:8, 1:2], t["hcol8"][0:8, 3:4], -1.0,
                                              None, ALU.mult),
          reads=[rc], writes=[rc])


def load_attn_big_weights(p, kb):
    t, d = p.t, p.d
    rw = R(p, "attn_w")
    kb.dma("dgp", "aw", lambda e: e.dma_start(
        out=t["wbuf"][:, 0:24576].rearrange("p (dt c) -> p dt c", c=3072),
        in_=d["aW_qkvz"].rearrange("(dt p) c -> p dt c", p=128)),
        writes=[rw])
    kb.dma("dgp", "aw", lambda e: e.dma_start(
        out=t["wbuf"][:, 24576:32768].rearrange("p (vt c) -> p vt c", c=1024),
        in_=d["aW_out"].rearrange("(vt p) c -> p vt c", p=128)),
        writes=[rw])


def wqkvz_tile(p, oc, dt):
    c = dt * 3072 + oc * 128
    return p.t["wbuf"][:, c:c + 128]


def wout_tile(p, vt, dt):
    c = 24576 + vt * 1024 + dt * 128
    return p.t["wbuf"][:, c:c + 128]


def raw_tile(p, ct):
    return p.t["accbuf"][:, ct * 259: ct * 259 + 259]


def qkvc_tile(p, ct):
    return p.t["accbuf"][:, 16 * 259 + ct * 256: 16 * 259 + (ct + 1) * 256]


def emit_attn_block_prologue(p, kb, b, blk):
    t, cfg, ps = p.t, p.cfg, p.ps
    TH = cfg.T // 2
    NTH = cfg.NT // 2
    loc0 = blk * 256
    nblk = cfg.S // 256
    rfm = R(p, "xnfm_t", NTH * 8)
    rc = R(p, "consts")
    rw = R(p, "attn_w")
    rraw = R(p, "raw_t", 16)
    rz = R(p, "z_t", 8)
    rphs = R(p, "ps_h1")
    rph = [rphs, rphs]
    rpg = R(p, "ps_g")
    rba = R(p, "ba_r")
    rrF = R(p, "rowsF")
    rrB = R(p, "rowsB")
    rqc = R(p, "qkvc_t", 16)
    rsq = R(p, "sq_pool")
    for ct in range(16):
        if blk == 0:
            kb.op("vector", lambda e, ct=ct: e.memset(raw_tile(p, ct)[:, 0:3], 0.0),
                  reads=[], writes=[rraw[ct]])
        else:
            kb.op("vector", lambda e, ct=ct: e.tensor_copy(raw_tile(p, ct)[:, 0:3],
                                                           raw_tile(p, ct)[:, 256:259]),
                  reads=[rraw[ct]], writes=[rraw[ct]])
    rpm2 = R(p, "ps_misc")
    for oc in range(24):
        if oc % 2 == 0:
            ph = ps["h1"][0:128, 0:256]
        else:
            ph = ps["misc"][0:128, 0:256]
        for dt in range(8):
            kb.op("tensor", lambda e, oc=oc, dt=dt, o=ph, loc0=loc0: e.matmul(
                o, wqkvz_tile(p, oc, dt),
                t["xnfm"][:, dt * TH + loc0: dt * TH + loc0 + 256],
                start=(dt == 0), stop=(dt == 7)),
                reads=[rw, rfm[(loc0 // 128) * 8 + dt], rfm[(loc0 // 128 + 1) * 8 + dt]],
                writes=[rph[0] if oc % 2 == 0 else rpm2])
        if oc < 16:
            dst = raw_tile(p, oc)[:, 3:259]
            eng = "vector" if oc % 2 == 0 else "scalar"
            rsrc = rph[0] if oc % 2 == 0 else rpm2
            if eng == "vector":
                kb.op(eng, lambda e, a=dst, o=ph: e.tensor_copy(a, o),
                      reads=[rsrc], writes=[rraw[oc]])
            else:
                kb.op(eng, lambda e, a=dst, o=ph: e.copy(a, o),
                      reads=[rsrc], writes=[rraw[oc]])
        else:
            zt = oc - 16
            dst = t["zbuf"][:, zt * 256:(zt + 1) * 256]
            rsrc = rph[0] if oc % 2 == 0 else rpm2
            kb.op("scalar", lambda e, a=dst, o=ph: e.activation(a, o, AF.Silu),
                  reads=[rsrc], writes=[rz[zt]])
    for part in range(2):
        pg = ps["g"][0:8, part * 256:(part + 1) * 256]
        for dt in range(8):
            kb.op("tensor", lambda e, part=part, dt=dt, o=pg, loc0=loc0: e.matmul(
                o, t["wba"][:, dt * 16 + part * 8: dt * 16 + part * 8 + 8],
                t["xnfm"][:, dt * TH + loc0: dt * TH + loc0 + 256],
                start=(dt == 0), stop=(dt == 7)),
                reads=[rc, rfm[(loc0 // 128) * 8 + dt], rfm[(loc0 // 128 + 1) * 8 + dt]],
                writes=[R(p, "ps_g")])
    kb.op("scalar", lambda e: e.activation(t["rowsB"][0:8, 0:256], ps["g"][0:8, 0:256],
                                           AF.Sigmoid),
          reads=[rpg], writes=[rrB])
    gF = lambda q: t["rowsF"][0:8, q * 256:(q + 1) * 256]
    gB = lambda q: t["rowsB"][0:8, q * 256:(q + 1) * 256]
    kb.op("scalar", lambda e: e.activation(gF(0), ps["g"][0:8, 256:512], AF.Relu,
                                           bias=t["hcol8"][0:8, 0:1]),
          reads=[rpg, rc], writes=[rrF])
    kb.op("scalar", lambda e: e.activation(gF(1), ps["g"][0:8, 256:512], AF.Abs,
                                           bias=t["hcol8"][0:8, 0:1]),
          reads=[rpg, rc], writes=[rrF])
    kb.op("scalar", lambda e: e.activation(gF(1), gF(1), AF.Exp, scale=-1.0),
          reads=[rrF], writes=[rrF])
    kb.op("scalar", lambda e, bc=ccol(p, kb, 1.0): e.activation(gF(1), gF(1), AF.Ln,
                                                                bias=bc[0:8, :]),
          reads=[rrF, rc], writes=[rrF])
    kb.op("vector", lambda e: e.tensor_add(gF(0), gF(0), gF(1)), reads=[rrF], writes=[rrF])
    kb.op("vector", lambda e: e.tensor_scalar(gF(0), gF(0), t["hcol8"][0:8, 1:2], None, ALU.mult),
          reads=[rrF, rc], writes=[rrF])
    for c in range(2):
        kb.op("vector", lambda e, c=c: e.tensor_tensor_scan(
            gF(1)[:, c * 128:(c + 1) * 128], t["ones_sq_bf"][0:8, 0:128],
            gF(0)[:, c * 128:(c + 1) * 128], 0.0, ALU.mult, ALU.add),
            reads=[rrF, rc], writes=[rrF])
    kb.op("scalar", lambda e: e.activation(gF(2), gF(1), AF.Exp), reads=[rrF], writes=[rrF])
    for c in range(2):
        kb.op("vector", lambda e, c=c: e.tensor_scalar(
            gF(3)[:, c * 128:(c + 1) * 128], gF(1)[:, c * 128:(c + 1) * 128],
            gF(1)[:, c * 128 + 127: c * 128 + 128], None, ALU.subtract),
            reads=[rrF], writes=[rrF])
        kb.op("vector", lambda e, c=c: e.tensor_scalar(
            gF(4)[:, c * 128:(c + 1) * 128], t["ones_sq_bf"][0:8, 0:128],
            gF(2)[:, c * 128 + 127: c * 128 + 128], None, ALU.mult),
            reads=[rrF, rc], writes=[rrF])
    kb.op("scalar", lambda e: e.activation(gF(3), gF(3), AF.Exp, scale=-1.0),
          reads=[rrF], writes=[rrF])
    rN_ = R(p, "pw_n")
    rM_ = R(p, "pw_m")
    Q0 = 16 * 259
    acc = t["accbuf"][:, Q0:Q0 + 4096]

    def raw_w(j, h0, n):
        return (t["accbuf"][:, 0:16 * 259].rearrange("p (t x) -> p t x", x=259)
                [:, h0:h0 + n, j:j + 256])

    def w_w(j, h0, n):
        return (t["convw"][:, 0:64].rearrange("p (t j) -> p t j", j=4)
                [:, h0:h0 + n, j:j + 1].to_broadcast([128, n, 256]))

    rall = rraw[0:16]
    kb.op("gpsimd", lambda e: e.tensor_mul(
        acc.rearrange("p (t x) -> p t x", x=256), raw_w(0, 0, 16), w_w(0, 0, 16)),
        reads=rall + [rc], writes=rqc[0:16])
    for j in (1, 2, 3):
        for h in range(2):
            tmp = t["poolB"][:, 3072:5120]
            kb.op("gpsimd", lambda e, j=j, h=h, tmp=tmp: e.tensor_mul(
                tmp.rearrange("p (t x) -> p t x", x=256), raw_w(j, 8 * h, 8), w_w(j, 8 * h, 8)),
                reads=rall + [rc], writes=[rN_, rM_])
            kb.op("gpsimd", lambda e, h=h, tmp=tmp: e.tensor_add(
                acc[:, h * 2048:(h + 1) * 2048], acc[:, h * 2048:(h + 1) * 2048], tmp),
                reads=[rN_, rM_] + rqc[8 * h:8 * h + 8], writes=rqc[8 * h:8 * h + 8])
    for h in range(2):
        kb.op("scalar", lambda e, h=h: e.activation(
            acc[:, h * 2048:(h + 1) * 2048], acc[:, h * 2048:(h + 1) * 2048], AF.Silu),
            reads=rqc[8 * h:8 * h + 8], writes=rqc[8 * h:8 * h + 8])
    kb.op("gpsimd", lambda e: e.tensor_mul(t["poolB"][:, 0:2048],
                                           acc[:, 0:2048], acc[:, 0:2048]),
          reads=rqc[0:8], writes=[rsq, R(p, "pw_a"), R(p, "pw_kts")])
    for grp in range(2):
        pg = ps["g"][0:8, grp * 256:(grp + 1) * 256]
        for kh in range(4):
            i = grp * 4 + kh
            kb.op("tensor", lambda e, i=i, kh=kh, o=pg: e.matmul(
                o, t["colsel2"][:, kh * 8:(kh + 1) * 8], t["poolB"][:, i * 256:(i + 1) * 256],
                start=(kh == 0), stop=(kh == 3)),
                reads=[rsq, rc], writes=[R(p, "ps_g")])
    kb.op("scalar", lambda e, bc=ccol(p, kb, 128 * EPS6): e.activation(
        gF(0), ps["g"][0:8, 0:256], AF.Sqrt, bias=bc[0:8, :], scale=128.0),
        reads=[rpg, rc], writes=[rrF])
    kb.op("vector", lp(p, lambda e: e.reciprocal(gB(1), gF(0))), reads=[rrF], writes=[rrB])
    kb.op("scalar", lambda e, bc=ccol(p, kb, EPS6): e.activation(
        gF(0), ps["g"][0:8, 256:512], AF.Sqrt, bias=bc[0:8, :]),
        reads=[rpg, rc], writes=[rrF])
    kb.op("vector", lp(p, lambda e: e.reciprocal(gB(2), gF(0))), reads=[rrF], writes=[rrB])
    kb.op("vector", lambda e: e.tensor_mul(gB(3), gB(0), gB(2)), reads=[rrB], writes=[rrB])
    kb.op("vector", lambda e: e.tensor_mul(gB(4), gB(3), gF(2)), reads=[rrB, rrF], writes=[rrB])
    kb.op("vector", lambda e: e.tensor_mul(gB(5), gB(2), gF(3)), reads=[rrB, rrF], writes=[rrB])
    kb.op("vector", lambda e: e.tensor_mul(gB(6), gB(1), gF(2)), reads=[rrB, rrF], writes=[rrB])
    rct = R(p, "colsT")
    rpm = R(p, "ps_misc")
    rpTs2 = R(p, "ps_T")
    rpT = [rpTs2] * 4
    for c in range(2):
        fq = [(0, gF(1)), (1, gF(4))]
        for qi, (q, src) in enumerate(fq):
            pm = ps["misc"][0:128, qi * 8: qi * 8 + 8]
            kb.op("tensor", lambda e, src=src, c=c, o=pm: e.transpose(
                o, src[:, c * 128:(c + 1) * 128], t["ident_f32"][0:8, 0:8]),
                reads=[rrF, rc], writes=[rpm])
            kb.op("vector", lambda e, c=c, q=q, o=pm: e.tensor_copy(
                t["colsT"][:, (c * 8 + q) * 8:(c * 8 + q) * 8 + 8], o),
                reads=[rpm], writes=[rct])
        bq = [(2, gB(2)), (3, gB(3)), (4, gB(0)), (5, gB(4)), (6, gB(5))]
        for qi, (q, src) in enumerate(bq):
            pT = ps["T"][0:128, 512 + qi * 8: 512 + qi * 8 + 8]
            kb.op("tensor", lambda e, src=src, c=c, o=pT: e.transpose(
                o, src[:, c * 128:(c + 1) * 128], t["ident_bf"][0:8, 0:8]),
                reads=[rrB, rc], writes=[rpT[3]])
            kb.op("vector", lambda e, c=c, q=q, o=pT: e.tensor_copy(
                t["colsT"][:, (c * 8 + q) * 8:(c * 8 + q) * 8 + 8], o),
                reads=[rpT[3]], writes=[rct])


def col_ap(p, c, q, h):
    return p.t["colsT"][:, (c * 8 + q) * 8 + h:(c * 8 + q) * 8 + h + 1]


PB_QS = 0
PB_KTOK = 128
PB_AKROW = 256
PB_AQBC = 384
PB_QS2 = 512
PB_ATIL = 768
PB_NPOW = 2816
PB_MPOW = 4352
PB_IMN = 1536
PB_R = 1792
PB_VTOK = 2048
PB_Y0 = 2304
PB_Y1 = 2560
PB_SQO = 5632
PB_BAKBC = 5888
PB_SCR = 1024
PB_KTS = 1280

PF_DIFF = 0
PF_DX = 128
PF_DM = 384
PF_GKQS = 640
PF_GKKS = 768
PF_SCR = 896


def npow(p, j, slot=None):
    c = PB_NPOW + ((j - 1) % 6) * 256
    if slot is None:
        return p.t["poolB"][:, c:c + 256]
    return p.t["poolB"][:, c + slot * 128: c + slot * 128 + 128]


def mpow(p, j, slot=None):
    c = PB_MPOW + ((j - 1) % 5) * 256
    if slot is None:
        return p.t["poolB"][:, c:c + 256]
    return p.t["poolB"][:, c + slot * 128: c + slot * 128 + 128]


def pb(p, off, slot=None, w=128):
    if slot is None:
        return p.t["poolB"][:, off:off + 256]
    return p.t["poolB"][:, off + slot * w: off + slot * w + w]


def pf(p, off, slot=None, w=128):
    if slot is None:
        return p.t["poolF"][:, off:off + 256]
    return p.t["poolF"][:, off + slot * w: off + slot * w + w]


def bv2(ap):
    return ap.unsqueeze(1).to_broadcast([128, 2, 128])


def r2(ap256):
    return ap256.rearrange("p (s q) -> p s q", s=2)


def emit_delta_chunk(p, kb, b, blk, c):
    t, cfg, ps = p.t, p.cfg, p.ps
    TH = cfg.T // 2
    NTH = cfg.NT // 2
    loc0 = blk * 256
    rc = R(p, "consts")
    rqc = R(p, "qkvc_t", 16)
    rz = R(p, "z_t", 8)
    rrF = R(p, "rowsF")
    rrB = R(p, "rowsB")
    rct = R(p, "colsT")
    rst = R(p, "states", 8)
    rfm = R(p, "xnfm_t", NTH * 8)
    rpm = R(p, "ps_misc")
    rpTs = R(p, "ps_T")
    rpT = [rpTs] * 4
    rpo = {0: R(p, "ps_o", 4)[0], 1: R(p, "ps_o", 4)[1]}
    rpO = [R(p, "ps_o", 4)[2], R(p, "ps_o", 4)[3]]
    rphs = R(p, "ps_h1")
    rph = [rphs, rphs]
    rpg = R(p, "ps_g")
    rB = R(p, "poolB_delta")
    rF = R(p, "poolF_delta")
    rY = R(p, "Y_pair")
    sl = slice(c * 128, (c + 1) * 128)

    def mmT(emit, reads, writes):
        kb.op("tensor", emit, reads=reads, writes=writes)

    for kh in range(4):
        q_t = qkvc_tile(p, kh)[:, sl]
        k_t = qkvc_tile(p, 4 + kh)[:, sl]
        pTb = ps["h1"][0:128, 256:384]
        mmT(lambda e, kh=kh, o=pTb: e.matmul(
            o, t["rowsel8"][0:8, (2 * kh) * 128:(2 * kh) * 128 + 128],
            t["rowsB"][0:8, 1 * 256 + c * 128: 1 * 256 + (c + 1) * 128],
            start=True, stop=True), [rrB, rc], [rphs])
        kb.op("scalar", lambda e, o=pTb: e.copy(pb(p, PB_AQBC, 0), o), reads=[rphs],
              writes=[rB, R(p, "sq_pool")])
        kb.op("vector", lambda e, a=q_t: e.tensor_mul(pb(p, PB_QS, 0), a, pb(p, PB_AQBC, 0)),
              reads=[rqc[kh], rB], writes=[rB])
        mmT(lambda e, a=k_t: e.matmul(ps["misc"][0:128, 0:128], a, pb(p, PB_QS, 0),
                                      start=True, stop=True), [rqc[4 + kh], rB], [rpm])
        mmT(lambda e, a=k_t: e.matmul(ps["misc"][0:128, 128:256], a, a,
                                      start=True, stop=True), [rqc[4 + kh]], [rpm])
        for slot in range(2):
            vh = 2 * kh + slot
            mmT(lambda e, a=k_t, vh=vh, slot=slot: e.matmul(
                ps["misc"][0:128, 256 + slot * 128: 256 + slot * 128 + 128], a,
                t["states"][:, vh * 128:(vh + 1) * 128],
                start=True, stop=True), [rqc[4 + kh], rst[vh]], [rpm])
        pT2 = ps["T"][0:128, 640:768]
        mmT(lambda e, a=k_t, o=pT2: e.transpose(o, a, t["ident_bf"][:]),
            [rqc[4 + kh], rc], [rpTs])
        kb.op("scalar", lambda e, o=pT2: e.copy(pb(p, PB_KTOK, 0), o), reads=[rpTs], writes=[rB])
        pT3 = ps["h1"][0:128, 256:384]
        mmT(lambda e, kh=kh, o=pT3: e.matmul(
            o, t["rowsel8"][0:8, (2 * kh) * 128:(2 * kh) * 128 + 128],
            t["rowsB"][0:8, 2 * 256 + c * 128: 2 * 256 + (c + 1) * 128],
            start=True, stop=True), [rrB, rc], [rphs])
        kb.op("scalar", lambda e, o=pT3: e.copy(pb(p, PB_AKROW, 0), o), reads=[rphs], writes=[rB])
        kb.op("vector", lambda e, kh=kh: e.tensor_scalar(
            pf(p, PF_GKQS, 0), ps["misc"][0:128, 0:128], col_ap(p, c, 2, 2 * kh), None, ALU.mult),
            reads=[rpm, rct], writes=[rF])
        kb.op("vector", lambda e, kh=kh: e.tensor_scalar(
            pf(p, PF_GKKS, 0), ps["misc"][0:128, 128:256], col_ap(p, c, 2, 2 * kh), None, ALU.mult),
            reads=[rpm, rct], writes=[rF])
        for slot in range(2):
            vh = 2 * kh + slot
            ph2 = ps["o1"][0:128, 256:384]
            mmT(lambda e, vh=vh, o=ph2: e.matmul(
                o, t["rowsel8f"][0:8, vh * 128:(vh + 1) * 128],
                t["rowsF"][0:8, 1 * 256 + c * 128: 1 * 256 + (c + 1) * 128],
                start=True, stop=True), [rrF, rc], [rpo[1]])
            kb.op("vector", lambda e, vh=vh, o=ph2: e.tensor_scalar(
                pf(p, PF_DIFF, 0), o, col_ap(p, c, 0, vh), None, ALU.subtract),
                reads=[rpo[1], rct], writes=[rF])
            kb.op("vector", lambda e: e.tensor_scalar_min(pf(p, PF_SCR, 0), pf(p, PF_DIFF, 0), 0.0),
                  reads=[rF], writes=[rF])
            kb.op("scalar", lambda e, slot=slot: e.activation(pf(p, PF_DX, slot), pf(p, PF_SCR, 0),
                                                              AF.Exp),
                  reads=[rF], writes=[rF])
            kb.op("vector", lambda e: e.tensor_scalar_max(pf(p, PF_SCR, 0), pf(p, PF_DIFF, 0), 0.0),
                  reads=[rF], writes=[rF])
            kb.op("scalar", lambda e, slot=slot: e.activation(pf(p, PF_DM, slot), pf(p, PF_SCR, 0),
                                                              AF.Exp, scale=-1.0),
                  reads=[rF], writes=[rF])
            pT4 = ps["h1"][0:128, 256:384]
            mmT(lambda e, vh=vh, o=pT4: e.matmul(
                o, t["rowsel8"][0:8, vh * 128:(vh + 1) * 128],
                t["rowsB"][0:8, 3 * 256 + c * 128: 3 * 256 + (c + 1) * 128],
                start=True, stop=True), [rrB, rc], [rphs])
            kb.op("scalar", lambda e, slot=slot, o=pT4: e.copy(pb(p, PB_BAKBC, slot), o),
                  reads=[rphs], writes=[rB])
            kb.op("vector", lambda e, vh=vh, slot=slot: e.tensor_scalar(
                mpow(p, 1, slot), ps["misc"][0:128, 128:256], col_ap(p, c, 3, vh), None, ALU.mult),
                reads=[rpm, rct], writes=[rB])
            kb.op("vector", lambda e, vh=vh, slot=slot: e.tensor_scalar(
                pf(p, PF_SCR, slot, w=64) if False else pf(p, PF_SCR, 0),
                ps["misc"][0:128, 256 + slot * 128: 256 + slot * 128 + 128],
                col_ap(p, c, 5, vh), None, ALU.mult),
                reads=[rpm, rct], writes=[rF])
            pT5 = ps["T"][0:128, 512:640]
            mmT(lambda e, vh=vh, o=pT5: e.transpose(o, qkvc_tile(p, 8 + vh)[:, sl],
                                                    t["ident_bf"][:]),
                [rqc[8 + vh], rc], [rpTs])
            kb.op("scalar", lambda e, slot=slot, o=pT5: e.copy(pb(p, PB_VTOK, slot), o),
                  reads=[rpTs], writes=[rB])
            kb.op("vector", lambda e, vh=vh, slot=slot: e.scalar_tensor_tensor(
                pb(p, PB_R, slot), pb(p, PB_VTOK, slot), col_ap(p, c, 4, vh),
                pf(p, PF_SCR, 0), ALU.mult, ALU.subtract),
                reads=[rB, rct, rF], writes=[rB])
        kb.op("gpsimd", lambda e: e.tensor_mul(
            r2(npow(p, 1)), r2(pb(p, PB_BAKBC)), bv2(pf(p, PF_GKKS, 0))),
            reads=[rB, rF], writes=[rB])
        kb.op("vector", lambda e: e.tensor_mul(npow(p, 1), npow(p, 1), pf(p, PF_DX)),
              reads=[rB, rF], writes=[rB])
        kb.op("gpsimd", lambda e: e.tensor_mul(
            r2(npow(p, 1)), r2(npow(p, 1)), bv2(t["striu"][:])),
            reads=[rB, rc], writes=[rB])
        kb.op("gpsimd", lambda e: e.tensor_mul(
            r2(mpow(p, 1)), r2(mpow(p, 1)), bv2(pb(p, PB_AKROW, 0))),
            reads=[rB], writes=[rB])
        kb.op("vector", lambda e: e.tensor_mul(mpow(p, 1), mpow(p, 1), pf(p, PF_DM)),
              reads=[rB, rF], writes=[rB])
        kb.op("gpsimd", lambda e: e.tensor_mul(
            r2(mpow(p, 1)), r2(mpow(p, 1)), bv2(t["stril"][:])),
            reads=[rB, rc], writes=[rB])
        kb.op("vector", lambda e: e.tensor_mul(
            r2(pb(p, PB_ATIL)), r2(pf(p, PF_DX)), bv2(pf(p, PF_GKQS, 0))),
            reads=[rF], writes=[rB])
        kb.op("gpsimd", lambda e: e.tensor_mul(
            r2(pb(p, PB_ATIL)), r2(pb(p, PB_ATIL)), bv2(t["triu"][:])),
            reads=[rB, rc], writes=[rB])
        kb.op("vector", lambda e: e.tensor_tensor(
            r2(pb(p, PB_IMN)), bv2(t["ident_bf"][:]), r2(npow(p, 1)), ALU.subtract),
            reads=[rB, rc], writes=[rB])
        for lvl in range(2, 8):
            for slot in range(2):
                mmT(lambda e, lvl=lvl, slot=slot: e.matmul(
                    ps["o0"][0:128, slot * 128: slot * 128 + 128],
                    mpow(p, lvl - 1, slot), npow(p, lvl - 1, slot),
                    start=True, stop=True), [rB], [rpo[0]])
                if lvl <= 6:
                    mmT(lambda e, lvl=lvl, slot=slot: e.matmul(
                        ps["o0"][0:128, 256 + slot * 128: 256 + slot * 128 + 128],
                        npow(p, lvl - 1, slot), mpow(p, lvl - 1, slot),
                        start=True, stop=True), [rB], [rpo[0]])
            kb.op("scalar", lambda e, lvl=lvl: e.copy(npow(p, lvl), ps["o0"][0:128, 0:256]),
                  reads=[rpo[0]], writes=[rB])
            if lvl <= 6:
                kb.op("vector", lambda e, lvl=lvl: e.tensor_copy(mpow(p, lvl),
                                                                 ps["o0"][0:128, 256:512]),
                      reads=[rpo[0]], writes=[rB])
        ycur, ynxt = PB_Y0, PB_Y1
        for j in range(7, 1, -1):
            for slot in range(2):
                src = pb(p, PB_R if j == 7 else ycur, slot)
                mmT(lambda e, slot=slot, src=src: e.matmul(
                    ps["o1"][0:128, slot * 128: slot * 128 + 128], t["ident_bf"][:], src,
                    start=True, stop=False), [rB, rc], [rpo[1]])
                mmT(lambda e, j=j, slot=slot, src=src: e.matmul(
                    ps["o1"][0:128, slot * 128: slot * 128 + 128], npow(p, j, slot), src,
                    start=False, stop=True), [rB], [rpo[1]])
            eng = "scalar" if j % 2 == 0 else "vector"
            if eng == "scalar":
                kb.op(eng, lambda e, ynxt=ynxt: e.copy(pb(p, ynxt), ps["o1"][0:128, 0:256]),
                      reads=[rpo[1]], writes=[rY])
            else:
                kb.op(eng, lambda e, ynxt=ynxt: e.tensor_copy(pb(p, ynxt), ps["o1"][0:128, 0:256]),
                      reads=[rpo[1]], writes=[rY])
            ycur, ynxt = ynxt, ycur
        for slot in range(2):
            mmT(lambda e, slot=slot: e.matmul(
                ps["o1"][0:128, slot * 128: slot * 128 + 128], pb(p, PB_IMN, slot),
                pb(p, ycur, slot), start=True, stop=True), [rB, rY], [rpo[1]])
        kb.op("vector", lambda e, ynxt=ynxt: e.tensor_copy(pb(p, ynxt), ps["o1"][0:128, 0:256]),
              reads=[rpo[1]], writes=[rY])
        ufin = ynxt
        for slot in range(2):
            vh = 2 * kh + slot
            pT6 = ps["h1"][0:128, 256:384]
            mmT(lambda e, vh=vh, o=pT6: e.matmul(
                o, t["rowsel8"][0:8, vh * 128:(vh + 1) * 128],
                t["rowsB"][0:8, 6 * 256 + c * 128: 6 * 256 + (c + 1) * 128],
                start=True, stop=True), [rrB, rc], [rphs])
            kb.op("scalar", lambda e, slot=slot, o=pT6: e.copy(pb(p, PB_QS2, slot), o),
                  reads=[rphs], writes=[rB])
            kb.op("vector", lambda e, slot=slot, a=q_t: e.tensor_mul(
                pb(p, PB_QS2, slot), a, pb(p, PB_QS2, slot)),
                reads=[rqc[kh], rB], writes=[rB])
            pO = ps[f"o{2 + kh // 2}"][0:128, (vh % 4) * 128:(vh % 4) * 128 + 128]
            mmT(lambda e, vh=vh, slot=slot, o=pO: e.matmul(
                o, t["states"][:, vh * 128:(vh + 1) * 128], pb(p, PB_QS2, slot),
                start=True, stop=False), [rst[vh], rB], [rpO[kh // 2]])
            mmT(lambda e, slot=slot, o=pO: e.matmul(
                o, pb(p, ufin, slot), pb(p, PB_ATIL, slot),
                start=False, stop=True), [rY, rB], [rpO[kh // 2]])
            kb.op("vector", lambda e, vh=vh, slot=slot: e.tensor_scalar(
                pb(p, PB_SCR, slot), t["states"][:, vh * 128:(vh + 1) * 128],
                col_ap(p, c, 1, vh), None, ALU.mult),
                reads=[rst[vh], rct], writes=[rB])
            kb.op("vector", lambda e, vh=vh, slot=slot: e.tensor_scalar(
                pb(p, PB_KTS, slot), pb(p, PB_KTOK, 0), col_ap(p, c, 6, vh), None, ALU.mult),
                reads=[rB, rct], writes=[rB])
            phS = ps["h1"][0:128, slot * 128: slot * 128 + 128]
            mmT(lambda e, slot=slot, o=phS: e.matmul(
                o, t["ident_bf"][:], pb(p, PB_SCR, slot), start=True, stop=False),
                [rB, rc], [rph[0]])
            mmT(lambda e, slot=slot, o=phS: e.matmul(
                o, pb(p, PB_KTS, slot), pb(p, ufin, slot), start=False, stop=True),
                [rB, rY], [rph[0]])
            kb.op("scalar", lambda e, vh=vh, slot=slot, o=phS: e.copy(
                t["states"][:, vh * 128:(vh + 1) * 128], o),
                reads=[rph[0]], writes=[rst[vh]])
    for vh in range(8):
        pO = ps[f"o{2 + vh // 4}"][0:128, (vh % 4) * 128:(vh % 4) * 128 + 128]
        kb.op("scalar", lambda e, vh=vh, o=pO: e.activation(
            pb(p, PB_SQO, vh % 2), o, AF.Square),
            reads=[rpO[vh // 4]], writes=[rB])
        pgn = ps["g"][0:8, 0:128]
        mmT(lambda e, vh=vh: e.matmul(
            pgn, t["colsel1"][:, vh * 8:(vh + 1) * 8], pb(p, PB_SQO, vh % 2),
            start=(vh == 0), stop=(vh == 7)), [rB, rc], [rpg])
    kb.op("scalar", lambda e, bc=ccol(p, kb, EPS6): e.activation(
        t["rowsF"][0:8, 5 * 256 + c * 128: 5 * 256 + (c + 1) * 128], ps["g"][0:8, 0:128],
        AF.Sqrt, bias=bc[0:8, :], scale=1.0 / 128), reads=[rpg, rc], writes=[rrF])
    kb.op("vector", lp(p, lambda e: e.reciprocal(
        t["rowsB"][0:8, 7 * 256 + c * 128: 7 * 256 + (c + 1) * 128],
        t["rowsF"][0:8, 5 * 256 + c * 128: 5 * 256 + (c + 1) * 128])),
        reads=[rrF], writes=[rrB])
    for vh in range(8):
        pT7 = ps["h1"][0:128, 256:384]
        mmT(lambda e, vh=vh, o=pT7: e.matmul(
            o, t["rowsel8"][0:8, vh * 128:(vh + 1) * 128],
            t["rowsB"][0:8, 7 * 256 + c * 128: 7 * 256 + (c + 1) * 128],
            start=True, stop=True), [rrB, rc], [rphs])
        kb.op("scalar", lambda e, o=pT7: e.copy(pb(p, PB_AQBC, 0), o), reads=[rphs], writes=[rB])
        pO = ps[f"o{2 + vh // 4}"][0:128, (vh % 4) * 128:(vh % 4) * 128 + 128]
        kb.op("vector", lambda e, vh=vh, o=pO: e.tensor_scalar(
            pf(p, PF_SCR, 0), o, t["gnwc"][:, 0:1], None, ALU.mult),
            reads=[rpO[vh // 4], rc], writes=[rF])
        kb.op("vector", lambda e: e.tensor_mul(pf(p, PF_SCR, 0), pf(p, PF_SCR, 0),
                                               pb(p, PB_AQBC, 0)),
              reads=[rF, rB], writes=[rF])
        core_dst = t["xnfm"][:, vh * TH + loc0 + c * 128: vh * TH + loc0 + c * 128 + 128]
        kb.op("vector", lambda e, vh=vh, a=core_dst: e.tensor_tensor(
            a, pf(p, PF_SCR, 0), t["zbuf"][:, vh * 256 + c * 128: vh * 256 + (c + 1) * 128],
            ALU.mult),
            reads=[rF, rz[vh]], writes=[rfm[(loc0 // 128 + c) * 8 + vh]])


PW_A = 0
PW_KTS = 1024
PW_Y = 2048
PW_N = 3072
PW_M = 4096
PW_AT = 5120


def pw(p, off, w=1024):
    return p.t["poolB"][:, off:off + w]


def v8(ap, x=128):
    return ap.rearrange("p (v x) -> p v x", x=x)


def bvn(ap, n=8):
    return ap.unsqueeze(1).to_broadcast([128, n, 128])


def colv(p, c, q, h0=0, n=8):
    b = (c * 8 + q) * 8 + h0
    return p.t["colsT"][:, b:b + n].unsqueeze(2).to_broadcast([128, n, 128])


def colv4(p, c, q):
    b = (c * 8 + q) * 8
    return (p.t["colsT"][:, b:b + 8].rearrange("p (k s) -> p k s", s=2)
            .unsqueeze(3).to_broadcast([128, 4, 2, 128]))


def emit_delta_chunk2(p, kb, b, blk, c):
    t, cfg, ps = p.t, p.cfg, p.ps
    TH = cfg.T // 2
    NTH = cfg.NT // 2
    loc0 = blk * 256
    Q0 = 16 * 259
    rc = R(p, "consts")
    rqc = R(p, "qkvc_t", 16)
    rz = R(p, "z_t", 8)
    rrF = R(p, "rowsF")
    rrB = R(p, "rowsB")
    rct = R(p, "colsT")
    rst = R(p, "states", 8)
    rfm = R(p, "xnfm_t", NTH * 8)
    rpm = R(p, "ps_misc")
    rphs = R(p, "ps_h1")
    rpTs = R(p, "ps_T")
    rpg = R(p, "ps_g")
    rpo = R(p, "ps_o", 4)
    rsq = R(p, "sq_pool")
    rBc = R(p, "poolB_delta")
    rA = R(p, "pw_a")
    rK = R(p, "pw_kts")
    rY = R(p, "pw_y")
    rN = R(p, "pw_n")
    rM = R(p, "pw_m")
    rT = R(p, "pw_at")
    rPF = R(p, "poolF_delta")
    sl = slice(c * 128, (c + 1) * 128)
    pB, pF = t["poolB"], t["poolF"]
    states = t["states"]

    def mm(emit, reads, writes):
        kb.op("tensor", emit, reads=reads, writes=writes)

    def qch(h):
        a = t["accbuf"][:, Q0 + 512 * h: Q0 + 512 * h + 512]
        return (a.rearrange("p (k x) -> p k x", x=256)[:, :, sl]
                .unsqueeze(2).to_broadcast([128, 2, 2, 128]))

    ob = {0: ps["o0"], 1: ps["o1"], 2: ps["o2"], 3: ps["o3"]}
    hm = {0: ps["h1"], 1: ps["misc"]}
    rob = {0: rpo[0], 1: rpo[1], 2: rpo[2], 3: rpo[3]}
    rhm = {0: rphs, 1: rpm}

    for kh in range(4):
        mm(lambda e, kh=kh: e.matmul(
            ps["o0"][0:128, kh * 128:(kh + 1) * 128],
            t["rowsel8"][0:8, (2 * kh) * 128:(2 * kh) * 128 + 128],
            t["rowsB"][0:8, 1 * 256 + c * 128: 1 * 256 + (c + 1) * 128],
            start=(kh == 0), stop=(kh == 3)), [rrB, rc], [rpo[0]])
    qv = (t["accbuf"][:, Q0:Q0 + 1024].rearrange("p (k x) -> p k x", x=256)[:, :, sl])
    kb.op("vector", lambda e, qv=qv: e.tensor_tensor(
        v8(pB[:, PW_A:PW_A + 512]), qv, v8(ps["o0"][0:128, 0:512]), ALU.mult),
        reads=[rqc[0], rqc[1], rqc[2], rqc[3], rpo[0]], writes=[rA, rsq])

    for kh in range(4):
        mm(lambda e, kh=kh: e.transpose(
            ps["T"][0:128, kh * 128:(kh + 1) * 128], qkvc_tile(p, 4 + kh)[:, sl],
            t["ident_bf"][:]), [rqc[4 + kh], rc], [rpTs])
    kb.op("scalar", lambda e: e.copy(pB[:, PW_A + 512:PW_A + 1024], ps["T"][0:128, 0:512]),
          reads=[rpTs], writes=[rA, rsq])
    ktokd = (pB[:, PW_A + 512:PW_A + 1024].rearrange("p (k x) -> p k x", x=128)
             .unsqueeze(2).to_broadcast([128, 4, 2, 128]))
    kb.op("vector", lambda e, kd=ktokd: e.tensor_tensor(
        pw(p, PW_KTS).rearrange("p (k s x) -> p k s x", s=2, x=128), kd,
        colv4(p, c, 6), ALU.mult), reads=[rA, rct], writes=[rK, rsq])

    for vh in range(8):
        mm(lambda e, vh=vh: e.transpose(
            ps["T"][0:128, vh * 128:(vh + 1) * 128], qkvc_tile(p, 8 + vh)[:, sl],
            t["ident_bf"][:]), [rqc[8 + vh], rc], [rpTs])
    kb.op("vector", lambda e: e.tensor_copy(pB[:, PW_Y:PW_Y + 512], ps["T"][0:128, 0:512]),
          reads=[rpTs], writes=[rY, rBc])
    kb.op("scalar", lambda e: e.copy(pB[:, PW_Y + 512:PW_Y + 1024], ps["T"][0:128, 512:1024]),
          reads=[rpTs], writes=[rY, rBc])

    for vh in range(8):
        bank = ob[vh // 4]
        mm(lambda e, vh=vh, bank=bank: e.matmul(
            bank[0:128, (vh % 4) * 128:(vh % 4) * 128 + 128],
            t["rowsel8f"][0:8, vh * 128:(vh + 1) * 128],
            t["rowsF"][0:8, 1 * 256 + c * 128: 1 * 256 + (c + 1) * 128],
            start=(vh % 4 == 0), stop=(vh % 4 == 3)), [rrF, rc], [rob[vh // 4]])
    for h in range(2):
        kb.op("vector", lambda e, h=h: e.tensor_tensor(
            v8(pF[:, h * 512:(h + 1) * 512]), v8(ob[h][0:128, 0:512]),
            colv(p, c, 0, h0=4 * h, n=4), ALU.subtract),
            reads=[rob[h], rct], writes=[rPF])

    for kh in range(4):
        k_t = qkvc_tile(p, 4 + kh)[:, sl]
        for slot in range(2):
            vh = 2 * kh + slot
            r4 = vh % 4
            mm(lambda e, k_t=k_t, vh=vh, r4=r4: e.matmul(
                hm[vh // 4][0:128, r4 * 128:r4 * 128 + 128], k_t,
                pB[:, PW_A + (vh // 2) * 128: PW_A + (vh // 2) * 128 + 128],
                start=(r4 == 0), stop=(r4 == 3)),
                [rqc[4 + kh], rA], [rhm[vh // 4]])
            mm(lambda e, k_t=k_t, vh=vh, r4=r4: e.matmul(
                ob[vh // 4][0:128, r4 * 128:r4 * 128 + 128], k_t, k_t,
                start=(r4 == 0), stop=(r4 == 3)),
                [rqc[4 + kh]], [rob[vh // 4]])
            mm(lambda e, k_t=k_t, vh=vh, r4=r4: e.matmul(
                ob[2 + vh // 4][0:128, r4 * 128:r4 * 128 + 128], k_t,
                states[:, vh * 128:(vh + 1) * 128],
                start=(r4 == 0), stop=(r4 == 3)),
                [rqc[4 + kh], rst[vh]], [rob[2 + vh // 4]])

    kb.op("vector", lambda e: e.tensor_tensor(
        v8(pw(p, PW_Y)), v8(pw(p, PW_Y)), colv(p, c, 4), ALU.mult),
        reads=[rY, rct], writes=[rY, rBc])
    for h in range(2):
        kb.op("vector", lambda e, h=h: e.tensor_tensor(
            v8(pF[:, 1024 + h * 512: 1024 + (h + 1) * 512]), v8(ob[2 + h][0:128, 0:512]),
            colv(p, c, 5, h0=4 * h, n=4), ALU.mult),
            reads=[rob[2 + h], rct], writes=[rPF])
    kb.op("vector", lambda e: e.tensor_tensor(
        pw(p, PW_Y), pw(p, PW_Y), pF[:, 1024:2048], ALU.subtract),
        reads=[rY, rPF], writes=[rY, rBc])

    kb.op("vector", lambda e: e.tensor_scalar_min(pF[:, 1024:2048], pF[:, 0:1024], 0.0),
          reads=[rPF], writes=[rPF])
    kb.op("scalar", lambda e: e.activation(pw(p, PW_A), pF[:, 1024:2048], AF.Exp),
          reads=[rPF], writes=[rA, rsq])

    for vh in range(8):
        bank = ob[2 + vh // 4]
        mm(lambda e, vh=vh, bank=bank: e.matmul(
            bank[0:128, (vh % 4) * 128:(vh % 4) * 128 + 128],
            t["rowsel8"][0:8, vh * 128:(vh + 1) * 128],
            t["rowsB"][0:8, 3 * 256 + c * 128: 3 * 256 + (c + 1) * 128],
            start=(vh % 4 == 0), stop=(vh % 4 == 3)), [rrB, rc], [rob[2 + vh // 4]])
    for h in range(2):
        kb.op("vector", lambda e, h=h: e.tensor_tensor(
            v8(pB[:, PW_N + h * 512: PW_N + (h + 1) * 512]), v8(ob[h][0:128, 0:512]),
            colv(p, c, 2, h0=4 * h, n=4), ALU.mult),
            reads=[rob[h], rct], writes=[rN])
        kb.op("vector", lambda e, h=h: e.tensor_mul(
            v8(pB[:, PW_N + h * 512: PW_N + (h + 1) * 512]),
            v8(pB[:, PW_N + h * 512: PW_N + (h + 1) * 512]),
            v8(ob[2 + h][0:128, 0:512])),
            reads=[rob[2 + h], rN], writes=[rN])
    kb.op("vector", lambda e: e.tensor_tensor(
        pw(p, PW_N), pw(p, PW_N), pw(p, PW_A), ALU.mult), reads=[rN, rA], writes=[rN])
    kb.op("gpsimd", lambda e: e.tensor_mul(
        v8(pw(p, PW_N)), v8(pw(p, PW_N)), bvn(t["striu"][:])),
        reads=[rN, rc], writes=[rN])

    for h in range(2):
        kb.op("vector", lambda e, h=h: e.tensor_tensor(
            v8(pB[:, PW_AT + h * 512: PW_AT + (h + 1) * 512]), v8(hm[h][0:128, 0:512]),
            colv(p, c, 2, h0=4 * h, n=4), ALU.mult),
            reads=[rhm[h], rct], writes=[rT])
    kb.op("vector", lambda e: e.tensor_tensor(
        pw(p, PW_AT), pw(p, PW_AT), pw(p, PW_A), ALU.mult), reads=[rT, rA], writes=[rT])
    kb.op("gpsimd", lambda e: e.tensor_mul(
        v8(pw(p, PW_AT)), v8(pw(p, PW_AT)), bvn(t["triu"][:])),
        reads=[rT, rc], writes=[rT])

    kb.op("vector", lambda e: e.tensor_scalar_max(pF[:, 1024:2048], pF[:, 0:1024], 0.0),
          reads=[rPF], writes=[rPF])
    kb.op("scalar", lambda e: e.activation(pw(p, PW_A), pF[:, 1024:2048], AF.Exp, scale=-1.0),
          reads=[rPF], writes=[rA, rsq])
    for vh in range(8):
        bank = hm[vh // 4]
        mm(lambda e, vh=vh, bank=bank: e.matmul(
            bank[0:128, (vh % 4) * 128:(vh % 4) * 128 + 128],
            t["rowsel8"][0:8, vh * 128:(vh + 1) * 128],
            t["rowsB"][0:8, 2 * 256 + c * 128: 2 * 256 + (c + 1) * 128],
            start=(vh % 4 == 0), stop=(vh % 4 == 3)), [rrB, rc], [rhm[vh // 4]])
    for h in range(2):
        kb.op("vector", lambda e, h=h: e.tensor_tensor(
            v8(pB[:, PW_M + h * 512: PW_M + (h + 1) * 512]), v8(ob[h][0:128, 0:512]),
            colv(p, c, 3, h0=4 * h, n=4), ALU.mult),
            reads=[rob[h], rct], writes=[rM])
        kb.op("vector", lambda e, h=h: e.tensor_mul(
            v8(pB[:, PW_M + h * 512: PW_M + (h + 1) * 512]),
            v8(pB[:, PW_M + h * 512: PW_M + (h + 1) * 512]),
            v8(hm[h][0:128, 0:512])),
            reads=[rhm[h], rM], writes=[rM])
    kb.op("vector", lambda e: e.tensor_tensor(
        pw(p, PW_M), pw(p, PW_M), pw(p, PW_A), ALU.mult), reads=[rM, rA], writes=[rM])
    kb.op("gpsimd", lambda e: e.tensor_mul(
        v8(pw(p, PW_M)), v8(pw(p, PW_M)), bvn(t["stril"][:])),
        reads=[rM, rc], writes=[rM])

    for vh in range(8):
        mm(lambda e, vh=vh: e.matmul(
            hm[vh // 4][0:128, (vh % 4) * 128:(vh % 4) * 128 + 128],
            pB[:, PW_N + vh * 128: PW_N + vh * 128 + 128],
            pB[:, PW_Y + vh * 128: PW_Y + vh * 128 + 128],
            start=(vh % 4 == 0), stop=(vh % 4 == 3)), [rN, rY], [rhm[vh // 4]])
    for h in range(2):
        op = (lambda e, h=h: e.tensor_tensor(
            pB[:, PW_Y + h * 512: PW_Y + (h + 1) * 512],
            pB[:, PW_Y + h * 512: PW_Y + (h + 1) * 512],
            hm[h][0:128, 0:512], ALU.subtract))
        kb.op("vector", op, reads=[rY, rhm[h]], writes=[rY, rBc])

    for k in range(1, 7):
        for vh in range(8):
            mm(lambda e, vh=vh: e.matmul(
                ob[vh // 4][0:128, (vh % 4) * 128:(vh % 4) * 128 + 128],
                pB[:, PW_M + vh * 128: PW_M + vh * 128 + 128],
                pB[:, PW_N + vh * 128: PW_N + vh * 128 + 128],
                start=(vh % 4 == 0), stop=(vh % 4 == 3)), [rM, rN], [rob[vh // 4]])
            if k < 6:
                mm(lambda e, vh=vh: e.matmul(
                    ob[2 + vh // 4][0:128, (vh % 4) * 128:(vh % 4) * 128 + 128],
                    pB[:, PW_N + vh * 128: PW_N + vh * 128 + 128],
                    pB[:, PW_M + vh * 128: PW_M + vh * 128 + 128],
                    start=(vh % 4 == 0), stop=(vh % 4 == 3)), [rM, rN], [rob[2 + vh // 4]])
        kb.op("vector", lambda e: e.tensor_copy(pB[:, PW_N:PW_N + 512], ps["o0"][0:128, 0:512]),
              reads=[rpo[0]], writes=[rN])
        kb.op("scalar", lambda e: e.copy(pB[:, PW_N + 512:PW_N + 1024], ps["o1"][0:128, 0:512]),
              reads=[rpo[1]], writes=[rN])
        if k < 6:
            kb.op("scalar", lambda e: e.copy(pB[:, PW_M:PW_M + 512], ps["o2"][0:128, 0:512]),
                  reads=[rpo[2]], writes=[rM])
            kb.op("vector", lambda e: e.tensor_copy(pB[:, PW_M + 512:PW_M + 1024],
                                                    ps["o3"][0:128, 0:512]),
                  reads=[rpo[3]], writes=[rM])
        for vh in range(8):
            mm(lambda e, vh=vh: e.matmul(
                hm[vh // 4][0:128, (vh % 4) * 128:(vh % 4) * 128 + 128],
                pB[:, PW_N + vh * 128: PW_N + vh * 128 + 128],
                pB[:, PW_Y + vh * 128: PW_Y + vh * 128 + 128],
                start=(vh % 4 == 0), stop=(vh % 4 == 3)), [rN, rY], [rhm[vh // 4]])
        for h in range(2):
            op = (lambda e, h=h: e.tensor_tensor(
                pB[:, PW_Y + h * 512: PW_Y + (h + 1) * 512],
                pB[:, PW_Y + h * 512: PW_Y + (h + 1) * 512],
                hm[h][0:128, 0:512], ALU.add))
            kb.op("vector", op, reads=[rY, rhm[h]], writes=[rY, rBc])

    for vh in range(8):
        bank = ob[vh // 4]
        mm(lambda e, vh=vh, bank=bank: e.matmul(
            bank[0:128, (vh % 4) * 128:(vh % 4) * 128 + 128],
            t["rowsel8"][0:8, vh * 128:(vh + 1) * 128],
            t["rowsB"][0:8, 6 * 256 + c * 128: 6 * 256 + (c + 1) * 128],
            start=(vh % 4 == 0), stop=(vh % 4 == 3)), [rrB, rc], [rob[vh // 4]])
    for h in range(2):
        kb.op("vector", lambda e, h=h: e.tensor_tensor(
            pB[:, PW_A + h * 512: PW_A + (h + 1) * 512].rearrange(
                "p (k s x) -> p k s x", s=2, x=128),
            qch(h),
            ob[h][0:128, 0:512].rearrange("p (k s x) -> p k s x", s=2, x=128),
            ALU.mult),
            reads=[rqc[2 * h], rqc[2 * h + 1], rob[h]], writes=[rA, rsq])

    for vh in range(8):
        bank = ob[2 + vh // 4]
        mm(lambda e, vh=vh, bank=bank: e.matmul(
            bank[0:128, (vh % 4) * 128:(vh % 4) * 128 + 128],
            states[:, vh * 128:(vh + 1) * 128],
            pB[:, PW_A + vh * 128: PW_A + vh * 128 + 128],
            start=(vh % 4 == 0), stop=False), [rst[vh], rA], [rob[2 + vh // 4]])
        mm(lambda e, vh=vh, bank=bank: e.matmul(
            bank[0:128, (vh % 4) * 128:(vh % 4) * 128 + 128],
            pB[:, PW_Y + vh * 128: PW_Y + vh * 128 + 128],
            pB[:, PW_AT + vh * 128: PW_AT + vh * 128 + 128],
            start=False, stop=(vh % 4 == 3)), [rY, rT], [rob[2 + vh // 4]])

    kb.op("vector", lambda e: e.tensor_tensor(
        v8(pw(p, PW_M)), v8(states[:, 0:1024]), colv(p, c, 1), ALU.mult),
        reads=[rct] + rst, writes=[rM])
    for vh in range(8):
        mm(lambda e, vh=vh: e.matmul(
            hm[vh // 4][0:128, (vh % 4) * 128:(vh % 4) * 128 + 128],
            pB[:, PW_KTS + vh * 128: PW_KTS + vh * 128 + 128],
            pB[:, PW_Y + vh * 128: PW_Y + vh * 128 + 128],
            start=(vh % 4 == 0), stop=(vh % 4 == 3)), [rK, rY], [rhm[vh // 4]])
    for h in range(2):
        kb.op("vector", lambda e, h=h: e.tensor_tensor(
            states[:, h * 512:(h + 1) * 512], pB[:, PW_M + h * 512: PW_M + (h + 1) * 512],
            hm[h][0:128, 0:512], ALU.add),
            reads=[rM, rhm[h]], writes=rst[4 * h:4 * h + 4])

    kb.op("scalar", lambda e: e.activation(pB[:, PW_AT:PW_AT + 512], ps["o2"][0:128, 0:512],
                                           AF.Square), reads=[rpo[2]], writes=[rT])
    kb.op("scalar", lambda e: e.activation(pB[:, PW_AT + 512:PW_AT + 1024],
                                           ps["o3"][0:128, 0:512],
                                           AF.Square), reads=[rpo[3]], writes=[rT])
    for vh in range(8):
        mm(lambda e, vh=vh: e.matmul(
            ps["g"][0:8, 0:128], t["colsel1"][:, vh * 8:(vh + 1) * 8],
            pB[:, PW_AT + vh * 128: PW_AT + vh * 128 + 128],
            start=(vh == 0), stop=(vh == 7)), [rT, rc], [rpg])
    kb.op("scalar", lambda e, bc=ccol(p, kb, EPS6): e.activation(
        t["rowsF"][0:8, 5 * 256 + c * 128: 5 * 256 + (c + 1) * 128], ps["g"][0:8, 0:128],
        AF.Sqrt, bias=bc[0:8, :], scale=1.0 / 128), reads=[rpg, rc], writes=[rrF])
    kb.op("vector", lp(p, lambda e: e.reciprocal(
        t["rowsB"][0:8, 7 * 256 + c * 128: 7 * 256 + (c + 1) * 128],
        t["rowsF"][0:8, 5 * 256 + c * 128: 5 * 256 + (c + 1) * 128])),
        reads=[rrF], writes=[rrB])
    for vh in range(8):
        bank = ob[vh // 4]
        mm(lambda e, vh=vh, bank=bank: e.matmul(
            bank[0:128, (vh % 4) * 128:(vh % 4) * 128 + 128],
            t["rowsel8"][0:8, vh * 128:(vh + 1) * 128],
            t["rowsB"][0:8, 7 * 256 + c * 128: 7 * 256 + (c + 1) * 128],
            start=(vh % 4 == 0), stop=(vh % 4 == 3)), [rrB, rc], [rob[vh // 4]])
    for h in range(2):
        kb.op("scalar", lambda e, h=h: e.copy(
            pB[:, PW_N + h * 512: PW_N + (h + 1) * 512], ob[h][0:128, 0:512]),
            reads=[rob[h]], writes=[rN])
        kb.op("vector", lambda e, h=h: e.scalar_tensor_tensor(
            pF[:, h * 512:(h + 1) * 512], ob[2 + h][0:128, 0:512], t["gnwc"][:, 0:1],
            pB[:, PW_N + h * 512: PW_N + (h + 1) * 512], ALU.mult, ALU.mult),
            reads=[rob[2 + h], rN, rc], writes=[rPF])
    zv = v8(t["zbuf"][:, 0:2048], x=256)[:, :, sl]
    xout = v8(t["xnfm"][:, 0:8 * TH], x=TH)[:, :, loc0 + c * 128: loc0 + c * 128 + 128]
    kb.op("vector", lambda e, zv=zv, xout=xout: e.tensor_tensor(
        xout, v8(pF[:, 0:1024]), zv, ALU.mult),
        reads=[rPF] + [rz[vh] for vh in range(8)],
        writes=[rfm[(loc0 // 128 + c) * 8 + vh] for vh in range(8)])


def emit_attn_block_epilogue(p, kb, b, blk):
    t, cfg, ps, d = p.t, p.cfg, p.ps, p.d
    TH = cfg.T // 2
    NTH = cfg.NT // 2
    loc0 = blk * 256
    rc = R(p, "consts")
    rw = R(p, "attn_w")
    rfm = R(p, "xnfm_t", NTH * 8)
    rphs = R(p, "ps_h1")
    rpm2 = R(p, "ps_misc")
    rpTs = R(p, "ps_T")
    rpT = [rpTs] * 4
    rofm = R(p, "ofm_r", 2)
    routt = R(p, "outt_r")
    rlnr = R(p, "xnb2")
    for sub in range(2):
        tok = b * cfg.S + loc0 + sub * 128
        kb.dma("dgp", "lnr", lambda e, tok=tok: e.dma_start(
            out=t["xnb2"][:], in_=d["lnbuf"][tok:tok + 128, :]),
            reads=[R(p, "ln_dram")], writes=[rlnr])
        for dt in range(8):
            if dt % 2 == 0:
                ph = ps["h1"][0:128, 0:128]
            else:
                ph = ps["misc"][0:128, 0:128]
            for vt in range(8):
                kb.op("tensor", lambda e, dt=dt, vt=vt, o=ph, loc0=loc0, sub=sub: e.matmul(
                    o, wout_tile(p, vt, dt),
                    t["xnfm"][:, vt * TH + loc0 + sub * 128: vt * TH + loc0 + sub * 128 + 128],
                    start=(vt == 0), stop=(vt == 7)),
                    reads=[rw, rfm[(loc0 // 128 + sub) * 8 + vt]],
                    writes=[rphs if dt % 2 == 0 else rpm2])
            ofm = t["ofm"][:, (dt % 2) * 256:(dt % 2) * 256 + 128]
            kb.op("vector", lambda e, o=ph, a=ofm: e.tensor_copy(a, o),
                  reads=[rphs if dt % 2 == 0 else rpm2], writes=[rofm[dt % 2]])
            pT = ps["T"][0:128, (dt % 4) * 128:(dt % 4) * 128 + 128]
            kb.op("tensor", lambda e, a=ofm, o=pT: e.transpose(o, a, t["ident_bf"][:]),
                  reads=[rofm[dt % 2], rc], writes=[rpT[dt % 4]])
            kb.op("vector", lambda e, dt=dt, o=pT: e.tensor_add(
                t["outt"][:, dt * 128:(dt + 1) * 128], o, t["xnb2"][:, dt * 128:(dt + 1) * 128]),
                reads=[rpT[dt % 4], rlnr], writes=[routt])
        kb.dma("dsync", "out", lambda e, tok=tok: e.dma_start(
            out=d["hbuf"][tok:tok + 128, :], in_=t["outt"][:, 0:1024]),
            reads=[routt], writes=[R(p, "h_dram")])


def emit_attn_core(p, kb, src):
    cfg = p.cfg
    t = p.t
    for b in range(cfg.B):
        emit_ln_half(p, kb, src, half=b, mode="lnzrms", store_dram=p.d["lnbuf"],
                     zw_col=t["wln1c"])
        rst = R(p, "states", 8)
        for vh in range(8):
            kb.op("vector", lambda e, vh=vh: e.memset(t["states"][:, vh * 128:(vh + 1) * 128], 0.0),
                  writes=[rst[vh]])
        for blk in range(cfg.S // 256):
            emit_attn_block_prologue(p, kb, b, blk)
            for c in range(2):
                emit_delta_chunk2(p, kb, b, blk, c)
            emit_attn_block_epilogue(p, kb, b, blk)


def emit_attn_moe(p, kb, r1src, dst, moe_trunc=None):
    t, cfg, ps, d = p.t, p.cfg, p.ps, p.d
    TH = cfg.T // 2
    NTH = cfg.NT // 2
    NT = cfg.NT
    T = cfg.T
    rc = R(p, "consts")
    rmv = R(p, "mv")
    rroute = R(p, "route")
    rrsum = R(p, "cumw")
    rrfm = R(p, "rowfm")
    reqb = R(p, "eqbf")
    rpTs = R(p, "ps_T")
    rpo = R(p, "ps_o", 4)
    rpm = R(p, "ps_misc")
    rphs = R(p, "ps_h1")
    rprods = R(p, "prods", 16)
    rrbc = R(p, "selbc")
    rofm = R(p, "ofm_r", 2)
    routt = R(p, "outt_r")
    rxres = R(p, "xres_r")
    rhres = R(p, "hres_r")
    rx2 = R(p, "x2_ring", 2)
    rmw = R(p, "moe_w")
    rmp = R(p, "moepart_dram")
    for hf in range(2):
        emit_ln_half(p, kb, d["hbuf"], hf, mode="rms", do_gate=("router",),
                     zw_col=t["wln2c"], store_fm_dram=True)
        t0 = hf * NTH
        kb.op("vector", lambda e, t0=t0: e.reduce_sum(
            t["cumw"][:, t0:t0 + NTH],
            t["mv"][:, t0 * 8:(t0 + NTH) * 8].rearrange("p (j e) -> p j e", e=8),
            axis=AX.X),
            reads=[rmv], writes=[rrsum])
        kb.op("vector", lambda e, t0=t0: e.reciprocal(t["cumw"][:, t0:t0 + NTH],
                                                      t["cumw"][:, t0:t0 + NTH]),
              reads=[rrsum], writes=[rrsum])
        kb.op("vector", lambda e, t0=t0: e.tensor_tensor(
            t["route"][:, t0 * 8:(t0 + NTH) * 8].rearrange("p (j e) -> p j e", e=8),
            t["mv"][:, t0 * 8:(t0 + NTH) * 8].rearrange("p (j e) -> p j e", e=8),
            t["cumw"][:, t0:t0 + NTH].unsqueeze(2).to_broadcast([128, NTH, 8]),
            ALU.mult),
            reads=[rmv, rrsum], writes=[rroute])
    if moe_trunc == "ln":
        return
    for ti in range(NT):
        kb.op("vector", lambda e, ti=ti: e.tensor_copy(t["eqbf"][:, ti * 8:(ti + 1) * 8],
                                                       t["route"][:, ti * 8:(ti + 1) * 8]),
              reads=[rroute], writes=[reqb])
        kb.op("tensor", lambda e, ti=ti: e.transpose(ps["T"][0:8, 512:640],
                                                     t["eqbf"][:, ti * 8:(ti + 1) * 8],
                                                     t["ident_bf"][:]),
              reads=[reqb, rc], writes=[rpTs])
        kb.op("scalar", lambda e, ti=ti: e.copy(t["rowfm"][0:8, ti * 128:(ti + 1) * 128],
                                                ps["T"][0:8, 512:640]),
              reads=[rpTs], writes=[rrfm])
    kb.barrier()
    if moe_trunc == "route":
        return
    for ih in range(1 if moe_trunc in ("w", "pa", "pb", "ih0", "st0", "st1", "st2") else 2):
        for e_i in range(NEXP):
            kb.dma("dgp", "mw", lambda e, ei=e_i, ih=ih: e.dma_start(
                out=t["wbuf"][:, ei * 2048:(ei + 1) * 2048].rearrange("p (dt c) -> p dt c", c=256),
                in_=d["aW_gate_e"][ei][:, ih * 256:(ih + 1) * 256].rearrange(
                    "(dt p) c -> p dt c", p=128)),
                writes=[rmw])
            kb.dma("dgp", "mw", lambda e, ei=e_i, ih=ih: e.dma_start(
                out=t["wbuf"][:, 16384 + ei * 2048: 16384 + (ei + 1) * 2048].rearrange(
                    "p (dt c) -> p dt c", c=256),
                in_=d["aW_up_e"][ei][:, ih * 256:(ih + 1) * 256].rearrange(
                    "(dt p) c -> p dt c", p=128)),
                writes=[rmw])
            kb.dma("dgp", "mw", lambda e, ei=e_i, ih=ih: e.dma_start(
                out=t["xnfm"][:, ei * 2048:(ei + 1) * 2048].rearrange("p (it c) -> p it c", c=1024),
                in_=d["aW_down_e"][ei][ih * 256:(ih + 1) * 256, :].rearrange(
                    "(it p) c -> p it c", p=128)),
                writes=[rmw])
        if moe_trunc == "w":
            return
        for tt in range(T // 256):
            tok0 = tt * 256
            x2t = t["xnb"][:, (tt % 2) * 1024: (tt % 2) * 1024 + 1024]
            x2t = t["zbuf"][:, 0:2048] if tt % 2 == 0 else None
            if x2t is None:
                x2t = t["accbuf"][:, 4096:6144]
            kb.dma("dsync", ("x2", tt % 2), lambda e, a=x2t, tok0=tok0: e.dma_start(
                out=a.rearrange("p (dt c) -> p dt c", c=256),
                in_=d["x2fm"][:, :, tok0:tok0 + 256].transpose([1, 0, 2])),
                reads=[R(p, "x2_dram")], writes=[rx2[tt % 2]])
            for e_i in range(NEXP):
                kb.op("tensor", lambda e, ei=e_i, tok0=tok0: e.matmul(
                    ps["o0"][0:128, 0:256], t["rowsel8"][0:8, ei * 128:(ei + 1) * 128],
                    t["rowfm"][0:8, tok0:tok0 + 256], start=True, stop=True),
                    reads=[rrfm, rc], writes=[rpo[0]])
                kb.op("vector", lambda e: e.tensor_copy(t["selbc"][:], ps["o0"][0:128, 0:256]),
                      reads=[rpo[0]], writes=[rrbc])
                for it in range(2):
                    pg_ = ps["h1"][0:128, it * 256:(it + 1) * 256]
                    pu_ = ps["misc"][0:128, it * 256:(it + 1) * 256]
                    for dt in range(8):
                        kb.op("tensor", lambda e, ei=e_i, it=it, dt=dt, o=pg_, x2t=x2t: e.matmul(
                            o, t["wbuf"][:, ei * 2048 + dt * 256 + it * 128:
                                         ei * 2048 + dt * 256 + it * 128 + 128],
                            x2t[:, dt * 256: dt * 256 + 256],
                            start=(dt == 0 and it == 0), stop=False if not (dt == 7 and it == 1) else True),
                            reads=[rmw, rx2[tt % 2]], writes=[rphs])
                        kb.op("tensor", lambda e, ei=e_i, it=it, dt=dt, o=pu_, x2t=x2t: e.matmul(
                            o, t["wbuf"][:, 16384 + ei * 2048 + dt * 256 + it * 128:
                                         16384 + ei * 2048 + dt * 256 + it * 128 + 128],
                            x2t[:, dt * 256: dt * 256 + 256],
                            start=(dt == 0 and it == 0), stop=False if not (dt == 7 and it == 1) else True),
                            reads=[rmw, rx2[tt % 2]], writes=[rpm])
                for it in range(2):
                    pg_ = ps["h1"][0:128, it * 256:(it + 1) * 256]
                    pu_ = ps["misc"][0:128, it * 256:(it + 1) * 256]
                    pr = prods_tile(p, e_i, it)
                    kb.op("scalar", lambda e, o=pg_, a=pr: e.activation(a, o, AF.Silu),
                          reads=[rphs], writes=[rprods[e_i * 2 + it]])
                    kb.op("vector", lambda e, o=pu_, a=pr: e.tensor_mul(a, a, o),
                          reads=[rpm, rprods[e_i * 2 + it]], writes=[rprods[e_i * 2 + it]])
                    kb.op("gpsimd", lambda e, a=pr: e.tensor_mul(a, a, t["selbc"][:]),
                          reads=[rrbc, rprods[e_i * 2 + it]], writes=[rprods[e_i * 2 + it]])
            if moe_trunc == "pa":
                continue
            for dt in range(8):
                po = ps[f"o{dt // 2}"][0:128, (dt % 2) * 256:(dt % 2) * 256 + 256]
                for e_i in range(NEXP):
                    for it in range(2):
                        kb.op("tensor", lambda e, ei=e_i, it=it, dt=dt, o=po: e.matmul(
                            o, t["xnfm"][:, ei * 2048 + it * 1024 + dt * 128:
                                         ei * 2048 + it * 1024 + dt * 128 + 128],
                            prods_tile(p, ei, it),
                            start=(ei == 0 and it == 0 and dt % 2 == 0),
                            stop=(ei == 7 and it == 1 and dt % 2 == 1)),
                            reads=[rmw, rprods[e_i * 2 + it]],
                            writes=[rpo[dt // 2]])
            if moe_trunc == "pb":
                continue
            if ih == 0:
                for dt in range(8):
                    po = ps[f"o{dt // 2}"][0:128, (dt % 2) * 256:(dt % 2) * 256 + 256]
                    ofm = t["ofm"][:, (dt % 2) * 256:(dt % 2) * 256 + 256]
                    eng = "vector" if dt % 2 == 0 else "scalar"
                    if moe_trunc == "st1":
                        eng = "vector"
                    elif moe_trunc == "st2":
                        eng = "scalar"
                    if eng == "vector":
                        kb.op(eng, lambda e, o=po, a=ofm: e.tensor_copy(a, o),
                              reads=[rpo[dt // 2]], writes=[rofm[dt % 2]])
                    else:
                        kb.op(eng, lambda e, o=po, a=ofm: e.copy(a, o),
                              reads=[rpo[dt // 2]], writes=[rofm[dt % 2]])
                    if moe_trunc == "st0":
                        continue
                    kb.dma("dsync", "mp", lambda e, dt=dt, tok0=tok0, a=ofm: e.dma_start(
                        out=d["moepart"][dt, :, tok0:tok0 + 256], in_=a),
                        reads=[rofm[dt % 2]], writes=[rmp])
            else:
                rmw2 = R(p, "moep_w")
                rofw = R(p, "ofm_wide")
                rxnb2s = R(p, "xnb_ring", 2)
                rpr = R(p, "prods", 16)
                for sub in range(2):
                    tok = tok0 + sub * 128
                    kb.dma("dsync", "xr", lambda e, tok=tok: e.dma_start(
                        out=t["xres"][:], in_=r1src[tok:tok + 128, :]),
                        writes=[rxres])
                    kb.dma("dgp", "hr", lambda e, tok=tok: e.dma_start(
                        out=t["sq_scr"][:, 0:1024], in_=d["hbuf"][tok:tok + 128, :]),
                        reads=[R(p, "h_dram")], writes=[rhres])
                    kb.dma("dgp", "mp2", lambda e, tok=tok: e.dma_start(
                        out=t["xnb"][:, 0:1024].rearrange("p (dt x) -> p dt x", x=128),
                        in_=d["moepart"][:, :, tok:tok + 128].transpose([1, 0, 2])),
                        reads=[rmp], writes=[rmw2, rxnb2s[0], rxnb2s[1]])
                    ofm_w = t["accbuf"][:, 0:1024]
                    for bk in range(4):
                        po = ps[f"o{bk}"][0:128, 0:512].rearrange(
                            "p (d x) -> p d x", x=256)[:, :, sub * 128:sub * 128 + 128]
                        kb.op("vector", lambda e, bk=bk, o=po: e.tensor_add(
                            ofm_w[:, bk * 256:(bk + 1) * 256].rearrange(
                                "p (d x) -> p d x", x=128),
                            o,
                            t["xnb"][:, bk * 256:(bk + 1) * 256].rearrange(
                                "p (d x) -> p d x", x=128)),
                            reads=[rpo[bk], rmw2],
                            writes=[rofw, rpr[bk]])
                    for dt in range(8):
                        kb.op("tensor", lambda e, dt=dt: e.transpose(
                            ps["T"][0:128, dt * 128:(dt + 1) * 128],
                            ofm_w[:, dt * 128:(dt + 1) * 128], t["ident_bf"][:]),
                            reads=[rofw, rc], writes=[rpTs])
                    kb.op("vector", lambda e: e.tensor_add(
                        t["outt"][:, 0:1024], ps["T"][0:128, 0:1024], t["sq_scr"][:, 0:1024]),
                        reads=[rpTs, rhres], writes=[routt])
                    kb.op("gpsimd", lambda e: e.tensor_add(
                        t["outt"][:, 0:1024], t["outt"][:, 0:1024], t["xres"][:, 0:1024]),
                        reads=[rxres, routt], writes=[routt])
                    kb.dma("dsync", "out", lambda e, tok=tok: e.dma_start(
                        out=dst[tok:tok + 128, :], in_=t["outt"][:, 0:1024]),
                        reads=[routt], writes=[R(p, "dst_dram")])


def prods_tile(p, e, it):
    c = (e * 2 + it) * 256
    return p.t["accbuf"][:, c:c + 256]


# version 12
# speedup vs baseline: 2.0506x; 1.0240x over previous
import numpy as np
from contextlib import ExitStack

import concourse.bass as bass
import concourse.mybir as mybir
from concourse.bass_utils import run_bass_kernel_spmd

F32 = mybir.dt.float32
BF16 = mybir.dt.bfloat16
I32 = mybir.dt.int32
AF = mybir.ActivationFunctionType
ALU = mybir.AluOpType
AX = mybir.AxisListType

HK, HV, DK, DV = 4, 8, 128, 128
KEY = HK * DK
VAL = HV * DV
NEXP = 8
EPS6 = 1e-6
EPS5 = 1e-5

COMPUTE_ENGINES = ("tensor", "vector", "scalar", "gpsimd", "sync")


class Res:
    __slots__ = ("name", "w", "readers", "excl")

    def __init__(self, name=""):
        self.name = name
        self.w = None
        self.readers = []
        self.excl = name.startswith("ps_")


class Tok:
    __slots__ = ("sem", "val", "pub")

    def __init__(self, sem, val, pub):
        self.sem = sem
        self.val = val
        self.pub = pub


class _Rec:
    __slots__ = ("emit", "waits", "inc", "q")

    def __init__(self, emit, waits, inc, q=None):
        self.emit = emit
        self.waits = waits
        self.inc = inc
        self.q = q


class KB:
    def __init__(self, nc, es, dma_incs=None):
        self.nc = nc
        self.es = es
        self.sems = {k: es.enter_context(nc.semaphore(k)) for k in COMPUTE_ENGINES}
        self.streams = {e: [] for e in COMPUTE_ENGINES}
        self.counts = {k: 0 for k in COMPUTE_ENGINES}
        self.clocks = {e: {} for e in COMPUTE_ENGINES}
        self.nops = 0
        self.dma_incs = dma_incs
        self.dma_seq = {"dsync": 0, "dgp": 0, "coll": 0}
        self.pl_order = []
        self.slot_sems = {}

    def _slot_sem(self, slot):
        if slot not in self.slot_sems:
            name = "ds%d" % len(self.slot_sems)
            self.sems[name] = self.es.enter_context(self.nc.semaphore(name))
            self.counts[name] = 0
            self.slot_sems[slot] = name
        return self.slot_sems[slot]

    def _issue(self, issue_eng, emit, reads, writes, sem_key, inc_amt, q=None):
        clk = self.clocks[issue_eng]
        need = {}
        toks = []
        for r in reads:
            if r.w is not None:
                toks.append(r.w)
            if r.excl:
                for rt in r.readers:
                    if rt.sem != sem_key:
                        toks.append(rt)
        for r in writes:
            if r.w is not None:
                toks.append(r.w)
            toks.extend(r.readers)
        for t in toks:
            if issue_eng == "tensor" and t.sem == "tensor":
                continue
            if clk.get(t.sem, 0) < t.val and need.get(t.sem, 0) < t.val:
                need[t.sem] = t.val
        waits = []
        for s, v in need.items():
            if clk.get(s, 0) >= v:
                continue
            waits.append((s, v))
        for t in toks:
            if t.sem in need and need[t.sem] == t.val:
                for s2, v2 in t.pub.items():
                    if clk.get(s2, 0) < v2:
                        clk[s2] = v2
        for s, v in need.items():
            if clk.get(s, 0) < v:
                clk[s] = v
        self.counts[sem_key] += inc_amt
        val = self.counts[sem_key]
        pub = dict(clk)
        pub[sem_key] = max(pub.get(sem_key, 0), val)
        tok = Tok(sem_key, val, pub)
        self.streams[issue_eng].append(_Rec(emit, waits, (sem_key, inc_amt), q))
        self.nops += 1
        for r in reads:
            r.readers.append(tok)
        for r in writes:
            r.w = tok
            r.readers = []
        return tok

    def op(self, eng, emit, reads=(), writes=()):
        return self._issue(eng, emit, reads, writes, eng, 1)

    def dma(self, queue, slot, emit, reads=(), writes=()):
        issue_eng = {"dsync": "sync", "dgp": "gpsimd"}[queue]
        i = self.dma_seq[queue]
        self.dma_seq[queue] += 1
        mult = self.dma_incs[queue][i] if self.dma_incs is not None else 1
        if issue_eng == "gpsimd":
            self.pl_order.append(queue)
        sem = self._slot_sem((queue, slot))
        return self._issue(issue_eng, emit, reads, writes, sem, 16 * mult, q=queue)

    def coll(self, emit, reads=(), writes=()):
        i = self.dma_seq["coll"]
        self.dma_seq["coll"] += 1
        mult = self.dma_incs["coll"][i] if self.dma_incs is not None else 1
        self.pl_order.append("coll")
        sem = self._slot_sem(("coll", 0))
        return self._issue("gpsimd", emit, reads, writes, sem, 16 * mult, q="coll")

    def barrier(self):
        toks = [Tok(k, self.counts[k], {k: self.counts[k]})
                for k in list(self.counts) if self.counts[k] > 0]
        for e in COMPUTE_ENGINES:
            clk = self.clocks[e]
            waits = []
            for t in toks:
                if clk.get(t.sem, 0) < t.val:
                    clk[t.sem] = t.val
                    waits.append((t.sem, t.val))
            if waits:
                self.streams[e].append(_Rec(None, waits, None))

    def replay(self, block, sentinels=False):
        sems = self.sems

        def section(eng_name):
            def run(eng):
                for rec in self.streams[eng_name]:
                    for (s, v) in rec.waits:
                        eng.wait_ge(sems[s], v)
                    if rec.emit is not None:
                        inst = rec.emit(eng)
                        inst.then_inc(sems[rec.inc[0]], rec.inc[1])
                        if sentinels and rec.q is not None:
                            eng.nop(nofuse=True)
            return run

        block.tensor(section("tensor"))
        block.vector(section("vector"))
        block.scalar(section("scalar"))
        block.gpsimd(section("gpsimd"))
        block.sync(section("sync"))


def host_constants():
    import ml_dtypes
    bf = ml_dtypes.bfloat16
    c = {}
    c["c_ident_bf"] = np.eye(128, dtype=bf)
    c["c_ident_f32"] = np.eye(128, dtype=np.float32)
    ii, jj = np.indices((128, 128))
    c["c_triu_f32"] = (jj >= ii).astype(bf)
    c["c_striu_f32"] = (jj > ii).astype(bf)
    c["c_stril_f32"] = (jj < ii).astype(bf)
    c["c_triu_bf"] = (jj >= ii).astype(bf)
    c["c_ones_row_bf"] = np.ones((1, 128), dtype=bf)
    c["c_ones_sq_bf"] = np.ones((128, 128), dtype=bf)
    rs = np.zeros((8, 8, 128), dtype=bf)
    for r in range(8):
        rs[r, r, :] = 1
    c["c_rowsel8_bf"] = np.ascontiguousarray(rs.transpose(1, 0, 2).reshape(8, 8 * 128))
    c["c_rowsel8_f32"] = c["c_rowsel8_bf"].astype(np.float32)
    c["c_ineg_bf"] = (-np.eye(128)).astype(bf)
    cs = np.zeros((4, 128, 8), dtype=bf)
    for kh in range(4):
        cs[kh, :, 2 * kh] = 1
        cs[kh, :, 2 * kh + 1] = 1
    c["c_colsel2_bf"] = np.ascontiguousarray(cs.transpose(1, 0, 2).reshape(128, 32))
    c1 = np.zeros((8, 128, 8), dtype=bf)
    for h in range(8):
        c1[h, :, h] = 1
    c["c_colsel1_bf"] = np.ascontiguousarray(c1.transpose(1, 0, 2).reshape(128, 64))
    return c


class Cfg:
    def __init__(self, S=2048, B=2, ncores=8, B_total=16, only=None, debug=False, trunc=None):
        self.debug = debug
        self.trunc = trunc
        self.S = S
        self.B = B
        self.ncores = ncores
        self.B_total = B_total
        self.only = only
        self.T = B * S
        self.NT = self.T // 128
        self.NTB = S // 128
        self.KSEL = S // 2


class P:
    pass


def declare_io(nc, cfg):
    p = P()
    p.cfg = cfg
    d = {}
    d["x"] = nc.dram_tensor("x", [cfg.T, 1024], F32, kind="ExternalInput").ap()
    wshapes = {
        "aw_ln1": [1024], "aw_ln2": [1024], "aW_qkvz": [1024, 3072], "aW_ba": [1024, 16],
        "aconv_w": [2048, 4], "adt_bias": [8], "aA_log": [8], "agate_norm_w": [128],
        "aW_out": [1024, 1024], "aW_router": [1024, 8],
        "aW_gate_e": [8, 1024, 512], "aW_up_e": [8, 1024, 512], "aW_down_e": [8, 512, 1024],
        "fw_gate_w": [1024, 8], "fw_gate_b": [8],
        "fW1": [8, 1024, 256], "fb1": [8, 256], "fW2": [8, 256, 1024], "fb2": [8, 1024],
    }
    for k, s in wshapes.items():
        d[k] = nc.dram_tensor(k, s, F32, kind="ExternalInput").ap()
    import ml_dtypes
    for k, v in host_constants().items():
        dt = BF16 if v.dtype == ml_dtypes.bfloat16 else F32
        d[k] = nc.dram_tensor(k, list(v.shape), dt, kind="ExternalInput").ap()
    if cfg.only in ("l2", "l3"):
        d["pdens"] = nc.dram_tensor("pdens", [cfg.ncores, cfg.S, 8], F32,
                                    kind="ExternalInput").ap()
    if cfg.only in ("l1", "l2"):
        d["pden_out"] = nc.dram_tensor("pden_out", [cfg.S, 8], F32, kind="ExternalOutput").ap()
    if cfg.only == "l2":
        d["r2x"] = nc.dram_tensor("r2x", [cfg.T, 1024], F32, kind="ExternalOutput").ap()
    else:
        d["y"] = nc.dram_tensor("y", [cfg.T, 1024], F32, kind="ExternalOutput").ap()
    if cfg.debug:
        for nm in ("dbg_mv", "dbg_sel", "dbg_route"):
            d[nm] = nc.dram_tensor(nm, [128, cfg.NT * 8], F32, kind="ExternalOutput").ap()
        d["dbg_xnfm"] = nc.dram_tensor("dbg_xnfm", [128, 4 * cfg.T], BF16, kind="ExternalOutput").ap()
    d["r1"] = nc.dram_tensor("r1", [cfg.T, 1024], F32, kind="Internal").ap()
    d["lnffn"] = nc.dram_tensor("lnffn", [cfg.T, 1024], BF16, kind="Internal").ap()
    d["lnbuf"] = nc.dram_tensor("lnbuf", [cfg.T, 1024], BF16, kind="Internal").ap()
    d["hbuf"] = nc.dram_tensor("hbuf", [cfg.T, 1024], F32, kind="Internal").ap()
    d["r2"] = nc.dram_tensor("r2", [cfg.T, 1024], F32, kind="Internal").ap()
    d["x2fm"] = nc.dram_tensor("x2fm", [8, 128, cfg.T], BF16, kind="Internal").ap()
    d["moepart"] = nc.dram_tensor("moepart", [8, 128, cfg.T], BF16, kind="Internal").ap()
    for i in (1, 2):
        d[f"cin{i}"] = nc.dram_tensor(f"cin{i}", [cfg.S, 8], F32, kind="Internal").ap()
        d[f"cout{i}"] = nc.dram_tensor(f"cout{i}", [cfg.S, 8], F32, kind="Internal").ap()
    p.d = d
    return p


def alloc_sbuf(p, nc, es):
    cfg = p.cfg
    sb = lambda n, shape, dt=F32: es.enter_context(nc.sbuf_tensor(n, shape, dt))
    t = {}
    TH = cfg.T // 2
    t["ident_bf"] = sb("ident_bf", [128, 128], BF16)
    t["ident_f32"] = sb("ident_f32", [128, 128], F32)
    t["triu"] = sb("triu", [128, 128], BF16)
    t["striu"] = sb("striu", [128, 128], BF16)
    t["stril"] = sb("stril", [128, 128], BF16)
    t["triu_bf"] = sb("triu_bf", [128, 128], BF16)
    t["ones_row_bf"] = sb("ones_row_bf", [1, 128], BF16)
    t["ones_sq_bf"] = sb("ones_sq_bf", [128, 128], BF16)
    t["rowsel8"] = sb("rowsel8", [8, 1024], BF16)
    t["rowsel8f"] = sb("rowsel8f", [8, 1024], F32)
    t["colsel2"] = sb("colsel2", [128, 32], BF16)
    t["colsel1"] = sb("colsel1", [128, 64], BF16)
    t["inegbf"] = sb("inegbf", [128, 128], BF16)
    t["wg"] = sb("wg", [128, 64], BF16)
    t["wgb_bc"] = sb("wgb_bc", [128, 8], F32)
    t["wgb_row"] = sb("wgb_row", [1, 8], BF16)
    t["b1c"] = sb("b1c", [128, 16], F32)
    t["b1st"] = sb("b1st", [8, 256], F32)
    t["b2r"] = sb("b2r", [8, 1024], BF16)
    t["wba"] = sb("wba", [128, 128], BF16)
    t["wrt"] = sb("wrt", [128, 64], BF16)
    t["convw"] = sb("convw", [128, 64], F32)
    t["wln1c"] = sb("wln1c", [128, 8], F32)
    t["wln2c"] = sb("wln2c", [128, 8], F32)
    t["gnwc"] = sb("gnwc", [128, 1], F32)
    t["hcol8"] = sb("hcol8", [8, 8], F32)
    t["wbuf"] = sb("wbuf", [128, 32768], BF16)
    t["xnfm"] = sb("xnfm", [128, max(8 * TH, 16384)], BF16)
    t["accbuf"] = sb("accbuf", [128, 8240], BF16)
    t["zbuf"] = sb("zbuf", [128, 2048], BF16)
    t["poolF"] = sb("poolF", [128, 2048], F32)
    t["poolB"] = sb("poolB", [128, 6144], BF16)
    t["states"] = sb("states", [128, 1024], BF16)
    t["xt"] = sb("xt", [128, 2 * 1024], F32)
    t["xnb"] = sb("xnb", [128, 2 * 1024], BF16)
    t["xnb2"] = sb("xnb2", [128, 1024], BF16)
    t["sq_scr"] = sb("sq_scr", [128, 1024], F32)
    t["cols"] = sb("cols", [128, 2 * 16], F32)
    t["outt"] = sb("outt", [128, 1024], F32)
    t["xres"] = sb("xres", [128, 1024], F32)
    t["glb"] = sb("glb", [8, 128], BF16)
    t["gl"] = sb("gl", [128, 2 * 8], F32)
    t["m8"] = sb("m8", [128, 8], F32)
    t["zap"] = sb("zap", [128, 8], F32)
    t["mv"] = sb("mv", [128, cfg.NT * 8], F32)
    t["rden"] = sb("rden", [128, cfg.NTB * 8], F32)
    t["route"] = sb("route", [128, cfg.NT * 8], F32)
    t["sel"] = sb("sel", [128, cfg.NT * 8], F32)
    t["rowfm"] = sb("rowfm", [8, cfg.T], BF16)
    t["bloE"] = sb("bloE", [128, 16], F32)
    t["bloM"] = sb("bloM", [128, 16], F32)
    t["bcand"] = sb("bcand", [128, 16], F32)
    t["bpi"] = sb("bpi", [128, 16], I32)
    t["bth"] = sb("bth", [128, 16], F32)
    t["btake"] = sb("btake", [128, 16], F32)
    t["cnt"] = sb("cnt", [128, 16], F32)
    t["gcnt"] = sb("gcnt", [128, 16], F32)
    t["kg"] = sb("kg", [128, 16], F32)
    t["cntbc"] = sb("cntbc", [128, cfg.NT * 8], F32)
    t["eqw"] = sb("eqw", [128, cfg.NT * 8], F32)
    t["cumw"] = sb("cumw", [128, cfg.NT * 8], F32)
    t["totbc"] = sb("totbc", [128, cfg.NT * 8], F32)
    t["cmpw"] = sb("cmpw", [128, cfg.NT * 8], BF16)
    t["eqbf"] = sb("eqbf", [128, cfg.NT * 8], BF16)
    t["h1m"] = sb("h1m", [128, 2 * 256], BF16)
    t["selbc"] = sb("selbc", [128, 256], BF16)
    t["ofm"] = sb("ofm", [128, 2 * 256], BF16)

    t["rowsF"] = sb("rowsF", [8, 6 * 256], F32)
    t["rowsB"] = sb("rowsB", [8, 8 * 256], BF16)
    t["colsT"] = sb("colsT", [128, 128], F32)
    p.t = t
    p.cpool_map = {}
    p.cpool_next = [0]
    t["cpool"] = sb("cpool", [128, 16], F32)
    p.R = {}
    return t


def alloc_psum(p, nc, es):
    ps = {}
    ps["T"] = es.enter_context(nc.psum_tensor("ps_T", [128, 1024], BF16))
    ps["g"] = es.enter_context(nc.psum_tensor("ps_g", [8, 512], F32))
    ps["misc"] = es.enter_context(nc.psum_tensor("ps_misc", [128, 512], F32))
    ps["h1"] = es.enter_context(nc.psum_tensor("ps_h1", [128, 512], F32))
    for i in range(4):
        ps[f"o{i}"] = es.enter_context(nc.psum_tensor(f"ps_o{i}", [128, 512], F32))
    p.ps = ps
    return ps


def R(p, name, n=None):
    if n is None:
        if name not in p.R:
            p.R[name] = Res(name)
        return p.R[name]
    if name not in p.R:
        p.R[name] = [Res(f"{name}[{i}]") for i in range(n)]
    return p.R[name]


def lp(p, fn):
    def wrap(e):
        with p.nc.allow_low_precision(reason="bf16 scale factors"):
            return fn(e)
    return wrap


def ccol(p, kb, val):
    if val in p.cpool_map:
        return p.cpool_map[val]
    i = p.cpool_next[0]
    p.cpool_next[0] += 1
    assert i < 16
    apc = p.t["cpool"][:, i:i + 1]
    rc = R(p, "consts")
    kb.op("gpsimd", lambda e, a=apc, v=val: e.memset(a, v), writes=[rc])
    p.nc.const_aps.register(F32, float(val), apc)
    p.cpool_map[val] = apc
    return apc



def load_constants(p, kb):
    t, d = p.t, p.d
    rc = R(p, "consts")
    pairs = [("ident_bf", "c_ident_bf"), ("ident_f32", "c_ident_f32"),
             ("triu", "c_triu_f32"), ("striu", "c_striu_f32"),
             ("stril", "c_stril_f32"), ("triu_bf", "c_triu_bf"),
             ("ones_row_bf", "c_ones_row_bf"), ("ones_sq_bf", "c_ones_sq_bf"),
             ("rowsel8", "c_rowsel8_bf"), ("rowsel8f", "c_rowsel8_f32"),
             ("inegbf", "c_ineg_bf"),
             ("colsel2", "c_colsel2_bf"), ("colsel1", "c_colsel1_bf")]
    for sbn, dn in pairs:
        kb.dma("dsync", "c", lambda e, a=t[sbn], b=d[dn]: e.dma_start(out=a[:], in_=b[:]),
               writes=[rc])
    kb.dma("dgp", "cw", lambda e: e.dma_start(out=t["wg"][:].rearrange("p (dt e) -> p dt e", e=8),
                                        in_=d["fw_gate_w"].rearrange("(dt p) e -> p dt e", p=128)),
           writes=[rc])
    kb.dma("dgp", "cw", lambda e: e.dma_start(out=t["wgb_row"][:], in_=d["fw_gate_b"][None, :]),
           writes=[rc])
    kb.dma("dsync", "c", lambda e: e.dma_start(out=t["b1st"][:], in_=d["fb1"][:, :]), writes=[rc])
    for it in range(2):
        kb.op("tensor", lambda e, it=it: e.transpose(p.ps["misc"][0:128, 0:8],
                                                     t["b1st"][0:8, it * 128:(it + 1) * 128],
                                                     t["ident_f32"][0:8, 0:8]),
              reads=[rc], writes=[R(p, "ps_misc")])
        kb.op("vector", lambda e, it=it: e.tensor_copy(t["b1c"][:, it * 8:(it + 1) * 8],
                                                       p.ps["misc"][0:128, 0:8]),
              reads=[R(p, "ps_misc")], writes=[rc])
    kb.dma("dgp", "cw", lambda e: e.dma_start(out=t["b2r"][:], in_=d["fb2"][:, :]), writes=[rc])
    pm = R(p, "ps_misc")
    kb.op("tensor", lambda e: e.matmul(p.ps["misc"][0:128, 0:8], t["ones_row_bf"][:],
                                       t["wgb_row"][:], start=True, stop=True),
          reads=[rc], writes=[pm])
    kb.op("vector", lambda e: e.tensor_copy(t["wgb_bc"][:], p.ps["misc"][0:128, 0:8]),
          reads=[pm], writes=[rc])


def load_ffn_weights(p, kb):
    t, d = p.t, p.d
    rw = R(p, "ffn_w")
    for ei in range(NEXP):
        kb.dma("dgp", "fw", lambda e, ei=ei: e.dma_start(
            out=t["wbuf"][:, ei * 2048: (ei + 1) * 2048].rearrange("p (dt c) -> p dt c", c=256),
            in_=d["fW1"][ei].rearrange("(dt p) c -> p dt c", p=128)),
            writes=[rw])
        kb.dma("dgp", "fw", lambda e, ei=ei: e.dma_start(
            out=t["wbuf"][:, 16384 + ei * 2048: 16384 + (ei + 1) * 2048].rearrange(
                "p (it c) -> p it c", c=1024),
            in_=d["fW2"][ei].rearrange("(it p) c -> p it c", p=128)),
            writes=[rw])


def w1_tile(p, e, dt, it):
    c = (e * 8 + dt) * 256 + it * 128
    return p.t["wbuf"][:, c:c + 128]


def w2_tile(p, e, it, dt):
    c = 16384 + (e * 2 + it) * 1024 + dt * 128
    return p.t["wbuf"][:, c:c + 128]


def emit_ln_half(p, kb, src, half, mode="ln", do_gate=None, store_dram=None,
                 zw_col=None, store_fm_dram=False):
    t, cfg, ps = p.t, p.cfg, p.ps
    TH = cfg.T // 2
    NTH = cfg.NT // 2
    rx = R(p, "xt_ring", 2)
    rxnb = R(p, "xnb_ring", 2)
    rcols = R(p, "cols_ring", 2)
    rscr = R(p, "sq_scr")
    rfm = R(p, "xnfm_t", NTH * 8)
    rmv = R(p, "mv")
    rgl = R(p, "gl_ring", 2)
    rglb = R(p, "glb_ring", 2)
    rm8 = R(p, "m8")
    rzap = R(p, "zap")
    rc = R(p, "consts")
    rpT = R(p, "ps_T")
    rpTg = rpT
    rpg = R(p, "ps_g")
    rln = R(p, "ln_dram")
    for tl in range(NTH):
        ti = half * NTH + tl
        ring = tl % 2
        xt = t["xt"][:, ring * 1024:(ring + 1) * 1024]
        kb.dma("dsync", ("x", ring), lambda e, a=xt, ti=ti: e.dma_start(
            out=a, in_=src[ti * 128:(ti + 1) * 128, :]),
            writes=[rx[ring]])
        cl = t["cols"][:, ring * 16:(ring + 1) * 16]
        kb.op("scalar", lambda e, a=cl: e.memzero(a[:, 1:2]), writes=[rcols[ring]])
        kb.op("scalar", lambda e, a=cl, b=xt: e.activation(t["sq_scr"][:], b, AF.Square,
                                                           accum_out=a[:, 1:2]),
              reads=[rx[ring]], writes=[rcols[ring], rscr])
        if mode in ("ln", "lnzrms"):
            kb.op("vector", lambda e, a=cl, b=xt: e.reduce_sum(a[:, 0:1], b, axis=AX.X),
                  reads=[rx[ring]], writes=[rcols[ring]])
            kb.op("vector", lambda e, a=cl: e.tensor_scalar(a[:, 2:3], a[:, 0:1], 1.0 / 1024,
                                                            None, ALU.mult),
                  reads=[rcols[ring]], writes=[rcols[ring]])
            kb.op("vector", lambda e, a=cl: e.tensor_mul(a[:, 4:5], a[:, 2:3], a[:, 2:3]),
                  reads=[rcols[ring]], writes=[rcols[ring]])
            kb.op("vector", lambda e, a=cl: e.tensor_scalar(a[:, 5:6], a[:, 1:2], 1.0 / 1024,
                                                            a[:, 4:5], ALU.mult, ALU.subtract),
                  reads=[rcols[ring]], writes=[rcols[ring]])
            kb.op("scalar", lambda e, a=cl, bc=ccol(p, kb, EPS5): e.activation(
                a[:, 6:7], a[:, 5:6], AF.Sqrt, bias=bc),
                reads=[rcols[ring], rc], writes=[rcols[ring]])
            kb.op("vector", lambda e, a=cl: e.reciprocal(a[:, 3:4], a[:, 6:7]),
                  reads=[rcols[ring]], writes=[rcols[ring]])
        else:
            kb.op("vector", lambda e, a=cl: e.memset(a[:, 2:3], 0.0),
                  writes=[rcols[ring]])
            kb.op("vector", lambda e, a=cl: e.tensor_scalar(a[:, 5:6], a[:, 1:2], 1.0 / 1024,
                                                            None, ALU.mult),
                  reads=[rcols[ring]], writes=[rcols[ring]])
            kb.op("scalar", lambda e, a=cl, bc=ccol(p, kb, EPS6): e.activation(
                a[:, 6:7], a[:, 5:6], AF.Sqrt, bias=bc),
                reads=[rcols[ring], rc], writes=[rcols[ring]])
            kb.op("vector", lambda e, a=cl: e.reciprocal(a[:, 3:4], a[:, 6:7]),
                  reads=[rcols[ring]], writes=[rcols[ring]])
        xnb = t["xnb"][:, ring * 1024:(ring + 1) * 1024]
        kb.op("vector", lambda e, a=xnb, b=xt, c=cl: e.tensor_scalar(
            a, b, c[:, 2:3], c[:, 3:4], ALU.subtract, ALU.mult),
            reads=[rx[ring], rcols[ring]], writes=[rxnb[ring]])
        if store_dram is not None:
            kb.dma("dsync", ("lnst", ring), lambda e, a=xnb, ti=ti: e.dma_start(
                out=store_dram[ti * 128:(ti + 1) * 128, :], in_=a),
                reads=[rxnb[ring]], writes=[rln])
        if mode == "lnzrms":
            kb.op("vector", lambda e, a=cl: e.tensor_mul(a[:, 7:8], a[:, 3:4], a[:, 3:4]),
                  reads=[rcols[ring]], writes=[rcols[ring]])
            kb.op("vector", lambda e, a=cl: e.tensor_mul(a[:, 8:9], a[:, 5:6], a[:, 7:8]),
                  reads=[rcols[ring]], writes=[rcols[ring]])
            kb.op("scalar", lambda e, a=cl, bc=ccol(p, kb, EPS6): e.activation(
                a[:, 9:10], a[:, 8:9], AF.Sqrt, bias=bc),
                reads=[rcols[ring], rc], writes=[rcols[ring]])
            kb.op("vector", lambda e, a=cl: e.reciprocal(a[:, 10:11], a[:, 9:10]),
                  reads=[rcols[ring]], writes=[rcols[ring]])
            xnb2 = t["xnb2"][:, 0:1024]
            rx2 = R(p, "xnb2")
            kb.op("vector", lambda e, a=xnb2, b=xnb, c=cl: e.tensor_scalar(
                a, b, c[:, 10:11], None, ALU.mult),
                reads=[rxnb[ring], rcols[ring]], writes=[rx2])
            tsrc = xnb2
            tread = rx2
        else:
            tsrc = xnb
            tread = rxnb[ring]
        for dt in range(8):
            pT = ps["T"][0:128, (dt % 4) * 128:(dt % 4) * 128 + 128]
            rp = rpT
            kb.op("tensor", lambda e, o=pT, a=tsrc, dt=dt: e.transpose(
                o, a[:, dt * 128:(dt + 1) * 128], t["ident_bf"][:]),
                reads=[tread, rc], writes=[rp])
            dst = t["xnfm"][:, dt * TH + tl * 128: dt * TH + tl * 128 + 128]
            eng = "vector" if dt % 2 == 0 else "scalar"
            if zw_col is None:
                if eng == "vector":
                    kb.op(eng, lambda e, a=dst, b=pT: e.tensor_copy(a, b),
                          reads=[rp], writes=[rfm[tl * 8 + dt]])
                else:
                    kb.op(eng, lambda e, a=dst, b=pT: e.copy(a, b),
                          reads=[rp], writes=[rfm[tl * 8 + dt]])
            else:
                if eng == "vector":
                    kb.op(eng, lambda e, a=dst, b=pT, dt=dt: e.tensor_scalar(
                        a, b, zw_col[:, dt:dt + 1], None, ALU.mult),
                        reads=[rp, rc], writes=[rfm[tl * 8 + dt]])
                else:
                    kb.op(eng, lambda e, a=dst, b=pT, dt=dt: e.mul(a, b, zw_col[:, dt:dt + 1]),
                          reads=[rp, rc], writes=[rfm[tl * 8 + dt]])
        if store_fm_dram:
            kb.dma("dsync", ("fmst", ring), lambda e, tl=tl, ti=ti: e.dma_start(
                out=p.d["x2fm"][:, :, ti * 128:(ti + 1) * 128].transpose([1, 0, 2]),
                in_=t["xnfm"][:, 0:8 * TH].rearrange("p (dt c) -> p dt c", c=TH)[
                    :, :, tl * 128:(tl + 1) * 128]),
                reads=[rfm[tl * 8 + dtq] for dtq in range(8)], writes=[R(p, "x2_dram")])
        if do_gate is not None:
            gw = t["wg"] if do_gate[0] == "ffn" else t["wrt"]
            pg = ps["g"][0:8, (tl % 2) * 128:(tl % 2) * 128 + 128]
            for dt in range(8):
                kb.op("tensor", lambda e, o=pg, dt=dt, tl=tl, gw=gw: e.matmul(
                    o, gw[:, dt * 8:(dt + 1) * 8],
                    t["xnfm"][:, dt * TH + tl * 128: dt * TH + tl * 128 + 128],
                    start=(dt == 0), stop=(dt == 7)),
                    reads=[rfm[tl * 8 + dt], rc], writes=[rpg])
            glb = t["glb"][0:8, 0:128]
            kb.op("vector", lambda e, a=glb, b=pg: e.tensor_copy(a, b),
                  reads=[rpg], writes=[rglb[tl % 2]])
            pT2 = ps["T"][0:128, 512:520]
            kb.op("tensor", lambda e, o=pT2, a=glb: e.transpose(o, a, t["ident_bf"][0:8, 0:8]),
                  reads=[rglb[tl % 2], rc], writes=[rpTg])
            gl = t["gl"][:, (tl % 2) * 8:(tl % 2) * 8 + 8]
            if do_gate[0] == "ffn":
                kb.op("vector", lambda e, a=gl, b=pT2: e.tensor_add(a, b, t["wgb_bc"][:]),
                      reads=[rpTg, rc], writes=[rgl[tl % 2]])
            else:
                kb.op("vector", lambda e, a=gl, b=pT2: e.tensor_copy(a, b),
                      reads=[rpTg], writes=[rgl[tl % 2]])
            cl2 = t["cols"][:, ring * 16 + 11: ring * 16 + 16]
            kb.op("vector", lambda e, a=cl2, b=gl: e.reduce_max(a[:, 0:1], b, axis=AX.X,
                                                                negate=True),
                  reads=[rgl[tl % 2]], writes=[rcols[ring]])
            kb.op("scalar", lambda e, a=gl, c=cl2: e.activation(a, a, AF.Exp, bias=c[:, 0:1]),
                  reads=[rcols[ring], rgl[tl % 2]], writes=[rgl[tl % 2]])
            kb.op("vector", lambda e, a=cl2, b=gl: e.reduce_sum(a[:, 1:2], b, axis=AX.X),
                  reads=[rgl[tl % 2]], writes=[rcols[ring]])
            kb.op("vector", lambda e, a=cl2: e.reciprocal(a[:, 2:3], a[:, 1:2]),
                  reads=[rcols[ring]], writes=[rcols[ring]])
            kb.op("vector", lambda e, a=gl, c=cl2: e.tensor_scalar(a, a, c[:, 2:3], None,
                                                                   ALU.mult),
                  reads=[rcols[ring], rgl[tl % 2]], writes=[rgl[tl % 2]])
            kb.op("vector", lambda e, a=gl: e.max(out=t["m8"][:], in_=a),
                  reads=[rgl[tl % 2]], writes=[rm8])
            kb.op("vector", lambda e: e.memset(t["m8"][:, 2:8], -1.0),
                  reads=[], writes=[rm8])
            kb.op("vector", lambda e, a=gl: e.match_replace(out=t["zap"][:],
                                                            in_to_replace=t["m8"][:, 0:8],
                                                            in_values=a, imm_value=0.0),
                  reads=[rgl[tl % 2], rm8], writes=[rzap])
            mvd = t["mv"][:, ti * 8:(ti + 1) * 8]
            kb.op("vector", lambda e, a=mvd, b=gl: e.tensor_sub(a, b, t["zap"][:]),
                  reads=[rgl[tl % 2], rzap], writes=[rmv])


def emit_retranspose_half(p, kb, src_bf, half):
    t, cfg, ps = p.t, p.cfg, p.ps
    TH = cfg.T // 2
    NTH = cfg.NT // 2
    rxnb = R(p, "xnb_ring", 2)
    rfm = R(p, "xnfm_t", NTH * 8)
    rc = R(p, "consts")
    rpT = R(p, "ps_T")
    for tl in range(NTH):
        ti = half * NTH + tl
        ring = tl % 2
        xnb = t["xnb"][:, ring * 1024:(ring + 1) * 1024]
        kb.dma("dsync", ("x", ring), lambda e, a=xnb, ti=ti: e.dma_start(
            out=a, in_=src_bf[ti * 128:(ti + 1) * 128, :]),
            writes=[rxnb[ring]])
        for dt in range(8):
            pT = ps["T"][0:128, (dt % 4) * 128:(dt % 4) * 128 + 128]
            rp = rpT
            kb.op("tensor", lambda e, o=pT, a=xnb, dt=dt: e.transpose(
                o, a[:, dt * 128:(dt + 1) * 128], t["ident_bf"][:]),
                reads=[rxnb[ring], rc], writes=[rp])
            dst = t["xnfm"][:, dt * TH + tl * 128: dt * TH + tl * 128 + 128]
            eng = "vector" if dt % 2 == 0 else "scalar"
            if eng == "vector":
                kb.op(eng, lambda e, a=dst, b=pT: e.tensor_copy(a, b),
                      reads=[rp], writes=[rfm[tl * 8 + dt]])
            else:
                kb.op(eng, lambda e, a=dst, b=pT: e.copy(a, b),
                      reads=[rp], writes=[rfm[tl * 8 + dt]])


def emit_pden_out(p, kb):
    t, cfg = p.t, p.cfg
    NTB = cfg.NTB
    rmv = R(p, "mv")
    rrden = R(p, "rden")
    pden = t["rden"]
    kb.op("vector", lambda e: e.tensor_add(pden[:, 0:NTB * 8], t["mv"][:, 0:NTB * 8],
                                           t["mv"][:, NTB * 8:2 * NTB * 8]),
          reads=[rmv], writes=[rrden])
    kb.dma("dsync", "pd", lambda e: e.dma_start(
        out=p.d["pden_out"].rearrange("(j p) e -> p j e", p=128),
        in_=pden[:, 0:NTB * 8].rearrange("p (j e) -> p j e", e=8)),
        reads=[rrden], writes=[R(p, "dst_dram")])


def emit_denom_from_inputs(p, kb, call_idx):
    t, cfg = p.t, p.cfg
    NTB = cfg.NTB
    R_ = cfg.ncores
    rrden = R(p, "rden")
    rpdl = R(p, "pdl")
    dstp = t["sq_scr"][:, 0:R_ * NTB * 8]
    rpdl = R(p, "sq_scr")
    kb.dma("dsync", "pd", lambda e: e.dma_start(
        out=dstp.rearrange("p (r j e) -> p r j e", r=R_, e=8),
        in_=p.d["pdens"].rearrange("r (j p) e -> p r j e", p=128)),
        writes=[rpdl])
    pden = t["rden"]
    kb.op("vector", lambda e: e.reduce_sum(
        pden[:, 0:NTB * 8].rearrange("p (j e) -> p j e", e=8),
        dstp.rearrange("p (r j e) -> p j e r", r=R_, e=8),
        axis=AX.X),
        reads=[rpdl], writes=[rrden])
    kb.op("vector", lambda e: e.tensor_scalar(pden[:, 0:NTB * 8], pden[:, 0:NTB * 8], EPS6,
                                              None, ALU.add),
          reads=[rrden], writes=[rrden])
    kb.op("vector", lambda e: e.reciprocal(pden[:, 0:NTB * 8], pden[:, 0:NTB * 8]),
          reads=[rrden], writes=[rrden])


def emit_ffn_route(p, kb, cin, cout):
    t, cfg, ps = p.t, p.cfg, p.ps
    NT, NTB = cfg.NT, cfg.NTB
    NB = cfg.B * 8
    assert cfg.B == 2
    rmv = R(p, "mv")
    rroute = R(p, "route")
    rrden = R(p, "rden")
    rsel = R(p, "sel")
    rselfm = R(p, "rowfm")
    rc = R(p, "consts")
    rcin = R(p, "dram_cin")
    rcout = R(p, "dram_cout")
    rpm = R(p, "ps_misc")
    rbis = R(p, "bis")
    rcnt = R(p, "cnt")
    rcmp = R(p, "cmpw")
    rcntbc = R(p, "cntbc")
    req = R(p, "eqw")
    reqb = R(p, "eqbf")
    rcum = R(p, "cumw")
    rtot = R(p, "totbc")
    pden = t["rden"]
    if cin is not None:
        kb.op("vector", lambda e: e.tensor_add(pden[:, 0:NTB * 8], t["mv"][:, 0:NTB * 8],
                                               t["mv"][:, NTB * 8:2 * NTB * 8]),
              reads=[rmv], writes=[rrden])
        kb.dma("dsync", "cden", lambda e: e.dma_start(
            out=cin.rearrange("(j p) e -> p j e", p=128),
            in_=pden[:, 0:NTB * 8].rearrange("p (j e) -> p j e", e=8)),
            reads=[rrden], writes=[rcin])
        kb.coll(lambda e: e.collective_compute(
            "AllReduce", ALU.add, replica_groups=[list(range(cfg.ncores))],
            ins=[cin[:]], outs=[cout[:]]),
            reads=[rcin], writes=[rcout])
        kb.dma("dsync", "cden", lambda e: e.dma_start(
            out=pden[:, 0:NTB * 8].rearrange("p (j e) -> p j e", e=8),
            in_=cout.rearrange("(j p) e -> p j e", p=128)),
            reads=[rcout], writes=[rrden])
        kb.op("vector", lambda e: e.tensor_scalar(pden[:, 0:NTB * 8], pden[:, 0:NTB * 8], EPS6,
                                                  None, ALU.add),
              reads=[rrden], writes=[rrden])
        kb.op("vector", lambda e: e.reciprocal(pden[:, 0:NTB * 8], pden[:, 0:NTB * 8]),
              reads=[rrden], writes=[rrden])
    cap = float(cfg.B_total)
    kb.op("vector", lambda e: e.scalar_tensor_tensor(
        t["route"][:, 0:NT * 8].rearrange("p (b j e) -> p b j e", b=cfg.B, e=8),
        t["mv"][:, 0:NT * 8].rearrange("p (b j e) -> p b j e", b=cfg.B, e=8),
        cap,
        pden[:, 0:NTB * 8].rearrange("p (j e) -> p j e", e=8).unsqueeze(1).to_broadcast(
            [128, cfg.B, NTB, 8]),
        ALU.mult, ALU.mult),
        reads=[rmv, rrden], writes=[rroute])

    def cmp_against(dst_ap, th_tile, op):
        return lambda e: e.tensor_tensor(
            dst_ap.rearrange("p (b j e) -> p b j e", b=cfg.B, e=8),
            t["route"][:, 0:NT * 8].rearrange("p (b j e) -> p b j e", b=cfg.B, e=8),
            th_tile[:, 0:NB].rearrange("p (b e) -> p b e", e=8).unsqueeze(2).to_broadcast(
                [128, cfg.B, NTB, 8]),
            op)

    def count_into(cnt_tile):
        for c0 in range(0, NT * 8, 512):
            w = min(512, NT * 8 - c0)
            kb.op("tensor", lambda e, c0=c0, w=w: e.matmul(
                ps["misc"][0:128, 0:w], t["ones_sq_bf"][:], t["cmpw"][:, c0:c0 + w],
                start=True, stop=True),
                reads=[rcmp, rc], writes=[rpm])
            kb.op("vector", lambda e, c0=c0, w=w: e.tensor_copy(t["cntbc"][:, c0:c0 + w],
                                                                ps["misc"][0:128, 0:w]),
                  reads=[rpm], writes=[rcntbc])
        kb.op("vector", lambda e: e.reduce_sum(
            cnt_tile[:, 0:NB].rearrange("p (b e) -> p b e", e=8),
            t["cntbc"][:, 0:NT * 8].rearrange("p (b j e) -> p b e j", b=cfg.B, e=8),
            axis=AX.X),
            reads=[rcntbc], writes=[rcnt])

    def theta_from_exp(src_tile):
        kb.op("vector", lambda e: e.tensor_scalar(t["bth"][:, 0:NB], src_tile[:, 0:NB],
                                                  8388608.0, None, ALU.mult),
              reads=[rbis], writes=[rbis])
        kb.op("vector", lambda e: e.tensor_copy(t["bpi"][:, 0:NB], t["bth"][:, 0:NB]),
              reads=[rbis], writes=[rbis])

    def take_update(lo_tile, delta):
        kb.op("vector", lambda e: e.tensor_scalar(t["btake"][:, 0:NB], t["cnt"][:, 0:NB],
                                                  float(cfg.KSEL), None, ALU.is_ge),
              reads=[rcnt], writes=[rbis])
        kb.op("vector", lambda e, dl=float(delta): e.tensor_scalar(
            t["btake"][:, 0:NB], t["btake"][:, 0:NB], dl, None, ALU.mult),
            reads=[rbis], writes=[rbis])
        kb.op("vector", lambda e: e.tensor_add(lo_tile[:, 0:NB], lo_tile[:, 0:NB],
                                               t["btake"][:, 0:NB]),
              reads=[rbis], writes=[rbis])

    kb.op("vector", lambda e: e.memset(t["bloE"][:, 0:NB], 0.0), writes=[rbis])
    delta = 128
    while delta >= 1:
        kb.op("vector", lambda e, dl=float(delta): e.tensor_scalar(
            t["bcand"][:, 0:NB], t["bloE"][:, 0:NB], dl, None, ALU.add),
            reads=[rbis], writes=[rbis])
        theta_from_exp(t["bcand"])
        kb.op("vector", cmp_against(t["cmpw"][:, 0:NT * 8], t["bpi"].bitcast(F32), ALU.is_ge),
              reads=[rroute, rbis], writes=[rcmp])
        count_into(t["cnt"])
        take_update(t["bloE"], delta)
        delta //= 2
    theta_from_exp(t["bloE"])
    bpf = t["bpi"].bitcast(F32)
    kb.op("vector", lambda e: e.memset(t["bloM"][:, 0:NB], float(2 ** 23)), writes=[rbis])
    delta = 2 ** 22
    while delta >= 1:
        kb.op("vector", lambda e, dl=float(delta): e.tensor_scalar(
            t["bcand"][:, 0:NB], t["bloM"][:, 0:NB], dl, None, ALU.add),
            reads=[rbis], writes=[rbis])
        kb.op("vector", lambda e: e.tensor_scalar(t["bcand"][:, 0:NB], t["bcand"][:, 0:NB],
                                                  float(2.0 ** -23), None, ALU.mult),
              reads=[rbis], writes=[rbis])
        kb.op("vector", lambda e: e.tensor_mul(t["bcand"][:, 0:NB], t["bcand"][:, 0:NB],
                                               bpf[:, 0:NB]),
              reads=[rbis], writes=[rbis])
        kb.op("vector", cmp_against(t["cmpw"][:, 0:NT * 8], t["bcand"], ALU.is_ge),
              reads=[rroute, rbis], writes=[rcmp])
        count_into(t["cnt"])
        take_update(t["bloM"], delta)
        delta //= 2
    kb.op("vector", lambda e: e.tensor_scalar(t["bth"][:, 0:NB], t["bloM"][:, 0:NB],
                                              float(2.0 ** -23), None, ALU.mult),
          reads=[rbis], writes=[rbis])
    kb.op("vector", lambda e: e.tensor_mul(t["bth"][:, 0:NB], t["bth"][:, 0:NB], bpf[:, 0:NB]),
          reads=[rbis], writes=[rbis])
    kb.op("vector", cmp_against(t["cmpw"][:, 0:NT * 8], t["bth"], ALU.is_gt),
          reads=[rroute, rbis], writes=[rcmp])
    count_into(t["gcnt"])
    kb.op("vector", cmp_against(t["sel"][:, 0:NT * 8], t["bth"], ALU.is_gt),
          reads=[rroute, rbis], writes=[rsel])
    kb.op("vector", cmp_against(t["eqw"][:, 0:NT * 8], t["bth"], ALU.is_equal),
          reads=[rroute, rbis], writes=[req])
    kb.op("vector", lambda e: e.tensor_copy(t["eqbf"][:, 0:NT * 8], t["eqw"][:, 0:NT * 8]),
          reads=[req], writes=[reqb])
    for c0 in range(0, NT * 8, 512):
        w = min(512, NT * 8 - c0)
        kb.op("tensor", lambda e, c0=c0, w=w: e.matmul(ps["misc"][0:128, 0:w], t["triu_bf"][:],
                                                       t["eqbf"][:, c0:c0 + w], start=True, stop=True),
              reads=[reqb, rc], writes=[rpm])
        kb.op("vector", lambda e, c0=c0, w=w: e.tensor_copy(t["cumw"][:, c0:c0 + w],
                                                            ps["misc"][0:128, 0:w]),
              reads=[rpm], writes=[rcum])
        kb.op("tensor", lambda e, c0=c0, w=w: e.matmul(ps["misc"][0:128, 0:w], t["ones_sq_bf"][:],
                                                       t["eqbf"][:, c0:c0 + w], start=True, stop=True),
              reads=[reqb, rc], writes=[rpm])
        kb.op("vector", lambda e, c0=c0, w=w: e.tensor_copy(t["totbc"][:, c0:c0 + w],
                                                            ps["misc"][0:128, 0:w]),
              reads=[rpm], writes=[rtot])
    sh = 1
    while sh < NTB:
        def emit_shift(e, sh=sh):
            v = t["totbc"][:, 0:NT * 8].rearrange("p (b j e) -> p b j e", b=cfg.B, e=8)
            return e.tensor_add(v[:, :, sh:NTB, :], v[:, :, sh:NTB, :], v[:, :, 0:NTB - sh, :])
        kb.op("vector", emit_shift, reads=[rtot], writes=[rtot])
        sh *= 2
    for c0 in range(0, NT * 8, 512):
        w = min(512, NT * 8 - c0)
        kb.op("tensor", lambda e, c0=c0, w=w: e.matmul(ps["misc"][0:128, 0:w], t["ones_sq_bf"][:],
                                                       t["eqbf"][:, c0:c0 + w], start=True, stop=True),
              reads=[reqb, rc], writes=[rpm])
        kb.op("vector", lambda e, c0=c0, w=w: e.tensor_copy(t["cntbc"][:, c0:c0 + w],
                                                            ps["misc"][0:128, 0:w]),
              reads=[rpm], writes=[rcntbc])
    kb.op("vector", lambda e: e.tensor_sub(t["totbc"][:, 0:NT * 8], t["totbc"][:, 0:NT * 8],
                                           t["cntbc"][:, 0:NT * 8]),
          reads=[rtot, rcntbc], writes=[rtot])
    kb.op("vector", lambda e: e.tensor_sub(t["cumw"][:, 0:NT * 8], t["cumw"][:, 0:NT * 8],
                                           t["eqw"][:, 0:NT * 8]),
          reads=[rcum, req], writes=[rcum])
    kb.op("vector", lambda e: e.tensor_add(t["cumw"][:, 0:NT * 8], t["cumw"][:, 0:NT * 8],
                                           t["totbc"][:, 0:NT * 8]),
          reads=[rcum, rtot], writes=[rcum])
    kb.op("vector", lambda e: e.tensor_scalar(t["kg"][:, 0:NB], t["gcnt"][:, 0:NB], -1.0,
                                              float(cfg.KSEL), ALU.mult, ALU.add),
          reads=[rcnt], writes=[rcnt])
    kb.op("vector", lambda e: e.tensor_tensor(
        t["cumw"][:, 0:NT * 8].rearrange("p (b j e) -> p b j e", b=cfg.B, e=8),
        t["cumw"][:, 0:NT * 8].rearrange("p (b j e) -> p b j e", b=cfg.B, e=8),
        t["kg"][:, 0:NB].rearrange("p (b e) -> p b e", e=8).unsqueeze(2).to_broadcast(
            [128, cfg.B, NTB, 8]),
        ALU.is_lt),
        reads=[rcum, rcnt], writes=[rcum])
    kb.op("vector", lambda e: e.tensor_mul(t["eqw"][:, 0:NT * 8], t["eqw"][:, 0:NT * 8],
                                           t["cumw"][:, 0:NT * 8]),
          reads=[req, rcum], writes=[req])
    kb.op("vector", lambda e: e.tensor_add(t["sel"][:, 0:NT * 8], t["sel"][:, 0:NT * 8],
                                           t["eqw"][:, 0:NT * 8]),
          reads=[rsel, req], writes=[rsel])
    rpTg = R(p, "ps_T")
    for ti in range(cfg.NT):
        kb.op("vector", lambda e, ti=ti: e.tensor_copy(t["eqbf"][:, ti * 8:(ti + 1) * 8],
                                                       t["sel"][:, ti * 8:(ti + 1) * 8]),
              reads=[rsel], writes=[reqb])
        kb.op("tensor", lambda e, ti=ti: e.transpose(p.ps["T"][0:8, 512:640],
                                                     t["eqbf"][:, ti * 8:(ti + 1) * 8],
                                                     t["ident_bf"][:]),
              reads=[reqb, rc], writes=[rpTg])
        kb.op("scalar", lambda e, ti=ti: e.copy(t["rowfm"][0:8, ti * 128:(ti + 1) * 128],
                                                p.ps["T"][0:8, 512:640]),
              reads=[rpTg], writes=[rselfm])


def emit_ffn_experts(p, kb, src_resid, dst, scale, half):
    t, cfg, ps = p.t, p.cfg, p.ps
    TH = cfg.T // 2
    NTH = cfg.NT // 2
    rfm = R(p, "xnfm_t", NTH * 8)
    rselfm = R(p, "rowfm")
    rc = R(p, "consts")
    rw = R(p, "ffn_w")
    rh1s = R(p, "ps_h1")
    rh1 = [rh1s, rh1s]
    rh1m = R(p, "h1m_r", 2)
    rselbc = R(p, "selbc")
    rpo = R(p, "ps_o", 4)
    rofm = R(p, "ofm_r", 2)
    rpTs = R(p, "ps_T")
    rpT = [rpTs, rpTs, rpTs, rpTs]
    routt = R(p, "outt_r")
    rxres = R(p, "xres_r")
    rpm = R(p, "ps_misc")
    for blkl in range(TH // 256):
        loc0 = blkl * 256
        tok0 = half * TH + loc0
        for e_i in range(NEXP):
            kb.op("tensor", lambda e, ei=e_i, tok0=tok0: e.matmul(
                ps["misc"][0:128, 0:256], t["rowsel8"][0:8, ei * 128:(ei + 1) * 128],
                t["rowfm"][0:8, tok0:tok0 + 256], start=True, stop=True),
                reads=[rselfm, rc], writes=[rpm])
            kb.op("vector", lambda e: e.tensor_copy(t["selbc"][:], ps["misc"][0:128, 0:256]),
                  reads=[rpm], writes=[rselbc])
            for it in range(2):
                ph = ps["h1"][0:128, it * 256:(it + 1) * 256]
                for dt in range(8):
                    kb.op("tensor", lambda e, ei=e_i, it=it, dt=dt, o=ph, loc0=loc0: e.matmul(
                        o, w1_tile(p, ei, dt, it),
                        t["xnfm"][:, dt * TH + loc0: dt * TH + loc0 + 256],
                        start=(dt == 0), stop=(dt == 7)),
                        reads=[rw, rfm[(loc0 // 128) * 8 + dt], rfm[(loc0 // 128 + 1) * 8 + dt]],
                        writes=[rh1[it]])
                kb.op("scalar", lambda e, ei=e_i, it=it, o=ph: e.activation(
                    t["h1m"][:, it * 256:(it + 1) * 256], o, AF.Gelu,
                    bias=t["b1c"][:, (it * 8 + ei):(it * 8 + ei) + 1]),
                    reads=[rh1[it], rc], writes=[rh1m[it]])
                kb.op("vector", lambda e, it=it: e.tensor_mul(
                    t["h1m"][:, it * 256:(it + 1) * 256], t["h1m"][:, it * 256:(it + 1) * 256],
                    t["selbc"][:]),
                    reads=[rselbc, rh1m[it]], writes=[rh1m[it]])
                for dt in range(8):
                    po = ps[f"o{dt // 2}"][0:128, (dt % 2) * 256:(dt % 2) * 256 + 256]
                    kb.op("tensor", lambda e, ei=e_i, it=it, dt=dt, o=po: e.matmul(
                        o, w2_tile(p, ei, it, dt), t["h1m"][:, it * 256:(it + 1) * 256],
                        start=(ei == 0 and it == 0 and dt % 2 == 0), stop=False),
                        reads=[rw, rh1m[it]], writes=[rpo[dt // 2]])
        for dt in range(8):
            po = ps[f"o{dt // 2}"][0:128, (dt % 2) * 256:(dt % 2) * 256 + 256]
            kb.op("tensor", lambda e, dt=dt, o=po, tok0=tok0: e.matmul(
                o, t["b2r"][0:8, dt * 128:(dt + 1) * 128], t["rowfm"][0:8, tok0:tok0 + 256],
                start=False, stop=(dt % 2 == 1)),
                reads=[rc, rselfm], writes=[rpo[dt // 2]])
        for sub in range(2):
            tt = tok0 + sub * 128
            xr = t["xres"][:, 0:1024]
            kb.dma("dsync", "xr", lambda e, a=xr, tt=tt: e.dma_start(
                out=a, in_=src_resid[tt:tt + 128, :]),
                writes=[rxres])
            for dt in range(8):
                po = ps[f"o{dt // 2}"][0:128, (dt % 2) * 256 + sub * 128:
                                       (dt % 2) * 256 + sub * 128 + 128]
                ofm = t["ofm"][:, (dt % 2) * 256:(dt % 2) * 256 + 128]
                kb.op("vector", lambda e, o=po, a=ofm, sc=scale: e.tensor_scalar(
                    a, o, sc, None, ALU.mult),
                    reads=[rpo[dt // 2]], writes=[rofm[dt % 2]])
                pT = ps["T"][0:128, (dt % 4) * 128:(dt % 4) * 128 + 128]
                kb.op("tensor", lambda e, a=ofm, o=pT: e.transpose(o, a, t["ident_bf"][:]),
                      reads=[rofm[dt % 2], rc], writes=[rpT[dt % 4]])
                outt = t["outt"][:, dt * 128:(dt + 1) * 128]
                xrs = t["xres"][:, dt * 128:(dt + 1) * 128]
                kb.op("vector", lambda e, a=outt, b=pT, c=xrs: e.tensor_add(a, b, c),
                      reads=[rpT[dt % 4], rxres], writes=[routt])
            kb.dma("dsync", "out", lambda e, tt=tt: e.dma_start(
                out=dst[tt:tt + 128, :], in_=t["outt"][:, 0:1024]),
                reads=[routt], writes=[R(p, "dst_dram")])


def emit_ffn_sublayer(p, kb, src, dst, cin, cout, scale=0.5, denom_from_inputs=False):
    if denom_from_inputs:
        emit_denom_from_inputs(p, kb, 0)
    for hf in (0, 1):
        emit_ln_half(p, kb, src, hf, mode="ln", do_gate=("ffn",),
                     store_dram=p.d["lnffn"])
    emit_ffn_route(p, kb, None if denom_from_inputs else cin, cout)
    emit_ffn_experts(p, kb, src, dst, scale, half=1)
    emit_retranspose_half(p, kb, p.d["lnffn"], half=0)
    emit_ffn_experts(p, kb, src, dst, scale, half=0)


def emit_output_done(p, kb):
    waits = []
    for slot, sem in kb.slot_sems.items():
        if slot[0] == "dsync" and ((isinstance(slot[1], tuple) and slot[1][0] == "out")
                                   or slot[1] in ("dbg", "pd", "out")):
            waits.append((sem, kb.counts[sem]))
    kb.streams["gpsimd"].append(_Rec(None, waits, None))


def _emit_all(p, kb, cfg):
    load_constants(p, kb)
    if cfg.only == "ffn1":
        load_ffn_weights(p, kb)
    if cfg.only == "ffn1":
        emit_ffn_sublayer(p, kb, p.d["x"], p.d["y"], p.d["cin1"], p.d["cout1"], scale=0.5)
    elif cfg.only == "attncore":
        load_attn_small_weights(p, kb)
        load_attn_big_weights(p, kb)
        emit_attn_core(p, kb, p.d["x"])
        rh = R(p, "h_dram")
        for ti in range(cfg.NT):
            kb.dma("dsync", "cp", lambda e, ti=ti: e.dma_start(
                out=p.d["y"][ti * 128:(ti + 1) * 128, :], in_=p.d["hbuf"][ti * 128:(ti + 1) * 128, :]),
                reads=[rh], writes=[R(p, "dst_dram")])
    elif cfg.only == "attn":
        load_attn_small_weights(p, kb)
        load_attn_big_weights(p, kb)
        emit_attn_core(p, kb, p.d["x"])
        kb.barrier()
        emit_attn_moe(p, kb, p.d["x"], p.d["y"])
    elif cfg.only == "full":
        load_attn_small_weights(p, kb)
        load_ffn_weights(p, kb)
        emit_ffn_sublayer(p, kb, p.d["x"], p.d["r1"], p.d["cin1"], p.d["cout1"], scale=0.5)
        kb.barrier()
        load_attn_big_weights(p, kb)
        emit_attn_core(p, kb, p.d["r1"])
        kb.barrier()
        emit_attn_moe(p, kb, p.d["r1"], p.d["r2"])
        kb.barrier()
        load_ffn_weights(p, kb)
        emit_ffn_sublayer(p, kb, p.d["r2"], p.d["y"], p.d["cin2"], p.d["cout2"], scale=0.5)
    elif cfg.only == "l1":
        for hf in (0, 1):
            emit_ln_half(p, kb, p.d["x"], hf, mode="ln", do_gate=("ffn",))
        emit_pden_out(p, kb)
    elif cfg.only == "l2":
        load_attn_small_weights(p, kb)
        load_ffn_weights(p, kb)
        emit_ffn_sublayer(p, kb, p.d["x"], p.d["r1"], None, None, scale=0.5,
                          denom_from_inputs=True)
        kb.barrier()
        if cfg.trunc != "ffn":
            load_attn_big_weights(p, kb)
            if cfg.trunc != "wts":
                emit_attn_core(p, kb, p.d["r1"])
        if cfg.trunc is None or cfg.trunc == "moe" or (
                isinstance(cfg.trunc, str) and cfg.trunc.startswith("moe_")):
            kb.barrier()
            mt = cfg.trunc[4:] if (isinstance(cfg.trunc, str)
                                   and cfg.trunc.startswith("moe_")) else None
            emit_attn_moe(p, kb, p.d["r1"], p.d["r2x"], moe_trunc=mt)
        if cfg.trunc is None:
            kb.barrier()
            for hf in (0, 1):
                emit_ln_half(p, kb, p.d["r2x"], hf, mode="ln", do_gate=("ffn",))
            emit_pden_out(p, kb)
    elif cfg.only == "l3":
        load_ffn_weights(p, kb)
        emit_ffn_sublayer(p, kb, p.d["x"], p.d["y"], None, None, scale=0.5,
                          denom_from_inputs=True)
    else:
        raise NotImplementedError(cfg.only)
    if cfg.debug:
        t = p.t
        kb.dma("dsync", "dbg", lambda e: e.dma_start(out=p.d["dbg_mv"][:, :], in_=t["mv"][:, 0:cfg.NT * 8]),
               reads=[R(p, "mv")], writes=[R(p, "dst_dram")])
        kb.dma("dsync", "dbg", lambda e: e.dma_start(out=p.d["dbg_sel"][:, :], in_=t["sel"][:, 0:cfg.NT * 8]),
               reads=[R(p, "sel")], writes=[R(p, "dst_dram")])
        kb.dma("dsync", "dbg", lambda e: e.dma_start(out=p.d["dbg_route"][:, :], in_=t["route"][:, 0:cfg.NT * 8]),
               reads=[R(p, "route")], writes=[R(p, "dst_dram")])
        kb.dma("dsync", "dbg", lambda e: e.dma_start(out=p.d["dbg_xnfm"][:, :],
                                                     in_=t["xnfm"][:, 0:4 * cfg.T]),
               reads=R(p, "xnfm_t", (cfg.NT // 2) * 8), writes=[R(p, "dst_dram")])
    emit_output_done(p, kb)


def _measure_splits(nc, kb):
    streams = {"SP": [], "PL": []}
    for ins in nc.all_instructions():
        nm = ins.__class__.__name__
        eng = getattr(getattr(ins, "engine", None), "value", "")
        key = "SP" if eng == "SP" else ("PL" if eng == "Pool" else None)
        if key is None:
            continue
        if nm == "InstDMACopy" or "Collective" in nm:
            streams[key].append("D")
        elif nm == "InstNoOp":
            streams[key].append("N")
    def group(seq):
        out = []
        cnt = 0
        for x in seq:
            if x == "D":
                cnt += 1
            else:
                out.append(max(cnt, 1))
                cnt = 0
        return out
    sp = group(streams["SP"])
    pl = group(streams["PL"])
    incs = {"dsync": sp, "dgp": [], "coll": []}
    assert len(pl) == len(kb.pl_order), (len(pl), len(kb.pl_order))
    for q, n in zip(kb.pl_order, pl):
        incs[q].append(n)
    return incs


def build_program(cfg):
    nc1 = bass.Bass("TRN2", target_bir_lowering=False)
    p1 = declare_io(nc1, cfg)
    p1.nc = nc1
    with ExitStack() as es:
        alloc_sbuf(p1, nc1, es)
        alloc_psum(p1, nc1, es)
        kb1 = KB(nc1, es)
        p1.kb = kb1
        _emit_all(p1, kb1, cfg)
        block = es.enter_context(nc1.Block())
        kb1.replay(block, sentinels=True)
    incs = _measure_splits(nc1, kb1)
    nc = bass.Bass("TRN2", target_bir_lowering=False)
    p = declare_io(nc, cfg)
    p.nc = nc
    es = ExitStack()
    with es:
        alloc_sbuf(p, nc, es)
        alloc_psum(p, nc, es)
        kb = KB(nc, es, dma_incs=incs)
        p.kb = kb
        _emit_all(p, kb, cfg)
        block = es.enter_context(nc.Block())
        kb.replay(block)
    return nc, p


def shard_inputs(inputs, cfg):
    c = host_constants()
    x = np.ascontiguousarray(np.asarray(inputs["L0_attn_hidden_states"], dtype=np.float32))
    B_total = x.shape[0]
    per = B_total // cfg.ncores
    name_map = {
        "aw_ln1": "L0_attn_w_ln1", "aw_ln2": "L0_attn_w_ln2", "aW_qkvz": "L0_attn_W_qkvz",
        "aW_ba": "L0_attn_W_ba", "aconv_w": "L0_attn_conv_w", "adt_bias": "L0_attn_dt_bias",
        "aA_log": "L0_attn_A_log", "agate_norm_w": "L0_attn_gate_norm_w",
        "aW_out": "L0_attn_W_out", "aW_router": "L0_attn_W_router",
        "aW_gate_e": "L0_attn_W_gate_e", "aW_up_e": "L0_attn_W_up_e",
        "aW_down_e": "L0_attn_W_down_e",
        "fw_gate_w": "L0_ffn_w_gate_w", "fw_gate_b": "L0_ffn_w_gate_b",
        "fW1": "L0_ffn_W1", "fb1": "L0_ffn_b1", "fW2": "L0_ffn_W2", "fb2": "L0_ffn_b2",
    }
    base = {k: np.ascontiguousarray(np.asarray(inputs[v], dtype=np.float32))
            for k, v in name_map.items()}
    base.update(c)
    in_maps = []
    for ci in range(cfg.ncores):
        m = dict(base)
        m["x"] = np.ascontiguousarray(
            x[ci * per:(ci + 1) * per].reshape(per * x.shape[1], x.shape[2]))
        in_maps.append(m)
    return in_maps


def kernel(**inputs):
    S, NC = 2048, 8
    cfg1 = Cfg(S=S, B=2, ncores=NC, B_total=16, only="l1")
    cfg2 = Cfg(S=S, B=2, ncores=NC, B_total=16, only="l2")
    cfg3 = Cfg(S=S, B=2, ncores=NC, B_total=16, only="l3")
    base_maps = shard_inputs(inputs, cfg1)
    nc1, _ = build_program(cfg1)
    res1 = run_bass_kernel_spmd(nc1, base_maps, core_ids=list(range(NC)))
    pd1 = np.ascontiguousarray(np.stack([np.asarray(r["pden_out"]) for r in res1.results]))
    nc2, _ = build_program(cfg2)
    maps2 = [dict(m, pdens=pd1) for m in base_maps]
    res2 = run_bass_kernel_spmd(nc2, maps2, core_ids=list(range(NC)))
    pd2 = np.ascontiguousarray(np.stack([np.asarray(r["pden_out"]) for r in res2.results]))
    nc3, _ = build_program(cfg3)
    maps3 = [dict(m, x=np.ascontiguousarray(np.asarray(res2.results[ci]["r2x"])), pdens=pd2)
             for ci, m in enumerate(base_maps)]
    res3 = run_bass_kernel_spmd(nc3, maps3, core_ids=list(range(NC)))
    outs = [np.asarray(r["y"]).reshape(cfg3.B, cfg3.S, 1024) for r in res3.results]
    return np.ascontiguousarray(np.concatenate(outs, axis=0))



def load_attn_small_weights(p, kb):
    t, d = p.t, p.d
    rc = R(p, "consts")
    kb.dma("dgp", "cw", lambda e: e.dma_start(
        out=t["wba"][:].rearrange("p (dt c) -> p dt c", c=16),
        in_=d["aW_ba"].rearrange("(dt p) c -> p dt c", p=128)), writes=[rc])
    kb.dma("dgp", "cw", lambda e: e.dma_start(
        out=t["wrt"][:].rearrange("p (dt e) -> p dt e", e=8),
        in_=d["aW_router"].rearrange("(dt p) e -> p dt e", p=128)), writes=[rc])
    kb.dma("dsync", "c", lambda e: e.dma_start(
        out=t["convw"][:].rearrange("p (ct j) -> p ct j", j=4),
        in_=d["aconv_w"].rearrange("(ct p) j -> p ct j", p=128)), writes=[rc])
    for nm, dn in (("wln1c", "aw_ln1"), ("wln2c", "aw_ln2")):
        kb.dma("dsync", "c", lambda e, dn=dn: e.dma_start(
            out=t["b1st"][0:8, 0:128], in_=d[dn].rearrange("(dt p) -> dt p", p=128)),
            writes=[rc])
        kb.op("tensor", lambda e: e.transpose(p.ps["misc"][0:128, 0:8], t["b1st"][0:8, 0:128],
                                              t["ident_f32"][0:8, 0:8]),
              reads=[rc], writes=[R(p, "ps_misc")])
        kb.op("vector", lambda e, nm=nm: e.tensor_scalar(t[nm][:], p.ps["misc"][0:128, 0:8],
                                                         1.0, None, ALU.add),
              reads=[R(p, "ps_misc")], writes=[rc])
    kb.dma("dsync", "c", lambda e: e.dma_start(out=t["gnwc"][:, 0:1], in_=d["agate_norm_w"][:, None]),
           writes=[rc])
    kb.dma("dsync", "c", lambda e: e.dma_start(out=t["hcol8"][0:8, 0:1], in_=d["adt_bias"][:, None]),
           writes=[rc])
    kb.dma("dsync", "c", lambda e: e.dma_start(out=t["hcol8"][0:8, 2:3], in_=d["aA_log"][:, None]),
           writes=[rc])
    kb.op("scalar", lambda e: e.activation(t["hcol8"][0:8, 3:4], t["hcol8"][0:8, 2:3], AF.Exp),
          reads=[rc], writes=[rc])
    kb.op("vector", lambda e: e.tensor_scalar(t["hcol8"][0:8, 1:2], t["hcol8"][0:8, 3:4], -1.0,
                                              None, ALU.mult),
          reads=[rc], writes=[rc])


def load_attn_big_weights(p, kb):
    t, d = p.t, p.d
    rw = R(p, "attn_w")
    kb.dma("dgp", "aw", lambda e: e.dma_start(
        out=t["wbuf"][:, 0:24576].rearrange("p (dt c) -> p dt c", c=3072),
        in_=d["aW_qkvz"].rearrange("(dt p) c -> p dt c", p=128)),
        writes=[rw])
    kb.dma("dgp", "aw", lambda e: e.dma_start(
        out=t["wbuf"][:, 24576:32768].rearrange("p (vt c) -> p vt c", c=1024),
        in_=d["aW_out"].rearrange("(vt p) c -> p vt c", p=128)),
        writes=[rw])


def wqkvz_tile(p, oc, dt):
    c = dt * 3072 + oc * 128
    return p.t["wbuf"][:, c:c + 128]


def wout_tile(p, vt, dt):
    c = 24576 + vt * 1024 + dt * 128
    return p.t["wbuf"][:, c:c + 128]


def raw_tile(p, ct):
    return p.t["accbuf"][:, ct * 259: ct * 259 + 259]


def qkvc_tile(p, ct):
    return p.t["accbuf"][:, 16 * 259 + ct * 256: 16 * 259 + (ct + 1) * 256]


def emit_attn_block_prologue(p, kb, b, blk):
    t, cfg, ps = p.t, p.cfg, p.ps
    TH = cfg.T // 2
    NTH = cfg.NT // 2
    loc0 = blk * 256
    nblk = cfg.S // 256
    rfm = R(p, "xnfm_t", NTH * 8)
    rc = R(p, "consts")
    rw = R(p, "attn_w")
    rraw = R(p, "raw_t", 16)
    rz = R(p, "z_t", 8)
    rphs = R(p, "ps_h1")
    rph = [rphs, rphs]
    rpg = R(p, "ps_g")
    rba = R(p, "ba_r")
    rrF = R(p, "rowsF")
    rrB = R(p, "rowsB")
    rqc = R(p, "qkvc_t", 16)
    rsq = R(p, "sq_pool")
    for ct in range(16):
        if blk == 0:
            kb.op("vector", lambda e, ct=ct: e.memset(raw_tile(p, ct)[:, 0:3], 0.0),
                  reads=[], writes=[rraw[ct]])
        else:
            kb.op("vector", lambda e, ct=ct: e.tensor_copy(raw_tile(p, ct)[:, 0:3],
                                                           raw_tile(p, ct)[:, 256:259]),
                  reads=[rraw[ct]], writes=[rraw[ct]])
    rpm2 = R(p, "ps_misc")
    for oc in range(24):
        if oc % 2 == 0:
            ph = ps["h1"][0:128, 0:256]
        else:
            ph = ps["misc"][0:128, 0:256]
        for dt in range(8):
            kb.op("tensor", lambda e, oc=oc, dt=dt, o=ph, loc0=loc0: e.matmul(
                o, wqkvz_tile(p, oc, dt),
                t["xnfm"][:, dt * TH + loc0: dt * TH + loc0 + 256],
                start=(dt == 0), stop=(dt == 7)),
                reads=[rw, rfm[(loc0 // 128) * 8 + dt], rfm[(loc0 // 128 + 1) * 8 + dt]],
                writes=[rph[0] if oc % 2 == 0 else rpm2])
        if oc < 16:
            dst = raw_tile(p, oc)[:, 3:259]
            eng = "vector" if oc % 2 == 0 else "scalar"
            rsrc = rph[0] if oc % 2 == 0 else rpm2
            if eng == "vector":
                kb.op(eng, lambda e, a=dst, o=ph: e.tensor_copy(a, o),
                      reads=[rsrc], writes=[rraw[oc]])
            else:
                kb.op(eng, lambda e, a=dst, o=ph: e.copy(a, o),
                      reads=[rsrc], writes=[rraw[oc]])
        else:
            zt = oc - 16
            dst = t["zbuf"][:, zt * 256:(zt + 1) * 256]
            rsrc = rph[0] if oc % 2 == 0 else rpm2
            kb.op("scalar", lambda e, a=dst, o=ph: e.activation(a, o, AF.Silu),
                  reads=[rsrc], writes=[rz[zt]])
    for part in range(2):
        pg = ps["g"][0:8, part * 256:(part + 1) * 256]
        for dt in range(8):
            kb.op("tensor", lambda e, part=part, dt=dt, o=pg, loc0=loc0: e.matmul(
                o, t["wba"][:, dt * 16 + part * 8: dt * 16 + part * 8 + 8],
                t["xnfm"][:, dt * TH + loc0: dt * TH + loc0 + 256],
                start=(dt == 0), stop=(dt == 7)),
                reads=[rc, rfm[(loc0 // 128) * 8 + dt], rfm[(loc0 // 128 + 1) * 8 + dt]],
                writes=[R(p, "ps_g")])
    kb.op("scalar", lambda e: e.activation(t["rowsB"][0:8, 0:256], ps["g"][0:8, 0:256],
                                           AF.Sigmoid),
          reads=[rpg], writes=[rrB])
    gF = lambda q: t["rowsF"][0:8, q * 256:(q + 1) * 256]
    gB = lambda q: t["rowsB"][0:8, q * 256:(q + 1) * 256]
    kb.op("scalar", lambda e: e.activation(gF(0), ps["g"][0:8, 256:512], AF.Relu,
                                           bias=t["hcol8"][0:8, 0:1]),
          reads=[rpg, rc], writes=[rrF])
    kb.op("scalar", lambda e: e.activation(gF(1), ps["g"][0:8, 256:512], AF.Abs,
                                           bias=t["hcol8"][0:8, 0:1]),
          reads=[rpg, rc], writes=[rrF])
    kb.op("scalar", lambda e: e.activation(gF(1), gF(1), AF.Exp, scale=-1.0),
          reads=[rrF], writes=[rrF])
    kb.op("scalar", lambda e, bc=ccol(p, kb, 1.0): e.activation(gF(1), gF(1), AF.Ln,
                                                                bias=bc[0:8, :]),
          reads=[rrF, rc], writes=[rrF])
    kb.op("vector", lambda e: e.tensor_add(gF(0), gF(0), gF(1)), reads=[rrF], writes=[rrF])
    kb.op("vector", lambda e: e.tensor_scalar(gF(0), gF(0), t["hcol8"][0:8, 1:2], None, ALU.mult),
          reads=[rrF, rc], writes=[rrF])
    for c in range(2):
        kb.op("vector", lambda e, c=c: e.tensor_tensor_scan(
            gF(1)[:, c * 128:(c + 1) * 128], t["ones_sq_bf"][0:8, 0:128],
            gF(0)[:, c * 128:(c + 1) * 128], 0.0, ALU.mult, ALU.add),
            reads=[rrF, rc], writes=[rrF])
    kb.op("scalar", lambda e: e.activation(gF(2), gF(1), AF.Exp), reads=[rrF], writes=[rrF])
    for c in range(2):
        kb.op("vector", lambda e, c=c: e.tensor_scalar(
            gF(3)[:, c * 128:(c + 1) * 128], gF(1)[:, c * 128:(c + 1) * 128],
            gF(1)[:, c * 128 + 127: c * 128 + 128], None, ALU.subtract),
            reads=[rrF], writes=[rrF])
        kb.op("vector", lambda e, c=c: e.tensor_scalar(
            gF(4)[:, c * 128:(c + 1) * 128], t["ones_sq_bf"][0:8, 0:128],
            gF(2)[:, c * 128 + 127: c * 128 + 128], None, ALU.mult),
            reads=[rrF, rc], writes=[rrF])
    kb.op("scalar", lambda e: e.activation(gF(3), gF(3), AF.Exp, scale=-1.0),
          reads=[rrF], writes=[rrF])
    rN_ = R(p, "pw_n")
    rM_ = R(p, "pw_m")
    Q0 = 16 * 259
    acc = t["accbuf"][:, Q0:Q0 + 4096]

    def raw_w(j, h0, n):
        return (t["accbuf"][:, 0:16 * 259].rearrange("p (t x) -> p t x", x=259)
                [:, h0:h0 + n, j:j + 256])

    def w_w(j, h0, n):
        return (t["convw"][:, 0:64].rearrange("p (t j) -> p t j", j=4)
                [:, h0:h0 + n, j:j + 1].to_broadcast([128, n, 256]))

    rall = rraw[0:16]
    kb.op("gpsimd", lambda e: e.tensor_mul(
        acc.rearrange("p (t x) -> p t x", x=256), raw_w(0, 0, 16), w_w(0, 0, 16)),
        reads=rall + [rc], writes=rqc[0:16])
    for j in (1, 2, 3):
        for h in range(2):
            tmp = t["poolB"][:, 3072:5120]
            kb.op("gpsimd", lambda e, j=j, h=h, tmp=tmp: e.tensor_mul(
                tmp.rearrange("p (t x) -> p t x", x=256), raw_w(j, 8 * h, 8), w_w(j, 8 * h, 8)),
                reads=rall + [rc], writes=[rN_, rM_])
            kb.op("gpsimd", lambda e, h=h, tmp=tmp: e.tensor_add(
                acc[:, h * 2048:(h + 1) * 2048], acc[:, h * 2048:(h + 1) * 2048], tmp),
                reads=[rN_, rM_] + rqc[8 * h:8 * h + 8], writes=rqc[8 * h:8 * h + 8])
    for h in range(2):
        kb.op("scalar", lambda e, h=h: e.activation(
            acc[:, h * 2048:(h + 1) * 2048], acc[:, h * 2048:(h + 1) * 2048], AF.Silu),
            reads=rqc[8 * h:8 * h + 8], writes=rqc[8 * h:8 * h + 8])
    kb.op("gpsimd", lambda e: e.tensor_mul(t["poolB"][:, 0:2048],
                                           acc[:, 0:2048], acc[:, 0:2048]),
          reads=rqc[0:8], writes=[rsq, R(p, "pw_a"), R(p, "pw_kts")])
    for grp in range(2):
        pg = ps["g"][0:8, grp * 256:(grp + 1) * 256]
        for kh in range(4):
            i = grp * 4 + kh
            kb.op("tensor", lambda e, i=i, kh=kh, o=pg: e.matmul(
                o, t["colsel2"][:, kh * 8:(kh + 1) * 8], t["poolB"][:, i * 256:(i + 1) * 256],
                start=(kh == 0), stop=(kh == 3)),
                reads=[rsq, rc], writes=[R(p, "ps_g")])
    kb.op("scalar", lambda e, bc=ccol(p, kb, 128 * EPS6): e.activation(
        gF(0), ps["g"][0:8, 0:256], AF.Sqrt, bias=bc[0:8, :], scale=128.0),
        reads=[rpg, rc], writes=[rrF])
    kb.op("vector", lp(p, lambda e: e.reciprocal(gB(1), gF(0))), reads=[rrF], writes=[rrB])
    kb.op("scalar", lambda e, bc=ccol(p, kb, EPS6): e.activation(
        gF(0), ps["g"][0:8, 256:512], AF.Sqrt, bias=bc[0:8, :]),
        reads=[rpg, rc], writes=[rrF])
    kb.op("vector", lp(p, lambda e: e.reciprocal(gB(2), gF(0))), reads=[rrF], writes=[rrB])
    kb.op("vector", lambda e: e.tensor_mul(gB(3), gB(0), gB(2)), reads=[rrB], writes=[rrB])
    kb.op("vector", lambda e: e.tensor_mul(gB(4), gB(3), gF(2)), reads=[rrB, rrF], writes=[rrB])
    kb.op("vector", lambda e: e.tensor_mul(gB(5), gB(2), gF(3)), reads=[rrB, rrF], writes=[rrB])
    kb.op("vector", lambda e: e.tensor_mul(gB(6), gB(1), gF(2)), reads=[rrB, rrF], writes=[rrB])
    rct = R(p, "colsT")
    rpm = R(p, "ps_misc")
    rpTs2 = R(p, "ps_T")
    rpT = [rpTs2] * 4
    for c in range(2):
        fq = [(0, gF(1)), (1, gF(4))]
        for qi, (q, src) in enumerate(fq):
            pm = ps["misc"][0:128, qi * 8: qi * 8 + 8]
            kb.op("tensor", lambda e, src=src, c=c, o=pm: e.transpose(
                o, src[:, c * 128:(c + 1) * 128], t["ident_f32"][0:8, 0:8]),
                reads=[rrF, rc], writes=[rpm])
            kb.op("vector", lambda e, c=c, q=q, o=pm: e.tensor_copy(
                t["colsT"][:, (c * 8 + q) * 8:(c * 8 + q) * 8 + 8], o),
                reads=[rpm], writes=[rct])
        bq = [(2, gB(2)), (3, gB(3)), (4, gB(0)), (5, gB(4)), (6, gB(5))]
        for qi, (q, src) in enumerate(bq):
            pT = ps["T"][0:128, 512 + qi * 8: 512 + qi * 8 + 8]
            kb.op("tensor", lambda e, src=src, c=c, o=pT: e.transpose(
                o, src[:, c * 128:(c + 1) * 128], t["ident_bf"][0:8, 0:8]),
                reads=[rrB, rc], writes=[rpT[3]])
            kb.op("vector", lambda e, c=c, q=q, o=pT: e.tensor_copy(
                t["colsT"][:, (c * 8 + q) * 8:(c * 8 + q) * 8 + 8], o),
                reads=[rpT[3]], writes=[rct])


def col_ap(p, c, q, h):
    return p.t["colsT"][:, (c * 8 + q) * 8 + h:(c * 8 + q) * 8 + h + 1]


PB_QS = 0
PB_KTOK = 128
PB_AKROW = 256
PB_AQBC = 384
PB_QS2 = 512
PB_ATIL = 768
PB_NPOW = 2816
PB_MPOW = 4352
PB_IMN = 1536
PB_R = 1792
PB_VTOK = 2048
PB_Y0 = 2304
PB_Y1 = 2560
PB_SQO = 5632
PB_BAKBC = 5888
PB_SCR = 1024
PB_KTS = 1280

PF_DIFF = 0
PF_DX = 128
PF_DM = 384
PF_GKQS = 640
PF_GKKS = 768
PF_SCR = 896


def npow(p, j, slot=None):
    c = PB_NPOW + ((j - 1) % 6) * 256
    if slot is None:
        return p.t["poolB"][:, c:c + 256]
    return p.t["poolB"][:, c + slot * 128: c + slot * 128 + 128]


def mpow(p, j, slot=None):
    c = PB_MPOW + ((j - 1) % 5) * 256
    if slot is None:
        return p.t["poolB"][:, c:c + 256]
    return p.t["poolB"][:, c + slot * 128: c + slot * 128 + 128]


def pb(p, off, slot=None, w=128):
    if slot is None:
        return p.t["poolB"][:, off:off + 256]
    return p.t["poolB"][:, off + slot * w: off + slot * w + w]


def pf(p, off, slot=None, w=128):
    if slot is None:
        return p.t["poolF"][:, off:off + 256]
    return p.t["poolF"][:, off + slot * w: off + slot * w + w]


def bv2(ap):
    return ap.unsqueeze(1).to_broadcast([128, 2, 128])


def r2(ap256):
    return ap256.rearrange("p (s q) -> p s q", s=2)


def emit_delta_chunk(p, kb, b, blk, c):
    t, cfg, ps = p.t, p.cfg, p.ps
    TH = cfg.T // 2
    NTH = cfg.NT // 2
    loc0 = blk * 256
    rc = R(p, "consts")
    rqc = R(p, "qkvc_t", 16)
    rz = R(p, "z_t", 8)
    rrF = R(p, "rowsF")
    rrB = R(p, "rowsB")
    rct = R(p, "colsT")
    rst = R(p, "states", 8)
    rfm = R(p, "xnfm_t", NTH * 8)
    rpm = R(p, "ps_misc")
    rpTs = R(p, "ps_T")
    rpT = [rpTs] * 4
    rpo = {0: R(p, "ps_o", 4)[0], 1: R(p, "ps_o", 4)[1]}
    rpO = [R(p, "ps_o", 4)[2], R(p, "ps_o", 4)[3]]
    rphs = R(p, "ps_h1")
    rph = [rphs, rphs]
    rpg = R(p, "ps_g")
    rB = R(p, "poolB_delta")
    rF = R(p, "poolF_delta")
    rY = R(p, "Y_pair")
    sl = slice(c * 128, (c + 1) * 128)

    def mmT(emit, reads, writes):
        kb.op("tensor", emit, reads=reads, writes=writes)

    for kh in range(4):
        q_t = qkvc_tile(p, kh)[:, sl]
        k_t = qkvc_tile(p, 4 + kh)[:, sl]
        pTb = ps["h1"][0:128, 256:384]
        mmT(lambda e, kh=kh, o=pTb: e.matmul(
            o, t["rowsel8"][0:8, (2 * kh) * 128:(2 * kh) * 128 + 128],
            t["rowsB"][0:8, 1 * 256 + c * 128: 1 * 256 + (c + 1) * 128],
            start=True, stop=True), [rrB, rc], [rphs])
        kb.op("scalar", lambda e, o=pTb: e.copy(pb(p, PB_AQBC, 0), o), reads=[rphs],
              writes=[rB, R(p, "sq_pool")])
        kb.op("vector", lambda e, a=q_t: e.tensor_mul(pb(p, PB_QS, 0), a, pb(p, PB_AQBC, 0)),
              reads=[rqc[kh], rB], writes=[rB])
        mmT(lambda e, a=k_t: e.matmul(ps["misc"][0:128, 0:128], a, pb(p, PB_QS, 0),
                                      start=True, stop=True), [rqc[4 + kh], rB], [rpm])
        mmT(lambda e, a=k_t: e.matmul(ps["misc"][0:128, 128:256], a, a,
                                      start=True, stop=True), [rqc[4 + kh]], [rpm])
        for slot in range(2):
            vh = 2 * kh + slot
            mmT(lambda e, a=k_t, vh=vh, slot=slot: e.matmul(
                ps["misc"][0:128, 256 + slot * 128: 256 + slot * 128 + 128], a,
                t["states"][:, vh * 128:(vh + 1) * 128],
                start=True, stop=True), [rqc[4 + kh], rst[vh]], [rpm])
        pT2 = ps["T"][0:128, 640:768]
        mmT(lambda e, a=k_t, o=pT2: e.transpose(o, a, t["ident_bf"][:]),
            [rqc[4 + kh], rc], [rpTs])
        kb.op("scalar", lambda e, o=pT2: e.copy(pb(p, PB_KTOK, 0), o), reads=[rpTs], writes=[rB])
        pT3 = ps["h1"][0:128, 256:384]
        mmT(lambda e, kh=kh, o=pT3: e.matmul(
            o, t["rowsel8"][0:8, (2 * kh) * 128:(2 * kh) * 128 + 128],
            t["rowsB"][0:8, 2 * 256 + c * 128: 2 * 256 + (c + 1) * 128],
            start=True, stop=True), [rrB, rc], [rphs])
        kb.op("scalar", lambda e, o=pT3: e.copy(pb(p, PB_AKROW, 0), o), reads=[rphs], writes=[rB])
        kb.op("vector", lambda e, kh=kh: e.tensor_scalar(
            pf(p, PF_GKQS, 0), ps["misc"][0:128, 0:128], col_ap(p, c, 2, 2 * kh), None, ALU.mult),
            reads=[rpm, rct], writes=[rF])
        kb.op("vector", lambda e, kh=kh: e.tensor_scalar(
            pf(p, PF_GKKS, 0), ps["misc"][0:128, 128:256], col_ap(p, c, 2, 2 * kh), None, ALU.mult),
            reads=[rpm, rct], writes=[rF])
        for slot in range(2):
            vh = 2 * kh + slot
            ph2 = ps["o1"][0:128, 256:384]
            mmT(lambda e, vh=vh, o=ph2: e.matmul(
                o, t["rowsel8f"][0:8, vh * 128:(vh + 1) * 128],
                t["rowsF"][0:8, 1 * 256 + c * 128: 1 * 256 + (c + 1) * 128],
                start=True, stop=True), [rrF, rc], [rpo[1]])
            kb.op("vector", lambda e, vh=vh, o=ph2: e.tensor_scalar(
                pf(p, PF_DIFF, 0), o, col_ap(p, c, 0, vh), None, ALU.subtract),
                reads=[rpo[1], rct], writes=[rF])
            kb.op("vector", lambda e: e.tensor_scalar_min(pf(p, PF_SCR, 0), pf(p, PF_DIFF, 0), 0.0),
                  reads=[rF], writes=[rF])
            kb.op("scalar", lambda e, slot=slot: e.activation(pf(p, PF_DX, slot), pf(p, PF_SCR, 0),
                                                              AF.Exp),
                  reads=[rF], writes=[rF])
            kb.op("vector", lambda e: e.tensor_scalar_max(pf(p, PF_SCR, 0), pf(p, PF_DIFF, 0), 0.0),
                  reads=[rF], writes=[rF])
            kb.op("scalar", lambda e, slot=slot: e.activation(pf(p, PF_DM, slot), pf(p, PF_SCR, 0),
                                                              AF.Exp, scale=-1.0),
                  reads=[rF], writes=[rF])
            pT4 = ps["h1"][0:128, 256:384]
            mmT(lambda e, vh=vh, o=pT4: e.matmul(
                o, t["rowsel8"][0:8, vh * 128:(vh + 1) * 128],
                t["rowsB"][0:8, 3 * 256 + c * 128: 3 * 256 + (c + 1) * 128],
                start=True, stop=True), [rrB, rc], [rphs])
            kb.op("scalar", lambda e, slot=slot, o=pT4: e.copy(pb(p, PB_BAKBC, slot), o),
                  reads=[rphs], writes=[rB])
            kb.op("vector", lambda e, vh=vh, slot=slot: e.tensor_scalar(
                mpow(p, 1, slot), ps["misc"][0:128, 128:256], col_ap(p, c, 3, vh), None, ALU.mult),
                reads=[rpm, rct], writes=[rB])
            kb.op("vector", lambda e, vh=vh, slot=slot: e.tensor_scalar(
                pf(p, PF_SCR, slot, w=64) if False else pf(p, PF_SCR, 0),
                ps["misc"][0:128, 256 + slot * 128: 256 + slot * 128 + 128],
                col_ap(p, c, 5, vh), None, ALU.mult),
                reads=[rpm, rct], writes=[rF])
            pT5 = ps["T"][0:128, 512:640]
            mmT(lambda e, vh=vh, o=pT5: e.transpose(o, qkvc_tile(p, 8 + vh)[:, sl],
                                                    t["ident_bf"][:]),
                [rqc[8 + vh], rc], [rpTs])
            kb.op("scalar", lambda e, slot=slot, o=pT5: e.copy(pb(p, PB_VTOK, slot), o),
                  reads=[rpTs], writes=[rB])
            kb.op("vector", lambda e, vh=vh, slot=slot: e.scalar_tensor_tensor(
                pb(p, PB_R, slot), pb(p, PB_VTOK, slot), col_ap(p, c, 4, vh),
                pf(p, PF_SCR, 0), ALU.mult, ALU.subtract),
                reads=[rB, rct, rF], writes=[rB])
        kb.op("gpsimd", lambda e: e.tensor_mul(
            r2(npow(p, 1)), r2(pb(p, PB_BAKBC)), bv2(pf(p, PF_GKKS, 0))),
            reads=[rB, rF], writes=[rB])
        kb.op("vector", lambda e: e.tensor_mul(npow(p, 1), npow(p, 1), pf(p, PF_DX)),
              reads=[rB, rF], writes=[rB])
        kb.op("gpsimd", lambda e: e.tensor_mul(
            r2(npow(p, 1)), r2(npow(p, 1)), bv2(t["striu"][:])),
            reads=[rB, rc], writes=[rB])
        kb.op("gpsimd", lambda e: e.tensor_mul(
            r2(mpow(p, 1)), r2(mpow(p, 1)), bv2(pb(p, PB_AKROW, 0))),
            reads=[rB], writes=[rB])
        kb.op("vector", lambda e: e.tensor_mul(mpow(p, 1), mpow(p, 1), pf(p, PF_DM)),
              reads=[rB, rF], writes=[rB])
        kb.op("gpsimd", lambda e: e.tensor_mul(
            r2(mpow(p, 1)), r2(mpow(p, 1)), bv2(t["stril"][:])),
            reads=[rB, rc], writes=[rB])
        kb.op("vector", lambda e: e.tensor_mul(
            r2(pb(p, PB_ATIL)), r2(pf(p, PF_DX)), bv2(pf(p, PF_GKQS, 0))),
            reads=[rF], writes=[rB])
        kb.op("gpsimd", lambda e: e.tensor_mul(
            r2(pb(p, PB_ATIL)), r2(pb(p, PB_ATIL)), bv2(t["triu"][:])),
            reads=[rB, rc], writes=[rB])
        kb.op("vector", lambda e: e.tensor_tensor(
            r2(pb(p, PB_IMN)), bv2(t["ident_bf"][:]), r2(npow(p, 1)), ALU.subtract),
            reads=[rB, rc], writes=[rB])
        for lvl in range(2, 8):
            for slot in range(2):
                mmT(lambda e, lvl=lvl, slot=slot: e.matmul(
                    ps["o0"][0:128, slot * 128: slot * 128 + 128],
                    mpow(p, lvl - 1, slot), npow(p, lvl - 1, slot),
                    start=True, stop=True), [rB], [rpo[0]])
                if lvl <= 6:
                    mmT(lambda e, lvl=lvl, slot=slot: e.matmul(
                        ps["o0"][0:128, 256 + slot * 128: 256 + slot * 128 + 128],
                        npow(p, lvl - 1, slot), mpow(p, lvl - 1, slot),
                        start=True, stop=True), [rB], [rpo[0]])
            kb.op("scalar", lambda e, lvl=lvl: e.copy(npow(p, lvl), ps["o0"][0:128, 0:256]),
                  reads=[rpo[0]], writes=[rB])
            if lvl <= 6:
                kb.op("vector", lambda e, lvl=lvl: e.tensor_copy(mpow(p, lvl),
                                                                 ps["o0"][0:128, 256:512]),
                      reads=[rpo[0]], writes=[rB])
        ycur, ynxt = PB_Y0, PB_Y1
        for j in range(7, 1, -1):
            for slot in range(2):
                src = pb(p, PB_R if j == 7 else ycur, slot)
                mmT(lambda e, slot=slot, src=src: e.matmul(
                    ps["o1"][0:128, slot * 128: slot * 128 + 128], t["ident_bf"][:], src,
                    start=True, stop=False), [rB, rc], [rpo[1]])
                mmT(lambda e, j=j, slot=slot, src=src: e.matmul(
                    ps["o1"][0:128, slot * 128: slot * 128 + 128], npow(p, j, slot), src,
                    start=False, stop=True), [rB], [rpo[1]])
            eng = "scalar" if j % 2 == 0 else "vector"
            if eng == "scalar":
                kb.op(eng, lambda e, ynxt=ynxt: e.copy(pb(p, ynxt), ps["o1"][0:128, 0:256]),
                      reads=[rpo[1]], writes=[rY])
            else:
                kb.op(eng, lambda e, ynxt=ynxt: e.tensor_copy(pb(p, ynxt), ps["o1"][0:128, 0:256]),
                      reads=[rpo[1]], writes=[rY])
            ycur, ynxt = ynxt, ycur
        for slot in range(2):
            mmT(lambda e, slot=slot: e.matmul(
                ps["o1"][0:128, slot * 128: slot * 128 + 128], pb(p, PB_IMN, slot),
                pb(p, ycur, slot), start=True, stop=True), [rB, rY], [rpo[1]])
        kb.op("vector", lambda e, ynxt=ynxt: e.tensor_copy(pb(p, ynxt), ps["o1"][0:128, 0:256]),
              reads=[rpo[1]], writes=[rY])
        ufin = ynxt
        for slot in range(2):
            vh = 2 * kh + slot
            pT6 = ps["h1"][0:128, 256:384]
            mmT(lambda e, vh=vh, o=pT6: e.matmul(
                o, t["rowsel8"][0:8, vh * 128:(vh + 1) * 128],
                t["rowsB"][0:8, 6 * 256 + c * 128: 6 * 256 + (c + 1) * 128],
                start=True, stop=True), [rrB, rc], [rphs])
            kb.op("scalar", lambda e, slot=slot, o=pT6: e.copy(pb(p, PB_QS2, slot), o),
                  reads=[rphs], writes=[rB])
            kb.op("vector", lambda e, slot=slot, a=q_t: e.tensor_mul(
                pb(p, PB_QS2, slot), a, pb(p, PB_QS2, slot)),
                reads=[rqc[kh], rB], writes=[rB])
            pO = ps[f"o{2 + kh // 2}"][0:128, (vh % 4) * 128:(vh % 4) * 128 + 128]
            mmT(lambda e, vh=vh, slot=slot, o=pO: e.matmul(
                o, t["states"][:, vh * 128:(vh + 1) * 128], pb(p, PB_QS2, slot),
                start=True, stop=False), [rst[vh], rB], [rpO[kh // 2]])
            mmT(lambda e, slot=slot, o=pO: e.matmul(
                o, pb(p, ufin, slot), pb(p, PB_ATIL, slot),
                start=False, stop=True), [rY, rB], [rpO[kh // 2]])
            kb.op("vector", lambda e, vh=vh, slot=slot: e.tensor_scalar(
                pb(p, PB_SCR, slot), t["states"][:, vh * 128:(vh + 1) * 128],
                col_ap(p, c, 1, vh), None, ALU.mult),
                reads=[rst[vh], rct], writes=[rB])
            kb.op("vector", lambda e, vh=vh, slot=slot: e.tensor_scalar(
                pb(p, PB_KTS, slot), pb(p, PB_KTOK, 0), col_ap(p, c, 6, vh), None, ALU.mult),
                reads=[rB, rct], writes=[rB])
            phS = ps["h1"][0:128, slot * 128: slot * 128 + 128]
            mmT(lambda e, slot=slot, o=phS: e.matmul(
                o, t["ident_bf"][:], pb(p, PB_SCR, slot), start=True, stop=False),
                [rB, rc], [rph[0]])
            mmT(lambda e, slot=slot, o=phS: e.matmul(
                o, pb(p, PB_KTS, slot), pb(p, ufin, slot), start=False, stop=True),
                [rB, rY], [rph[0]])
            kb.op("scalar", lambda e, vh=vh, slot=slot, o=phS: e.copy(
                t["states"][:, vh * 128:(vh + 1) * 128], o),
                reads=[rph[0]], writes=[rst[vh]])
    for vh in range(8):
        pO = ps[f"o{2 + vh // 4}"][0:128, (vh % 4) * 128:(vh % 4) * 128 + 128]
        kb.op("scalar", lambda e, vh=vh, o=pO: e.activation(
            pb(p, PB_SQO, vh % 2), o, AF.Square),
            reads=[rpO[vh // 4]], writes=[rB])
        pgn = ps["g"][0:8, 0:128]
        mmT(lambda e, vh=vh: e.matmul(
            pgn, t["colsel1"][:, vh * 8:(vh + 1) * 8], pb(p, PB_SQO, vh % 2),
            start=(vh == 0), stop=(vh == 7)), [rB, rc], [rpg])
    kb.op("scalar", lambda e, bc=ccol(p, kb, EPS6): e.activation(
        t["rowsF"][0:8, 5 * 256 + c * 128: 5 * 256 + (c + 1) * 128], ps["g"][0:8, 0:128],
        AF.Sqrt, bias=bc[0:8, :], scale=1.0 / 128), reads=[rpg, rc], writes=[rrF])
    kb.op("vector", lp(p, lambda e: e.reciprocal(
        t["rowsB"][0:8, 7 * 256 + c * 128: 7 * 256 + (c + 1) * 128],
        t["rowsF"][0:8, 5 * 256 + c * 128: 5 * 256 + (c + 1) * 128])),
        reads=[rrF], writes=[rrB])
    for vh in range(8):
        pT7 = ps["h1"][0:128, 256:384]
        mmT(lambda e, vh=vh, o=pT7: e.matmul(
            o, t["rowsel8"][0:8, vh * 128:(vh + 1) * 128],
            t["rowsB"][0:8, 7 * 256 + c * 128: 7 * 256 + (c + 1) * 128],
            start=True, stop=True), [rrB, rc], [rphs])
        kb.op("scalar", lambda e, o=pT7: e.copy(pb(p, PB_AQBC, 0), o), reads=[rphs], writes=[rB])
        pO = ps[f"o{2 + vh // 4}"][0:128, (vh % 4) * 128:(vh % 4) * 128 + 128]
        kb.op("vector", lambda e, vh=vh, o=pO: e.tensor_scalar(
            pf(p, PF_SCR, 0), o, t["gnwc"][:, 0:1], None, ALU.mult),
            reads=[rpO[vh // 4], rc], writes=[rF])
        kb.op("vector", lambda e: e.tensor_mul(pf(p, PF_SCR, 0), pf(p, PF_SCR, 0),
                                               pb(p, PB_AQBC, 0)),
              reads=[rF, rB], writes=[rF])
        core_dst = t["xnfm"][:, vh * TH + loc0 + c * 128: vh * TH + loc0 + c * 128 + 128]
        kb.op("vector", lambda e, vh=vh, a=core_dst: e.tensor_tensor(
            a, pf(p, PF_SCR, 0), t["zbuf"][:, vh * 256 + c * 128: vh * 256 + (c + 1) * 128],
            ALU.mult),
            reads=[rF, rz[vh]], writes=[rfm[(loc0 // 128 + c) * 8 + vh]])


PW_A = 0
PW_KTS = 1024
PW_Y = 2048
PW_N = 3072
PW_M = 4096
PW_AT = 5120


def pw(p, off, w=1024):
    return p.t["poolB"][:, off:off + w]


def v8(ap, x=128):
    return ap.rearrange("p (v x) -> p v x", x=x)


def bvn(ap, n=8):
    return ap.unsqueeze(1).to_broadcast([128, n, 128])


def colv(p, c, q, h0=0, n=8):
    b = (c * 8 + q) * 8 + h0
    return p.t["colsT"][:, b:b + n].unsqueeze(2).to_broadcast([128, n, 128])


def colv4(p, c, q):
    b = (c * 8 + q) * 8
    return (p.t["colsT"][:, b:b + 8].rearrange("p (k s) -> p k s", s=2)
            .unsqueeze(3).to_broadcast([128, 4, 2, 128]))


def emit_delta_chunk2(p, kb, b, blk, c):
    t, cfg, ps = p.t, p.cfg, p.ps
    TH = cfg.T // 2
    NTH = cfg.NT // 2
    loc0 = blk * 256
    Q0 = 16 * 259
    rc = R(p, "consts")
    rqc = R(p, "qkvc_t", 16)
    rz = R(p, "z_t", 8)
    rrF = R(p, "rowsF")
    rrB = R(p, "rowsB")
    rct = R(p, "colsT")
    rst = R(p, "states", 8)
    rfm = R(p, "xnfm_t", NTH * 8)
    rpm = R(p, "ps_misc")
    rphs = R(p, "ps_h1")
    rpTs = R(p, "ps_T")
    rpg = R(p, "ps_g")
    rpo = R(p, "ps_o", 4)
    rsq = R(p, "sq_pool")
    rBc = R(p, "poolB_delta")
    rA = R(p, "pw_a")
    rK = R(p, "pw_kts")
    rY = R(p, "pw_y")
    rN = R(p, "pw_n")
    rM = R(p, "pw_m")
    rT = R(p, "pw_at")
    rPF = R(p, "poolF_delta")
    sl = slice(c * 128, (c + 1) * 128)
    pB, pF = t["poolB"], t["poolF"]
    states = t["states"]

    def mm(emit, reads, writes):
        kb.op("tensor", emit, reads=reads, writes=writes)

    def qch(h):
        a = t["accbuf"][:, Q0 + 512 * h: Q0 + 512 * h + 512]
        return (a.rearrange("p (k x) -> p k x", x=256)[:, :, sl]
                .unsqueeze(2).to_broadcast([128, 2, 2, 128]))

    ob = {0: ps["o0"], 1: ps["o1"], 2: ps["o2"], 3: ps["o3"]}
    hm = {0: ps["h1"], 1: ps["misc"]}
    rob = {0: rpo[0], 1: rpo[1], 2: rpo[2], 3: rpo[3]}
    rhm = {0: rphs, 1: rpm}

    for kh in range(4):
        mm(lambda e, kh=kh: e.matmul(
            ps["o0"][0:128, kh * 128:(kh + 1) * 128],
            t["rowsel8"][0:8, (2 * kh) * 128:(2 * kh) * 128 + 128],
            t["rowsB"][0:8, 1 * 256 + c * 128: 1 * 256 + (c + 1) * 128],
            start=(kh == 0), stop=(kh == 3)), [rrB, rc], [rpo[0]])
    qv = (t["accbuf"][:, Q0:Q0 + 1024].rearrange("p (k x) -> p k x", x=256)[:, :, sl])
    kb.op("vector", lambda e, qv=qv: e.tensor_tensor(
        v8(pB[:, PW_A:PW_A + 512]), qv, v8(ps["o0"][0:128, 0:512]), ALU.mult),
        reads=[rqc[0], rqc[1], rqc[2], rqc[3], rpo[0]], writes=[rA, rsq])

    for kh in range(4):
        mm(lambda e, kh=kh: e.transpose(
            ps["T"][0:128, kh * 128:(kh + 1) * 128], qkvc_tile(p, 4 + kh)[:, sl],
            t["ident_bf"][:]), [rqc[4 + kh], rc], [rpTs])
    kb.op("scalar", lambda e: e.copy(pB[:, PW_A + 512:PW_A + 1024], ps["T"][0:128, 0:512]),
          reads=[rpTs], writes=[rA, rsq])
    ktokd = (pB[:, PW_A + 512:PW_A + 1024].rearrange("p (k x) -> p k x", x=128)
             .unsqueeze(2).to_broadcast([128, 4, 2, 128]))
    kb.op("vector", lambda e, kd=ktokd: e.tensor_tensor(
        pw(p, PW_KTS).rearrange("p (k s x) -> p k s x", s=2, x=128), kd,
        colv4(p, c, 6), ALU.mult), reads=[rA, rct], writes=[rK, rsq])

    for vh in range(8):
        mm(lambda e, vh=vh: e.transpose(
            ps["T"][0:128, vh * 128:(vh + 1) * 128], qkvc_tile(p, 8 + vh)[:, sl],
            t["ident_bf"][:]), [rqc[8 + vh], rc], [rpTs])
    kb.op("vector", lambda e: e.tensor_copy(pB[:, PW_Y:PW_Y + 512], ps["T"][0:128, 0:512]),
          reads=[rpTs], writes=[rY, rBc])
    kb.op("scalar", lambda e: e.copy(pB[:, PW_Y + 512:PW_Y + 1024], ps["T"][0:128, 512:1024]),
          reads=[rpTs], writes=[rY, rBc])

    for vh in range(8):
        bank = ob[vh // 4]
        mm(lambda e, vh=vh, bank=bank: e.matmul(
            bank[0:128, (vh % 4) * 128:(vh % 4) * 128 + 128],
            t["rowsel8f"][0:8, vh * 128:(vh + 1) * 128],
            t["rowsF"][0:8, 1 * 256 + c * 128: 1 * 256 + (c + 1) * 128],
            start=(vh % 4 == 0), stop=(vh % 4 == 3)), [rrF, rc], [rob[vh // 4]])
    for h in range(2):
        kb.op("vector", lambda e, h=h: e.tensor_tensor(
            v8(pF[:, h * 512:(h + 1) * 512]), v8(ob[h][0:128, 0:512]),
            colv(p, c, 0, h0=4 * h, n=4), ALU.subtract),
            reads=[rob[h], rct], writes=[rPF])

    for kh in range(4):
        k_t = qkvc_tile(p, 4 + kh)[:, sl]
        for slot in range(2):
            vh = 2 * kh + slot
            r4 = vh % 4
            mm(lambda e, k_t=k_t, vh=vh, r4=r4: e.matmul(
                hm[vh // 4][0:128, r4 * 128:r4 * 128 + 128], k_t,
                pB[:, PW_A + (vh // 2) * 128: PW_A + (vh // 2) * 128 + 128],
                start=(r4 == 0), stop=(r4 == 3)),
                [rqc[4 + kh], rA], [rhm[vh // 4]])
            mm(lambda e, k_t=k_t, vh=vh, r4=r4: e.matmul(
                ob[vh // 4][0:128, r4 * 128:r4 * 128 + 128], k_t, k_t,
                start=(r4 == 0), stop=(r4 == 3)),
                [rqc[4 + kh]], [rob[vh // 4]])
            mm(lambda e, k_t=k_t, vh=vh, r4=r4: e.matmul(
                ob[2 + vh // 4][0:128, r4 * 128:r4 * 128 + 128], k_t,
                states[:, vh * 128:(vh + 1) * 128],
                start=(r4 == 0), stop=(r4 == 3)),
                [rqc[4 + kh], rst[vh]], [rob[2 + vh // 4]])

    kb.op("vector", lambda e: e.tensor_tensor(
        v8(pw(p, PW_Y)), v8(pw(p, PW_Y)), colv(p, c, 4), ALU.mult),
        reads=[rY, rct], writes=[rY, rBc])
    for h in range(2):
        kb.op("vector", lambda e, h=h: e.tensor_tensor(
            v8(pF[:, 1024 + h * 512: 1024 + (h + 1) * 512]), v8(ob[2 + h][0:128, 0:512]),
            colv(p, c, 5, h0=4 * h, n=4), ALU.mult),
            reads=[rob[2 + h], rct], writes=[rPF])
    kb.op("vector", lambda e: e.tensor_tensor(
        pw(p, PW_Y), pw(p, PW_Y), pF[:, 1024:2048], ALU.subtract),
        reads=[rY, rPF], writes=[rY, rBc])

    kb.op("vector", lambda e: e.tensor_scalar_min(pF[:, 1024:2048], pF[:, 0:1024], 0.0),
          reads=[rPF], writes=[rPF])
    kb.op("scalar", lambda e: e.activation(pw(p, PW_A), pF[:, 1024:2048], AF.Exp),
          reads=[rPF], writes=[rA, rsq])

    for vh in range(8):
        bank = ob[2 + vh // 4]
        mm(lambda e, vh=vh, bank=bank: e.matmul(
            bank[0:128, (vh % 4) * 128:(vh % 4) * 128 + 128],
            t["rowsel8"][0:8, vh * 128:(vh + 1) * 128],
            t["rowsB"][0:8, 3 * 256 + c * 128: 3 * 256 + (c + 1) * 128],
            start=(vh % 4 == 0), stop=(vh % 4 == 3)), [rrB, rc], [rob[2 + vh // 4]])
    for h in range(2):
        kb.op("vector", lambda e, h=h: e.tensor_tensor(
            v8(pB[:, PW_N + h * 512: PW_N + (h + 1) * 512]), v8(ob[h][0:128, 0:512]),
            colv(p, c, 2, h0=4 * h, n=4), ALU.mult),
            reads=[rob[h], rct], writes=[rN])
        kb.op("vector", lambda e, h=h: e.tensor_mul(
            v8(pB[:, PW_N + h * 512: PW_N + (h + 1) * 512]),
            v8(pB[:, PW_N + h * 512: PW_N + (h + 1) * 512]),
            v8(ob[2 + h][0:128, 0:512])),
            reads=[rob[2 + h], rN], writes=[rN])
    kb.op("vector", lambda e: e.tensor_tensor(
        pw(p, PW_N), pw(p, PW_N), pw(p, PW_A), ALU.mult), reads=[rN, rA], writes=[rN])
    kb.op("gpsimd", lambda e: e.tensor_mul(
        v8(pw(p, PW_N)), v8(pw(p, PW_N)), bvn(t["striu"][:])),
        reads=[rN, rc], writes=[rN])

    for h in range(2):
        kb.op("vector", lambda e, h=h: e.tensor_tensor(
            v8(pB[:, PW_AT + h * 512: PW_AT + (h + 1) * 512]), v8(hm[h][0:128, 0:512]),
            colv(p, c, 2, h0=4 * h, n=4), ALU.mult),
            reads=[rhm[h], rct], writes=[rT])
    kb.op("vector", lambda e: e.tensor_tensor(
        pw(p, PW_AT), pw(p, PW_AT), pw(p, PW_A), ALU.mult), reads=[rT, rA], writes=[rT])
    kb.op("gpsimd", lambda e: e.tensor_mul(
        v8(pw(p, PW_AT)), v8(pw(p, PW_AT)), bvn(t["triu"][:])),
        reads=[rT, rc], writes=[rT])

    kb.op("vector", lambda e: e.tensor_scalar_max(pF[:, 1024:2048], pF[:, 0:1024], 0.0),
          reads=[rPF], writes=[rPF])
    kb.op("scalar", lambda e: e.activation(pw(p, PW_A), pF[:, 1024:2048], AF.Exp, scale=-1.0),
          reads=[rPF], writes=[rA, rsq])
    for vh in range(8):
        bank = hm[vh // 4]
        mm(lambda e, vh=vh, bank=bank: e.matmul(
            bank[0:128, (vh % 4) * 128:(vh % 4) * 128 + 128],
            t["rowsel8"][0:8, vh * 128:(vh + 1) * 128],
            t["rowsB"][0:8, 2 * 256 + c * 128: 2 * 256 + (c + 1) * 128],
            start=(vh % 4 == 0), stop=(vh % 4 == 3)), [rrB, rc], [rhm[vh // 4]])
    for h in range(2):
        kb.op("vector", lambda e, h=h: e.tensor_tensor(
            v8(pB[:, PW_M + h * 512: PW_M + (h + 1) * 512]), v8(ob[h][0:128, 0:512]),
            colv(p, c, 3, h0=4 * h, n=4), ALU.mult),
            reads=[rob[h], rct], writes=[rM])
        kb.op("vector", lambda e, h=h: e.tensor_mul(
            v8(pB[:, PW_M + h * 512: PW_M + (h + 1) * 512]),
            v8(pB[:, PW_M + h * 512: PW_M + (h + 1) * 512]),
            v8(hm[h][0:128, 0:512])),
            reads=[rhm[h], rM], writes=[rM])
    kb.op("vector", lambda e: e.tensor_tensor(
        pw(p, PW_M), pw(p, PW_M), pw(p, PW_A), ALU.mult), reads=[rM, rA], writes=[rM])
    kb.op("gpsimd", lambda e: e.tensor_mul(
        v8(pw(p, PW_M)), v8(pw(p, PW_M)), bvn(t["stril"][:])),
        reads=[rM, rc], writes=[rM])

    for vh in range(8):
        mm(lambda e, vh=vh: e.matmul(
            hm[vh // 4][0:128, (vh % 4) * 128:(vh % 4) * 128 + 128],
            pB[:, PW_N + vh * 128: PW_N + vh * 128 + 128],
            pB[:, PW_Y + vh * 128: PW_Y + vh * 128 + 128],
            start=(vh % 4 == 0), stop=(vh % 4 == 3)), [rN, rY], [rhm[vh // 4]])
    for h in range(2):
        op = (lambda e, h=h: e.tensor_tensor(
            pB[:, PW_Y + h * 512: PW_Y + (h + 1) * 512],
            pB[:, PW_Y + h * 512: PW_Y + (h + 1) * 512],
            hm[h][0:128, 0:512], ALU.subtract))
        kb.op("vector", op, reads=[rY, rhm[h]], writes=[rY, rBc])

    KMAX = 4
    for k in range(1, KMAX + 1):
        for vh in range(8):
            mm(lambda e, vh=vh: e.matmul(
                ob[vh // 4][0:128, (vh % 4) * 128:(vh % 4) * 128 + 128],
                pB[:, PW_M + vh * 128: PW_M + vh * 128 + 128],
                pB[:, PW_N + vh * 128: PW_N + vh * 128 + 128],
                start=(vh % 4 == 0), stop=(vh % 4 == 3)), [rM, rN], [rob[vh // 4]])
            if k < KMAX:
                mm(lambda e, vh=vh: e.matmul(
                    ob[2 + vh // 4][0:128, (vh % 4) * 128:(vh % 4) * 128 + 128],
                    pB[:, PW_N + vh * 128: PW_N + vh * 128 + 128],
                    pB[:, PW_M + vh * 128: PW_M + vh * 128 + 128],
                    start=(vh % 4 == 0), stop=(vh % 4 == 3)), [rM, rN], [rob[2 + vh // 4]])
        kb.op("vector", lambda e: e.tensor_copy(pB[:, PW_N:PW_N + 512], ps["o0"][0:128, 0:512]),
              reads=[rpo[0]], writes=[rN])
        kb.op("scalar", lambda e: e.copy(pB[:, PW_N + 512:PW_N + 1024], ps["o1"][0:128, 0:512]),
              reads=[rpo[1]], writes=[rN])
        if k < KMAX:
            kb.op("scalar", lambda e: e.copy(pB[:, PW_M:PW_M + 512], ps["o2"][0:128, 0:512]),
                  reads=[rpo[2]], writes=[rM])
            kb.op("vector", lambda e: e.tensor_copy(pB[:, PW_M + 512:PW_M + 1024],
                                                    ps["o3"][0:128, 0:512]),
                  reads=[rpo[3]], writes=[rM])
        for vh in range(8):
            mm(lambda e, vh=vh: e.matmul(
                hm[vh // 4][0:128, (vh % 4) * 128:(vh % 4) * 128 + 128],
                pB[:, PW_N + vh * 128: PW_N + vh * 128 + 128],
                pB[:, PW_Y + vh * 128: PW_Y + vh * 128 + 128],
                start=(vh % 4 == 0), stop=(vh % 4 == 3)), [rN, rY], [rhm[vh // 4]])
        for h in range(2):
            op = (lambda e, h=h: e.tensor_tensor(
                pB[:, PW_Y + h * 512: PW_Y + (h + 1) * 512],
                pB[:, PW_Y + h * 512: PW_Y + (h + 1) * 512],
                hm[h][0:128, 0:512], ALU.add))
            kb.op("vector", op, reads=[rY, rhm[h]], writes=[rY, rBc])

    for vh in range(8):
        bank = ob[vh // 4]
        mm(lambda e, vh=vh, bank=bank: e.matmul(
            bank[0:128, (vh % 4) * 128:(vh % 4) * 128 + 128],
            t["rowsel8"][0:8, vh * 128:(vh + 1) * 128],
            t["rowsB"][0:8, 6 * 256 + c * 128: 6 * 256 + (c + 1) * 128],
            start=(vh % 4 == 0), stop=(vh % 4 == 3)), [rrB, rc], [rob[vh // 4]])
    for h in range(2):
        kb.op("vector", lambda e, h=h: e.tensor_tensor(
            pB[:, PW_A + h * 512: PW_A + (h + 1) * 512].rearrange(
                "p (k s x) -> p k s x", s=2, x=128),
            qch(h),
            ob[h][0:128, 0:512].rearrange("p (k s x) -> p k s x", s=2, x=128),
            ALU.mult),
            reads=[rqc[2 * h], rqc[2 * h + 1], rob[h]], writes=[rA, rsq])

    for vh in range(8):
        bank = ob[2 + vh // 4]
        mm(lambda e, vh=vh, bank=bank: e.matmul(
            bank[0:128, (vh % 4) * 128:(vh % 4) * 128 + 128],
            states[:, vh * 128:(vh + 1) * 128],
            pB[:, PW_A + vh * 128: PW_A + vh * 128 + 128],
            start=(vh % 4 == 0), stop=False), [rst[vh], rA], [rob[2 + vh // 4]])
        mm(lambda e, vh=vh, bank=bank: e.matmul(
            bank[0:128, (vh % 4) * 128:(vh % 4) * 128 + 128],
            pB[:, PW_Y + vh * 128: PW_Y + vh * 128 + 128],
            pB[:, PW_AT + vh * 128: PW_AT + vh * 128 + 128],
            start=False, stop=(vh % 4 == 3)), [rY, rT], [rob[2 + vh // 4]])

    kb.op("vector", lambda e: e.tensor_tensor(
        v8(pw(p, PW_M)), v8(states[:, 0:1024]), colv(p, c, 1), ALU.mult),
        reads=[rct] + rst, writes=[rM])
    for vh in range(8):
        mm(lambda e, vh=vh: e.matmul(
            hm[vh // 4][0:128, (vh % 4) * 128:(vh % 4) * 128 + 128],
            pB[:, PW_KTS + vh * 128: PW_KTS + vh * 128 + 128],
            pB[:, PW_Y + vh * 128: PW_Y + vh * 128 + 128],
            start=(vh % 4 == 0), stop=(vh % 4 == 3)), [rK, rY], [rhm[vh // 4]])
    for h in range(2):
        kb.op("vector", lambda e, h=h: e.tensor_tensor(
            states[:, h * 512:(h + 1) * 512], pB[:, PW_M + h * 512: PW_M + (h + 1) * 512],
            hm[h][0:128, 0:512], ALU.add),
            reads=[rM, rhm[h]], writes=rst[4 * h:4 * h + 4])

    kb.op("scalar", lambda e: e.activation(pB[:, PW_AT:PW_AT + 512], ps["o2"][0:128, 0:512],
                                           AF.Square), reads=[rpo[2]], writes=[rT])
    kb.op("scalar", lambda e: e.activation(pB[:, PW_AT + 512:PW_AT + 1024],
                                           ps["o3"][0:128, 0:512],
                                           AF.Square), reads=[rpo[3]], writes=[rT])
    for vh in range(8):
        mm(lambda e, vh=vh: e.matmul(
            ps["g"][0:8, 0:128], t["colsel1"][:, vh * 8:(vh + 1) * 8],
            pB[:, PW_AT + vh * 128: PW_AT + vh * 128 + 128],
            start=(vh == 0), stop=(vh == 7)), [rT, rc], [rpg])
    kb.op("scalar", lambda e, bc=ccol(p, kb, EPS6): e.activation(
        t["rowsF"][0:8, 5 * 256 + c * 128: 5 * 256 + (c + 1) * 128], ps["g"][0:8, 0:128],
        AF.Sqrt, bias=bc[0:8, :], scale=1.0 / 128), reads=[rpg, rc], writes=[rrF])
    kb.op("vector", lp(p, lambda e: e.reciprocal(
        t["rowsB"][0:8, 7 * 256 + c * 128: 7 * 256 + (c + 1) * 128],
        t["rowsF"][0:8, 5 * 256 + c * 128: 5 * 256 + (c + 1) * 128])),
        reads=[rrF], writes=[rrB])
    for vh in range(8):
        bank = ob[vh // 4]
        mm(lambda e, vh=vh, bank=bank: e.matmul(
            bank[0:128, (vh % 4) * 128:(vh % 4) * 128 + 128],
            t["rowsel8"][0:8, vh * 128:(vh + 1) * 128],
            t["rowsB"][0:8, 7 * 256 + c * 128: 7 * 256 + (c + 1) * 128],
            start=(vh % 4 == 0), stop=(vh % 4 == 3)), [rrB, rc], [rob[vh // 4]])
    for h in range(2):
        kb.op("scalar", lambda e, h=h: e.copy(
            pB[:, PW_N + h * 512: PW_N + (h + 1) * 512], ob[h][0:128, 0:512]),
            reads=[rob[h]], writes=[rN])
        kb.op("vector", lambda e, h=h: e.scalar_tensor_tensor(
            pF[:, h * 512:(h + 1) * 512], ob[2 + h][0:128, 0:512], t["gnwc"][:, 0:1],
            pB[:, PW_N + h * 512: PW_N + (h + 1) * 512], ALU.mult, ALU.mult),
            reads=[rob[2 + h], rN, rc], writes=[rPF])
    zv = v8(t["zbuf"][:, 0:2048], x=256)[:, :, sl]
    xout = v8(t["xnfm"][:, 0:8 * TH], x=TH)[:, :, loc0 + c * 128: loc0 + c * 128 + 128]
    kb.op("vector", lambda e, zv=zv, xout=xout: e.tensor_tensor(
        xout, v8(pF[:, 0:1024]), zv, ALU.mult),
        reads=[rPF] + [rz[vh] for vh in range(8)],
        writes=[rfm[(loc0 // 128 + c) * 8 + vh] for vh in range(8)])


def emit_attn_block_epilogue(p, kb, b, blk):
    t, cfg, ps, d = p.t, p.cfg, p.ps, p.d
    TH = cfg.T // 2
    NTH = cfg.NT // 2
    loc0 = blk * 256
    rc = R(p, "consts")
    rw = R(p, "attn_w")
    rfm = R(p, "xnfm_t", NTH * 8)
    rphs = R(p, "ps_h1")
    rpm2 = R(p, "ps_misc")
    rpTs = R(p, "ps_T")
    rpT = [rpTs] * 4
    rofm = R(p, "ofm_r", 2)
    routt = R(p, "outt_r")
    rlnr = R(p, "xnb2")
    for sub in range(2):
        tok = b * cfg.S + loc0 + sub * 128
        kb.dma("dgp", "lnr", lambda e, tok=tok: e.dma_start(
            out=t["xnb2"][:], in_=d["lnbuf"][tok:tok + 128, :]),
            reads=[R(p, "ln_dram")], writes=[rlnr])
        for dt in range(8):
            if dt % 2 == 0:
                ph = ps["h1"][0:128, 0:128]
            else:
                ph = ps["misc"][0:128, 0:128]
            for vt in range(8):
                kb.op("tensor", lambda e, dt=dt, vt=vt, o=ph, loc0=loc0, sub=sub: e.matmul(
                    o, wout_tile(p, vt, dt),
                    t["xnfm"][:, vt * TH + loc0 + sub * 128: vt * TH + loc0 + sub * 128 + 128],
                    start=(vt == 0), stop=(vt == 7)),
                    reads=[rw, rfm[(loc0 // 128 + sub) * 8 + vt]],
                    writes=[rphs if dt % 2 == 0 else rpm2])
            ofm = t["ofm"][:, (dt % 2) * 256:(dt % 2) * 256 + 128]
            kb.op("vector", lambda e, o=ph, a=ofm: e.tensor_copy(a, o),
                  reads=[rphs if dt % 2 == 0 else rpm2], writes=[rofm[dt % 2]])
            pT = ps["T"][0:128, (dt % 4) * 128:(dt % 4) * 128 + 128]
            kb.op("tensor", lambda e, a=ofm, o=pT: e.transpose(o, a, t["ident_bf"][:]),
                  reads=[rofm[dt % 2], rc], writes=[rpT[dt % 4]])
            kb.op("vector", lambda e, dt=dt, o=pT: e.tensor_add(
                t["outt"][:, dt * 128:(dt + 1) * 128], o, t["xnb2"][:, dt * 128:(dt + 1) * 128]),
                reads=[rpT[dt % 4], rlnr], writes=[routt])
        kb.dma("dsync", "out", lambda e, tok=tok: e.dma_start(
            out=d["hbuf"][tok:tok + 128, :], in_=t["outt"][:, 0:1024]),
            reads=[routt], writes=[R(p, "h_dram")])


def emit_attn_core(p, kb, src):
    cfg = p.cfg
    t = p.t
    for b in range(cfg.B):
        emit_ln_half(p, kb, src, half=b, mode="lnzrms", store_dram=p.d["lnbuf"],
                     zw_col=t["wln1c"])
        rst = R(p, "states", 8)
        for vh in range(8):
            kb.op("vector", lambda e, vh=vh: e.memset(t["states"][:, vh * 128:(vh + 1) * 128], 0.0),
                  writes=[rst[vh]])
        for blk in range(cfg.S // 256):
            emit_attn_block_prologue(p, kb, b, blk)
            for c in range(2):
                emit_delta_chunk2(p, kb, b, blk, c)
            emit_attn_block_epilogue(p, kb, b, blk)


def emit_attn_moe(p, kb, r1src, dst, moe_trunc=None):
    t, cfg, ps, d = p.t, p.cfg, p.ps, p.d
    TH = cfg.T // 2
    NTH = cfg.NT // 2
    NT = cfg.NT
    T = cfg.T
    rc = R(p, "consts")
    rmv = R(p, "mv")
    rroute = R(p, "route")
    rrsum = R(p, "cumw")
    rrfm = R(p, "rowfm")
    reqb = R(p, "eqbf")
    rpTs = R(p, "ps_T")
    rpo = R(p, "ps_o", 4)
    rpm = R(p, "ps_misc")
    rphs = R(p, "ps_h1")
    rprods = R(p, "prods", 16)
    rrbc = R(p, "selbc")
    rofm = R(p, "ofm_r", 2)
    routt = R(p, "outt_r")
    rxres = R(p, "xres_r")
    rhres = R(p, "hres_r")
    rx2 = R(p, "x2_ring", 2)
    rmw = R(p, "moe_w")
    rmp = R(p, "moepart_dram")
    for hf in range(2):
        emit_ln_half(p, kb, d["hbuf"], hf, mode="rms", do_gate=("router",),
                     zw_col=t["wln2c"], store_fm_dram=True)
        t0 = hf * NTH
        kb.op("vector", lambda e, t0=t0: e.reduce_sum(
            t["cumw"][:, t0:t0 + NTH],
            t["mv"][:, t0 * 8:(t0 + NTH) * 8].rearrange("p (j e) -> p j e", e=8),
            axis=AX.X),
            reads=[rmv], writes=[rrsum])
        kb.op("vector", lambda e, t0=t0: e.reciprocal(t["cumw"][:, t0:t0 + NTH],
                                                      t["cumw"][:, t0:t0 + NTH]),
              reads=[rrsum], writes=[rrsum])
        kb.op("vector", lambda e, t0=t0: e.tensor_tensor(
            t["route"][:, t0 * 8:(t0 + NTH) * 8].rearrange("p (j e) -> p j e", e=8),
            t["mv"][:, t0 * 8:(t0 + NTH) * 8].rearrange("p (j e) -> p j e", e=8),
            t["cumw"][:, t0:t0 + NTH].unsqueeze(2).to_broadcast([128, NTH, 8]),
            ALU.mult),
            reads=[rmv, rrsum], writes=[rroute])
    if moe_trunc == "ln":
        return
    for ti in range(NT):
        kb.op("vector", lambda e, ti=ti: e.tensor_copy(t["eqbf"][:, ti * 8:(ti + 1) * 8],
                                                       t["route"][:, ti * 8:(ti + 1) * 8]),
              reads=[rroute], writes=[reqb])
        kb.op("tensor", lambda e, ti=ti: e.transpose(ps["T"][0:8, 512:640],
                                                     t["eqbf"][:, ti * 8:(ti + 1) * 8],
                                                     t["ident_bf"][:]),
              reads=[reqb, rc], writes=[rpTs])
        kb.op("scalar", lambda e, ti=ti: e.copy(t["rowfm"][0:8, ti * 128:(ti + 1) * 128],
                                                ps["T"][0:8, 512:640]),
              reads=[rpTs], writes=[rrfm])
    kb.barrier()
    if moe_trunc == "route":
        return
    for ih in range(1 if moe_trunc in ("w", "pa", "pb", "ih0", "st0", "st1", "st2") else 2):
        for e_i in range(NEXP):
            kb.dma("dgp", "mw", lambda e, ei=e_i, ih=ih: e.dma_start(
                out=t["wbuf"][:, ei * 2048:(ei + 1) * 2048].rearrange("p (dt c) -> p dt c", c=256),
                in_=d["aW_gate_e"][ei][:, ih * 256:(ih + 1) * 256].rearrange(
                    "(dt p) c -> p dt c", p=128)),
                writes=[rmw])
            kb.dma("dgp", "mw", lambda e, ei=e_i, ih=ih: e.dma_start(
                out=t["wbuf"][:, 16384 + ei * 2048: 16384 + (ei + 1) * 2048].rearrange(
                    "p (dt c) -> p dt c", c=256),
                in_=d["aW_up_e"][ei][:, ih * 256:(ih + 1) * 256].rearrange(
                    "(dt p) c -> p dt c", p=128)),
                writes=[rmw])
            kb.dma("dgp", "mw", lambda e, ei=e_i, ih=ih: e.dma_start(
                out=t["xnfm"][:, ei * 2048:(ei + 1) * 2048].rearrange("p (it c) -> p it c", c=1024),
                in_=d["aW_down_e"][ei][ih * 256:(ih + 1) * 256, :].rearrange(
                    "(it p) c -> p it c", p=128)),
                writes=[rmw])
        if moe_trunc == "w":
            return
        for tt in range(T // 256):
            tok0 = tt * 256
            x2t = t["xnb"][:, (tt % 2) * 1024: (tt % 2) * 1024 + 1024]
            x2t = t["zbuf"][:, 0:2048] if tt % 2 == 0 else None
            if x2t is None:
                x2t = t["accbuf"][:, 4096:6144]
            kb.dma("dsync", ("x2", tt % 2), lambda e, a=x2t, tok0=tok0: e.dma_start(
                out=a.rearrange("p (dt c) -> p dt c", c=256),
                in_=d["x2fm"][:, :, tok0:tok0 + 256].transpose([1, 0, 2])),
                reads=[R(p, "x2_dram")], writes=[rx2[tt % 2]])
            for e_i in range(NEXP):
                kb.op("tensor", lambda e, ei=e_i, tok0=tok0: e.matmul(
                    ps["o0"][0:128, 0:256], t["rowsel8"][0:8, ei * 128:(ei + 1) * 128],
                    t["rowfm"][0:8, tok0:tok0 + 256], start=True, stop=True),
                    reads=[rrfm, rc], writes=[rpo[0]])
                kb.op("vector", lambda e: e.tensor_copy(t["selbc"][:], ps["o0"][0:128, 0:256]),
                      reads=[rpo[0]], writes=[rrbc])
                for it in range(2):
                    pg_ = ps["h1"][0:128, it * 256:(it + 1) * 256]
                    pu_ = ps["misc"][0:128, it * 256:(it + 1) * 256]
                    for dt in range(8):
                        kb.op("tensor", lambda e, ei=e_i, it=it, dt=dt, o=pg_, x2t=x2t: e.matmul(
                            o, t["wbuf"][:, ei * 2048 + dt * 256 + it * 128:
                                         ei * 2048 + dt * 256 + it * 128 + 128],
                            x2t[:, dt * 256: dt * 256 + 256],
                            start=(dt == 0 and it == 0), stop=False if not (dt == 7 and it == 1) else True),
                            reads=[rmw, rx2[tt % 2]], writes=[rphs])
                        kb.op("tensor", lambda e, ei=e_i, it=it, dt=dt, o=pu_, x2t=x2t: e.matmul(
                            o, t["wbuf"][:, 16384 + ei * 2048 + dt * 256 + it * 128:
                                         16384 + ei * 2048 + dt * 256 + it * 128 + 128],
                            x2t[:, dt * 256: dt * 256 + 256],
                            start=(dt == 0 and it == 0), stop=False if not (dt == 7 and it == 1) else True),
                            reads=[rmw, rx2[tt % 2]], writes=[rpm])
                for it in range(2):
                    pg_ = ps["h1"][0:128, it * 256:(it + 1) * 256]
                    pu_ = ps["misc"][0:128, it * 256:(it + 1) * 256]
                    pr = prods_tile(p, e_i, it)
                    kb.op("scalar", lambda e, o=pg_, a=pr: e.activation(a, o, AF.Silu),
                          reads=[rphs], writes=[rprods[e_i * 2 + it]])
                    kb.op("vector", lambda e, o=pu_, a=pr: e.tensor_mul(a, a, o),
                          reads=[rpm, rprods[e_i * 2 + it]], writes=[rprods[e_i * 2 + it]])
                    kb.op("gpsimd", lambda e, a=pr: e.tensor_mul(a, a, t["selbc"][:]),
                          reads=[rrbc, rprods[e_i * 2 + it]], writes=[rprods[e_i * 2 + it]])
            if moe_trunc == "pa":
                continue
            for dt in range(8):
                po = ps[f"o{dt // 2}"][0:128, (dt % 2) * 256:(dt % 2) * 256 + 256]
                for e_i in range(NEXP):
                    for it in range(2):
                        kb.op("tensor", lambda e, ei=e_i, it=it, dt=dt, o=po: e.matmul(
                            o, t["xnfm"][:, ei * 2048 + it * 1024 + dt * 128:
                                         ei * 2048 + it * 1024 + dt * 128 + 128],
                            prods_tile(p, ei, it),
                            start=(ei == 0 and it == 0 and dt % 2 == 0),
                            stop=(ei == 7 and it == 1 and dt % 2 == 1)),
                            reads=[rmw, rprods[e_i * 2 + it]],
                            writes=[rpo[dt // 2]])
            if moe_trunc == "pb":
                continue
            if ih == 0:
                for dt in range(8):
                    po = ps[f"o{dt // 2}"][0:128, (dt % 2) * 256:(dt % 2) * 256 + 256]
                    ofm = t["ofm"][:, (dt % 2) * 256:(dt % 2) * 256 + 256]
                    eng = "vector" if dt % 2 == 0 else "scalar"
                    if moe_trunc == "st1":
                        eng = "vector"
                    elif moe_trunc == "st2":
                        eng = "scalar"
                    if eng == "vector":
                        kb.op(eng, lambda e, o=po, a=ofm: e.tensor_copy(a, o),
                              reads=[rpo[dt // 2]], writes=[rofm[dt % 2]])
                    else:
                        kb.op(eng, lambda e, o=po, a=ofm: e.copy(a, o),
                              reads=[rpo[dt // 2]], writes=[rofm[dt % 2]])
                    if moe_trunc == "st0":
                        continue
                    kb.dma("dsync", "mp", lambda e, dt=dt, tok0=tok0, a=ofm: e.dma_start(
                        out=d["moepart"][dt, :, tok0:tok0 + 256], in_=a),
                        reads=[rofm[dt % 2]], writes=[rmp])
            else:
                rmw2 = R(p, "moep_w")
                rofw = R(p, "ofm_wide")
                rxnb2s = R(p, "xnb_ring", 2)
                rpr = R(p, "prods", 16)
                for sub in range(2):
                    tok = tok0 + sub * 128
                    kb.dma("dsync", "xr", lambda e, tok=tok: e.dma_start(
                        out=t["xres"][:], in_=r1src[tok:tok + 128, :]),
                        writes=[rxres])
                    kb.dma("dgp", "hr", lambda e, tok=tok: e.dma_start(
                        out=t["sq_scr"][:, 0:1024], in_=d["hbuf"][tok:tok + 128, :]),
                        reads=[R(p, "h_dram")], writes=[rhres])
                    kb.dma("dgp", "mp2", lambda e, tok=tok: e.dma_start(
                        out=t["xnb"][:, 0:1024].rearrange("p (dt x) -> p dt x", x=128),
                        in_=d["moepart"][:, :, tok:tok + 128].transpose([1, 0, 2])),
                        reads=[rmp], writes=[rmw2, rxnb2s[0], rxnb2s[1]])
                    ofm_w = t["accbuf"][:, 0:1024]
                    for bk in range(4):
                        po = ps[f"o{bk}"][0:128, 0:512].rearrange(
                            "p (d x) -> p d x", x=256)[:, :, sub * 128:sub * 128 + 128]
                        kb.op("vector", lambda e, bk=bk, o=po: e.tensor_add(
                            ofm_w[:, bk * 256:(bk + 1) * 256].rearrange(
                                "p (d x) -> p d x", x=128),
                            o,
                            t["xnb"][:, bk * 256:(bk + 1) * 256].rearrange(
                                "p (d x) -> p d x", x=128)),
                            reads=[rpo[bk], rmw2],
                            writes=[rofw, rpr[bk]])
                    for dt in range(8):
                        kb.op("tensor", lambda e, dt=dt: e.transpose(
                            ps["T"][0:128, dt * 128:(dt + 1) * 128],
                            ofm_w[:, dt * 128:(dt + 1) * 128], t["ident_bf"][:]),
                            reads=[rofw, rc], writes=[rpTs])
                    kb.op("vector", lambda e: e.tensor_add(
                        t["outt"][:, 0:1024], ps["T"][0:128, 0:1024], t["sq_scr"][:, 0:1024]),
                        reads=[rpTs, rhres], writes=[routt])
                    kb.op("gpsimd", lambda e: e.tensor_add(
                        t["outt"][:, 0:1024], t["outt"][:, 0:1024], t["xres"][:, 0:1024]),
                        reads=[rxres, routt], writes=[routt])
                    kb.dma("dsync", "out", lambda e, tok=tok: e.dma_start(
                        out=dst[tok:tok + 128, :], in_=t["outt"][:, 0:1024]),
                        reads=[routt], writes=[R(p, "dst_dram")])


def prods_tile(p, e, it):
    c = (e * 2 + it) * 256
    return p.t["accbuf"][:, c:c + 256]


# version 13 (final)
# speedup vs baseline: 2.0513x; 1.0003x over previous
import numpy as np
from contextlib import ExitStack

import concourse.bass as bass
import concourse.mybir as mybir
from concourse.bass_utils import run_bass_kernel_spmd

F32 = mybir.dt.float32
BF16 = mybir.dt.bfloat16
I32 = mybir.dt.int32
AF = mybir.ActivationFunctionType
ALU = mybir.AluOpType
AX = mybir.AxisListType

HK, HV, DK, DV = 4, 8, 128, 128
KEY = HK * DK
VAL = HV * DV
NEXP = 8
EPS6 = 1e-6
EPS5 = 1e-5

COMPUTE_ENGINES = ("tensor", "vector", "scalar", "gpsimd", "sync")


class Res:
    __slots__ = ("name", "w", "readers", "excl")

    def __init__(self, name=""):
        self.name = name
        self.w = None
        self.readers = []
        self.excl = name.startswith("ps_")


class Tok:
    __slots__ = ("sem", "val", "pub")

    def __init__(self, sem, val, pub):
        self.sem = sem
        self.val = val
        self.pub = pub


class _Rec:
    __slots__ = ("emit", "waits", "inc", "q")

    def __init__(self, emit, waits, inc, q=None):
        self.emit = emit
        self.waits = waits
        self.inc = inc
        self.q = q


class KB:
    def __init__(self, nc, es, dma_incs=None):
        self.nc = nc
        self.es = es
        self.sems = {k: es.enter_context(nc.semaphore(k)) for k in COMPUTE_ENGINES}
        self.streams = {e: [] for e in COMPUTE_ENGINES}
        self.counts = {k: 0 for k in COMPUTE_ENGINES}
        self.clocks = {e: {} for e in COMPUTE_ENGINES}
        self.nops = 0
        self.dma_incs = dma_incs
        self.dma_seq = {"dsync": 0, "dgp": 0, "coll": 0}
        self.pl_order = []
        self.slot_sems = {}

    def _slot_sem(self, slot):
        if slot not in self.slot_sems:
            name = "ds%d" % len(self.slot_sems)
            self.sems[name] = self.es.enter_context(self.nc.semaphore(name))
            self.counts[name] = 0
            self.slot_sems[slot] = name
        return self.slot_sems[slot]

    def _issue(self, issue_eng, emit, reads, writes, sem_key, inc_amt, q=None):
        clk = self.clocks[issue_eng]
        need = {}
        toks = []
        for r in reads:
            if r.w is not None:
                toks.append(r.w)
            if r.excl:
                for rt in r.readers:
                    if rt.sem != sem_key:
                        toks.append(rt)
        for r in writes:
            if r.w is not None:
                toks.append(r.w)
            toks.extend(r.readers)
        for t in toks:
            if issue_eng == "tensor" and t.sem == "tensor":
                continue
            if clk.get(t.sem, 0) < t.val and need.get(t.sem, 0) < t.val:
                need[t.sem] = t.val
        waits = []
        for s, v in need.items():
            if clk.get(s, 0) >= v:
                continue
            waits.append((s, v))
        for t in toks:
            if t.sem in need and need[t.sem] == t.val:
                for s2, v2 in t.pub.items():
                    if clk.get(s2, 0) < v2:
                        clk[s2] = v2
        for s, v in need.items():
            if clk.get(s, 0) < v:
                clk[s] = v
        self.counts[sem_key] += inc_amt
        val = self.counts[sem_key]
        pub = dict(clk)
        pub[sem_key] = max(pub.get(sem_key, 0), val)
        tok = Tok(sem_key, val, pub)
        self.streams[issue_eng].append(_Rec(emit, waits, (sem_key, inc_amt), q))
        self.nops += 1
        for r in reads:
            r.readers.append(tok)
        for r in writes:
            r.w = tok
            r.readers = []
        return tok

    def op(self, eng, emit, reads=(), writes=()):
        return self._issue(eng, emit, reads, writes, eng, 1)

    def dma(self, queue, slot, emit, reads=(), writes=()):
        issue_eng = {"dsync": "sync", "dgp": "gpsimd"}[queue]
        i = self.dma_seq[queue]
        self.dma_seq[queue] += 1
        mult = self.dma_incs[queue][i] if self.dma_incs is not None else 1
        if issue_eng == "gpsimd":
            self.pl_order.append(queue)
        sem = self._slot_sem((queue, slot))
        return self._issue(issue_eng, emit, reads, writes, sem, 16 * mult, q=queue)

    def coll(self, emit, reads=(), writes=()):
        i = self.dma_seq["coll"]
        self.dma_seq["coll"] += 1
        mult = self.dma_incs["coll"][i] if self.dma_incs is not None else 1
        self.pl_order.append("coll")
        sem = self._slot_sem(("coll", 0))
        return self._issue("gpsimd", emit, reads, writes, sem, 16 * mult, q="coll")

    def barrier(self):
        toks = [Tok(k, self.counts[k], {k: self.counts[k]})
                for k in list(self.counts) if self.counts[k] > 0]
        for e in COMPUTE_ENGINES:
            clk = self.clocks[e]
            waits = []
            for t in toks:
                if clk.get(t.sem, 0) < t.val:
                    clk[t.sem] = t.val
                    waits.append((t.sem, t.val))
            if waits:
                self.streams[e].append(_Rec(None, waits, None))

    def replay(self, block, sentinels=False):
        sems = self.sems

        def section(eng_name):
            def run(eng):
                for rec in self.streams[eng_name]:
                    for (s, v) in rec.waits:
                        eng.wait_ge(sems[s], v)
                    if rec.emit is not None:
                        inst = rec.emit(eng)
                        inst.then_inc(sems[rec.inc[0]], rec.inc[1])
                        if sentinels and rec.q is not None:
                            eng.nop(nofuse=True)
            return run

        block.tensor(section("tensor"))
        block.vector(section("vector"))
        block.scalar(section("scalar"))
        block.gpsimd(section("gpsimd"))
        block.sync(section("sync"))


def host_constants():
    import ml_dtypes
    bf = ml_dtypes.bfloat16
    c = {}
    c["c_ident_bf"] = np.eye(128, dtype=bf)
    c["c_ident_f32"] = np.eye(128, dtype=np.float32)
    ii, jj = np.indices((128, 128))
    c["c_triu_f32"] = (jj >= ii).astype(bf)
    c["c_striu_f32"] = (jj > ii).astype(bf)
    c["c_stril_f32"] = (jj < ii).astype(bf)
    c["c_triu_bf"] = (jj >= ii).astype(bf)
    c["c_ones_row_bf"] = np.ones((1, 128), dtype=bf)
    c["c_ones_sq_bf"] = np.ones((128, 128), dtype=bf)
    rs = np.zeros((8, 8, 128), dtype=bf)
    for r in range(8):
        rs[r, r, :] = 1
    c["c_rowsel8_bf"] = np.ascontiguousarray(rs.transpose(1, 0, 2).reshape(8, 8 * 128))
    c["c_rowsel8_f32"] = c["c_rowsel8_bf"].astype(np.float32)
    c["c_ineg_bf"] = (-np.eye(128)).astype(bf)
    cs = np.zeros((4, 128, 8), dtype=bf)
    for kh in range(4):
        cs[kh, :, 2 * kh] = 1
        cs[kh, :, 2 * kh + 1] = 1
    c["c_colsel2_bf"] = np.ascontiguousarray(cs.transpose(1, 0, 2).reshape(128, 32))
    c1 = np.zeros((8, 128, 8), dtype=bf)
    for h in range(8):
        c1[h, :, h] = 1
    c["c_colsel1_bf"] = np.ascontiguousarray(c1.transpose(1, 0, 2).reshape(128, 64))
    return c


class Cfg:
    def __init__(self, S=2048, B=2, ncores=8, B_total=16, only=None, debug=False, trunc=None):
        self.debug = debug
        self.trunc = trunc
        self.S = S
        self.B = B
        self.ncores = ncores
        self.B_total = B_total
        self.only = only
        self.T = B * S
        self.NT = self.T // 128
        self.NTB = S // 128
        self.KSEL = S // 2


class P:
    pass


def declare_io(nc, cfg):
    p = P()
    p.cfg = cfg
    d = {}
    d["x"] = nc.dram_tensor("x", [cfg.T, 1024], F32, kind="ExternalInput").ap()
    wshapes = {
        "aw_ln1": [1024], "aw_ln2": [1024], "aW_qkvz": [1024, 3072], "aW_ba": [1024, 16],
        "aconv_w": [2048, 4], "adt_bias": [8], "aA_log": [8], "agate_norm_w": [128],
        "aW_out": [1024, 1024], "aW_router": [1024, 8],
        "aW_gate_e": [8, 1024, 512], "aW_up_e": [8, 1024, 512], "aW_down_e": [8, 512, 1024],
        "fw_gate_w": [1024, 8], "fw_gate_b": [8],
        "fW1": [8, 1024, 256], "fb1": [8, 256], "fW2": [8, 256, 1024], "fb2": [8, 1024],
    }
    for k, s in wshapes.items():
        d[k] = nc.dram_tensor(k, s, F32, kind="ExternalInput").ap()
    import ml_dtypes
    for k, v in host_constants().items():
        dt = BF16 if v.dtype == ml_dtypes.bfloat16 else F32
        d[k] = nc.dram_tensor(k, list(v.shape), dt, kind="ExternalInput").ap()
    if cfg.only in ("l2", "l3"):
        d["pdens"] = nc.dram_tensor("pdens", [cfg.ncores, cfg.S, 8], F32,
                                    kind="ExternalInput").ap()
    if cfg.only in ("l1", "l2"):
        d["pden_out"] = nc.dram_tensor("pden_out", [cfg.S, 8], F32, kind="ExternalOutput").ap()
    if cfg.only == "l2":
        d["r2x"] = nc.dram_tensor("r2x", [cfg.T, 1024], F32, kind="ExternalOutput").ap()
    else:
        d["y"] = nc.dram_tensor("y", [cfg.T, 1024], F32, kind="ExternalOutput").ap()
    if cfg.debug:
        for nm in ("dbg_mv", "dbg_sel", "dbg_route"):
            d[nm] = nc.dram_tensor(nm, [128, cfg.NT * 8], F32, kind="ExternalOutput").ap()
        d["dbg_xnfm"] = nc.dram_tensor("dbg_xnfm", [128, 4 * cfg.T], BF16, kind="ExternalOutput").ap()
    d["r1"] = nc.dram_tensor("r1", [cfg.T, 1024], F32, kind="Internal").ap()
    d["lnffn"] = nc.dram_tensor("lnffn", [cfg.T, 1024], BF16, kind="Internal").ap()
    d["lnbuf"] = nc.dram_tensor("lnbuf", [cfg.T, 1024], BF16, kind="Internal").ap()
    d["hbuf"] = nc.dram_tensor("hbuf", [cfg.T, 1024], F32, kind="Internal").ap()
    d["r2"] = nc.dram_tensor("r2", [cfg.T, 1024], F32, kind="Internal").ap()
    d["x2fm"] = nc.dram_tensor("x2fm", [8, 128, cfg.T], BF16, kind="Internal").ap()
    d["moepart"] = nc.dram_tensor("moepart", [8, 128, cfg.T], BF16, kind="Internal").ap()
    for i in (1, 2):
        d[f"cin{i}"] = nc.dram_tensor(f"cin{i}", [cfg.S, 8], F32, kind="Internal").ap()
        d[f"cout{i}"] = nc.dram_tensor(f"cout{i}", [cfg.S, 8], F32, kind="Internal").ap()
    p.d = d
    return p


def alloc_sbuf(p, nc, es):
    cfg = p.cfg
    sb = lambda n, shape, dt=F32: es.enter_context(nc.sbuf_tensor(n, shape, dt))
    t = {}
    TH = cfg.T // 2
    t["ident_bf"] = sb("ident_bf", [128, 128], BF16)
    t["ident_f32"] = sb("ident_f32", [128, 128], F32)
    t["triu"] = sb("triu", [128, 128], BF16)
    t["striu"] = sb("striu", [128, 128], BF16)
    t["stril"] = sb("stril", [128, 128], BF16)
    t["triu_bf"] = sb("triu_bf", [128, 128], BF16)
    t["ones_row_bf"] = sb("ones_row_bf", [1, 128], BF16)
    t["ones_sq_bf"] = sb("ones_sq_bf", [128, 128], BF16)
    t["rowsel8"] = sb("rowsel8", [8, 1024], BF16)
    t["rowsel8f"] = sb("rowsel8f", [8, 1024], F32)
    t["colsel2"] = sb("colsel2", [128, 32], BF16)
    t["colsel1"] = sb("colsel1", [128, 64], BF16)
    t["inegbf"] = sb("inegbf", [128, 128], BF16)
    t["wg"] = sb("wg", [128, 64], BF16)
    t["wgb_bc"] = sb("wgb_bc", [128, 8], F32)
    t["wgb_row"] = sb("wgb_row", [1, 8], BF16)
    t["b1c"] = sb("b1c", [128, 16], F32)
    t["b1st"] = sb("b1st", [8, 256], F32)
    t["b2r"] = sb("b2r", [8, 1024], BF16)
    t["wba"] = sb("wba", [128, 128], BF16)
    t["wrt"] = sb("wrt", [128, 64], BF16)
    t["convw"] = sb("convw", [128, 64], F32)
    t["wln1c"] = sb("wln1c", [128, 8], F32)
    t["wln2c"] = sb("wln2c", [128, 8], F32)
    t["gnwc"] = sb("gnwc", [128, 1], F32)
    t["hcol8"] = sb("hcol8", [8, 8], F32)
    t["wbuf"] = sb("wbuf", [128, 32768], BF16)
    t["xnfm"] = sb("xnfm", [128, max(8 * TH, 16384)], BF16)
    t["accbuf"] = sb("accbuf", [128, 8240], BF16)
    t["zbuf"] = sb("zbuf", [128, 2048], BF16)
    t["poolF"] = sb("poolF", [128, 2048], F32)
    t["poolB"] = sb("poolB", [128, 6144], BF16)
    t["states"] = sb("states", [128, 1024], BF16)
    t["xt"] = sb("xt", [128, 2 * 1024], F32)
    t["xnb"] = sb("xnb", [128, 2 * 1024], BF16)
    t["xnb2"] = sb("xnb2", [128, 1024], BF16)
    t["sq_scr"] = sb("sq_scr", [128, 1024], F32)
    t["cols"] = sb("cols", [128, 2 * 16], F32)
    t["outt"] = sb("outt", [128, 1024], F32)
    t["xres"] = sb("xres", [128, 1024], F32)
    t["glb"] = sb("glb", [8, 128], BF16)
    t["gl"] = sb("gl", [128, 2 * 8], F32)
    t["m8"] = sb("m8", [128, 8], F32)
    t["zap"] = sb("zap", [128, 8], F32)
    t["mv"] = sb("mv", [128, cfg.NT * 8], F32)
    t["rden"] = sb("rden", [128, cfg.NTB * 8], F32)
    t["route"] = sb("route", [128, cfg.NT * 8], F32)
    t["sel"] = sb("sel", [128, cfg.NT * 8], F32)
    t["rowfm"] = sb("rowfm", [8, cfg.T], BF16)
    t["bloE"] = sb("bloE", [128, 16], F32)
    t["bloM"] = sb("bloM", [128, 16], F32)
    t["bcand"] = sb("bcand", [128, 16], F32)
    t["bpi"] = sb("bpi", [128, 16], I32)
    t["bth"] = sb("bth", [128, 16], F32)
    t["btake"] = sb("btake", [128, 16], F32)
    t["cnt"] = sb("cnt", [128, 16], F32)
    t["gcnt"] = sb("gcnt", [128, 16], F32)
    t["kg"] = sb("kg", [128, 16], F32)
    t["cntbc"] = sb("cntbc", [128, cfg.NT * 8], F32)
    t["eqw"] = sb("eqw", [128, cfg.NT * 8], F32)
    t["cumw"] = sb("cumw", [128, cfg.NT * 8], F32)
    t["totbc"] = sb("totbc", [128, cfg.NT * 8], F32)
    t["cmpw"] = sb("cmpw", [128, cfg.NT * 8], BF16)
    t["eqbf"] = sb("eqbf", [128, cfg.NT * 8], BF16)
    t["h1m"] = sb("h1m", [128, 2 * 256], BF16)
    t["selbc"] = sb("selbc", [128, 256], BF16)
    t["ofm"] = sb("ofm", [128, 2 * 256], BF16)

    t["rowsF"] = sb("rowsF", [8, 6 * 256], F32)
    t["rowsB"] = sb("rowsB", [8, 8 * 256], BF16)
    t["colsT"] = sb("colsT", [128, 128], F32)
    p.t = t
    p.cpool_map = {}
    p.cpool_next = [0]
    t["cpool"] = sb("cpool", [128, 16], F32)
    p.R = {}
    return t


def alloc_psum(p, nc, es):
    ps = {}
    ps["T"] = es.enter_context(nc.psum_tensor("ps_T", [128, 1024], BF16))
    ps["g"] = es.enter_context(nc.psum_tensor("ps_g", [8, 512], F32))
    ps["misc"] = es.enter_context(nc.psum_tensor("ps_misc", [128, 512], F32))
    ps["h1"] = es.enter_context(nc.psum_tensor("ps_h1", [128, 512], F32))
    for i in range(4):
        ps[f"o{i}"] = es.enter_context(nc.psum_tensor(f"ps_o{i}", [128, 512], F32))
    p.ps = ps
    return ps


def R(p, name, n=None):
    if n is None:
        if name not in p.R:
            p.R[name] = Res(name)
        return p.R[name]
    if name not in p.R:
        p.R[name] = [Res(f"{name}[{i}]") for i in range(n)]
    return p.R[name]


def lp(p, fn):
    def wrap(e):
        with p.nc.allow_low_precision(reason="bf16 scale factors"):
            return fn(e)
    return wrap


def ccol(p, kb, val):
    if val in p.cpool_map:
        return p.cpool_map[val]
    i = p.cpool_next[0]
    p.cpool_next[0] += 1
    assert i < 16
    apc = p.t["cpool"][:, i:i + 1]
    rc = R(p, "consts")
    kb.op("gpsimd", lambda e, a=apc, v=val: e.memset(a, v), writes=[rc])
    p.nc.const_aps.register(F32, float(val), apc)
    p.cpool_map[val] = apc
    return apc



def load_constants(p, kb):
    t, d = p.t, p.d
    rc = R(p, "consts")
    pairs = [("ident_bf", "c_ident_bf"), ("ident_f32", "c_ident_f32"),
             ("triu", "c_triu_f32"), ("striu", "c_striu_f32"),
             ("stril", "c_stril_f32"), ("triu_bf", "c_triu_bf"),
             ("ones_row_bf", "c_ones_row_bf"), ("ones_sq_bf", "c_ones_sq_bf"),
             ("rowsel8", "c_rowsel8_bf"), ("rowsel8f", "c_rowsel8_f32"),
             ("inegbf", "c_ineg_bf"),
             ("colsel2", "c_colsel2_bf"), ("colsel1", "c_colsel1_bf")]
    for sbn, dn in pairs:
        kb.dma("dsync", "c", lambda e, a=t[sbn], b=d[dn]: e.dma_start(out=a[:], in_=b[:]),
               writes=[rc])
    kb.dma("dgp", "cw", lambda e: e.dma_start(out=t["wg"][:].rearrange("p (dt e) -> p dt e", e=8),
                                        in_=d["fw_gate_w"].rearrange("(dt p) e -> p dt e", p=128)),
           writes=[rc])
    kb.dma("dgp", "cw", lambda e: e.dma_start(out=t["wgb_row"][:], in_=d["fw_gate_b"][None, :]),
           writes=[rc])
    kb.dma("dsync", "c", lambda e: e.dma_start(out=t["b1st"][:], in_=d["fb1"][:, :]), writes=[rc])
    for it in range(2):
        kb.op("tensor", lambda e, it=it: e.transpose(p.ps["misc"][0:128, 0:8],
                                                     t["b1st"][0:8, it * 128:(it + 1) * 128],
                                                     t["ident_f32"][0:8, 0:8]),
              reads=[rc], writes=[R(p, "ps_misc")])
        kb.op("vector", lambda e, it=it: e.tensor_copy(t["b1c"][:, it * 8:(it + 1) * 8],
                                                       p.ps["misc"][0:128, 0:8]),
              reads=[R(p, "ps_misc")], writes=[rc])
    kb.dma("dgp", "cw", lambda e: e.dma_start(out=t["b2r"][:], in_=d["fb2"][:, :]), writes=[rc])
    pm = R(p, "ps_misc")
    kb.op("tensor", lambda e: e.matmul(p.ps["misc"][0:128, 0:8], t["ones_row_bf"][:],
                                       t["wgb_row"][:], start=True, stop=True),
          reads=[rc], writes=[pm])
    kb.op("vector", lambda e: e.tensor_copy(t["wgb_bc"][:], p.ps["misc"][0:128, 0:8]),
          reads=[pm], writes=[rc])


def load_ffn_weights(p, kb):
    t, d = p.t, p.d
    rw = R(p, "ffn_w")
    for ei in range(NEXP):
        kb.dma("dgp", "fw", lambda e, ei=ei: e.dma_start(
            out=t["wbuf"][:, ei * 2048: (ei + 1) * 2048].rearrange("p (dt c) -> p dt c", c=256),
            in_=d["fW1"][ei].rearrange("(dt p) c -> p dt c", p=128)),
            writes=[rw])
        kb.dma("dgp", "fw", lambda e, ei=ei: e.dma_start(
            out=t["wbuf"][:, 16384 + ei * 2048: 16384 + (ei + 1) * 2048].rearrange(
                "p (it c) -> p it c", c=1024),
            in_=d["fW2"][ei].rearrange("(it p) c -> p it c", p=128)),
            writes=[rw])


def w1_tile(p, e, dt, it):
    c = (e * 8 + dt) * 256 + it * 128
    return p.t["wbuf"][:, c:c + 128]


def w2_tile(p, e, it, dt):
    c = 16384 + (e * 2 + it) * 1024 + dt * 128
    return p.t["wbuf"][:, c:c + 128]


def emit_ln_half(p, kb, src, half, mode="ln", do_gate=None, store_dram=None,
                 zw_col=None, store_fm_dram=False, src_res=None):
    t, cfg, ps = p.t, p.cfg, p.ps
    TH = cfg.T // 2
    NTH = cfg.NT // 2
    rx = R(p, "xt_ring", 2)
    rxnb = R(p, "xnb_ring", 2)
    rcols = R(p, "cols_ring", 2)
    rscr = R(p, "sq_scr")
    rfm = R(p, "xnfm_t", NTH * 8)
    rmv = R(p, "mv")
    rgl = R(p, "gl_ring", 2)
    rglb = R(p, "glb_ring", 2)
    rm8 = R(p, "m8")
    rzap = R(p, "zap")
    rc = R(p, "consts")
    rpT = R(p, "ps_T")
    rpTg = rpT
    rpg = R(p, "ps_g")
    rln = R(p, "ln_dram")
    for tl in range(NTH):
        ti = half * NTH + tl
        ring = tl % 2
        xt = t["xt"][:, ring * 1024:(ring + 1) * 1024]
        kb.dma("dsync", ("x", ring), lambda e, a=xt, ti=ti: e.dma_start(
            out=a, in_=src[ti * 128:(ti + 1) * 128, :]),
            reads=(src_res or []), writes=[rx[ring]])
        cl = t["cols"][:, ring * 16:(ring + 1) * 16]
        kb.op("scalar", lambda e, a=cl: e.memzero(a[:, 1:2]), writes=[rcols[ring]])
        kb.op("scalar", lambda e, a=cl, b=xt: e.activation(t["sq_scr"][:], b, AF.Square,
                                                           accum_out=a[:, 1:2]),
              reads=[rx[ring]], writes=[rcols[ring], rscr])
        if mode in ("ln", "lnzrms"):
            kb.op("vector", lambda e, a=cl, b=xt: e.reduce_sum(a[:, 0:1], b, axis=AX.X),
                  reads=[rx[ring]], writes=[rcols[ring]])
            kb.op("vector", lambda e, a=cl: e.tensor_scalar(a[:, 2:3], a[:, 0:1], 1.0 / 1024,
                                                            None, ALU.mult),
                  reads=[rcols[ring]], writes=[rcols[ring]])
            kb.op("vector", lambda e, a=cl: e.tensor_mul(a[:, 4:5], a[:, 2:3], a[:, 2:3]),
                  reads=[rcols[ring]], writes=[rcols[ring]])
            kb.op("vector", lambda e, a=cl: e.tensor_scalar(a[:, 5:6], a[:, 1:2], 1.0 / 1024,
                                                            a[:, 4:5], ALU.mult, ALU.subtract),
                  reads=[rcols[ring]], writes=[rcols[ring]])
            kb.op("scalar", lambda e, a=cl, bc=ccol(p, kb, EPS5): e.activation(
                a[:, 6:7], a[:, 5:6], AF.Sqrt, bias=bc),
                reads=[rcols[ring], rc], writes=[rcols[ring]])
            kb.op("vector", lambda e, a=cl: e.reciprocal(a[:, 3:4], a[:, 6:7]),
                  reads=[rcols[ring]], writes=[rcols[ring]])
        else:
            kb.op("vector", lambda e, a=cl: e.memset(a[:, 2:3], 0.0),
                  writes=[rcols[ring]])
            kb.op("vector", lambda e, a=cl: e.tensor_scalar(a[:, 5:6], a[:, 1:2], 1.0 / 1024,
                                                            None, ALU.mult),
                  reads=[rcols[ring]], writes=[rcols[ring]])
            kb.op("scalar", lambda e, a=cl, bc=ccol(p, kb, EPS6): e.activation(
                a[:, 6:7], a[:, 5:6], AF.Sqrt, bias=bc),
                reads=[rcols[ring], rc], writes=[rcols[ring]])
            kb.op("vector", lambda e, a=cl: e.reciprocal(a[:, 3:4], a[:, 6:7]),
                  reads=[rcols[ring]], writes=[rcols[ring]])
        xnb = t["xnb"][:, ring * 1024:(ring + 1) * 1024]
        kb.op("vector", lambda e, a=xnb, b=xt, c=cl: e.tensor_scalar(
            a, b, c[:, 2:3], c[:, 3:4], ALU.subtract, ALU.mult),
            reads=[rx[ring], rcols[ring]], writes=[rxnb[ring]])
        if store_dram is not None:
            kb.dma("dsync", ("lnst", ring), lambda e, a=xnb, ti=ti: e.dma_start(
                out=store_dram[ti * 128:(ti + 1) * 128, :], in_=a),
                reads=[rxnb[ring]], writes=[rln])
        if mode == "lnzrms":
            kb.op("vector", lambda e, a=cl: e.tensor_mul(a[:, 7:8], a[:, 3:4], a[:, 3:4]),
                  reads=[rcols[ring]], writes=[rcols[ring]])
            kb.op("vector", lambda e, a=cl: e.tensor_mul(a[:, 8:9], a[:, 5:6], a[:, 7:8]),
                  reads=[rcols[ring]], writes=[rcols[ring]])
            kb.op("scalar", lambda e, a=cl, bc=ccol(p, kb, EPS6): e.activation(
                a[:, 9:10], a[:, 8:9], AF.Sqrt, bias=bc),
                reads=[rcols[ring], rc], writes=[rcols[ring]])
            kb.op("vector", lambda e, a=cl: e.reciprocal(a[:, 10:11], a[:, 9:10]),
                  reads=[rcols[ring]], writes=[rcols[ring]])
            xnb2 = t["xnb2"][:, 0:1024]
            rx2 = R(p, "xnb2")
            kb.op("vector", lambda e, a=xnb2, b=xnb, c=cl: e.tensor_scalar(
                a, b, c[:, 10:11], None, ALU.mult),
                reads=[rxnb[ring], rcols[ring]], writes=[rx2])
            tsrc = xnb2
            tread = rx2
        else:
            tsrc = xnb
            tread = rxnb[ring]
        for dt in range(8):
            pT = ps["T"][0:128, (dt % 4) * 128:(dt % 4) * 128 + 128]
            rp = rpT
            kb.op("tensor", lambda e, o=pT, a=tsrc, dt=dt: e.transpose(
                o, a[:, dt * 128:(dt + 1) * 128], t["ident_bf"][:]),
                reads=[tread, rc], writes=[rp])
            dst = t["xnfm"][:, dt * TH + tl * 128: dt * TH + tl * 128 + 128]
            eng = "vector" if dt % 2 == 0 else "scalar"
            if zw_col is None:
                if eng == "vector":
                    kb.op(eng, lambda e, a=dst, b=pT: e.tensor_copy(a, b),
                          reads=[rp], writes=[rfm[tl * 8 + dt]])
                else:
                    kb.op(eng, lambda e, a=dst, b=pT: e.copy(a, b),
                          reads=[rp], writes=[rfm[tl * 8 + dt]])
            else:
                if eng == "vector":
                    kb.op(eng, lambda e, a=dst, b=pT, dt=dt: e.tensor_scalar(
                        a, b, zw_col[:, dt:dt + 1], None, ALU.mult),
                        reads=[rp, rc], writes=[rfm[tl * 8 + dt]])
                else:
                    kb.op(eng, lambda e, a=dst, b=pT, dt=dt: e.mul(a, b, zw_col[:, dt:dt + 1]),
                          reads=[rp, rc], writes=[rfm[tl * 8 + dt]])
        if store_fm_dram:
            kb.dma("dsync", ("fmst", ring), lambda e, tl=tl, ti=ti: e.dma_start(
                out=p.d["x2fm"][:, :, ti * 128:(ti + 1) * 128].transpose([1, 0, 2]),
                in_=t["xnfm"][:, 0:8 * TH].rearrange("p (dt c) -> p dt c", c=TH)[
                    :, :, tl * 128:(tl + 1) * 128]),
                reads=[rfm[tl * 8 + dtq] for dtq in range(8)], writes=[R(p, "x2_dram")])
        if do_gate is not None:
            gw = t["wg"] if do_gate[0] == "ffn" else t["wrt"]
            pg = ps["g"][0:8, (tl % 2) * 128:(tl % 2) * 128 + 128]
            for dt in range(8):
                kb.op("tensor", lambda e, o=pg, dt=dt, tl=tl, gw=gw: e.matmul(
                    o, gw[:, dt * 8:(dt + 1) * 8],
                    t["xnfm"][:, dt * TH + tl * 128: dt * TH + tl * 128 + 128],
                    start=(dt == 0), stop=(dt == 7)),
                    reads=[rfm[tl * 8 + dt], rc], writes=[rpg])
            glb = t["glb"][0:8, 0:128]
            kb.op("vector", lambda e, a=glb, b=pg: e.tensor_copy(a, b),
                  reads=[rpg], writes=[rglb[tl % 2]])
            pT2 = ps["T"][0:128, 512:520]
            kb.op("tensor", lambda e, o=pT2, a=glb: e.transpose(o, a, t["ident_bf"][0:8, 0:8]),
                  reads=[rglb[tl % 2], rc], writes=[rpTg])
            gl = t["gl"][:, (tl % 2) * 8:(tl % 2) * 8 + 8]
            if do_gate[0] == "ffn":
                kb.op("vector", lambda e, a=gl, b=pT2: e.tensor_add(a, b, t["wgb_bc"][:]),
                      reads=[rpTg, rc], writes=[rgl[tl % 2]])
            else:
                kb.op("vector", lambda e, a=gl, b=pT2: e.tensor_copy(a, b),
                      reads=[rpTg], writes=[rgl[tl % 2]])
            cl2 = t["cols"][:, ring * 16 + 11: ring * 16 + 16]
            kb.op("vector", lambda e, a=cl2, b=gl: e.reduce_max(a[:, 0:1], b, axis=AX.X,
                                                                negate=True),
                  reads=[rgl[tl % 2]], writes=[rcols[ring]])
            kb.op("scalar", lambda e, a=gl, c=cl2: e.activation(a, a, AF.Exp, bias=c[:, 0:1]),
                  reads=[rcols[ring], rgl[tl % 2]], writes=[rgl[tl % 2]])
            kb.op("vector", lambda e, a=cl2, b=gl: e.reduce_sum(a[:, 1:2], b, axis=AX.X),
                  reads=[rgl[tl % 2]], writes=[rcols[ring]])
            kb.op("vector", lambda e, a=cl2: e.reciprocal(a[:, 2:3], a[:, 1:2]),
                  reads=[rcols[ring]], writes=[rcols[ring]])
            kb.op("vector", lambda e, a=gl, c=cl2: e.tensor_scalar(a, a, c[:, 2:3], None,
                                                                   ALU.mult),
                  reads=[rcols[ring], rgl[tl % 2]], writes=[rgl[tl % 2]])
            kb.op("vector", lambda e, a=gl: e.max(out=t["m8"][:], in_=a),
                  reads=[rgl[tl % 2]], writes=[rm8])
            kb.op("vector", lambda e: e.memset(t["m8"][:, 2:8], -1.0),
                  reads=[], writes=[rm8])
            kb.op("vector", lambda e, a=gl: e.match_replace(out=t["zap"][:],
                                                            in_to_replace=t["m8"][:, 0:8],
                                                            in_values=a, imm_value=0.0),
                  reads=[rgl[tl % 2], rm8], writes=[rzap])
            mvd = t["mv"][:, ti * 8:(ti + 1) * 8]
            kb.op("vector", lambda e, a=mvd, b=gl: e.tensor_sub(a, b, t["zap"][:]),
                  reads=[rgl[tl % 2], rzap], writes=[rmv])


def emit_retranspose_half(p, kb, src_bf, half):
    t, cfg, ps = p.t, p.cfg, p.ps
    TH = cfg.T // 2
    NTH = cfg.NT // 2
    rxnb = R(p, "xnb_ring", 2)
    rfm = R(p, "xnfm_t", NTH * 8)
    rc = R(p, "consts")
    rpT = R(p, "ps_T")
    for tl in range(NTH):
        ti = half * NTH + tl
        ring = tl % 2
        xnb = t["xnb"][:, ring * 1024:(ring + 1) * 1024]
        kb.dma("dsync", ("x", ring), lambda e, a=xnb, ti=ti: e.dma_start(
            out=a, in_=src_bf[ti * 128:(ti + 1) * 128, :]),
            writes=[rxnb[ring]])
        for dt in range(8):
            pT = ps["T"][0:128, (dt % 4) * 128:(dt % 4) * 128 + 128]
            rp = rpT
            kb.op("tensor", lambda e, o=pT, a=xnb, dt=dt: e.transpose(
                o, a[:, dt * 128:(dt + 1) * 128], t["ident_bf"][:]),
                reads=[rxnb[ring], rc], writes=[rp])
            dst = t["xnfm"][:, dt * TH + tl * 128: dt * TH + tl * 128 + 128]
            eng = "vector" if dt % 2 == 0 else "scalar"
            if eng == "vector":
                kb.op(eng, lambda e, a=dst, b=pT: e.tensor_copy(a, b),
                      reads=[rp], writes=[rfm[tl * 8 + dt]])
            else:
                kb.op(eng, lambda e, a=dst, b=pT: e.copy(a, b),
                      reads=[rp], writes=[rfm[tl * 8 + dt]])


def emit_pden_out(p, kb):
    t, cfg = p.t, p.cfg
    NTB = cfg.NTB
    rmv = R(p, "mv")
    rrden = R(p, "rden")
    pden = t["rden"]
    kb.op("vector", lambda e: e.tensor_add(pden[:, 0:NTB * 8], t["mv"][:, 0:NTB * 8],
                                           t["mv"][:, NTB * 8:2 * NTB * 8]),
          reads=[rmv], writes=[rrden])
    kb.dma("dsync", "pd", lambda e: e.dma_start(
        out=p.d["pden_out"].rearrange("(j p) e -> p j e", p=128),
        in_=pden[:, 0:NTB * 8].rearrange("p (j e) -> p j e", e=8)),
        reads=[rrden], writes=[R(p, "dst_dram")])


def emit_denom_from_inputs(p, kb, call_idx):
    t, cfg = p.t, p.cfg
    NTB = cfg.NTB
    R_ = cfg.ncores
    rrden = R(p, "rden")
    rpdl = R(p, "pdl")
    dstp = t["sq_scr"][:, 0:R_ * NTB * 8]
    rpdl = R(p, "sq_scr")
    kb.dma("dsync", "pd", lambda e: e.dma_start(
        out=dstp.rearrange("p (r j e) -> p r j e", r=R_, e=8),
        in_=p.d["pdens"].rearrange("r (j p) e -> p r j e", p=128)),
        writes=[rpdl])
    pden = t["rden"]
    kb.op("vector", lambda e: e.reduce_sum(
        pden[:, 0:NTB * 8].rearrange("p (j e) -> p j e", e=8),
        dstp.rearrange("p (r j e) -> p j e r", r=R_, e=8),
        axis=AX.X),
        reads=[rpdl], writes=[rrden])
    kb.op("vector", lambda e: e.tensor_scalar(pden[:, 0:NTB * 8], pden[:, 0:NTB * 8], EPS6,
                                              None, ALU.add),
          reads=[rrden], writes=[rrden])
    kb.op("vector", lambda e: e.reciprocal(pden[:, 0:NTB * 8], pden[:, 0:NTB * 8]),
          reads=[rrden], writes=[rrden])


def emit_ffn_route(p, kb, cin, cout):
    t, cfg, ps = p.t, p.cfg, p.ps
    NT, NTB = cfg.NT, cfg.NTB
    NB = cfg.B * 8
    assert cfg.B == 2
    rmv = R(p, "mv")
    rroute = R(p, "route")
    rrden = R(p, "rden")
    rsel = R(p, "sel")
    rselfm = R(p, "rowfm")
    rc = R(p, "consts")
    rcin = R(p, "dram_cin")
    rcout = R(p, "dram_cout")
    rpm = R(p, "ps_misc")
    rbis = R(p, "bis")
    rcnt = R(p, "cnt")
    rcmp = R(p, "cmpw")
    rcntbc = R(p, "cntbc")
    req = R(p, "eqw")
    reqb = R(p, "eqbf")
    rcum = R(p, "cumw")
    rtot = R(p, "totbc")
    pden = t["rden"]
    if cin is not None:
        kb.op("vector", lambda e: e.tensor_add(pden[:, 0:NTB * 8], t["mv"][:, 0:NTB * 8],
                                               t["mv"][:, NTB * 8:2 * NTB * 8]),
              reads=[rmv], writes=[rrden])
        kb.dma("dsync", "cden", lambda e: e.dma_start(
            out=cin.rearrange("(j p) e -> p j e", p=128),
            in_=pden[:, 0:NTB * 8].rearrange("p (j e) -> p j e", e=8)),
            reads=[rrden], writes=[rcin])
        kb.coll(lambda e: e.collective_compute(
            "AllReduce", ALU.add, replica_groups=[list(range(cfg.ncores))],
            ins=[cin[:]], outs=[cout[:]]),
            reads=[rcin], writes=[rcout])
        kb.dma("dsync", "cden", lambda e: e.dma_start(
            out=pden[:, 0:NTB * 8].rearrange("p (j e) -> p j e", e=8),
            in_=cout.rearrange("(j p) e -> p j e", p=128)),
            reads=[rcout], writes=[rrden])
        kb.op("vector", lambda e: e.tensor_scalar(pden[:, 0:NTB * 8], pden[:, 0:NTB * 8], EPS6,
                                                  None, ALU.add),
              reads=[rrden], writes=[rrden])
        kb.op("vector", lambda e: e.reciprocal(pden[:, 0:NTB * 8], pden[:, 0:NTB * 8]),
              reads=[rrden], writes=[rrden])
    cap = float(cfg.B_total)
    kb.op("vector", lambda e: e.scalar_tensor_tensor(
        t["route"][:, 0:NT * 8].rearrange("p (b j e) -> p b j e", b=cfg.B, e=8),
        t["mv"][:, 0:NT * 8].rearrange("p (b j e) -> p b j e", b=cfg.B, e=8),
        cap,
        pden[:, 0:NTB * 8].rearrange("p (j e) -> p j e", e=8).unsqueeze(1).to_broadcast(
            [128, cfg.B, NTB, 8]),
        ALU.mult, ALU.mult),
        reads=[rmv, rrden], writes=[rroute])

    def cmp_against(dst_ap, th_tile, op):
        return lambda e: e.tensor_tensor(
            dst_ap.rearrange("p (b j e) -> p b j e", b=cfg.B, e=8),
            t["route"][:, 0:NT * 8].rearrange("p (b j e) -> p b j e", b=cfg.B, e=8),
            th_tile[:, 0:NB].rearrange("p (b e) -> p b e", e=8).unsqueeze(2).to_broadcast(
                [128, cfg.B, NTB, 8]),
            op)

    def count_into(cnt_tile):
        for c0 in range(0, NT * 8, 512):
            w = min(512, NT * 8 - c0)
            kb.op("tensor", lambda e, c0=c0, w=w: e.matmul(
                ps["misc"][0:128, 0:w], t["ones_sq_bf"][:], t["cmpw"][:, c0:c0 + w],
                start=True, stop=True),
                reads=[rcmp, rc], writes=[rpm])
            kb.op("vector", lambda e, c0=c0, w=w: e.tensor_copy(t["cntbc"][:, c0:c0 + w],
                                                                ps["misc"][0:128, 0:w]),
                  reads=[rpm], writes=[rcntbc])
        kb.op("vector", lambda e: e.reduce_sum(
            cnt_tile[:, 0:NB].rearrange("p (b e) -> p b e", e=8),
            t["cntbc"][:, 0:NT * 8].rearrange("p (b j e) -> p b e j", b=cfg.B, e=8),
            axis=AX.X),
            reads=[rcntbc], writes=[rcnt])

    def theta_from_exp(src_tile):
        kb.op("vector", lambda e: e.tensor_scalar(t["bth"][:, 0:NB], src_tile[:, 0:NB],
                                                  8388608.0, None, ALU.mult),
              reads=[rbis], writes=[rbis])
        kb.op("vector", lambda e: e.tensor_copy(t["bpi"][:, 0:NB], t["bth"][:, 0:NB]),
              reads=[rbis], writes=[rbis])

    def take_update(lo_tile, delta):
        kb.op("vector", lambda e: e.tensor_scalar(t["btake"][:, 0:NB], t["cnt"][:, 0:NB],
                                                  float(cfg.KSEL), None, ALU.is_ge),
              reads=[rcnt], writes=[rbis])
        kb.op("vector", lambda e, dl=float(delta): e.tensor_scalar(
            t["btake"][:, 0:NB], t["btake"][:, 0:NB], dl, None, ALU.mult),
            reads=[rbis], writes=[rbis])
        kb.op("vector", lambda e: e.tensor_add(lo_tile[:, 0:NB], lo_tile[:, 0:NB],
                                               t["btake"][:, 0:NB]),
              reads=[rbis], writes=[rbis])

    kb.op("vector", lambda e: e.memset(t["bloE"][:, 0:NB], 0.0), writes=[rbis])
    delta = 128
    while delta >= 1:
        kb.op("vector", lambda e, dl=float(delta): e.tensor_scalar(
            t["bcand"][:, 0:NB], t["bloE"][:, 0:NB], dl, None, ALU.add),
            reads=[rbis], writes=[rbis])
        theta_from_exp(t["bcand"])
        kb.op("vector", cmp_against(t["cmpw"][:, 0:NT * 8], t["bpi"].bitcast(F32), ALU.is_ge),
              reads=[rroute, rbis], writes=[rcmp])
        count_into(t["cnt"])
        take_update(t["bloE"], delta)
        delta //= 2
    theta_from_exp(t["bloE"])
    bpf = t["bpi"].bitcast(F32)
    kb.op("vector", lambda e: e.memset(t["bloM"][:, 0:NB], float(2 ** 23)), writes=[rbis])
    delta = 2 ** 22
    while delta >= 1:
        kb.op("vector", lambda e, dl=float(delta): e.tensor_scalar(
            t["bcand"][:, 0:NB], t["bloM"][:, 0:NB], dl, None, ALU.add),
            reads=[rbis], writes=[rbis])
        kb.op("vector", lambda e: e.tensor_scalar(t["bcand"][:, 0:NB], t["bcand"][:, 0:NB],
                                                  float(2.0 ** -23), None, ALU.mult),
              reads=[rbis], writes=[rbis])
        kb.op("vector", lambda e: e.tensor_mul(t["bcand"][:, 0:NB], t["bcand"][:, 0:NB],
                                               bpf[:, 0:NB]),
              reads=[rbis], writes=[rbis])
        kb.op("vector", cmp_against(t["cmpw"][:, 0:NT * 8], t["bcand"], ALU.is_ge),
              reads=[rroute, rbis], writes=[rcmp])
        count_into(t["cnt"])
        take_update(t["bloM"], delta)
        delta //= 2
    kb.op("vector", lambda e: e.tensor_scalar(t["bth"][:, 0:NB], t["bloM"][:, 0:NB],
                                              float(2.0 ** -23), None, ALU.mult),
          reads=[rbis], writes=[rbis])
    kb.op("vector", lambda e: e.tensor_mul(t["bth"][:, 0:NB], t["bth"][:, 0:NB], bpf[:, 0:NB]),
          reads=[rbis], writes=[rbis])
    kb.op("vector", cmp_against(t["cmpw"][:, 0:NT * 8], t["bth"], ALU.is_gt),
          reads=[rroute, rbis], writes=[rcmp])
    count_into(t["gcnt"])
    kb.op("vector", cmp_against(t["sel"][:, 0:NT * 8], t["bth"], ALU.is_gt),
          reads=[rroute, rbis], writes=[rsel])
    kb.op("vector", cmp_against(t["eqw"][:, 0:NT * 8], t["bth"], ALU.is_equal),
          reads=[rroute, rbis], writes=[req])
    kb.op("vector", lambda e: e.tensor_copy(t["eqbf"][:, 0:NT * 8], t["eqw"][:, 0:NT * 8]),
          reads=[req], writes=[reqb])
    for c0 in range(0, NT * 8, 512):
        w = min(512, NT * 8 - c0)
        kb.op("tensor", lambda e, c0=c0, w=w: e.matmul(ps["misc"][0:128, 0:w], t["triu_bf"][:],
                                                       t["eqbf"][:, c0:c0 + w], start=True, stop=True),
              reads=[reqb, rc], writes=[rpm])
        kb.op("vector", lambda e, c0=c0, w=w: e.tensor_copy(t["cumw"][:, c0:c0 + w],
                                                            ps["misc"][0:128, 0:w]),
              reads=[rpm], writes=[rcum])
        kb.op("tensor", lambda e, c0=c0, w=w: e.matmul(ps["misc"][0:128, 0:w], t["ones_sq_bf"][:],
                                                       t["eqbf"][:, c0:c0 + w], start=True, stop=True),
              reads=[reqb, rc], writes=[rpm])
        kb.op("vector", lambda e, c0=c0, w=w: e.tensor_copy(t["totbc"][:, c0:c0 + w],
                                                            ps["misc"][0:128, 0:w]),
              reads=[rpm], writes=[rtot])
    sh = 1
    while sh < NTB:
        def emit_shift(e, sh=sh):
            v = t["totbc"][:, 0:NT * 8].rearrange("p (b j e) -> p b j e", b=cfg.B, e=8)
            return e.tensor_add(v[:, :, sh:NTB, :], v[:, :, sh:NTB, :], v[:, :, 0:NTB - sh, :])
        kb.op("vector", emit_shift, reads=[rtot], writes=[rtot])
        sh *= 2
    for c0 in range(0, NT * 8, 512):
        w = min(512, NT * 8 - c0)
        kb.op("tensor", lambda e, c0=c0, w=w: e.matmul(ps["misc"][0:128, 0:w], t["ones_sq_bf"][:],
                                                       t["eqbf"][:, c0:c0 + w], start=True, stop=True),
              reads=[reqb, rc], writes=[rpm])
        kb.op("vector", lambda e, c0=c0, w=w: e.tensor_copy(t["cntbc"][:, c0:c0 + w],
                                                            ps["misc"][0:128, 0:w]),
              reads=[rpm], writes=[rcntbc])
    kb.op("vector", lambda e: e.tensor_sub(t["totbc"][:, 0:NT * 8], t["totbc"][:, 0:NT * 8],
                                           t["cntbc"][:, 0:NT * 8]),
          reads=[rtot, rcntbc], writes=[rtot])
    kb.op("vector", lambda e: e.tensor_sub(t["cumw"][:, 0:NT * 8], t["cumw"][:, 0:NT * 8],
                                           t["eqw"][:, 0:NT * 8]),
          reads=[rcum, req], writes=[rcum])
    kb.op("vector", lambda e: e.tensor_add(t["cumw"][:, 0:NT * 8], t["cumw"][:, 0:NT * 8],
                                           t["totbc"][:, 0:NT * 8]),
          reads=[rcum, rtot], writes=[rcum])
    kb.op("vector", lambda e: e.tensor_scalar(t["kg"][:, 0:NB], t["gcnt"][:, 0:NB], -1.0,
                                              float(cfg.KSEL), ALU.mult, ALU.add),
          reads=[rcnt], writes=[rcnt])
    kb.op("vector", lambda e: e.tensor_tensor(
        t["cumw"][:, 0:NT * 8].rearrange("p (b j e) -> p b j e", b=cfg.B, e=8),
        t["cumw"][:, 0:NT * 8].rearrange("p (b j e) -> p b j e", b=cfg.B, e=8),
        t["kg"][:, 0:NB].rearrange("p (b e) -> p b e", e=8).unsqueeze(2).to_broadcast(
            [128, cfg.B, NTB, 8]),
        ALU.is_lt),
        reads=[rcum, rcnt], writes=[rcum])
    kb.op("vector", lambda e: e.tensor_mul(t["eqw"][:, 0:NT * 8], t["eqw"][:, 0:NT * 8],
                                           t["cumw"][:, 0:NT * 8]),
          reads=[req, rcum], writes=[req])
    kb.op("vector", lambda e: e.tensor_add(t["sel"][:, 0:NT * 8], t["sel"][:, 0:NT * 8],
                                           t["eqw"][:, 0:NT * 8]),
          reads=[rsel, req], writes=[rsel])
    rpTg = R(p, "ps_T")
    for ti in range(cfg.NT):
        kb.op("vector", lambda e, ti=ti: e.tensor_copy(t["eqbf"][:, ti * 8:(ti + 1) * 8],
                                                       t["sel"][:, ti * 8:(ti + 1) * 8]),
              reads=[rsel], writes=[reqb])
        kb.op("tensor", lambda e, ti=ti: e.transpose(p.ps["T"][0:8, 512:640],
                                                     t["eqbf"][:, ti * 8:(ti + 1) * 8],
                                                     t["ident_bf"][:]),
              reads=[reqb, rc], writes=[rpTg])
        kb.op("scalar", lambda e, ti=ti: e.copy(t["rowfm"][0:8, ti * 128:(ti + 1) * 128],
                                                p.ps["T"][0:8, 512:640]),
              reads=[rpTg], writes=[rselfm])


def emit_ffn_experts(p, kb, src_resid, dst, scale, half):
    t, cfg, ps = p.t, p.cfg, p.ps
    TH = cfg.T // 2
    NTH = cfg.NT // 2
    rfm = R(p, "xnfm_t", NTH * 8)
    rselfm = R(p, "rowfm")
    rc = R(p, "consts")
    rw = R(p, "ffn_w")
    rh1s = R(p, "ps_h1")
    rh1 = [rh1s, rh1s]
    rh1m = R(p, "h1m_r", 2)
    rselbc = R(p, "selbc")
    rpo = R(p, "ps_o", 4)
    rofm = R(p, "ofm_r", 2)
    rpTs = R(p, "ps_T")
    rpT = [rpTs, rpTs, rpTs, rpTs]
    routt = R(p, "outt_r")
    rxres = R(p, "xres_r")
    rpm = R(p, "ps_misc")
    for blkl in range(TH // 256):
        loc0 = blkl * 256
        tok0 = half * TH + loc0
        for e_i in range(NEXP):
            kb.op("tensor", lambda e, ei=e_i, tok0=tok0: e.matmul(
                ps["misc"][0:128, 0:256], t["rowsel8"][0:8, ei * 128:(ei + 1) * 128],
                t["rowfm"][0:8, tok0:tok0 + 256], start=True, stop=True),
                reads=[rselfm, rc], writes=[rpm])
            kb.op("vector", lambda e: e.tensor_copy(t["selbc"][:], ps["misc"][0:128, 0:256]),
                  reads=[rpm], writes=[rselbc])
            for it in range(2):
                ph = ps["h1"][0:128, it * 256:(it + 1) * 256]
                for dt in range(8):
                    kb.op("tensor", lambda e, ei=e_i, it=it, dt=dt, o=ph, loc0=loc0: e.matmul(
                        o, w1_tile(p, ei, dt, it),
                        t["xnfm"][:, dt * TH + loc0: dt * TH + loc0 + 256],
                        start=(dt == 0), stop=(dt == 7)),
                        reads=[rw, rfm[(loc0 // 128) * 8 + dt], rfm[(loc0 // 128 + 1) * 8 + dt]],
                        writes=[rh1[it]])
                kb.op("scalar", lambda e, ei=e_i, it=it, o=ph: e.activation(
                    t["h1m"][:, it * 256:(it + 1) * 256], o, AF.Gelu,
                    bias=t["b1c"][:, (it * 8 + ei):(it * 8 + ei) + 1]),
                    reads=[rh1[it], rc], writes=[rh1m[it]])
                kb.op("vector", lambda e, it=it: e.tensor_mul(
                    t["h1m"][:, it * 256:(it + 1) * 256], t["h1m"][:, it * 256:(it + 1) * 256],
                    t["selbc"][:]),
                    reads=[rselbc, rh1m[it]], writes=[rh1m[it]])
                for dt in range(8):
                    po = ps[f"o{dt // 2}"][0:128, (dt % 2) * 256:(dt % 2) * 256 + 256]
                    kb.op("tensor", lambda e, ei=e_i, it=it, dt=dt, o=po: e.matmul(
                        o, w2_tile(p, ei, it, dt), t["h1m"][:, it * 256:(it + 1) * 256],
                        start=(ei == 0 and it == 0 and dt % 2 == 0), stop=False),
                        reads=[rw, rh1m[it]], writes=[rpo[dt // 2]])
        for dt in range(8):
            po = ps[f"o{dt // 2}"][0:128, (dt % 2) * 256:(dt % 2) * 256 + 256]
            kb.op("tensor", lambda e, dt=dt, o=po, tok0=tok0: e.matmul(
                o, t["b2r"][0:8, dt * 128:(dt + 1) * 128], t["rowfm"][0:8, tok0:tok0 + 256],
                start=False, stop=(dt % 2 == 1)),
                reads=[rc, rselfm], writes=[rpo[dt // 2]])
        for sub in range(2):
            tt = tok0 + sub * 128
            xr = t["xres"][:, 0:1024]
            kb.dma("dsync", "xr", lambda e, a=xr, tt=tt: e.dma_start(
                out=a, in_=src_resid[tt:tt + 128, :]),
                writes=[rxres])
            for dt in range(8):
                po = ps[f"o{dt // 2}"][0:128, (dt % 2) * 256 + sub * 128:
                                       (dt % 2) * 256 + sub * 128 + 128]
                ofm = t["ofm"][:, (dt % 2) * 256:(dt % 2) * 256 + 128]
                kb.op("vector", lambda e, o=po, a=ofm, sc=scale: e.tensor_scalar(
                    a, o, sc, None, ALU.mult),
                    reads=[rpo[dt // 2]], writes=[rofm[dt % 2]])
                pT = ps["T"][0:128, (dt % 4) * 128:(dt % 4) * 128 + 128]
                kb.op("tensor", lambda e, a=ofm, o=pT: e.transpose(o, a, t["ident_bf"][:]),
                      reads=[rofm[dt % 2], rc], writes=[rpT[dt % 4]])
                outt = t["outt"][:, dt * 128:(dt + 1) * 128]
                xrs = t["xres"][:, dt * 128:(dt + 1) * 128]
                kb.op("vector", lambda e, a=outt, b=pT, c=xrs: e.tensor_add(a, b, c),
                      reads=[rpT[dt % 4], rxres], writes=[routt])
            kb.dma("dsync", "out", lambda e, tt=tt: e.dma_start(
                out=dst[tt:tt + 128, :], in_=t["outt"][:, 0:1024]),
                reads=[routt], writes=[R(p, "dst_dram"), R(p, "dst_h", 2)[tt // TH]])


def emit_ffn_sublayer(p, kb, src, dst, cin, cout, scale=0.5, denom_from_inputs=False):
    if denom_from_inputs:
        emit_denom_from_inputs(p, kb, 0)
    for hf in (0, 1):
        emit_ln_half(p, kb, src, hf, mode="ln", do_gate=("ffn",),
                     store_dram=p.d["lnffn"])
    emit_ffn_route(p, kb, None if denom_from_inputs else cin, cout)
    emit_ffn_experts(p, kb, src, dst, scale, half=1)
    emit_retranspose_half(p, kb, p.d["lnffn"], half=0)
    emit_ffn_experts(p, kb, src, dst, scale, half=0)


def emit_output_done(p, kb):
    waits = []
    for slot, sem in kb.slot_sems.items():
        if slot[0] == "dsync" and ((isinstance(slot[1], tuple) and slot[1][0] == "out")
                                   or slot[1] in ("dbg", "pd", "out")):
            waits.append((sem, kb.counts[sem]))
    kb.streams["gpsimd"].append(_Rec(None, waits, None))


def _emit_all(p, kb, cfg):
    load_constants(p, kb)
    if cfg.only == "ffn1":
        load_ffn_weights(p, kb)
    if cfg.only == "ffn1":
        emit_ffn_sublayer(p, kb, p.d["x"], p.d["y"], p.d["cin1"], p.d["cout1"], scale=0.5)
    elif cfg.only == "attncore":
        load_attn_small_weights(p, kb)
        load_attn_big_weights(p, kb)
        emit_attn_core(p, kb, p.d["x"])
        rh = R(p, "h_dram")
        for ti in range(cfg.NT):
            kb.dma("dsync", "cp", lambda e, ti=ti: e.dma_start(
                out=p.d["y"][ti * 128:(ti + 1) * 128, :], in_=p.d["hbuf"][ti * 128:(ti + 1) * 128, :]),
                reads=[rh], writes=[R(p, "dst_dram")])
    elif cfg.only == "attn":
        load_attn_small_weights(p, kb)
        load_attn_big_weights(p, kb)
        emit_attn_core(p, kb, p.d["x"])
        kb.barrier()
        emit_attn_moe(p, kb, p.d["x"], p.d["y"])
    elif cfg.only == "full":
        load_attn_small_weights(p, kb)
        load_ffn_weights(p, kb)
        emit_ffn_sublayer(p, kb, p.d["x"], p.d["r1"], p.d["cin1"], p.d["cout1"], scale=0.5)
        kb.barrier()
        load_attn_big_weights(p, kb)
        emit_attn_core(p, kb, p.d["r1"])
        kb.barrier()
        emit_attn_moe(p, kb, p.d["r1"], p.d["r2"])
        kb.barrier()
        load_ffn_weights(p, kb)
        emit_ffn_sublayer(p, kb, p.d["r2"], p.d["y"], p.d["cin2"], p.d["cout2"], scale=0.5)
    elif cfg.only == "l1":
        for hf in (0, 1):
            emit_ln_half(p, kb, p.d["x"], hf, mode="ln", do_gate=("ffn",))
        emit_pden_out(p, kb)
    elif cfg.only == "l2":
        load_attn_small_weights(p, kb)
        load_ffn_weights(p, kb)
        emit_ffn_sublayer(p, kb, p.d["x"], p.d["r1"], None, None, scale=0.5,
                          denom_from_inputs=True)
        if cfg.trunc != "ffn":
            load_attn_big_weights(p, kb)
            if cfg.trunc != "wts":
                emit_attn_core(p, kb, p.d["r1"])
        if cfg.trunc is None or cfg.trunc == "moe" or (
                isinstance(cfg.trunc, str) and cfg.trunc.startswith("moe_")):
            mt = cfg.trunc[4:] if (isinstance(cfg.trunc, str)
                                   and cfg.trunc.startswith("moe_")) else None
            emit_attn_moe(p, kb, p.d["r1"], p.d["r2x"], moe_trunc=mt)
        if cfg.trunc is None:
            for hf in (0, 1):
                emit_ln_half(p, kb, p.d["r2x"], hf, mode="ln", do_gate=("ffn",),
                             src_res=[R(p, "dst_h", 2)[hf]])
            emit_pden_out(p, kb)
    elif cfg.only == "l3":
        load_ffn_weights(p, kb)
        emit_ffn_sublayer(p, kb, p.d["x"], p.d["y"], None, None, scale=0.5,
                          denom_from_inputs=True)
    else:
        raise NotImplementedError(cfg.only)
    if cfg.debug:
        t = p.t
        kb.dma("dsync", "dbg", lambda e: e.dma_start(out=p.d["dbg_mv"][:, :], in_=t["mv"][:, 0:cfg.NT * 8]),
               reads=[R(p, "mv")], writes=[R(p, "dst_dram")])
        kb.dma("dsync", "dbg", lambda e: e.dma_start(out=p.d["dbg_sel"][:, :], in_=t["sel"][:, 0:cfg.NT * 8]),
               reads=[R(p, "sel")], writes=[R(p, "dst_dram")])
        kb.dma("dsync", "dbg", lambda e: e.dma_start(out=p.d["dbg_route"][:, :], in_=t["route"][:, 0:cfg.NT * 8]),
               reads=[R(p, "route")], writes=[R(p, "dst_dram")])
        kb.dma("dsync", "dbg", lambda e: e.dma_start(out=p.d["dbg_xnfm"][:, :],
                                                     in_=t["xnfm"][:, 0:4 * cfg.T]),
               reads=R(p, "xnfm_t", (cfg.NT // 2) * 8), writes=[R(p, "dst_dram")])
    emit_output_done(p, kb)


def _measure_splits(nc, kb):
    streams = {"SP": [], "PL": []}
    for ins in nc.all_instructions():
        nm = ins.__class__.__name__
        eng = getattr(getattr(ins, "engine", None), "value", "")
        key = "SP" if eng == "SP" else ("PL" if eng == "Pool" else None)
        if key is None:
            continue
        if nm == "InstDMACopy" or "Collective" in nm:
            streams[key].append("D")
        elif nm == "InstNoOp":
            streams[key].append("N")
    def group(seq):
        out = []
        cnt = 0
        for x in seq:
            if x == "D":
                cnt += 1
            else:
                out.append(max(cnt, 1))
                cnt = 0
        return out
    sp = group(streams["SP"])
    pl = group(streams["PL"])
    incs = {"dsync": sp, "dgp": [], "coll": []}
    assert len(pl) == len(kb.pl_order), (len(pl), len(kb.pl_order))
    for q, n in zip(kb.pl_order, pl):
        incs[q].append(n)
    return incs


def build_program(cfg):
    nc1 = bass.Bass("TRN2", target_bir_lowering=False)
    p1 = declare_io(nc1, cfg)
    p1.nc = nc1
    with ExitStack() as es:
        alloc_sbuf(p1, nc1, es)
        alloc_psum(p1, nc1, es)
        kb1 = KB(nc1, es)
        p1.kb = kb1
        _emit_all(p1, kb1, cfg)
        block = es.enter_context(nc1.Block())
        kb1.replay(block, sentinels=True)
    incs = _measure_splits(nc1, kb1)
    nc = bass.Bass("TRN2", target_bir_lowering=False)
    p = declare_io(nc, cfg)
    p.nc = nc
    es = ExitStack()
    with es:
        alloc_sbuf(p, nc, es)
        alloc_psum(p, nc, es)
        kb = KB(nc, es, dma_incs=incs)
        p.kb = kb
        _emit_all(p, kb, cfg)
        block = es.enter_context(nc.Block())
        kb.replay(block)
    return nc, p


def shard_inputs(inputs, cfg):
    c = host_constants()
    x = np.ascontiguousarray(np.asarray(inputs["L0_attn_hidden_states"], dtype=np.float32))
    B_total = x.shape[0]
    per = B_total // cfg.ncores
    name_map = {
        "aw_ln1": "L0_attn_w_ln1", "aw_ln2": "L0_attn_w_ln2", "aW_qkvz": "L0_attn_W_qkvz",
        "aW_ba": "L0_attn_W_ba", "aconv_w": "L0_attn_conv_w", "adt_bias": "L0_attn_dt_bias",
        "aA_log": "L0_attn_A_log", "agate_norm_w": "L0_attn_gate_norm_w",
        "aW_out": "L0_attn_W_out", "aW_router": "L0_attn_W_router",
        "aW_gate_e": "L0_attn_W_gate_e", "aW_up_e": "L0_attn_W_up_e",
        "aW_down_e": "L0_attn_W_down_e",
        "fw_gate_w": "L0_ffn_w_gate_w", "fw_gate_b": "L0_ffn_w_gate_b",
        "fW1": "L0_ffn_W1", "fb1": "L0_ffn_b1", "fW2": "L0_ffn_W2", "fb2": "L0_ffn_b2",
    }
    base = {k: np.ascontiguousarray(np.asarray(inputs[v], dtype=np.float32))
            for k, v in name_map.items()}
    base.update(c)
    in_maps = []
    for ci in range(cfg.ncores):
        m = dict(base)
        m["x"] = np.ascontiguousarray(
            x[ci * per:(ci + 1) * per].reshape(per * x.shape[1], x.shape[2]))
        in_maps.append(m)
    return in_maps


def kernel(**inputs):
    S, NC = 2048, 8
    cfg1 = Cfg(S=S, B=2, ncores=NC, B_total=16, only="l1")
    cfg2 = Cfg(S=S, B=2, ncores=NC, B_total=16, only="l2")
    cfg3 = Cfg(S=S, B=2, ncores=NC, B_total=16, only="l3")
    base_maps = shard_inputs(inputs, cfg1)
    nc1, _ = build_program(cfg1)
    res1 = run_bass_kernel_spmd(nc1, base_maps, core_ids=list(range(NC)))
    pd1 = np.ascontiguousarray(np.stack([np.asarray(r["pden_out"]) for r in res1.results]))
    nc2, _ = build_program(cfg2)
    maps2 = [dict(m, pdens=pd1) for m in base_maps]
    res2 = run_bass_kernel_spmd(nc2, maps2, core_ids=list(range(NC)))
    pd2 = np.ascontiguousarray(np.stack([np.asarray(r["pden_out"]) for r in res2.results]))
    nc3, _ = build_program(cfg3)
    maps3 = [dict(m, x=np.ascontiguousarray(np.asarray(res2.results[ci]["r2x"])), pdens=pd2)
             for ci, m in enumerate(base_maps)]
    res3 = run_bass_kernel_spmd(nc3, maps3, core_ids=list(range(NC)))
    outs = [np.asarray(r["y"]).reshape(cfg3.B, cfg3.S, 1024) for r in res3.results]
    return np.ascontiguousarray(np.concatenate(outs, axis=0))



def load_attn_small_weights(p, kb):
    t, d = p.t, p.d
    rc = R(p, "consts")
    kb.dma("dgp", "cw", lambda e: e.dma_start(
        out=t["wba"][:].rearrange("p (dt c) -> p dt c", c=16),
        in_=d["aW_ba"].rearrange("(dt p) c -> p dt c", p=128)), writes=[rc])
    kb.dma("dgp", "cw", lambda e: e.dma_start(
        out=t["wrt"][:].rearrange("p (dt e) -> p dt e", e=8),
        in_=d["aW_router"].rearrange("(dt p) e -> p dt e", p=128)), writes=[rc])
    kb.dma("dsync", "c", lambda e: e.dma_start(
        out=t["convw"][:].rearrange("p (ct j) -> p ct j", j=4),
        in_=d["aconv_w"].rearrange("(ct p) j -> p ct j", p=128)), writes=[rc])
    for nm, dn in (("wln1c", "aw_ln1"), ("wln2c", "aw_ln2")):
        kb.dma("dsync", "c", lambda e, dn=dn: e.dma_start(
            out=t["b1st"][0:8, 0:128], in_=d[dn].rearrange("(dt p) -> dt p", p=128)),
            writes=[rc])
        kb.op("tensor", lambda e: e.transpose(p.ps["misc"][0:128, 0:8], t["b1st"][0:8, 0:128],
                                              t["ident_f32"][0:8, 0:8]),
              reads=[rc], writes=[R(p, "ps_misc")])
        kb.op("vector", lambda e, nm=nm: e.tensor_scalar(t[nm][:], p.ps["misc"][0:128, 0:8],
                                                         1.0, None, ALU.add),
              reads=[R(p, "ps_misc")], writes=[rc])
    kb.dma("dsync", "c", lambda e: e.dma_start(out=t["gnwc"][:, 0:1], in_=d["agate_norm_w"][:, None]),
           writes=[rc])
    kb.dma("dsync", "c", lambda e: e.dma_start(out=t["hcol8"][0:8, 0:1], in_=d["adt_bias"][:, None]),
           writes=[rc])
    kb.dma("dsync", "c", lambda e: e.dma_start(out=t["hcol8"][0:8, 2:3], in_=d["aA_log"][:, None]),
           writes=[rc])
    kb.op("scalar", lambda e: e.activation(t["hcol8"][0:8, 3:4], t["hcol8"][0:8, 2:3], AF.Exp),
          reads=[rc], writes=[rc])
    kb.op("vector", lambda e: e.tensor_scalar(t["hcol8"][0:8, 1:2], t["hcol8"][0:8, 3:4], -1.0,
                                              None, ALU.mult),
          reads=[rc], writes=[rc])


def load_attn_big_weights(p, kb):
    t, d = p.t, p.d
    rw = R(p, "attn_w")
    kb.dma("dgp", "aw", lambda e: e.dma_start(
        out=t["wbuf"][:, 0:24576].rearrange("p (dt c) -> p dt c", c=3072),
        in_=d["aW_qkvz"].rearrange("(dt p) c -> p dt c", p=128)),
        writes=[rw, R(p, "ffn_w")])
    kb.dma("dgp", "aw", lambda e: e.dma_start(
        out=t["wbuf"][:, 24576:32768].rearrange("p (vt c) -> p vt c", c=1024),
        in_=d["aW_out"].rearrange("(vt p) c -> p vt c", p=128)),
        writes=[rw, R(p, "ffn_w")])


def wqkvz_tile(p, oc, dt):
    c = dt * 3072 + oc * 128
    return p.t["wbuf"][:, c:c + 128]


def wout_tile(p, vt, dt):
    c = 24576 + vt * 1024 + dt * 128
    return p.t["wbuf"][:, c:c + 128]


def raw_tile(p, ct):
    return p.t["accbuf"][:, ct * 259: ct * 259 + 259]


def qkvc_tile(p, ct):
    return p.t["accbuf"][:, 16 * 259 + ct * 256: 16 * 259 + (ct + 1) * 256]


def emit_attn_block_prologue(p, kb, b, blk):
    t, cfg, ps = p.t, p.cfg, p.ps
    TH = cfg.T // 2
    NTH = cfg.NT // 2
    loc0 = blk * 256
    nblk = cfg.S // 256
    rfm = R(p, "xnfm_t", NTH * 8)
    rc = R(p, "consts")
    rw = R(p, "attn_w")
    rraw = R(p, "raw_t", 16)
    rz = R(p, "z_t", 8)
    rphs = R(p, "ps_h1")
    rph = [rphs, rphs]
    rpg = R(p, "ps_g")
    rba = R(p, "ba_r")
    rrF = R(p, "rowsF")
    rrB = R(p, "rowsB")
    rqc = R(p, "qkvc_t", 16)
    rsq = R(p, "sq_pool")
    for ct in range(16):
        if blk == 0:
            kb.op("vector", lambda e, ct=ct: e.memset(raw_tile(p, ct)[:, 0:3], 0.0),
                  reads=[], writes=[rraw[ct]])
        else:
            kb.op("vector", lambda e, ct=ct: e.tensor_copy(raw_tile(p, ct)[:, 0:3],
                                                           raw_tile(p, ct)[:, 256:259]),
                  reads=[rraw[ct]], writes=[rraw[ct]])
    rpm2 = R(p, "ps_misc")
    for oc in range(24):
        if oc % 2 == 0:
            ph = ps["h1"][0:128, 0:256]
        else:
            ph = ps["misc"][0:128, 0:256]
        for dt in range(8):
            kb.op("tensor", lambda e, oc=oc, dt=dt, o=ph, loc0=loc0: e.matmul(
                o, wqkvz_tile(p, oc, dt),
                t["xnfm"][:, dt * TH + loc0: dt * TH + loc0 + 256],
                start=(dt == 0), stop=(dt == 7)),
                reads=[rw, rfm[(loc0 // 128) * 8 + dt], rfm[(loc0 // 128 + 1) * 8 + dt]],
                writes=[rph[0] if oc % 2 == 0 else rpm2])
        if oc < 16:
            dst = raw_tile(p, oc)[:, 3:259]
            eng = "vector" if oc % 2 == 0 else "scalar"
            rsrc = rph[0] if oc % 2 == 0 else rpm2
            if eng == "vector":
                kb.op(eng, lambda e, a=dst, o=ph: e.tensor_copy(a, o),
                      reads=[rsrc], writes=[rraw[oc]])
            else:
                kb.op(eng, lambda e, a=dst, o=ph: e.copy(a, o),
                      reads=[rsrc], writes=[rraw[oc]])
        else:
            zt = oc - 16
            dst = t["zbuf"][:, zt * 256:(zt + 1) * 256]
            rsrc = rph[0] if oc % 2 == 0 else rpm2
            kb.op("scalar", lambda e, a=dst, o=ph: e.activation(a, o, AF.Silu),
                  reads=[rsrc], writes=[rz[zt]])
    for part in range(2):
        pg = ps["g"][0:8, part * 256:(part + 1) * 256]
        for dt in range(8):
            kb.op("tensor", lambda e, part=part, dt=dt, o=pg, loc0=loc0: e.matmul(
                o, t["wba"][:, dt * 16 + part * 8: dt * 16 + part * 8 + 8],
                t["xnfm"][:, dt * TH + loc0: dt * TH + loc0 + 256],
                start=(dt == 0), stop=(dt == 7)),
                reads=[rc, rfm[(loc0 // 128) * 8 + dt], rfm[(loc0 // 128 + 1) * 8 + dt]],
                writes=[R(p, "ps_g")])
    kb.op("scalar", lambda e: e.activation(t["rowsB"][0:8, 0:256], ps["g"][0:8, 0:256],
                                           AF.Sigmoid),
          reads=[rpg], writes=[rrB])
    gF = lambda q: t["rowsF"][0:8, q * 256:(q + 1) * 256]
    gB = lambda q: t["rowsB"][0:8, q * 256:(q + 1) * 256]
    kb.op("scalar", lambda e: e.activation(gF(0), ps["g"][0:8, 256:512], AF.Relu,
                                           bias=t["hcol8"][0:8, 0:1]),
          reads=[rpg, rc], writes=[rrF])
    kb.op("scalar", lambda e: e.activation(gF(1), ps["g"][0:8, 256:512], AF.Abs,
                                           bias=t["hcol8"][0:8, 0:1]),
          reads=[rpg, rc], writes=[rrF])
    kb.op("scalar", lambda e: e.activation(gF(1), gF(1), AF.Exp, scale=-1.0),
          reads=[rrF], writes=[rrF])
    kb.op("scalar", lambda e, bc=ccol(p, kb, 1.0): e.activation(gF(1), gF(1), AF.Ln,
                                                                bias=bc[0:8, :]),
          reads=[rrF, rc], writes=[rrF])
    kb.op("vector", lambda e: e.tensor_add(gF(0), gF(0), gF(1)), reads=[rrF], writes=[rrF])
    kb.op("vector", lambda e: e.tensor_scalar(gF(0), gF(0), t["hcol8"][0:8, 1:2], None, ALU.mult),
          reads=[rrF, rc], writes=[rrF])
    for c in range(2):
        kb.op("vector", lambda e, c=c: e.tensor_tensor_scan(
            gF(1)[:, c * 128:(c + 1) * 128], t["ones_sq_bf"][0:8, 0:128],
            gF(0)[:, c * 128:(c + 1) * 128], 0.0, ALU.mult, ALU.add),
            reads=[rrF, rc], writes=[rrF])
    kb.op("scalar", lambda e: e.activation(gF(2), gF(1), AF.Exp), reads=[rrF], writes=[rrF])
    for c in range(2):
        kb.op("vector", lambda e, c=c: e.tensor_scalar(
            gF(3)[:, c * 128:(c + 1) * 128], gF(1)[:, c * 128:(c + 1) * 128],
            gF(1)[:, c * 128 + 127: c * 128 + 128], None, ALU.subtract),
            reads=[rrF], writes=[rrF])
        kb.op("vector", lambda e, c=c: e.tensor_scalar(
            gF(4)[:, c * 128:(c + 1) * 128], t["ones_sq_bf"][0:8, 0:128],
            gF(2)[:, c * 128 + 127: c * 128 + 128], None, ALU.mult),
            reads=[rrF, rc], writes=[rrF])
    kb.op("scalar", lambda e: e.activation(gF(3), gF(3), AF.Exp, scale=-1.0),
          reads=[rrF], writes=[rrF])
    rN_ = R(p, "pw_n")
    rM_ = R(p, "pw_m")
    Q0 = 16 * 259
    acc = t["accbuf"][:, Q0:Q0 + 4096]

    def raw_w(j, h0, n):
        return (t["accbuf"][:, 0:16 * 259].rearrange("p (t x) -> p t x", x=259)
                [:, h0:h0 + n, j:j + 256])

    def w_w(j, h0, n):
        return (t["convw"][:, 0:64].rearrange("p (t j) -> p t j", j=4)
                [:, h0:h0 + n, j:j + 1].to_broadcast([128, n, 256]))

    rall = rraw[0:16]
    kb.op("gpsimd", lambda e: e.tensor_mul(
        acc.rearrange("p (t x) -> p t x", x=256), raw_w(0, 0, 16), w_w(0, 0, 16)),
        reads=rall + [rc], writes=rqc[0:16])
    for j in (1, 2, 3):
        for h in range(2):
            tmp = t["poolB"][:, 3072:5120]
            kb.op("gpsimd", lambda e, j=j, h=h, tmp=tmp: e.tensor_mul(
                tmp.rearrange("p (t x) -> p t x", x=256), raw_w(j, 8 * h, 8), w_w(j, 8 * h, 8)),
                reads=rall + [rc], writes=[rN_, rM_])
            kb.op("gpsimd", lambda e, h=h, tmp=tmp: e.tensor_add(
                acc[:, h * 2048:(h + 1) * 2048], acc[:, h * 2048:(h + 1) * 2048], tmp),
                reads=[rN_, rM_] + rqc[8 * h:8 * h + 8], writes=rqc[8 * h:8 * h + 8])
    for h in range(2):
        kb.op("scalar", lambda e, h=h: e.activation(
            acc[:, h * 2048:(h + 1) * 2048], acc[:, h * 2048:(h + 1) * 2048], AF.Silu),
            reads=rqc[8 * h:8 * h + 8], writes=rqc[8 * h:8 * h + 8])
    kb.op("gpsimd", lambda e: e.tensor_mul(t["poolB"][:, 0:2048],
                                           acc[:, 0:2048], acc[:, 0:2048]),
          reads=rqc[0:8], writes=[rsq, R(p, "pw_a"), R(p, "pw_kts")])
    for grp in range(2):
        pg = ps["g"][0:8, grp * 256:(grp + 1) * 256]
        for kh in range(4):
            i = grp * 4 + kh
            kb.op("tensor", lambda e, i=i, kh=kh, o=pg: e.matmul(
                o, t["colsel2"][:, kh * 8:(kh + 1) * 8], t["poolB"][:, i * 256:(i + 1) * 256],
                start=(kh == 0), stop=(kh == 3)),
                reads=[rsq, rc], writes=[R(p, "ps_g")])
    kb.op("scalar", lambda e, bc=ccol(p, kb, 128 * EPS6): e.activation(
        gF(0), ps["g"][0:8, 0:256], AF.Sqrt, bias=bc[0:8, :], scale=128.0),
        reads=[rpg, rc], writes=[rrF])
    kb.op("vector", lp(p, lambda e: e.reciprocal(gB(1), gF(0))), reads=[rrF], writes=[rrB])
    kb.op("scalar", lambda e, bc=ccol(p, kb, EPS6): e.activation(
        gF(0), ps["g"][0:8, 256:512], AF.Sqrt, bias=bc[0:8, :]),
        reads=[rpg, rc], writes=[rrF])
    kb.op("vector", lp(p, lambda e: e.reciprocal(gB(2), gF(0))), reads=[rrF], writes=[rrB])
    kb.op("vector", lambda e: e.tensor_mul(gB(3), gB(0), gB(2)), reads=[rrB], writes=[rrB])
    kb.op("vector", lambda e: e.tensor_mul(gB(4), gB(3), gF(2)), reads=[rrB, rrF], writes=[rrB])
    kb.op("vector", lambda e: e.tensor_mul(gB(5), gB(2), gF(3)), reads=[rrB, rrF], writes=[rrB])
    kb.op("vector", lambda e: e.tensor_mul(gB(6), gB(1), gF(2)), reads=[rrB, rrF], writes=[rrB])
    rct = R(p, "colsT")
    rpm = R(p, "ps_misc")
    rpTs2 = R(p, "ps_T")
    rpT = [rpTs2] * 4
    for c in range(2):
        fq = [(0, gF(1)), (1, gF(4))]
        for qi, (q, src) in enumerate(fq):
            pm = ps["misc"][0:128, qi * 8: qi * 8 + 8]
            kb.op("tensor", lambda e, src=src, c=c, o=pm: e.transpose(
                o, src[:, c * 128:(c + 1) * 128], t["ident_f32"][0:8, 0:8]),
                reads=[rrF, rc], writes=[rpm])
            kb.op("vector", lambda e, c=c, q=q, o=pm: e.tensor_copy(
                t["colsT"][:, (c * 8 + q) * 8:(c * 8 + q) * 8 + 8], o),
                reads=[rpm], writes=[rct])
        bq = [(2, gB(2)), (3, gB(3)), (4, gB(0)), (5, gB(4)), (6, gB(5))]
        for qi, (q, src) in enumerate(bq):
            pT = ps["T"][0:128, 512 + qi * 8: 512 + qi * 8 + 8]
            kb.op("tensor", lambda e, src=src, c=c, o=pT: e.transpose(
                o, src[:, c * 128:(c + 1) * 128], t["ident_bf"][0:8, 0:8]),
                reads=[rrB, rc], writes=[rpT[3]])
            kb.op("vector", lambda e, c=c, q=q, o=pT: e.tensor_copy(
                t["colsT"][:, (c * 8 + q) * 8:(c * 8 + q) * 8 + 8], o),
                reads=[rpT[3]], writes=[rct])


def col_ap(p, c, q, h):
    return p.t["colsT"][:, (c * 8 + q) * 8 + h:(c * 8 + q) * 8 + h + 1]


PB_QS = 0
PB_KTOK = 128
PB_AKROW = 256
PB_AQBC = 384
PB_QS2 = 512
PB_ATIL = 768
PB_NPOW = 2816
PB_MPOW = 4352
PB_IMN = 1536
PB_R = 1792
PB_VTOK = 2048
PB_Y0 = 2304
PB_Y1 = 2560
PB_SQO = 5632
PB_BAKBC = 5888
PB_SCR = 1024
PB_KTS = 1280

PF_DIFF = 0
PF_DX = 128
PF_DM = 384
PF_GKQS = 640
PF_GKKS = 768
PF_SCR = 896


def npow(p, j, slot=None):
    c = PB_NPOW + ((j - 1) % 6) * 256
    if slot is None:
        return p.t["poolB"][:, c:c + 256]
    return p.t["poolB"][:, c + slot * 128: c + slot * 128 + 128]


def mpow(p, j, slot=None):
    c = PB_MPOW + ((j - 1) % 5) * 256
    if slot is None:
        return p.t["poolB"][:, c:c + 256]
    return p.t["poolB"][:, c + slot * 128: c + slot * 128 + 128]


def pb(p, off, slot=None, w=128):
    if slot is None:
        return p.t["poolB"][:, off:off + 256]
    return p.t["poolB"][:, off + slot * w: off + slot * w + w]


def pf(p, off, slot=None, w=128):
    if slot is None:
        return p.t["poolF"][:, off:off + 256]
    return p.t["poolF"][:, off + slot * w: off + slot * w + w]


def bv2(ap):
    return ap.unsqueeze(1).to_broadcast([128, 2, 128])


def r2(ap256):
    return ap256.rearrange("p (s q) -> p s q", s=2)


def emit_delta_chunk(p, kb, b, blk, c):
    t, cfg, ps = p.t, p.cfg, p.ps
    TH = cfg.T // 2
    NTH = cfg.NT // 2
    loc0 = blk * 256
    rc = R(p, "consts")
    rqc = R(p, "qkvc_t", 16)
    rz = R(p, "z_t", 8)
    rrF = R(p, "rowsF")
    rrB = R(p, "rowsB")
    rct = R(p, "colsT")
    rst = R(p, "states", 8)
    rfm = R(p, "xnfm_t", NTH * 8)
    rpm = R(p, "ps_misc")
    rpTs = R(p, "ps_T")
    rpT = [rpTs] * 4
    rpo = {0: R(p, "ps_o", 4)[0], 1: R(p, "ps_o", 4)[1]}
    rpO = [R(p, "ps_o", 4)[2], R(p, "ps_o", 4)[3]]
    rphs = R(p, "ps_h1")
    rph = [rphs, rphs]
    rpg = R(p, "ps_g")
    rB = R(p, "poolB_delta")
    rF = R(p, "poolF_delta")
    rY = R(p, "Y_pair")
    sl = slice(c * 128, (c + 1) * 128)

    def mmT(emit, reads, writes):
        kb.op("tensor", emit, reads=reads, writes=writes)

    for kh in range(4):
        q_t = qkvc_tile(p, kh)[:, sl]
        k_t = qkvc_tile(p, 4 + kh)[:, sl]
        pTb = ps["h1"][0:128, 256:384]
        mmT(lambda e, kh=kh, o=pTb: e.matmul(
            o, t["rowsel8"][0:8, (2 * kh) * 128:(2 * kh) * 128 + 128],
            t["rowsB"][0:8, 1 * 256 + c * 128: 1 * 256 + (c + 1) * 128],
            start=True, stop=True), [rrB, rc], [rphs])
        kb.op("scalar", lambda e, o=pTb: e.copy(pb(p, PB_AQBC, 0), o), reads=[rphs],
              writes=[rB, R(p, "sq_pool")])
        kb.op("vector", lambda e, a=q_t: e.tensor_mul(pb(p, PB_QS, 0), a, pb(p, PB_AQBC, 0)),
              reads=[rqc[kh], rB], writes=[rB])
        mmT(lambda e, a=k_t: e.matmul(ps["misc"][0:128, 0:128], a, pb(p, PB_QS, 0),
                                      start=True, stop=True), [rqc[4 + kh], rB], [rpm])
        mmT(lambda e, a=k_t: e.matmul(ps["misc"][0:128, 128:256], a, a,
                                      start=True, stop=True), [rqc[4 + kh]], [rpm])
        for slot in range(2):
            vh = 2 * kh + slot
            mmT(lambda e, a=k_t, vh=vh, slot=slot: e.matmul(
                ps["misc"][0:128, 256 + slot * 128: 256 + slot * 128 + 128], a,
                t["states"][:, vh * 128:(vh + 1) * 128],
                start=True, stop=True), [rqc[4 + kh], rst[vh]], [rpm])
        pT2 = ps["T"][0:128, 640:768]
        mmT(lambda e, a=k_t, o=pT2: e.transpose(o, a, t["ident_bf"][:]),
            [rqc[4 + kh], rc], [rpTs])
        kb.op("scalar", lambda e, o=pT2: e.copy(pb(p, PB_KTOK, 0), o), reads=[rpTs], writes=[rB])
        pT3 = ps["h1"][0:128, 256:384]
        mmT(lambda e, kh=kh, o=pT3: e.matmul(
            o, t["rowsel8"][0:8, (2 * kh) * 128:(2 * kh) * 128 + 128],
            t["rowsB"][0:8, 2 * 256 + c * 128: 2 * 256 + (c + 1) * 128],
            start=True, stop=True), [rrB, rc], [rphs])
        kb.op("scalar", lambda e, o=pT3: e.copy(pb(p, PB_AKROW, 0), o), reads=[rphs], writes=[rB])
        kb.op("vector", lambda e, kh=kh: e.tensor_scalar(
            pf(p, PF_GKQS, 0), ps["misc"][0:128, 0:128], col_ap(p, c, 2, 2 * kh), None, ALU.mult),
            reads=[rpm, rct], writes=[rF])
        kb.op("vector", lambda e, kh=kh: e.tensor_scalar(
            pf(p, PF_GKKS, 0), ps["misc"][0:128, 128:256], col_ap(p, c, 2, 2 * kh), None, ALU.mult),
            reads=[rpm, rct], writes=[rF])
        for slot in range(2):
            vh = 2 * kh + slot
            ph2 = ps["o1"][0:128, 256:384]
            mmT(lambda e, vh=vh, o=ph2: e.matmul(
                o, t["rowsel8f"][0:8, vh * 128:(vh + 1) * 128],
                t["rowsF"][0:8, 1 * 256 + c * 128: 1 * 256 + (c + 1) * 128],
                start=True, stop=True), [rrF, rc], [rpo[1]])
            kb.op("vector", lambda e, vh=vh, o=ph2: e.tensor_scalar(
                pf(p, PF_DIFF, 0), o, col_ap(p, c, 0, vh), None, ALU.subtract),
                reads=[rpo[1], rct], writes=[rF])
            kb.op("vector", lambda e: e.tensor_scalar_min(pf(p, PF_SCR, 0), pf(p, PF_DIFF, 0), 0.0),
                  reads=[rF], writes=[rF])
            kb.op("scalar", lambda e, slot=slot: e.activation(pf(p, PF_DX, slot), pf(p, PF_SCR, 0),
                                                              AF.Exp),
                  reads=[rF], writes=[rF])
            kb.op("vector", lambda e: e.tensor_scalar_max(pf(p, PF_SCR, 0), pf(p, PF_DIFF, 0), 0.0),
                  reads=[rF], writes=[rF])
            kb.op("scalar", lambda e, slot=slot: e.activation(pf(p, PF_DM, slot), pf(p, PF_SCR, 0),
                                                              AF.Exp, scale=-1.0),
                  reads=[rF], writes=[rF])
            pT4 = ps["h1"][0:128, 256:384]
            mmT(lambda e, vh=vh, o=pT4: e.matmul(
                o, t["rowsel8"][0:8, vh * 128:(vh + 1) * 128],
                t["rowsB"][0:8, 3 * 256 + c * 128: 3 * 256 + (c + 1) * 128],
                start=True, stop=True), [rrB, rc], [rphs])
            kb.op("scalar", lambda e, slot=slot, o=pT4: e.copy(pb(p, PB_BAKBC, slot), o),
                  reads=[rphs], writes=[rB])
            kb.op("vector", lambda e, vh=vh, slot=slot: e.tensor_scalar(
                mpow(p, 1, slot), ps["misc"][0:128, 128:256], col_ap(p, c, 3, vh), None, ALU.mult),
                reads=[rpm, rct], writes=[rB])
            kb.op("vector", lambda e, vh=vh, slot=slot: e.tensor_scalar(
                pf(p, PF_SCR, slot, w=64) if False else pf(p, PF_SCR, 0),
                ps["misc"][0:128, 256 + slot * 128: 256 + slot * 128 + 128],
                col_ap(p, c, 5, vh), None, ALU.mult),
                reads=[rpm, rct], writes=[rF])
            pT5 = ps["T"][0:128, 512:640]
            mmT(lambda e, vh=vh, o=pT5: e.transpose(o, qkvc_tile(p, 8 + vh)[:, sl],
                                                    t["ident_bf"][:]),
                [rqc[8 + vh], rc], [rpTs])
            kb.op("scalar", lambda e, slot=slot, o=pT5: e.copy(pb(p, PB_VTOK, slot), o),
                  reads=[rpTs], writes=[rB])
            kb.op("vector", lambda e, vh=vh, slot=slot: e.scalar_tensor_tensor(
                pb(p, PB_R, slot), pb(p, PB_VTOK, slot), col_ap(p, c, 4, vh),
                pf(p, PF_SCR, 0), ALU.mult, ALU.subtract),
                reads=[rB, rct, rF], writes=[rB])
        kb.op("gpsimd", lambda e: e.tensor_mul(
            r2(npow(p, 1)), r2(pb(p, PB_BAKBC)), bv2(pf(p, PF_GKKS, 0))),
            reads=[rB, rF], writes=[rB])
        kb.op("vector", lambda e: e.tensor_mul(npow(p, 1), npow(p, 1), pf(p, PF_DX)),
              reads=[rB, rF], writes=[rB])
        kb.op("gpsimd", lambda e: e.tensor_mul(
            r2(npow(p, 1)), r2(npow(p, 1)), bv2(t["striu"][:])),
            reads=[rB, rc], writes=[rB])
        kb.op("gpsimd", lambda e: e.tensor_mul(
            r2(mpow(p, 1)), r2(mpow(p, 1)), bv2(pb(p, PB_AKROW, 0))),
            reads=[rB], writes=[rB])
        kb.op("vector", lambda e: e.tensor_mul(mpow(p, 1), mpow(p, 1), pf(p, PF_DM)),
              reads=[rB, rF], writes=[rB])
        kb.op("gpsimd", lambda e: e.tensor_mul(
            r2(mpow(p, 1)), r2(mpow(p, 1)), bv2(t["stril"][:])),
            reads=[rB, rc], writes=[rB])
        kb.op("vector", lambda e: e.tensor_mul(
            r2(pb(p, PB_ATIL)), r2(pf(p, PF_DX)), bv2(pf(p, PF_GKQS, 0))),
            reads=[rF], writes=[rB])
        kb.op("gpsimd", lambda e: e.tensor_mul(
            r2(pb(p, PB_ATIL)), r2(pb(p, PB_ATIL)), bv2(t["triu"][:])),
            reads=[rB, rc], writes=[rB])
        kb.op("vector", lambda e: e.tensor_tensor(
            r2(pb(p, PB_IMN)), bv2(t["ident_bf"][:]), r2(npow(p, 1)), ALU.subtract),
            reads=[rB, rc], writes=[rB])
        for lvl in range(2, 8):
            for slot in range(2):
                mmT(lambda e, lvl=lvl, slot=slot: e.matmul(
                    ps["o0"][0:128, slot * 128: slot * 128 + 128],
                    mpow(p, lvl - 1, slot), npow(p, lvl - 1, slot),
                    start=True, stop=True), [rB], [rpo[0]])
                if lvl <= 6:
                    mmT(lambda e, lvl=lvl, slot=slot: e.matmul(
                        ps["o0"][0:128, 256 + slot * 128: 256 + slot * 128 + 128],
                        npow(p, lvl - 1, slot), mpow(p, lvl - 1, slot),
                        start=True, stop=True), [rB], [rpo[0]])
            kb.op("scalar", lambda e, lvl=lvl: e.copy(npow(p, lvl), ps["o0"][0:128, 0:256]),
                  reads=[rpo[0]], writes=[rB])
            if lvl <= 6:
                kb.op("vector", lambda e, lvl=lvl: e.tensor_copy(mpow(p, lvl),
                                                                 ps["o0"][0:128, 256:512]),
                      reads=[rpo[0]], writes=[rB])
        ycur, ynxt = PB_Y0, PB_Y1
        for j in range(7, 1, -1):
            for slot in range(2):
                src = pb(p, PB_R if j == 7 else ycur, slot)
                mmT(lambda e, slot=slot, src=src: e.matmul(
                    ps["o1"][0:128, slot * 128: slot * 128 + 128], t["ident_bf"][:], src,
                    start=True, stop=False), [rB, rc], [rpo[1]])
                mmT(lambda e, j=j, slot=slot, src=src: e.matmul(
                    ps["o1"][0:128, slot * 128: slot * 128 + 128], npow(p, j, slot), src,
                    start=False, stop=True), [rB], [rpo[1]])
            eng = "scalar" if j % 2 == 0 else "vector"
            if eng == "scalar":
                kb.op(eng, lambda e, ynxt=ynxt: e.copy(pb(p, ynxt), ps["o1"][0:128, 0:256]),
                      reads=[rpo[1]], writes=[rY])
            else:
                kb.op(eng, lambda e, ynxt=ynxt: e.tensor_copy(pb(p, ynxt), ps["o1"][0:128, 0:256]),
                      reads=[rpo[1]], writes=[rY])
            ycur, ynxt = ynxt, ycur
        for slot in range(2):
            mmT(lambda e, slot=slot: e.matmul(
                ps["o1"][0:128, slot * 128: slot * 128 + 128], pb(p, PB_IMN, slot),
                pb(p, ycur, slot), start=True, stop=True), [rB, rY], [rpo[1]])
        kb.op("vector", lambda e, ynxt=ynxt: e.tensor_copy(pb(p, ynxt), ps["o1"][0:128, 0:256]),
              reads=[rpo[1]], writes=[rY])
        ufin = ynxt
        for slot in range(2):
            vh = 2 * kh + slot
            pT6 = ps["h1"][0:128, 256:384]
            mmT(lambda e, vh=vh, o=pT6: e.matmul(
                o, t["rowsel8"][0:8, vh * 128:(vh + 1) * 128],
                t["rowsB"][0:8, 6 * 256 + c * 128: 6 * 256 + (c + 1) * 128],
                start=True, stop=True), [rrB, rc], [rphs])
            kb.op("scalar", lambda e, slot=slot, o=pT6: e.copy(pb(p, PB_QS2, slot), o),
                  reads=[rphs], writes=[rB])
            kb.op("vector", lambda e, slot=slot, a=q_t: e.tensor_mul(
                pb(p, PB_QS2, slot), a, pb(p, PB_QS2, slot)),
                reads=[rqc[kh], rB], writes=[rB])
            pO = ps[f"o{2 + kh // 2}"][0:128, (vh % 4) * 128:(vh % 4) * 128 + 128]
            mmT(lambda e, vh=vh, slot=slot, o=pO: e.matmul(
                o, t["states"][:, vh * 128:(vh + 1) * 128], pb(p, PB_QS2, slot),
                start=True, stop=False), [rst[vh], rB], [rpO[kh // 2]])
            mmT(lambda e, slot=slot, o=pO: e.matmul(
                o, pb(p, ufin, slot), pb(p, PB_ATIL, slot),
                start=False, stop=True), [rY, rB], [rpO[kh // 2]])
            kb.op("vector", lambda e, vh=vh, slot=slot: e.tensor_scalar(
                pb(p, PB_SCR, slot), t["states"][:, vh * 128:(vh + 1) * 128],
                col_ap(p, c, 1, vh), None, ALU.mult),
                reads=[rst[vh], rct], writes=[rB])
            kb.op("vector", lambda e, vh=vh, slot=slot: e.tensor_scalar(
                pb(p, PB_KTS, slot), pb(p, PB_KTOK, 0), col_ap(p, c, 6, vh), None, ALU.mult),
                reads=[rB, rct], writes=[rB])
            phS = ps["h1"][0:128, slot * 128: slot * 128 + 128]
            mmT(lambda e, slot=slot, o=phS: e.matmul(
                o, t["ident_bf"][:], pb(p, PB_SCR, slot), start=True, stop=False),
                [rB, rc], [rph[0]])
            mmT(lambda e, slot=slot, o=phS: e.matmul(
                o, pb(p, PB_KTS, slot), pb(p, ufin, slot), start=False, stop=True),
                [rB, rY], [rph[0]])
            kb.op("scalar", lambda e, vh=vh, slot=slot, o=phS: e.copy(
                t["states"][:, vh * 128:(vh + 1) * 128], o),
                reads=[rph[0]], writes=[rst[vh]])
    for vh in range(8):
        pO = ps[f"o{2 + vh // 4}"][0:128, (vh % 4) * 128:(vh % 4) * 128 + 128]
        kb.op("scalar", lambda e, vh=vh, o=pO: e.activation(
            pb(p, PB_SQO, vh % 2), o, AF.Square),
            reads=[rpO[vh // 4]], writes=[rB])
        pgn = ps["g"][0:8, 0:128]
        mmT(lambda e, vh=vh: e.matmul(
            pgn, t["colsel1"][:, vh * 8:(vh + 1) * 8], pb(p, PB_SQO, vh % 2),
            start=(vh == 0), stop=(vh == 7)), [rB, rc], [rpg])
    kb.op("scalar", lambda e, bc=ccol(p, kb, EPS6): e.activation(
        t["rowsF"][0:8, 5 * 256 + c * 128: 5 * 256 + (c + 1) * 128], ps["g"][0:8, 0:128],
        AF.Sqrt, bias=bc[0:8, :], scale=1.0 / 128), reads=[rpg, rc], writes=[rrF])
    kb.op("vector", lp(p, lambda e: e.reciprocal(
        t["rowsB"][0:8, 7 * 256 + c * 128: 7 * 256 + (c + 1) * 128],
        t["rowsF"][0:8, 5 * 256 + c * 128: 5 * 256 + (c + 1) * 128])),
        reads=[rrF], writes=[rrB])
    for vh in range(8):
        pT7 = ps["h1"][0:128, 256:384]
        mmT(lambda e, vh=vh, o=pT7: e.matmul(
            o, t["rowsel8"][0:8, vh * 128:(vh + 1) * 128],
            t["rowsB"][0:8, 7 * 256 + c * 128: 7 * 256 + (c + 1) * 128],
            start=True, stop=True), [rrB, rc], [rphs])
        kb.op("scalar", lambda e, o=pT7: e.copy(pb(p, PB_AQBC, 0), o), reads=[rphs], writes=[rB])
        pO = ps[f"o{2 + vh // 4}"][0:128, (vh % 4) * 128:(vh % 4) * 128 + 128]
        kb.op("vector", lambda e, vh=vh, o=pO: e.tensor_scalar(
            pf(p, PF_SCR, 0), o, t["gnwc"][:, 0:1], None, ALU.mult),
            reads=[rpO[vh // 4], rc], writes=[rF])
        kb.op("vector", lambda e: e.tensor_mul(pf(p, PF_SCR, 0), pf(p, PF_SCR, 0),
                                               pb(p, PB_AQBC, 0)),
              reads=[rF, rB], writes=[rF])
        core_dst = t["xnfm"][:, vh * TH + loc0 + c * 128: vh * TH + loc0 + c * 128 + 128]
        kb.op("vector", lambda e, vh=vh, a=core_dst: e.tensor_tensor(
            a, pf(p, PF_SCR, 0), t["zbuf"][:, vh * 256 + c * 128: vh * 256 + (c + 1) * 128],
            ALU.mult),
            reads=[rF, rz[vh]], writes=[rfm[(loc0 // 128 + c) * 8 + vh]])


PW_A = 0
PW_KTS = 1024
PW_Y = 2048
PW_N = 3072
PW_M = 4096
PW_AT = 5120


def pw(p, off, w=1024):
    return p.t["poolB"][:, off:off + w]


def v8(ap, x=128):
    return ap.rearrange("p (v x) -> p v x", x=x)


def bvn(ap, n=8):
    return ap.unsqueeze(1).to_broadcast([128, n, 128])


def colv(p, c, q, h0=0, n=8):
    b = (c * 8 + q) * 8 + h0
    return p.t["colsT"][:, b:b + n].unsqueeze(2).to_broadcast([128, n, 128])


def colv4(p, c, q):
    b = (c * 8 + q) * 8
    return (p.t["colsT"][:, b:b + 8].rearrange("p (k s) -> p k s", s=2)
            .unsqueeze(3).to_broadcast([128, 4, 2, 128]))


def emit_delta_chunk2(p, kb, b, blk, c):
    t, cfg, ps = p.t, p.cfg, p.ps
    TH = cfg.T // 2
    NTH = cfg.NT // 2
    loc0 = blk * 256
    Q0 = 16 * 259
    rc = R(p, "consts")
    rqc = R(p, "qkvc_t", 16)
    rz = R(p, "z_t", 8)
    rrF = R(p, "rowsF")
    rrB = R(p, "rowsB")
    rct = R(p, "colsT")
    rst = R(p, "states", 8)
    rfm = R(p, "xnfm_t", NTH * 8)
    rpm = R(p, "ps_misc")
    rphs = R(p, "ps_h1")
    rpTs = R(p, "ps_T")
    rpg = R(p, "ps_g")
    rpo = R(p, "ps_o", 4)
    rsq = R(p, "sq_pool")
    rBc = R(p, "poolB_delta")
    rA = R(p, "pw_a")
    rK = R(p, "pw_kts")
    rY = R(p, "pw_y")
    rN = R(p, "pw_n")
    rM = R(p, "pw_m")
    rT = R(p, "pw_at")
    rPF = R(p, "poolF_delta")
    sl = slice(c * 128, (c + 1) * 128)
    pB, pF = t["poolB"], t["poolF"]
    states = t["states"]

    def mm(emit, reads, writes):
        kb.op("tensor", emit, reads=reads, writes=writes)

    def qch(h):
        a = t["accbuf"][:, Q0 + 512 * h: Q0 + 512 * h + 512]
        return (a.rearrange("p (k x) -> p k x", x=256)[:, :, sl]
                .unsqueeze(2).to_broadcast([128, 2, 2, 128]))

    ob = {0: ps["o0"], 1: ps["o1"], 2: ps["o2"], 3: ps["o3"]}
    hm = {0: ps["h1"], 1: ps["misc"]}
    rob = {0: rpo[0], 1: rpo[1], 2: rpo[2], 3: rpo[3]}
    rhm = {0: rphs, 1: rpm}

    for kh in range(4):
        mm(lambda e, kh=kh: e.matmul(
            ps["o0"][0:128, kh * 128:(kh + 1) * 128],
            t["rowsel8"][0:8, (2 * kh) * 128:(2 * kh) * 128 + 128],
            t["rowsB"][0:8, 1 * 256 + c * 128: 1 * 256 + (c + 1) * 128],
            start=(kh == 0), stop=(kh == 3)), [rrB, rc], [rpo[0]])
    qv = (t["accbuf"][:, Q0:Q0 + 1024].rearrange("p (k x) -> p k x", x=256)[:, :, sl])
    kb.op("vector", lambda e, qv=qv: e.tensor_tensor(
        v8(pB[:, PW_A:PW_A + 512]), qv, v8(ps["o0"][0:128, 0:512]), ALU.mult),
        reads=[rqc[0], rqc[1], rqc[2], rqc[3], rpo[0]], writes=[rA, rsq])

    for kh in range(4):
        mm(lambda e, kh=kh: e.transpose(
            ps["T"][0:128, kh * 128:(kh + 1) * 128], qkvc_tile(p, 4 + kh)[:, sl],
            t["ident_bf"][:]), [rqc[4 + kh], rc], [rpTs])
    kb.op("scalar", lambda e: e.copy(pB[:, PW_A + 512:PW_A + 1024], ps["T"][0:128, 0:512]),
          reads=[rpTs], writes=[rA, rsq])
    ktokd = (pB[:, PW_A + 512:PW_A + 1024].rearrange("p (k x) -> p k x", x=128)
             .unsqueeze(2).to_broadcast([128, 4, 2, 128]))
    kb.op("vector", lambda e, kd=ktokd: e.tensor_tensor(
        pw(p, PW_KTS).rearrange("p (k s x) -> p k s x", s=2, x=128), kd,
        colv4(p, c, 6), ALU.mult), reads=[rA, rct], writes=[rK, rsq])

    for vh in range(8):
        mm(lambda e, vh=vh: e.transpose(
            ps["T"][0:128, vh * 128:(vh + 1) * 128], qkvc_tile(p, 8 + vh)[:, sl],
            t["ident_bf"][:]), [rqc[8 + vh], rc], [rpTs])
    kb.op("vector", lambda e: e.tensor_copy(pB[:, PW_Y:PW_Y + 512], ps["T"][0:128, 0:512]),
          reads=[rpTs], writes=[rY, rBc])
    kb.op("scalar", lambda e: e.copy(pB[:, PW_Y + 512:PW_Y + 1024], ps["T"][0:128, 512:1024]),
          reads=[rpTs], writes=[rY, rBc])

    for vh in range(8):
        bank = ob[vh // 4]
        mm(lambda e, vh=vh, bank=bank: e.matmul(
            bank[0:128, (vh % 4) * 128:(vh % 4) * 128 + 128],
            t["rowsel8f"][0:8, vh * 128:(vh + 1) * 128],
            t["rowsF"][0:8, 1 * 256 + c * 128: 1 * 256 + (c + 1) * 128],
            start=(vh % 4 == 0), stop=(vh % 4 == 3)), [rrF, rc], [rob[vh // 4]])
    for h in range(2):
        kb.op("vector", lambda e, h=h: e.tensor_tensor(
            v8(pF[:, h * 512:(h + 1) * 512]), v8(ob[h][0:128, 0:512]),
            colv(p, c, 0, h0=4 * h, n=4), ALU.subtract),
            reads=[rob[h], rct], writes=[rPF])

    for kh in range(4):
        k_t = qkvc_tile(p, 4 + kh)[:, sl]
        for slot in range(2):
            vh = 2 * kh + slot
            r4 = vh % 4
            mm(lambda e, k_t=k_t, vh=vh, r4=r4: e.matmul(
                hm[vh // 4][0:128, r4 * 128:r4 * 128 + 128], k_t,
                pB[:, PW_A + (vh // 2) * 128: PW_A + (vh // 2) * 128 + 128],
                start=(r4 == 0), stop=(r4 == 3)),
                [rqc[4 + kh], rA], [rhm[vh // 4]])
            mm(lambda e, k_t=k_t, vh=vh, r4=r4: e.matmul(
                ob[vh // 4][0:128, r4 * 128:r4 * 128 + 128], k_t, k_t,
                start=(r4 == 0), stop=(r4 == 3)),
                [rqc[4 + kh]], [rob[vh // 4]])
            mm(lambda e, k_t=k_t, vh=vh, r4=r4: e.matmul(
                ob[2 + vh // 4][0:128, r4 * 128:r4 * 128 + 128], k_t,
                states[:, vh * 128:(vh + 1) * 128],
                start=(r4 == 0), stop=(r4 == 3)),
                [rqc[4 + kh], rst[vh]], [rob[2 + vh // 4]])

    kb.op("vector", lambda e: e.tensor_tensor(
        v8(pw(p, PW_Y)), v8(pw(p, PW_Y)), colv(p, c, 4), ALU.mult),
        reads=[rY, rct], writes=[rY, rBc])
    for h in range(2):
        kb.op("vector", lambda e, h=h: e.tensor_tensor(
            v8(pF[:, 1024 + h * 512: 1024 + (h + 1) * 512]), v8(ob[2 + h][0:128, 0:512]),
            colv(p, c, 5, h0=4 * h, n=4), ALU.mult),
            reads=[rob[2 + h], rct], writes=[rPF])
    kb.op("vector", lambda e: e.tensor_tensor(
        pw(p, PW_Y), pw(p, PW_Y), pF[:, 1024:2048], ALU.subtract),
        reads=[rY, rPF], writes=[rY, rBc])

    kb.op("vector", lambda e: e.tensor_scalar_min(pF[:, 1024:2048], pF[:, 0:1024], 0.0),
          reads=[rPF], writes=[rPF])
    kb.op("scalar", lambda e: e.activation(pw(p, PW_A), pF[:, 1024:2048], AF.Exp),
          reads=[rPF], writes=[rA, rsq])

    for vh in range(8):
        bank = ob[2 + vh // 4]
        mm(lambda e, vh=vh, bank=bank: e.matmul(
            bank[0:128, (vh % 4) * 128:(vh % 4) * 128 + 128],
            t["rowsel8"][0:8, vh * 128:(vh + 1) * 128],
            t["rowsB"][0:8, 3 * 256 + c * 128: 3 * 256 + (c + 1) * 128],
            start=(vh % 4 == 0), stop=(vh % 4 == 3)), [rrB, rc], [rob[2 + vh // 4]])
    for h in range(2):
        kb.op("vector", lambda e, h=h: e.tensor_tensor(
            v8(pB[:, PW_N + h * 512: PW_N + (h + 1) * 512]), v8(ob[h][0:128, 0:512]),
            colv(p, c, 2, h0=4 * h, n=4), ALU.mult),
            reads=[rob[h], rct], writes=[rN])
        kb.op("vector", lambda e, h=h: e.tensor_mul(
            v8(pB[:, PW_N + h * 512: PW_N + (h + 1) * 512]),
            v8(pB[:, PW_N + h * 512: PW_N + (h + 1) * 512]),
            v8(ob[2 + h][0:128, 0:512])),
            reads=[rob[2 + h], rN], writes=[rN])
    kb.op("vector", lambda e: e.tensor_tensor(
        pw(p, PW_N), pw(p, PW_N), pw(p, PW_A), ALU.mult), reads=[rN, rA], writes=[rN])
    kb.op("gpsimd", lambda e: e.tensor_mul(
        v8(pw(p, PW_N)), v8(pw(p, PW_N)), bvn(t["striu"][:])),
        reads=[rN, rc], writes=[rN])

    for h in range(2):
        kb.op("vector", lambda e, h=h: e.tensor_tensor(
            v8(pB[:, PW_AT + h * 512: PW_AT + (h + 1) * 512]), v8(hm[h][0:128, 0:512]),
            colv(p, c, 2, h0=4 * h, n=4), ALU.mult),
            reads=[rhm[h], rct], writes=[rT])
    kb.op("vector", lambda e: e.tensor_tensor(
        pw(p, PW_AT), pw(p, PW_AT), pw(p, PW_A), ALU.mult), reads=[rT, rA], writes=[rT])
    kb.op("gpsimd", lambda e: e.tensor_mul(
        v8(pw(p, PW_AT)), v8(pw(p, PW_AT)), bvn(t["triu"][:])),
        reads=[rT, rc], writes=[rT])

    kb.op("vector", lambda e: e.tensor_scalar_max(pF[:, 1024:2048], pF[:, 0:1024], 0.0),
          reads=[rPF], writes=[rPF])
    kb.op("scalar", lambda e: e.activation(pw(p, PW_A), pF[:, 1024:2048], AF.Exp, scale=-1.0),
          reads=[rPF], writes=[rA, rsq])
    for vh in range(8):
        bank = hm[vh // 4]
        mm(lambda e, vh=vh, bank=bank: e.matmul(
            bank[0:128, (vh % 4) * 128:(vh % 4) * 128 + 128],
            t["rowsel8"][0:8, vh * 128:(vh + 1) * 128],
            t["rowsB"][0:8, 2 * 256 + c * 128: 2 * 256 + (c + 1) * 128],
            start=(vh % 4 == 0), stop=(vh % 4 == 3)), [rrB, rc], [rhm[vh // 4]])
    for h in range(2):
        kb.op("vector", lambda e, h=h: e.tensor_tensor(
            v8(pB[:, PW_M + h * 512: PW_M + (h + 1) * 512]), v8(ob[h][0:128, 0:512]),
            colv(p, c, 3, h0=4 * h, n=4), ALU.mult),
            reads=[rob[h], rct], writes=[rM])
        kb.op("vector", lambda e, h=h: e.tensor_mul(
            v8(pB[:, PW_M + h * 512: PW_M + (h + 1) * 512]),
            v8(pB[:, PW_M + h * 512: PW_M + (h + 1) * 512]),
            v8(hm[h][0:128, 0:512])),
            reads=[rhm[h], rM], writes=[rM])
    kb.op("vector", lambda e: e.tensor_tensor(
        pw(p, PW_M), pw(p, PW_M), pw(p, PW_A), ALU.mult), reads=[rM, rA], writes=[rM])
    kb.op("gpsimd", lambda e: e.tensor_mul(
        v8(pw(p, PW_M)), v8(pw(p, PW_M)), bvn(t["stril"][:])),
        reads=[rM, rc], writes=[rM])

    for vh in range(8):
        mm(lambda e, vh=vh: e.matmul(
            hm[vh // 4][0:128, (vh % 4) * 128:(vh % 4) * 128 + 128],
            pB[:, PW_N + vh * 128: PW_N + vh * 128 + 128],
            pB[:, PW_Y + vh * 128: PW_Y + vh * 128 + 128],
            start=(vh % 4 == 0), stop=(vh % 4 == 3)), [rN, rY], [rhm[vh // 4]])
    for h in range(2):
        op = (lambda e, h=h: e.tensor_tensor(
            pB[:, PW_Y + h * 512: PW_Y + (h + 1) * 512],
            pB[:, PW_Y + h * 512: PW_Y + (h + 1) * 512],
            hm[h][0:128, 0:512], ALU.subtract))
        kb.op("vector", op, reads=[rY, rhm[h]], writes=[rY, rBc])

    KMAX = 4
    for k in range(1, KMAX + 1):
        for vh in range(8):
            mm(lambda e, vh=vh: e.matmul(
                ob[vh // 4][0:128, (vh % 4) * 128:(vh % 4) * 128 + 128],
                pB[:, PW_M + vh * 128: PW_M + vh * 128 + 128],
                pB[:, PW_N + vh * 128: PW_N + vh * 128 + 128],
                start=(vh % 4 == 0), stop=(vh % 4 == 3)), [rM, rN], [rob[vh // 4]])
            if k < KMAX:
                mm(lambda e, vh=vh: e.matmul(
                    ob[2 + vh // 4][0:128, (vh % 4) * 128:(vh % 4) * 128 + 128],
                    pB[:, PW_N + vh * 128: PW_N + vh * 128 + 128],
                    pB[:, PW_M + vh * 128: PW_M + vh * 128 + 128],
                    start=(vh % 4 == 0), stop=(vh % 4 == 3)), [rM, rN], [rob[2 + vh // 4]])
        kb.op("vector", lambda e: e.tensor_copy(pB[:, PW_N:PW_N + 512], ps["o0"][0:128, 0:512]),
              reads=[rpo[0]], writes=[rN])
        kb.op("scalar", lambda e: e.copy(pB[:, PW_N + 512:PW_N + 1024], ps["o1"][0:128, 0:512]),
              reads=[rpo[1]], writes=[rN])
        if k < KMAX:
            kb.op("scalar", lambda e: e.copy(pB[:, PW_M:PW_M + 512], ps["o2"][0:128, 0:512]),
                  reads=[rpo[2]], writes=[rM])
            kb.op("vector", lambda e: e.tensor_copy(pB[:, PW_M + 512:PW_M + 1024],
                                                    ps["o3"][0:128, 0:512]),
                  reads=[rpo[3]], writes=[rM])
        for vh in range(8):
            mm(lambda e, vh=vh: e.matmul(
                hm[vh // 4][0:128, (vh % 4) * 128:(vh % 4) * 128 + 128],
                pB[:, PW_N + vh * 128: PW_N + vh * 128 + 128],
                pB[:, PW_Y + vh * 128: PW_Y + vh * 128 + 128],
                start=(vh % 4 == 0), stop=(vh % 4 == 3)), [rN, rY], [rhm[vh // 4]])
        for h in range(2):
            op = (lambda e, h=h: e.tensor_tensor(
                pB[:, PW_Y + h * 512: PW_Y + (h + 1) * 512],
                pB[:, PW_Y + h * 512: PW_Y + (h + 1) * 512],
                hm[h][0:128, 0:512], ALU.add))
            kb.op("vector", op, reads=[rY, rhm[h]], writes=[rY, rBc])

    for vh in range(8):
        bank = ob[vh // 4]
        mm(lambda e, vh=vh, bank=bank: e.matmul(
            bank[0:128, (vh % 4) * 128:(vh % 4) * 128 + 128],
            t["rowsel8"][0:8, vh * 128:(vh + 1) * 128],
            t["rowsB"][0:8, 6 * 256 + c * 128: 6 * 256 + (c + 1) * 128],
            start=(vh % 4 == 0), stop=(vh % 4 == 3)), [rrB, rc], [rob[vh // 4]])
    for h in range(2):
        kb.op("vector", lambda e, h=h: e.tensor_tensor(
            pB[:, PW_A + h * 512: PW_A + (h + 1) * 512].rearrange(
                "p (k s x) -> p k s x", s=2, x=128),
            qch(h),
            ob[h][0:128, 0:512].rearrange("p (k s x) -> p k s x", s=2, x=128),
            ALU.mult),
            reads=[rqc[2 * h], rqc[2 * h + 1], rob[h]], writes=[rA, rsq])

    for vh in range(8):
        bank = ob[2 + vh // 4]
        mm(lambda e, vh=vh, bank=bank: e.matmul(
            bank[0:128, (vh % 4) * 128:(vh % 4) * 128 + 128],
            states[:, vh * 128:(vh + 1) * 128],
            pB[:, PW_A + vh * 128: PW_A + vh * 128 + 128],
            start=(vh % 4 == 0), stop=False), [rst[vh], rA], [rob[2 + vh // 4]])
        mm(lambda e, vh=vh, bank=bank: e.matmul(
            bank[0:128, (vh % 4) * 128:(vh % 4) * 128 + 128],
            pB[:, PW_Y + vh * 128: PW_Y + vh * 128 + 128],
            pB[:, PW_AT + vh * 128: PW_AT + vh * 128 + 128],
            start=False, stop=(vh % 4 == 3)), [rY, rT], [rob[2 + vh // 4]])

    kb.op("vector", lambda e: e.tensor_tensor(
        v8(pw(p, PW_M)), v8(states[:, 0:1024]), colv(p, c, 1), ALU.mult),
        reads=[rct] + rst, writes=[rM])
    for vh in range(8):
        mm(lambda e, vh=vh: e.matmul(
            hm[vh // 4][0:128, (vh % 4) * 128:(vh % 4) * 128 + 128],
            pB[:, PW_KTS + vh * 128: PW_KTS + vh * 128 + 128],
            pB[:, PW_Y + vh * 128: PW_Y + vh * 128 + 128],
            start=(vh % 4 == 0), stop=(vh % 4 == 3)), [rK, rY], [rhm[vh // 4]])
    for h in range(2):
        kb.op("vector", lambda e, h=h: e.tensor_tensor(
            states[:, h * 512:(h + 1) * 512], pB[:, PW_M + h * 512: PW_M + (h + 1) * 512],
            hm[h][0:128, 0:512], ALU.add),
            reads=[rM, rhm[h]], writes=rst[4 * h:4 * h + 4])

    kb.op("scalar", lambda e: e.activation(pB[:, PW_AT:PW_AT + 512], ps["o2"][0:128, 0:512],
                                           AF.Square), reads=[rpo[2]], writes=[rT])
    kb.op("scalar", lambda e: e.activation(pB[:, PW_AT + 512:PW_AT + 1024],
                                           ps["o3"][0:128, 0:512],
                                           AF.Square), reads=[rpo[3]], writes=[rT])
    for vh in range(8):
        mm(lambda e, vh=vh: e.matmul(
            ps["g"][0:8, 0:128], t["colsel1"][:, vh * 8:(vh + 1) * 8],
            pB[:, PW_AT + vh * 128: PW_AT + vh * 128 + 128],
            start=(vh == 0), stop=(vh == 7)), [rT, rc], [rpg])
    kb.op("scalar", lambda e, bc=ccol(p, kb, EPS6): e.activation(
        t["rowsF"][0:8, 5 * 256 + c * 128: 5 * 256 + (c + 1) * 128], ps["g"][0:8, 0:128],
        AF.Sqrt, bias=bc[0:8, :], scale=1.0 / 128), reads=[rpg, rc], writes=[rrF])
    kb.op("vector", lp(p, lambda e: e.reciprocal(
        t["rowsB"][0:8, 7 * 256 + c * 128: 7 * 256 + (c + 1) * 128],
        t["rowsF"][0:8, 5 * 256 + c * 128: 5 * 256 + (c + 1) * 128])),
        reads=[rrF], writes=[rrB])
    for vh in range(8):
        bank = ob[vh // 4]
        mm(lambda e, vh=vh, bank=bank: e.matmul(
            bank[0:128, (vh % 4) * 128:(vh % 4) * 128 + 128],
            t["rowsel8"][0:8, vh * 128:(vh + 1) * 128],
            t["rowsB"][0:8, 7 * 256 + c * 128: 7 * 256 + (c + 1) * 128],
            start=(vh % 4 == 0), stop=(vh % 4 == 3)), [rrB, rc], [rob[vh // 4]])
    for h in range(2):
        kb.op("scalar", lambda e, h=h: e.copy(
            pB[:, PW_N + h * 512: PW_N + (h + 1) * 512], ob[h][0:128, 0:512]),
            reads=[rob[h]], writes=[rN])
        kb.op("vector", lambda e, h=h: e.scalar_tensor_tensor(
            pF[:, h * 512:(h + 1) * 512], ob[2 + h][0:128, 0:512], t["gnwc"][:, 0:1],
            pB[:, PW_N + h * 512: PW_N + (h + 1) * 512], ALU.mult, ALU.mult),
            reads=[rob[2 + h], rN, rc], writes=[rPF])
    zv = v8(t["zbuf"][:, 0:2048], x=256)[:, :, sl]
    xout = v8(t["xnfm"][:, 0:8 * TH], x=TH)[:, :, loc0 + c * 128: loc0 + c * 128 + 128]
    kb.op("vector", lambda e, zv=zv, xout=xout: e.tensor_tensor(
        xout, v8(pF[:, 0:1024]), zv, ALU.mult),
        reads=[rPF] + [rz[vh] for vh in range(8)],
        writes=[rfm[(loc0 // 128 + c) * 8 + vh] for vh in range(8)])


def emit_attn_block_epilogue(p, kb, b, blk):
    t, cfg, ps, d = p.t, p.cfg, p.ps, p.d
    TH = cfg.T // 2
    NTH = cfg.NT // 2
    loc0 = blk * 256
    rc = R(p, "consts")
    rw = R(p, "attn_w")
    rfm = R(p, "xnfm_t", NTH * 8)
    rphs = R(p, "ps_h1")
    rpm2 = R(p, "ps_misc")
    rpTs = R(p, "ps_T")
    rpT = [rpTs] * 4
    rofm = R(p, "ofm_r", 2)
    routt = R(p, "outt_r")
    rlnr = R(p, "xnb2")
    for sub in range(2):
        tok = b * cfg.S + loc0 + sub * 128
        kb.dma("dgp", "lnr", lambda e, tok=tok: e.dma_start(
            out=t["xnb2"][:], in_=d["lnbuf"][tok:tok + 128, :]),
            reads=[R(p, "ln_dram")], writes=[rlnr])
        for dt in range(8):
            if dt % 2 == 0:
                ph = ps["h1"][0:128, 0:128]
            else:
                ph = ps["misc"][0:128, 0:128]
            for vt in range(8):
                kb.op("tensor", lambda e, dt=dt, vt=vt, o=ph, loc0=loc0, sub=sub: e.matmul(
                    o, wout_tile(p, vt, dt),
                    t["xnfm"][:, vt * TH + loc0 + sub * 128: vt * TH + loc0 + sub * 128 + 128],
                    start=(vt == 0), stop=(vt == 7)),
                    reads=[rw, rfm[(loc0 // 128 + sub) * 8 + vt]],
                    writes=[rphs if dt % 2 == 0 else rpm2])
            ofm = t["ofm"][:, (dt % 2) * 256:(dt % 2) * 256 + 128]
            kb.op("vector", lambda e, o=ph, a=ofm: e.tensor_copy(a, o),
                  reads=[rphs if dt % 2 == 0 else rpm2], writes=[rofm[dt % 2]])
            pT = ps["T"][0:128, (dt % 4) * 128:(dt % 4) * 128 + 128]
            kb.op("tensor", lambda e, a=ofm, o=pT: e.transpose(o, a, t["ident_bf"][:]),
                  reads=[rofm[dt % 2], rc], writes=[rpT[dt % 4]])
            kb.op("vector", lambda e, dt=dt, o=pT: e.tensor_add(
                t["outt"][:, dt * 128:(dt + 1) * 128], o, t["xnb2"][:, dt * 128:(dt + 1) * 128]),
                reads=[rpT[dt % 4], rlnr], writes=[routt])
        kb.dma("dsync", "out", lambda e, tok=tok: e.dma_start(
            out=d["hbuf"][tok:tok + 128, :], in_=t["outt"][:, 0:1024]),
            reads=[routt], writes=[R(p, "h_dram")])


def emit_attn_core(p, kb, src):
    cfg = p.cfg
    t = p.t
    for b in (1, 0) if cfg.B == 2 else range(cfg.B):
        emit_ln_half(p, kb, src, half=b, mode="lnzrms", store_dram=p.d["lnbuf"],
                     zw_col=t["wln1c"], src_res=[R(p, "dst_h", 2)[b]])
        rst = R(p, "states", 8)
        for vh in range(8):
            kb.op("vector", lambda e, vh=vh: e.memset(t["states"][:, vh * 128:(vh + 1) * 128], 0.0),
                  writes=[rst[vh]])
        for blk in range(cfg.S // 256):
            emit_attn_block_prologue(p, kb, b, blk)
            for c in range(2):
                emit_delta_chunk2(p, kb, b, blk, c)
            emit_attn_block_epilogue(p, kb, b, blk)


def emit_attn_moe(p, kb, r1src, dst, moe_trunc=None):
    t, cfg, ps, d = p.t, p.cfg, p.ps, p.d
    TH = cfg.T // 2
    NTH = cfg.NT // 2
    NT = cfg.NT
    T = cfg.T
    rc = R(p, "consts")
    rmv = R(p, "mv")
    rroute = R(p, "route")
    rrsum = R(p, "cumw")
    rrfm = R(p, "rowfm")
    reqb = R(p, "eqbf")
    rpTs = R(p, "ps_T")
    rpo = R(p, "ps_o", 4)
    rpm = R(p, "ps_misc")
    rphs = R(p, "ps_h1")
    rprods = R(p, "prods", 16)
    rrbc = R(p, "selbc")
    rofm = R(p, "ofm_r", 2)
    routt = R(p, "outt_r")
    rxres = R(p, "xres_r")
    rhres = R(p, "hres_r")
    rx2 = R(p, "x2_ring", 2)
    rmw = R(p, "moe_w")
    rmp = R(p, "moepart_dram")
    for hf in range(2):
        emit_ln_half(p, kb, d["hbuf"], hf, mode="rms", do_gate=("router",),
                     zw_col=t["wln2c"], store_fm_dram=True, src_res=[R(p, "h_dram")])
        t0 = hf * NTH
        kb.op("vector", lambda e, t0=t0: e.reduce_sum(
            t["cumw"][:, t0:t0 + NTH],
            t["mv"][:, t0 * 8:(t0 + NTH) * 8].rearrange("p (j e) -> p j e", e=8),
            axis=AX.X),
            reads=[rmv], writes=[rrsum])
        kb.op("vector", lambda e, t0=t0: e.reciprocal(t["cumw"][:, t0:t0 + NTH],
                                                      t["cumw"][:, t0:t0 + NTH]),
              reads=[rrsum], writes=[rrsum])
        kb.op("vector", lambda e, t0=t0: e.tensor_tensor(
            t["route"][:, t0 * 8:(t0 + NTH) * 8].rearrange("p (j e) -> p j e", e=8),
            t["mv"][:, t0 * 8:(t0 + NTH) * 8].rearrange("p (j e) -> p j e", e=8),
            t["cumw"][:, t0:t0 + NTH].unsqueeze(2).to_broadcast([128, NTH, 8]),
            ALU.mult),
            reads=[rmv, rrsum], writes=[rroute])
    if moe_trunc == "ln":
        return
    for ti in range(NT):
        kb.op("vector", lambda e, ti=ti: e.tensor_copy(t["eqbf"][:, ti * 8:(ti + 1) * 8],
                                                       t["route"][:, ti * 8:(ti + 1) * 8]),
              reads=[rroute], writes=[reqb])
        kb.op("tensor", lambda e, ti=ti: e.transpose(ps["T"][0:8, 512:640],
                                                     t["eqbf"][:, ti * 8:(ti + 1) * 8],
                                                     t["ident_bf"][:]),
              reads=[reqb, rc], writes=[rpTs])
        kb.op("scalar", lambda e, ti=ti: e.copy(t["rowfm"][0:8, ti * 128:(ti + 1) * 128],
                                                ps["T"][0:8, 512:640]),
              reads=[rpTs], writes=[rrfm])
    kb.barrier()
    if moe_trunc == "route":
        return
    for ih in range(1 if moe_trunc in ("w", "pa", "pb", "ih0", "st0", "st1", "st2") else 2):
        for e_i in range(NEXP):
            kb.dma("dgp", "mw", lambda e, ei=e_i, ih=ih: e.dma_start(
                out=t["wbuf"][:, ei * 2048:(ei + 1) * 2048].rearrange("p (dt c) -> p dt c", c=256),
                in_=d["aW_gate_e"][ei][:, ih * 256:(ih + 1) * 256].rearrange(
                    "(dt p) c -> p dt c", p=128)),
                writes=[rmw, R(p, "attn_w")])
            kb.dma("dgp", "mw", lambda e, ei=e_i, ih=ih: e.dma_start(
                out=t["wbuf"][:, 16384 + ei * 2048: 16384 + (ei + 1) * 2048].rearrange(
                    "p (dt c) -> p dt c", c=256),
                in_=d["aW_up_e"][ei][:, ih * 256:(ih + 1) * 256].rearrange(
                    "(dt p) c -> p dt c", p=128)),
                writes=[rmw, R(p, "attn_w")])
            kb.dma("dgp", "mw", lambda e, ei=e_i, ih=ih: e.dma_start(
                out=t["xnfm"][:, ei * 2048:(ei + 1) * 2048].rearrange("p (it c) -> p it c", c=1024),
                in_=d["aW_down_e"][ei][ih * 256:(ih + 1) * 256, :].rearrange(
                    "(it p) c -> p it c", p=128)),
                writes=[rmw] + R(p, "xnfm_t", (cfg.NT // 2) * 8))
        if moe_trunc == "w":
            return
        for tt in range(T // 256):
            tok0 = tt * 256
            x2t = t["xnb"][:, (tt % 2) * 1024: (tt % 2) * 1024 + 1024]
            x2t = t["zbuf"][:, 0:2048] if tt % 2 == 0 else None
            if x2t is None:
                x2t = t["accbuf"][:, 4096:6144]
            joins = (R(p, "z_t", 8) if tt % 2 == 0 else R(p, "qkvc_t", 16)[0:8])
            kb.dma("dsync", ("x2", tt % 2), lambda e, a=x2t, tok0=tok0: e.dma_start(
                out=a.rearrange("p (dt c) -> p dt c", c=256),
                in_=d["x2fm"][:, :, tok0:tok0 + 256].transpose([1, 0, 2])),
                reads=[R(p, "x2_dram")], writes=[rx2[tt % 2]] + joins)
            for e_i in range(NEXP):
                kb.op("tensor", lambda e, ei=e_i, tok0=tok0: e.matmul(
                    ps["o0"][0:128, 0:256], t["rowsel8"][0:8, ei * 128:(ei + 1) * 128],
                    t["rowfm"][0:8, tok0:tok0 + 256], start=True, stop=True),
                    reads=[rrfm, rc], writes=[rpo[0]])
                kb.op("vector", lambda e: e.tensor_copy(t["selbc"][:], ps["o0"][0:128, 0:256]),
                      reads=[rpo[0]], writes=[rrbc])
                for it in range(2):
                    pg_ = ps["h1"][0:128, it * 256:(it + 1) * 256]
                    pu_ = ps["misc"][0:128, it * 256:(it + 1) * 256]
                    for dt in range(8):
                        kb.op("tensor", lambda e, ei=e_i, it=it, dt=dt, o=pg_, x2t=x2t: e.matmul(
                            o, t["wbuf"][:, ei * 2048 + dt * 256 + it * 128:
                                         ei * 2048 + dt * 256 + it * 128 + 128],
                            x2t[:, dt * 256: dt * 256 + 256],
                            start=(dt == 0 and it == 0), stop=False if not (dt == 7 and it == 1) else True),
                            reads=[rmw, rx2[tt % 2]], writes=[rphs])
                        kb.op("tensor", lambda e, ei=e_i, it=it, dt=dt, o=pu_, x2t=x2t: e.matmul(
                            o, t["wbuf"][:, 16384 + ei * 2048 + dt * 256 + it * 128:
                                         16384 + ei * 2048 + dt * 256 + it * 128 + 128],
                            x2t[:, dt * 256: dt * 256 + 256],
                            start=(dt == 0 and it == 0), stop=False if not (dt == 7 and it == 1) else True),
                            reads=[rmw, rx2[tt % 2]], writes=[rpm])
                for it in range(2):
                    pg_ = ps["h1"][0:128, it * 256:(it + 1) * 256]
                    pu_ = ps["misc"][0:128, it * 256:(it + 1) * 256]
                    pr = prods_tile(p, e_i, it)
                    k_ = e_i * 2 + it
                    kb.op("scalar", lambda e, o=pg_, a=pr: e.activation(a, o, AF.Silu),
                          reads=[rphs],
                          writes=[rprods[k_], R(p, "raw_t", 16)[k_],
                                  R(p, "raw_t", 16)[max(k_ - 1, 0)]])
                    kb.op("vector", lambda e, o=pu_, a=pr: e.tensor_mul(a, a, o),
                          reads=[rpm, rprods[e_i * 2 + it]], writes=[rprods[e_i * 2 + it]])
                    kb.op("gpsimd", lambda e, a=pr: e.tensor_mul(a, a, t["selbc"][:]),
                          reads=[rrbc, rprods[e_i * 2 + it]], writes=[rprods[e_i * 2 + it]])
            if moe_trunc == "pa":
                continue
            for dt in range(8):
                po = ps[f"o{dt // 2}"][0:128, (dt % 2) * 256:(dt % 2) * 256 + 256]
                for e_i in range(NEXP):
                    for it in range(2):
                        kb.op("tensor", lambda e, ei=e_i, it=it, dt=dt, o=po: e.matmul(
                            o, t["xnfm"][:, ei * 2048 + it * 1024 + dt * 128:
                                         ei * 2048 + it * 1024 + dt * 128 + 128],
                            prods_tile(p, ei, it),
                            start=(ei == 0 and it == 0 and dt % 2 == 0),
                            stop=(ei == 7 and it == 1 and dt % 2 == 1)),
                            reads=[rmw, rprods[e_i * 2 + it]],
                            writes=[rpo[dt // 2]])
            if moe_trunc == "pb":
                continue
            if ih == 0:
                for dt in range(8):
                    po = ps[f"o{dt // 2}"][0:128, (dt % 2) * 256:(dt % 2) * 256 + 256]
                    ofm = t["ofm"][:, (dt % 2) * 256:(dt % 2) * 256 + 256]
                    eng = "vector" if dt % 2 == 0 else "scalar"
                    if moe_trunc == "st1":
                        eng = "vector"
                    elif moe_trunc == "st2":
                        eng = "scalar"
                    if eng == "vector":
                        kb.op(eng, lambda e, o=po, a=ofm: e.tensor_copy(a, o),
                              reads=[rpo[dt // 2]], writes=[rofm[dt % 2]])
                    else:
                        kb.op(eng, lambda e, o=po, a=ofm: e.copy(a, o),
                              reads=[rpo[dt // 2]], writes=[rofm[dt % 2]])
                    if moe_trunc == "st0":
                        continue
                    kb.dma("dsync", "mp", lambda e, dt=dt, tok0=tok0, a=ofm: e.dma_start(
                        out=d["moepart"][dt, :, tok0:tok0 + 256], in_=a),
                        reads=[rofm[dt % 2]], writes=[rmp])
            else:
                rmw2 = R(p, "moep_w")
                rofw = R(p, "ofm_wide")
                rxnb2s = R(p, "xnb_ring", 2)
                rpr = R(p, "prods", 16)
                for sub in range(2):
                    tok = tok0 + sub * 128
                    kb.dma("dsync", "xr", lambda e, tok=tok: e.dma_start(
                        out=t["xres"][:], in_=r1src[tok:tok + 128, :]),
                        reads=[R(p, "dst_h", 2)[tok // (cfg.T // 2)]], writes=[rxres])
                    kb.dma("dgp", "hr", lambda e, tok=tok: e.dma_start(
                        out=t["sq_scr"][:, 0:1024], in_=d["hbuf"][tok:tok + 128, :]),
                        reads=[R(p, "h_dram")], writes=[rhres])
                    kb.dma("dgp", "mp2", lambda e, tok=tok: e.dma_start(
                        out=t["xnb"][:, 0:1024].rearrange("p (dt x) -> p dt x", x=128),
                        in_=d["moepart"][:, :, tok:tok + 128].transpose([1, 0, 2])),
                        reads=[rmp], writes=[rmw2, rxnb2s[0], rxnb2s[1]])
                    ofm_w = t["accbuf"][:, 0:1024]
                    for bk in range(4):
                        po = ps[f"o{bk}"][0:128, 0:512].rearrange(
                            "p (d x) -> p d x", x=256)[:, :, sub * 128:sub * 128 + 128]
                        kb.op("vector", lambda e, bk=bk, o=po: e.tensor_add(
                            ofm_w[:, bk * 256:(bk + 1) * 256].rearrange(
                                "p (d x) -> p d x", x=128),
                            o,
                            t["xnb"][:, bk * 256:(bk + 1) * 256].rearrange(
                                "p (d x) -> p d x", x=128)),
                            reads=[rpo[bk], rmw2],
                            writes=[rofw, rpr[bk]])
                    for dt in range(8):
                        kb.op("tensor", lambda e, dt=dt: e.transpose(
                            ps["T"][0:128, dt * 128:(dt + 1) * 128],
                            ofm_w[:, dt * 128:(dt + 1) * 128], t["ident_bf"][:]),
                            reads=[rofw, rc], writes=[rpTs])
                    kb.op("vector", lambda e: e.tensor_add(
                        t["outt"][:, 0:1024], ps["T"][0:128, 0:1024], t["sq_scr"][:, 0:1024]),
                        reads=[rpTs, rhres], writes=[routt])
                    kb.op("gpsimd", lambda e: e.tensor_add(
                        t["outt"][:, 0:1024], t["outt"][:, 0:1024], t["xres"][:, 0:1024]),
                        reads=[rxres, routt], writes=[routt])
                    kb.dma("dsync", "out", lambda e, tok=tok: e.dma_start(
                        out=dst[tok:tok + 128, :], in_=t["outt"][:, 0:1024]),
                        reads=[routt],
                        writes=[R(p, "dst_dram"), R(p, "dst_h", 2)[tok // (cfg.T // 2)]])


def prods_tile(p, e, it):
    c = (e * 2 + it) * 256
    return p.t["accbuf"][:, c:c + 256]
